# version 02
# baseline (speedup 1.0000x reference)
import numpy as np
import concourse.bass as bass
import concourse.mybir as mybir
from concourse.bass_utils import run_bass_kernel_spmd

F32 = mybir.dt.float32
BF16 = mybir.dt.bfloat16
AF = mybir.ActivationFunctionType
ALU = mybir.AluOpType
AX = mybir.AxisListType

S, D, DH, H, DK, DFF, NE = 4096, 1024, 512, 8, 64, 4096, 8
NCH, NSL = 32, 8

ALPHA = 0.1
BETA = 0.1
A11 = 1.0 - ALPHA
A12 = 1.0 - ALPHA
A21 = -(ALPHA * BETA)
A22 = 1.0 - ALPHA * BETA
B1 = ALPHA
B2 = ALPHA * BETA
LNEPS = 1e-5


class Sem:
    REG = None

    def __init__(self, es, nc, name, step=1):
        self.h = es.enter_context(nc.semaphore(name))
        self.n = 0
        self.step = step
        if Sem.REG is not None:
            Sem.REG.append(self)

    def inc(self, inst, k=1):
        inst.then_inc(self.h, self.step * k)
        self.n += self.step * k
        return self.n

    def bump(self, k=1):
        self.n += self.step * k
        return self.n

    def wait(self, eng, val=None):
        eng.wait_ge(self.h, self.n if val is None else val)


class PSem:

    def __init__(self, es, nc, name):
        self.s = [Sem(es, nc, name + "a", 16), Sem(es, nc, name + "b", 16)]

    def inc(self, inst, i):
        return self.s[i % 2].inc(inst)

    def bump(self, i, k=1):
        return self.s[i % 2].bump(k)

    def wait(self, eng, i, val):
        eng.wait_ge(self.s[i % 2].h, val)


class Arena:
    def __init__(self, ap, elems):
        self.ap, self.elems, self.off, self.stack = ap, elems, 0, []

    def alloc(self, shape):
        n = int(np.prod(shape[1:]))
        assert self.off + n <= self.elems, (self.off, n, self.elems)
        flat = self.ap[:, self.off:self.off + n]
        self.off += n
        if len(shape) == 3:
            flat = flat.rearrange("p (a b) -> p a b", a=shape[1], b=shape[2])
        return flat

    def at(self, off, shape):
        n = int(np.prod(shape[1:]))
        assert off + n <= self.elems
        flat = self.ap[:, off:off + n]
        if len(shape) == 3:
            flat = flat.rearrange("p (a b) -> p a b", a=shape[1], b=shape[2])
        return flat

    def push(self):
        self.stack.append(self.off)

    def pop(self):
        self.off = self.stack.pop()


def build_program(debug=(), stop_after=None):
    nc = bass.Bass("TRN2", target_bir_lowering=False)

    x_in = nc.dram_tensor("x", [S, D], F32, kind="ExternalInput").ap()
    out_hbm = nc.dram_tensor("out", [S, D], F32, kind="ExternalOutput").ap()

    def din(name, shape):
        return nc.dram_tensor(name, shape, F32, kind="ExternalInput").ap()

    wq = [din("wq1", [DH, DH]), din("wq2", [DH, DH])]
    wk = [din("wk1", [DH, DH]), din("wk2", [DH, DH])]
    wv = [din("wv1", [DH, DH]), din("wv2", [DH, DH])]
    wo = [din("wo1", [DH, DH]), din("wo2", [DH, DH])]
    bqi = [din("bq1", [DH]), din("bq2", [DH])]
    bki = [din("bk1", [DH]), din("bk2", [DH])]
    bvi = [din("bv1", [DH]), din("bv2", [DH])]
    boi = [din("bo1", [DH]), din("bo2", [DH])]
    gbi = [(din("g1", [DH]), din("be1", [DH])), (din("g2", [DH]), din("be2", [DH]))]
    wff1_in = din("wff1", [D, DFF]); bff1_in = din("bff1", [DFF])
    wff2_in = din("wff2", [DFF, D]); bff2_in = din("bff2", [D])
    gff_in = din("gff", [D]); beff_in = din("beff", [D])
    wr_in = din("wr", [D, D]); wkk_in = din("wkk", [D, D]); wvv_in = din("wvv", [D, D])
    mur_in = din("mur", [D]); muk_in = din("muk", [D])
    wel_in = din("wel", [D, 24]); bel_in = din("bel", [24])
    wle_in = din("wle", [24, D]); ble_in = din("ble", [D])
    hmt_in = din("hmt", [24, 12])
    wgred_in = din("wgred", [D, 16])
    wgm_in = din("wgm", [NE, 16]); wgt_in = din("wgt", [16, NE])
    b1m_in = din("b1m", [NE, DFF]); b2m_in = din("b2m", [NE, D])
    w1m_in = din("w1m", [NE, D, DFF]); w2m_in = din("w2m", [NE, DFF, D])
    ident_in = din("ident", [128, 128])
    x1sp = nc.dram_tensor("x1sp", [S, DH], BF16, kind="Internal").ap()
    x2sp = nc.dram_tensor("x2sp", [S, DH], BF16, kind="Internal").ap()
    xshsp = nc.dram_tensor("xshsp", [S, DH], BF16, kind="Internal").ap()

    dbg = {}
    for name, shape in debug:
        dbg[name] = nc.dram_tensor("dbg_" + name, list(shape), F32,
                                   kind="ExternalOutput").ap()

    from contextlib import ExitStack
    es = ExitStack()

    BFA_N, FPA_N = 84480, 7936
    BFAt = es.enter_context(nc.sbuf_tensor("BFA", [128, BFA_N], BF16))
    FPAt = es.enter_context(nc.sbuf_tensor("FPA", [128, FPA_N], F32))
    bfa = Arena(BFAt[:, :], BFA_N)
    fpa = Arena(FPAt[:, :], FPA_N)
    carry = fpa.alloc([128, 2, 512])

    ident = es.enter_context(nc.sbuf_tensor("idents", [128, 128], F32))
    ones1 = es.enter_context(nc.sbuf_tensor("ones1r", [1, 512], F32))
    ones1b = es.enter_context(nc.sbuf_tensor("ones1b", [1, 512], BF16))
    onescb = es.enter_context(nc.sbuf_tensor("onescb", [128, 1], BF16))
    srow = es.enter_context(nc.sbuf_tensor("srow", [1, 64], F32))
    spow = es.enter_context(nc.sbuf_tensor("spow", [2, 8, 2], F32))
    tpow = es.enter_context(nc.sbuf_tensor("tpow", [2, 8, 2], F32))
    cpair = es.enter_context(nc.sbuf_tensor("cpair", [2, 520], F32))
    rseq = es.enter_context(nc.sbuf_tensor("rseq", [2, 128], F32))
    e1seq = es.enter_context(nc.sbuf_tensor("e1seq", [2, 128], F32))
    cpt = es.enter_context(nc.sbuf_tensor("cpt", [2, 128], F32))
    w2t = es.enter_context(nc.sbuf_tensor("w2t", [128, 2], F32))
    w2ct = es.enter_context(nc.sbuf_tensor("w2ct", [128, 2], F32))
    t0t = es.enter_context(nc.sbuf_tensor("t0t", [128, 128], BF16))
    t0ct = es.enter_context(nc.sbuf_tensor("t0ct", [128, 128], BF16))
    wrow = es.enter_context(nc.sbuf_tensor("wrow", [1, 128], F32))
    epscol = es.enter_context(nc.sbuf_tensor("epscol", [128, 1], F32))

    ln0 = es.enter_context(nc.sbuf_tensor("ln0s", [128, NCH, 2], F32))
    gatG = es.enter_context(nc.sbuf_tensor("gatG", [128, NCH, 8], F32))

    PSQ = es.enter_context(nc.psum_tensor("PSQ", [128, 2048], F32))
    PB = [es.enter_context(nc.psum_tensor(f"PB{i}", [128, 512], F32)) for i in range(4)]

    Sem.REG = []
    sQ = Sem(es, nc, "sQ", 16)
    sT = Sem(es, nc, "sT", 16)
    sG = Sem(es, nc, "sG", 16)
    cP = Sem(es, nc, "cP")
    cV = Sem(es, nc, "cV")
    cA = Sem(es, nc, "cA")
    cG = Sem(es, nc, "cG")
    cB = Sem(es, nc, "cB")

    def phase_barrier():
        sems = [(s_.h, s_.n) for s_ in Sem.REG if s_.n > 0 and s_ is not cB]
        @block.sync
        def _(s):
            for h_, n_ in sems:
                s.wait_ge(h_, n_)
            s.nop().then_inc(cB.h, 1)
        cB.bump()
        return cB.n

    block = es.enter_context(nc.Block())

    @block.sync
    def _(s):
        s.dma_start(out=ident[:, :], in_=ident_in[:, :]).then_inc(sQ.h, 16)
        sQ.bump()
    sq_ident = sQ.n

    @block.vector
    def _(v):
        v.memset(ones1[:, :], 1.0)
        v.memset(ones1b[:, :], 1.0)
        v.memset(onescb[:, :], 1.0)
        vals = [A11, A12, A21, A22, A11, A21, A12, A22,
                1.0, 0.0, 0.0, 1.0, B1, B2]
        for i, val in enumerate(vals):
            v.memset(srow[0:1, i:i + 1], val)
        v.memset(cpair[:, :], 0.0)
        v.memset(rseq[:, :], 0.0)
        v.memset(e1seq[:, :], 0.0)
        v.memset(cpt[:, :], 0.0)
        v.memset(carry[:, :, :], 0.0)
        v.memset(t0t[:, :], 0.0)
        v.memset(epscol[:, :], LNEPS)
        v.drain().then_inc(cV.h, 1)
        cV.bump()
    cv0 = cV.n

    @block.gpsimd
    def _(g):
        g.wait_ge(cV.h, cv0)
        g.dma_start(out=spow[:, 0, :], in_=srow[0:1, 0:4]).then_inc(sG.h, 16)
        g.dma_start(out=tpow[:, 0, :], in_=srow[0:1, 4:8]).then_inc(sG.h, 16)
        g.dma_start(out=cpair[:, 0:2], in_=srow[0:1, 8:12]).then_inc(sG.h, 16)
        g.dma_start(out=rseq[:, 127:128], in_=srow[0:1, 12:14]).then_inc(sG.h, 16)
        sG.bump(4)
    sg0 = sG.n

    cp_base = cP.n
    @block.tensor
    def _(t):
        t.wait_ge(sG.h, sg0)
        for k in range(7):
            if k > 0:
                t.wait_ge(cV.h, cv0 + k)
            m = 1 << k
            t.matmul(PB[0][0:2, 0:2], tpow[:, k, :], spow[:, k, :], start=True, stop=True)
            t.matmul(PB[0][0:2, 4:6], spow[:, k, :], tpow[:, k, :], start=True, stop=True)
            t.matmul(PB[1][0:2, 0:2 * m], tpow[:, k, :], cpair[:, 0:2 * m],
                     start=True, stop=True)
            mm = t.matmul(PB[2][0:2, 0:m], tpow[:, k, :], rseq[:, 128 - m:128],
                          start=True, stop=True)
            cP.inc(mm)

    @block.vector
    def _(v):
        for k in range(7):
            v.wait_ge(cP.h, cp_base + k + 1)
            m = 1 << k
            v.tensor_copy(spow[:, k + 1, :], PB[0][0:2, 0:2])
            v.tensor_copy(tpow[:, k + 1, :], PB[0][0:2, 4:6])
            v.tensor_copy(cpair[:, 2 * m:4 * m], PB[1][0:2, 0:2 * m])
            v.tensor_copy(rseq[:, 128 - 2 * m:128 - m], PB[2][0:2, 0:m])
            v.drain().then_inc(cV.h, 1)
            cV.bump()
    cv_dbl = cV.n

    @block.tensor
    def _(t):
        t.wait_ge(cV.h, cv_dbl)
        mm = t.matmul(PB[1][0:2, 0:2], tpow[:, 7, :], cpair[:, 0:2], start=True, stop=True)
        cP.inc(mm)
    cp_a128 = cP.n

    @block.vector
    def _(v):
        v.wait_ge(cP.h, cp_a128)
        v.tensor_copy(cpair[:, 256:258], PB[1][0:2, 0:2])
        v.tensor_scalar(wrow[0:1, :], cpair[0:1, 0:256:2], B1, None, ALU.mult)
        v.drain()
        v.scalar_tensor_tensor(wrow[0:1, :], cpair[0:1, 1:257:2], B2, wrow[0:1, :],
                               ALU.mult, ALU.add)
        v.tensor_copy(cpt[0:1, :], cpair[0:1, 2:258:2])
        v.tensor_copy(e1seq[0:1, :], cpair[0:1, 0:256:2])
        v.drain().then_inc(cV.h, 1)
        cV.bump()
    cv_rows = cV.n

    @block.tensor
    def _(t):
        t.wait_ge(cV.h, cv_rows)
        t.wait_ge(sQ.h, sq_ident)
        t.matmul(PB[0][0:2, 0:128], srow[0:1, 8:10], cpair[0:1, 2:258:2],
                 start=True, stop=False)
        t.matmul(PB[0][0:2, 0:128], srow[0:1, 10:12], cpair[0:1, 3:259:2],
                 start=False, stop=True)
        t.matmul(PB[1][0:2, 0:128], srow[0:1, 8:10], cpair[0:1, 0:256:2],
                 start=True, stop=False)
        t.matmul(PB[1][0:2, 0:128], srow[0:1, 10:12], cpair[0:1, 1:257:2],
                 start=False, stop=True)
        t.transpose(PB[2][0:128, 0:2], rseq[0:2, :], ident[0:2, 0:2])
        mm = t.matmul(PB[3][0:1, 0:2], cpair[:, 254:255], ident[0:2, 0:2],
                      start=True, stop=True)
        cP.inc(mm)
    cp_rows2 = cP.n

    @block.vector
    def _(v):
        v.wait_ge(cP.h, cp_rows2)
        v.tensor_copy(cpt[:, :], PB[0][0:2, 0:128])
        v.tensor_copy(e1seq[:, :], PB[1][0:2, 0:128])
        v.tensor_copy(w2t[:, :], PB[2][0:128, 0:2])
        v.tensor_copy(srow[0:1, 16:18], PB[3][0:1, 0:2])
        cc = v.tensor_copy(t0t[0:1, 0:1], wrow[0:1, 0:1])
        cV.inc(cc)
    sg_rows = sG.n
    cv_rows2 = cV.n

    cp_t0b = cP.n
    sg_t0b = sG.n
    cv_t0b = cV.n
    @block.tensor
    def _(t):
        t.wait_ge(cV.h, cv_rows2)
        for k in range(7):
            m = 1 << k
            if k > 0:
                t.wait_ge(sG.h, sg_t0b + 16 * k)
            mm = t.matmul(PB[3][0:m, 0:m], rseq[:, 127 - m:127], e1seq[:, 0:m],
                          start=True, stop=True)
            cP.inc(mm)

    @block.vector
    def _(v):
        for k in range(7):
            m = 1 << k
            v.wait_ge(cP.h, cp_t0b + k + 1)
            if k > 0:
                v.wait_ge(sG.h, sg_t0b + 16 * k)
            cc = v.tensor_copy(t0t[0:m, m:2 * m], PB[3][0:m, 0:m])
            cV.inc(cc)

    @block.gpsimd
    def _(g):
        for k in range(7):
            m = 1 << k
            g.wait_ge(cV.h, cv_t0b + k + 1)
            g.dma_start(out=t0t[m:2 * m, m:2 * m], in_=t0t[0:m, 0:m]).then_inc(sG.h, 16)
            sG.bump()
    sg_t0d = sG.n

    @block.vector
    def _(v):
        v.wait_ge(sG.h, sg_t0d)
        v.tensor_copy(t0ct[:, :], t0t[:, :])
        v.drain()
        v.tensor_copy(t0ct[0:1, :], cpair[0:1, 0:256:2])
        v.tensor_copy(w2ct[:, :], w2t[:, :])
        v.drain()
        v.tensor_copy(w2ct[0:1, 0:2], srow[0:1, 16:18])
        v.drain().then_inc(cV.h, 1)
        cV.bump()
    cv_t0c = cV.n
    sg_c0 = sG.n
    cv_c0 = cV.n

    def tap_small(name, src_ap):
        if name not in dbg:
            return
        @block.sync
        def _(s):
            s.wait_ge(sG.h, sg_c0)
            s.wait_ge(cV.h, cv_c0)
            s.dma_start(out=dbg[name][:, :], in_=src_ap).then_inc(sQ.h, 16)
            sQ.bump()

    tap_small("cpair", cpair[:, :])
    tap_small("rseq", rseq[:, :])
    tap_small("w2t", w2t[:, :])
    tap_small("w2ct", w2ct[:, :])
    tap_small("cpt", cpt[:, :])
    tap_small("wrow", wrow[:, :])

    if "t0t" in dbg or "t0ct" in dbg:
        fpa.push()
        tf = fpa.alloc([128, 128])
        tf2 = fpa.alloc([128, 128])
        @block.scalar
        def _(sc):
            sc.wait_ge(cV.h, cv_c0)
            sc.wait_ge(sG.h, sg_c0)
            sc.copy(tf, t0t[:, :])
            a = sc.copy(tf2, t0ct[:, :])
            cA.inc(a)
        ca_t0 = cA.n
        @block.sync
        def _(s):
            s.wait_ge(cA.h, ca_t0)
            if "t0t" in dbg:
                s.dma_start(out=dbg["t0t"][:, :], in_=tf).then_inc(sQ.h, 16)
                sQ.bump()
            if "t0ct" in dbg:
                s.dma_start(out=dbg["t0ct"][:, :], in_=tf2).then_inc(sQ.h, 16)
                sQ.bump()
        fpa.pop()

    XBF = bfa.at(0, [128, NCH, 1024])
    R1 = 32768

    fpa.push()
    xfp = [fpa.alloc([128, 1024]) for _ in range(2)]
    bnst = fpa.alloc([128, 2, 12])
    bnag = fpa.alloc([128, 2, 2])
    sdc = fpa.alloc([128, 2, 1])

    qx = PSem(es, nc, "qx")
    a1_v0, a1_a0, a1_g0 = cV.n, cA.n, cG.n

    @block.sync
    def _(s):
        s.wait_ge(sQ.h, sQ.n)
        for c in range(NCH):
            if c >= 2:
                s.wait_ge(cV.h, a1_v0 + 2 * (c - 1))
                s.wait_ge(cG.h, a1_g0 + (c - 1))
            qx.inc(s.dma_start(out=xfp[c % 2], in_=x_in[128 * c:128 * (c + 1), :]), c)

    @block.vector
    def _(v):
        for c in range(NCH):
            qx.wait(v, c, 16 * (c // 2 + 1))
            p = c % 2
            v.bn_stats(bnst[:, p, 0:6], xfp[p][:, 0:512])
            v.bn_stats(bnst[:, p, 6:12], xfp[p][:, 512:1024])
            v.drain()
            v.bn_aggr(bnag[:, p, :], bnst[:, p, 0:12])
            v.drain()
            v.tensor_copy(ln0[:, c, 0:1], bnag[:, p, 0:1])
            v.drain().then_inc(cV.h, 1)
            cV.bump()
            if c >= 1:
                v.wait_ge(cA.h, a1_a0 + c)
                rr = v.reciprocal(ln0[:, c - 1, 1:2], sdc[:, (c - 1) % 2, :])
                cV.inc(rr)
        v.wait_ge(cA.h, a1_a0 + NCH)
        rr = v.reciprocal(ln0[:, NCH - 1, 1:2], sdc[:, (NCH - 1) % 2, :])
        cV.inc(rr)
    a1_v1 = cV.n

    @block.scalar
    def _(sc):
        for c in range(NCH):
            sc.wait_ge(cV.h, a1_v0 + 2 * c + 1)
            a = sc.activation(sdc[:, c % 2, :], bnag[:, c % 2, 1:2], AF.Sqrt,
                              bias=epscol[:, :])
            cA.inc(a)

    @block.gpsimd
    def _(g):
        for c in range(NCH):
            qx.wait(g, c, 16 * (c // 2 + 1))
            cc = g.tensor_copy(XBF[:, c, :], xfp[c % 2])
            cG.inc(cc)
    a1_g1 = cG.n
    fpa.pop()


    if "ln0m" in dbg:
        fpa.push()
        tf = fpa.alloc([128, 32])
        tf2 = fpa.alloc([128, 32])
        @block.vector
        def _(v):
            v.wait_ge(cV.h, a1_v1)
            v.wait_ge(cG.h, a1_g1)
            v.tensor_copy(tf, ln0[:, :, 0])
            cc = v.tensor_copy(tf2, ln0[:, :, 1])
            cV.inc(cc)
        vv = cV.n
        @block.sync
        def _(s):
            s.wait_ge(cV.h, vv)
            s.dma_start(out=dbg["ln0m"][:, :], in_=tf).then_inc(sQ.h, 16)
            s.dma_start(out=dbg["ln0r"][:, :], in_=tf2).then_inc(sQ.h, 16)
            sQ.bump(2)
        fpa.pop()

    bfa.off = R1
    bfa.push()
    ubf = [bfa.alloc([128, 512]) for _ in range(2)]
    xst = [bfa.alloc([128, 512]) for _ in range(2)]
    fpa.push()
    ufp = [fpa.alloc([128, 512]) for _ in range(2)]
    qH = PSem(es, nc, "qH")

    h_v0, h_p0 = cV.n, cP.n
    h_q0 = sQ.n

    @block.vector
    def _(v):
        v.drain()
        v.wait_ge(sQ.h, h_q0)
        v.wait_ge(cG.h, a1_g1)
        for c in range(NCH):
            if c >= 1:
                v.wait_ge(cP.h, h_p0 + 2 * c)
                cc = v.tensor_copy(carry[0:2, c % 2, :], PB[1][0:2, :])
                cV.inc(cc)
                if c >= 3:
                    qH.wait(v, c - 1, 16 * ((c - 3) // 2 + 1))
                cc = v.tensor_copy(xst[(c - 1) % 2], PB[0][:, :])
                cV.inc(cc)
            p = c % 2
            v.tensor_scalar(ufp[p], XBF[:, c, 0:512], ln0[:, c, 0:1],
                            ln0[:, c, 1:2], ALU.subtract, ALU.mult)
            v.drain()
            cc = v.tensor_copy(ubf[p], ufp[p])
            cV.inc(cc)
        v.wait_ge(cP.h, h_p0 + 2 * NCH)
        qH.wait(v, NCH - 1, 16 * ((NCH - 3) // 2 + 1))
        cc = v.tensor_copy(xst[(NCH - 1) % 2], PB[0][:, :])
        cV.inc(cc)
    h_v2 = cV.n

    @block.gpsimd
    def _(g):
        for c in range(NCH):
            if c < NCH - 1:
                g.wait_ge(cV.h, h_v0 + 3 * (c + 1))
            else:
                g.wait_ge(cV.h, h_v2)
            qH.inc(g.dma_start(out=xshsp[128 * c:128 * (c + 1), :],
                               in_=xst[c % 2]), c)
    

    @block.tensor
    def _(t):
        for c in range(NCH):
            t.wait_ge(cV.h, h_v0 + 3 * c + 1)
            p = c % 2
            tt = t0ct if c == 0 else t0t
            ww = w2ct if c == 0 else w2t
            t.matmul(PB[0][:, :], tt[:, :], ubf[p], start=True, stop=False)
            mm = t.matmul(PB[0][:, :], cpt[:, :], carry[0:2, c % 2, :],
                          start=False, stop=True)
            cP.inc(mm)
            t.matmul(PB[1][0:2, :], ww[:, :], ufp[p], start=True, stop=False)
            mm = t.matmul(PB[1][0:2, :], tpow[:, 7, :], carry[0:2, c % 2, :],
                          start=False, stop=True)
            cP.inc(mm)
    h_p1 = cP.n

    bfa.pop()
    fpa.pop()


    def emit_mha(li, dst_sp):
        bfa.push()
        fpa.push()
        K1T = bfa.alloc([128, 4, S])
        V1 = bfa.alloc([128, NCH, 520])
        wqb = bfa.alloc([128, 4, 512])
        wob = bfa.alloc([128, 8, 512])
        tmsl = [bfa.alloc([128, 4, 512])]
        tsl = [bfa.alloc([128, 4, 512])]
        wcb = [bfa.alloc([128, 1024]) for _ in range(2)]
        browb = bfa.alloc([1, 4, 512])
        qtt = bfa.alloc([128, 512])
        aq = [bfa.alloc([128, 1024]) for _ in range(2)]
        ohb = bfa.alloc([128, 512])
        stg = [bfa.alloc([128, 512]) for _ in range(2)]

        wfs = [fpa.alloc([128, 1024]) for _ in range(2)]
        bqkc = fpa.alloc([128, 4, 2])
        bcB = fpa.alloc([128, 512])
        bcG = fpa.alloc([128, 512])
        bcE = fpa.alloc([128, 512])
        oacc = fpa.alloc([128, 4, 512])
        denA = fpa.alloc([128, 512])
        rcq = fpa.alloc([128, 4])
        sfp = fpa.alloc([128, 512])
        stat = fpa.alloc([128, 18])

        qW = PSem(es, nc, "qW" + str(li))
        qK = PSem(es, nc, "qK" + str(li))
        qM = PSem(es, nc, "qM" + str(li))
        tT = PSem(es, nc, "tT" + str(li))
        tQ = Sem(es, nc, "tQ" + str(li), 16)
        qO = PSem(es, nc, "qO" + str(li))

        wqi, wki, wvi, woi = wq[li], wk[li], wv[li], wo[li]
        bqI, bkI, bvI, boI = bqi[li], bki[li], bvi[li], boi[li]
        gI, beI = gbi[li]

        ebar = phase_barrier()

        mw_g0 = cG.n
        @block.sync
        def _(s):
            s.wait_ge(cB.h, ebar)
            for i in range(12):
                if i >= 2:
                    s.wait_ge(cG.h, mw_g0 + i - 1)
                if i < 4:
                    dd = s.dma_start(out=wfs[i % 2][:, 0:512],
                                     in_=wqi[128 * i:128 * (i + 1), :])
                else:
                    hh = i - 4
                    dd = s.dma_start(out=wfs[i % 2][0:64, 0:512],
                                     in_=woi[64 * hh:64 * (hh + 1), :])
                qW.inc(dd, i)
            for c_ in range(4):
                s.dma_start(out=bqkc[:, c_, 0:1],
                            in_=bqI[128 * c_:128 * (c_ + 1)].unsqueeze(1)).then_inc(sQ.h, 16)
                s.dma_start(out=bqkc[:, c_, 1:2],
                            in_=bkI[128 * c_:128 * (c_ + 1)].unsqueeze(1)).then_inc(sQ.h, 16)
            sQ.bump(8)
        mw_q1 = sQ.n

        @block.gpsimd
        def _(g):
            g.wait_ge(cB.h, ebar)
            g.dma_start(out=browb[0:1, 0, :], in_=bvI.unsqueeze(0)).then_inc(sG.h, 16)
            g.dma_start(out=browb[0:1, 1, :], in_=boI.unsqueeze(0)).then_inc(sG.h, 16)
            g.dma_start(out=browb[0:1, 2, :], in_=gI.unsqueeze(0)).then_inc(sG.h, 16)
            g.dma_start(out=browb[0:1, 3, :], in_=beI.unsqueeze(0)).then_inc(sG.h, 16)
            sG.bump(4)
            for i in range(12):
                qW.wait(g, i, 16 * (i // 2 + 1))
                if i < 4:
                    cc = g.tensor_copy(wqb[:, i, :], wfs[i % 2][:, 0:512])
                else:
                    hh = i - 4
                    cc = g.tensor_copy(wob[0:64, hh, :], wfs[i % 2][0:64, 0:512])
                cG.inc(cc)
        mw_g1 = cG.n
        mw_sg1 = sG.n

        @block.vector
        def _(v):
            v.drain()
            v.wait_ge(cB.h, ebar)
            v.wait_ge(sQ.h, mw_q1)
            v.tensor_scalar(bqkc[:, :, 0:1], bqkc[:, :, 0:1], 0.125, None, ALU.mult)
            v.memset(V1[:, :, 64:520:65], 1.0)
            v.drain().then_inc(cV.h, 1)
            cV.bump()
        mw_v1 = cV.n

        @block.tensor
        def _(t):
            t.wait_ge(cB.h, ebar)
            t.wait_ge(sG.h, mw_sg1)
            t.matmul(PB[0][:, :], ones1b[0:1, 0:128], browb[0:1, 1, :], start=True, stop=True)
            t.matmul(PB[1][:, :], ones1b[0:1, 0:128], browb[0:1, 2, :], start=True, stop=True)
            mm = t.matmul(PB[2][:, :], ones1b[0:1, 0:128], browb[0:1, 3, :], start=True, stop=True)
            cP.inc(mm)
        mw_p1 = cP.n

        @block.vector
        def _(v):
            v.wait_ge(cP.h, mw_p1)
            v.tensor_copy(bcB, PB[0][:, :])
            v.tensor_copy(bcG, PB[1][:, :])
            cc = v.tensor_copy(bcE, PB[2][:, :])
            cV.inc(cc)
        mw_v2 = cV.n

        kv_p0, kv_v0, kv_a0, kv_g0 = cP.n, cV.n, cA.n, cG.n
        NVI = 5 if li == 0 else 4
        for ts in range(NSL):
            if li == 0:
                @block.vector
                def _(v):
                    v.drain()
                    if ts == 0:
                        v.wait_ge(cV.h, mw_v2)
                    if ts >= 1:
                        v.wait_ge(tT.s[0].h, 256 * ts)
                    for tc in range(4):
                        c = 4 * ts + tc
                        cc = v.tensor_scalar(tmsl[0][:, tc, :], XBF[:, c, 0:512],
                                             ln0[:, c, 0:1], ln0[:, c, 1:2],
                                             ALU.subtract, ALU.mult)
                    cV.inc(cc)
            else:
                @block.sync
                def _(s):
                    if ts >= 1:
                        s.wait_ge(tT.s[0].h, 256 * ts)
                    for tc in range(4):
                        r0 = 512 * ts + 128 * tc
                        qM.s[0].inc(s.dma_start(out=tmsl[0][:, tc, :],
                                                in_=xshsp[r0:r0 + 128, :]))

            @block.sync
            def _(s):
                for cin in range(4):
                    gidx = 4 * ts + cin
                    s.wait_ge(cG.h, kv_g0 + max(0, gidx - 1))
                    d1 = s.dma_start(out=wfs[gidx % 2][:, 0:512],
                                     in_=wki[128 * cin:128 * (cin + 1), :])
                    qK.inc(d1, gidx)
                    d2 = s.dma_start(out=wfs[gidx % 2][:, 512:1024],
                                     in_=wvi[128 * cin:128 * (cin + 1), :])
                    qK.inc(d2, gidx)

            @block.gpsimd
            def _(g):
                for cin in range(4):
                    gidx = 4 * ts + cin
                    qK.wait(g, gidx, 32 * (gidx // 2 + 1))
                    if gidx >= 2:
                        ts_, cin_ = (gidx - 2) // 4, (gidx - 2) % 4
                        g.wait_ge(cP.h, kv_p0 + 8 * ts_ + cin_ + 1)
                    cc = g.tensor_copy(wcb[gidx % 2], wfs[gidx % 2][:, 0:1024])
                    cG.inc(cc)

            @block.scalar
            def _(sc):
                if ts == 0:
                    sc.wait_ge(cB.h, ebar)
                if li == 0:
                    sc.wait_ge(cV.h, kv_v0 + NVI * ts + 1)
                else:
                    sc.wait_ge(qM.s[0].h, 64 * (ts + 1))
                if ts >= 1:
                    sc.wait_ge(cP.h, kv_p0 + 8 * ts)
                for tc in range(4):
                    for k in range(4):
                        tT.s[0].inc(sc.dma_start_transpose(
                            tsl[0][:, k, 128 * tc:128 * (tc + 1)],
                            tmsl[0][:, tc, 128 * k:128 * (k + 1)]))

            @block.tensor
            def _(t):
                t.wait_ge(tT.s[0].h, 256 * (ts + 1))
                if ts == 0:
                    t.wait_ge(cG.h, mw_g1)
                    t.wait_ge(cV.h, mw_v2)
                if ts >= 1:
                    t.wait_ge(cA.h, kv_a0 + 4 * ts)
                    t.wait_ge(cV.h, kv_v0 + NVI * ts)
                for cin in range(4):
                    gidx = 4 * ts + cin
                    t.wait_ge(cG.h, kv_g0 + gidx + 1)
                    for kc in range(4):
                        t.matmul(PSQ[:, 512 * kc:512 * (kc + 1)],
                                 wcb[gidx % 2][:, 128 * kc:128 * (kc + 1)],
                                 tsl[0][:, cin, :],
                                 start=(cin == 0), stop=(cin == 3))
                    for tc in range(4):
                        mm = t.matmul(PB[tc][:, :],
                                      tsl[0][:, cin, 128 * tc:128 * (tc + 1)],
                                      wcb[gidx % 2][:, 512:1024],
                                      start=(cin == 0), stop=False)
                    cP.inc(mm)
                for tc in range(4):
                    mm = t.matmul(PB[tc][:, :], ones1b[0:1, 0:128], browb[0:1, 0, :],
                                  start=False, stop=True)
                    cP.inc(mm)

            @block.scalar
            def _(sc):
                for kc in range(4):
                    sc.wait_ge(cP.h, kv_p0 + 8 * ts + 4)
                    a = sc.activation(K1T[:, kc, 512 * ts:512 * (ts + 1)],
                                      PSQ[:, 512 * kc:512 * (kc + 1)], AF.Identity,
                                      bias=bqkc[:, kc, 1:2])
                    cA.inc(a)

            @block.vector
            def _(v):
                v.drain()
                for tc in range(4):
                    v.wait_ge(cP.h, kv_p0 + 8 * ts + 4 + tc + 1)
                    vv = V1[:, 4 * ts + tc, 0:520].rearrange("p (h w) -> p h w", h=8, w=65)
                    cc = v.tensor_copy(vv[:, :, 0:64],
                                       PB[tc][:, :].rearrange("p (h w) -> p h w", h=8, w=64))
                    cV.inc(cc)
        kv_p1, kv_v1, kv_a1 = cP.n, cV.n, cA.n

        at_p0, at_v0, at_a0 = cP.n, cV.n, cA.n
        at_g0 = cG.n
        VQ0 = 1 if li == 0 else 0
        VQ = VQ0 + 48 + 8
        PQS = 184
        AQS = 140
        for qs in range(NSL):
            pq = at_p0 + PQS * qs
            vq = at_v0 + VQ * qs
            aqb = at_a0 + AQS * qs
            vas = vq + VQ0 + 48

            if li == 0:
                @block.vector
                def _(v):
                    v.drain()
                    if qs == 0:
                        v.wait_ge(tT.s[0].h, 256 * NSL)
                    if qs >= 1:
                        v.wait_ge(tQ.h, 256 * qs)
                    for tc in range(4):
                        c = 4 * qs + tc
                        cc = v.tensor_scalar(tmsl[0][:, tc, :], XBF[:, c, 0:512],
                                             ln0[:, c, 0:1], ln0[:, c, 1:2],
                                             ALU.subtract, ALU.mult)
                    cV.inc(cc)
            else:
                @block.sync
                def _(s):
                    s.wait_ge(cV.h, vq)
                    s.wait_ge(tQ.h, 256 * qs)
                    if qs == 0:
                        s.wait_ge(tT.s[0].h, 256 * NSL)
                    for tc in range(4):
                        r0 = 512 * qs + 128 * tc
                        qM.s[0].inc(s.dma_start(out=tmsl[0][:, tc, :],
                                                in_=xshsp[r0:r0 + 128, :]))

            @block.scalar
            def _(sc):
                if li == 0:
                    sc.wait_ge(cV.h, vq + 1)
                else:
                    sc.wait_ge(qM.s[0].h, 64 * (NSL + qs + 1))
                sc.wait_ge(cP.h, pq)
                for tc in range(4):
                    for k in range(4):
                        sc.dma_start_transpose(
                            tsl[0][:, k, 128 * tc:128 * (tc + 1)],
                            tmsl[0][:, tc, 128 * k:128 * (k + 1)]).then_inc(tQ.h, 16)
                        tQ.bump()

            @block.tensor
            def _(t):
                t.wait_ge(tQ.h, 256 * (qs + 1))
                if qs == 0:
                    t.wait_ge(cP.h, kv_p1)
                    t.wait_ge(cA.h, kv_a1)
                    t.wait_ge(cV.h, kv_v1)
                for h in range(H):
                    ph = pq + 23 * h
                    ah = aqb + 17 * h
                    vh = vq + VQ0 + 6 * h
                    bp = 64 * (h % 2)
                    hc = h // 2
                    for cin in range(4):
                        mm = t.matmul(PB[1][bp:bp + 64, :],
                                      wqb[:, cin, 64 * h:64 * h + 64],
                                      tsl[0][:, cin, :],
                                      start=(cin == 0), stop=(cin == 3))
                    cP.inc(mm)
                    for d2 in range(16):
                        dp = d2 % 2
                        if d2 >= 2:
                            t.wait_ge(cA.h, ah + d2)
                        if d2 == 0:
                            t.wait_ge(cA.h, ah + 1)
                        for j in range(2):
                            kt = 2 * d2 + j
                            mm = t.matmul(PSQ[:, 1024 * dp + 512 * j:1024 * dp + 512 * (j + 1)],
                                          K1T[bp:bp + 64, hc, 128 * kt:128 * (kt + 1)],
                                          qtt[bp:bp + 64, :], start=True, stop=True)
                        if d2 >= 1:
                            t.wait_ge(cA.h, ah + 2 + (d2 - 1))
                            if d2 == 1:
                                if h >= 1:
                                    t.wait_ge(cV.h, vh - 6 + 1)
                                elif qs >= 1:
                                    t.wait_ge(cV.h, vq - VQ + VQ0 + 6 * 7 + 1)
                            for j in range(2):
                                kt = 2 * (d2 - 1) + j
                                mm = t.matmul(PB[0][0:65, :], V1[:, kt, 65 * h:65 * h + 65],
                                              aq[(d2 - 1) % 2][:, 512 * j:512 * (j + 1)],
                                              start=(kt == 0), stop=False)
                        cP.inc(mm)
                    t.wait_ge(cA.h, ah + 2 + 15)
                    for j in range(2):
                        kt = 30 + j
                        mm = t.matmul(PB[0][0:65, :], V1[:, kt, 65 * h:65 * h + 65],
                                      aq[1][:, 512 * j:512 * (j + 1)],
                                      start=False, stop=(j == 1))
                    cP.inc(mm)
                    t.wait_ge(cV.h, vh + 1)
                    for tc in range(4):
                        mm = t.transpose(PB[2][:, tc:tc + 1],
                                         denA[64:65, 128 * tc:128 * (tc + 1)],
                                         ident[64:65, 64:65])
                    cP.inc(mm)
                    for tc in range(4):
                        if tc == 1:
                            t.wait_ge(cV.h, vh + 2)
                        elif tc >= 2:
                            t.wait_ge(cV.h, vh + 1 + tc)
                        bank = PB[3] if tc % 2 == 0 else PB[2]
                        mm = t.matmul(bank[:, :], ohb[0:64, 128 * tc:128 * (tc + 1)],
                                      wob[0:64, h, :], start=True, stop=True)
                        cP.inc(mm)

            @block.scalar
            def _(sc):
                for h in range(H):
                    ph = pq + 23 * h
                    ah = aqb + 17 * h
                    bp = 64 * (h % 2)
                    sc.wait_ge(cP.h, ph + 1)
                    a = sc.activation(qtt[bp:bp + 64, :], PB[1][bp:bp + 64, :], AF.Identity,
                                      bias=bqkc[bp:bp + 64, h // 2, 0:1])
                    cA.inc(a)
                    for d2 in range(16):
                        dp = d2 % 2
                        sc.wait_ge(cP.h, ph + 2 + d2)
                        a = sc.activation(aq[dp], PSQ[:, 1024 * dp:1024 * (dp + 1)],
                                          AF.Exp)
                        cA.inc(a)
                for tc in range(4):
                    sc.wait_ge(cV.h, vas + 2 * tc + 1)
                    a = sc.activation(stat[:, 16:17], stat[:, 13:14], AF.Sqrt,
                                      bias=epscol[:, :])
                    cA.inc(a)

            @block.vector
            def _(v):
                v.drain()
                for h in range(H):
                    ph = pq + 23 * h
                    vh = vq + VQ0 + 6 * h
                    v.wait_ge(cP.h, ph + 18)
                    v.tensor_copy(ohb[0:64, :], PB[0][0:64, :])
                    cc = v.tensor_copy(denA[64:65, :], PB[0][64:65, :])
                    cV.inc(cc)
                    v.wait_ge(cP.h, ph + 19)
                    cc = v.reciprocal(rcq[:, 0:4], PB[2][:, 0:4])
                    cV.inc(cc)
                    v.drain()
                    for tc in range(4):
                        v.wait_ge(cP.h, ph + 20 + tc)
                        bank = PB[3] if tc % 2 == 0 else PB[2]
                        if h == 0:
                            cc = v.tensor_scalar(oacc[:, tc, :], bank[:, :],
                                                 rcq[:, tc:tc + 1], None, ALU.mult)
                        else:
                            cc = v.scalar_tensor_tensor(oacc[:, tc, :], bank[:, :],
                                                        rcq[:, tc:tc + 1], oacc[:, tc, :],
                                                        ALU.mult, ALU.add)
                        cV.inc(cc)
                for tc in range(4):
                    i = 4 * qs + tc
                    if i >= 2:
                        qO.wait(v, i, 16 * ((i - 2) // 2 + 1))
                    v.drain()
                    v.tensor_tensor(sfp, tmsl[0][:, tc, :], oacc[:, tc, :], ALU.add)
                    v.drain()
                    v.tensor_tensor(sfp, sfp, bcB, ALU.add)
                    v.drain()
                    v.bn_stats(stat[:, 0:6], sfp)
                    v.drain()
                    v.bn_aggr(stat[:, 12:14], stat[:, 0:6])
                    v.drain().then_inc(cV.h, 1)
                    cV.bump()
                    v.wait_ge(cA.h, aqb + 136 + tc + 1)
                    v.reciprocal(stat[:, 17:18], stat[:, 16:17])
                    v.drain()
                    v.tensor_scalar(sfp, sfp, stat[:, 12:13], stat[:, 17:18],
                                    ALU.subtract, ALU.mult)
                    v.drain()
                    v.tensor_tensor(sfp, sfp, bcG, ALU.mult)
                    v.drain()
                    cc = v.tensor_tensor(stg[tc % 2], sfp, bcE, ALU.add)
                    cV.inc(cc)

            @block.sync
            def _(s):
                for tc in range(4):
                    i = 4 * qs + tc
                    s.wait_ge(cV.h, vas + 2 * tc + 2)
                    r0 = 512 * qs + 128 * tc
                    qO.inc(s.dma_start(out=dst_sp[r0:r0 + 128, :], in_=stg[tc % 2]), i)

        at_p1, at_v1, at_a1 = cP.n, cV.n, cA.n
        assert at_p1 == at_p0 + PQS * NSL, (at_p1, at_p0)
        assert at_v1 == at_v0 + VQ * NSL, (at_v1, at_v0)
        assert at_a1 == at_a0 + AQS * NSL, (at_a1, at_a0)
        bfa.pop()
        fpa.pop()
        return qO

    qO1 = emit_mha(0, x1sp)
    qO2 = emit_mha(1, x2sp)



    def emit_ffn0():
        bfa.push()
        fpa.push()
        ytm = bfa.alloc([128, 8, 512])
        ytr = bfa.alloc([128, 8, 512])
        hsl = bfa.alloc([128, 32, 512])
        w1b = [bfa.alloc([128, 512]) for _ in range(2)]
        w2b = [bfa.alloc([128, 1024]) for _ in range(2)]
        brw = bfa.alloc([1, 3, 1024])
        bf1c = fpa.alloc([128, 32, 1])
        w1f = [fpa.alloc([128, 512]) for _ in range(2)]
        w2f = [fpa.alloc([128, 1024]) for _ in range(2)]
        bcG2 = fpa.alloc([128, 1024])
        bcE2 = fpa.alloc([128, 1024])
        sfp = fpa.alloc([128, 1024])
        stat = fpa.alloc([128, 20])

        qY = PSem(es, nc, "qY")
        qW1 = PSem(es, nc, "qF1")
        qW2 = PSem(es, nc, "qF2")
        tY = Sem(es, nc, "tY", 16)

        ebar = phase_barrier()

        @block.gpsimd
        def _(g):
            g.wait_ge(cB.h, ebar)
            g.dma_start(out=brw[0:1, 0, :], in_=bff2_in.unsqueeze(0)).then_inc(sG.h, 16)
            g.dma_start(out=brw[0:1, 1, :], in_=gff_in.unsqueeze(0)).then_inc(sG.h, 16)
            g.dma_start(out=brw[0:1, 2, :], in_=beff_in.unsqueeze(0)).then_inc(sG.h, 16)
            sG.bump(3)
        f_sg1 = sG.n

        @block.sync
        def _(s):
            s.wait_ge(cB.h, ebar)
            for c in range(32):
                s.dma_start(out=bf1c[:, c, :],
                            in_=bff1_in[128 * c:128 * (c + 1)].unsqueeze(1)).then_inc(sQ.h, 16)
            sQ.bump(32)
        f_q1 = sQ.n

        @block.tensor
        def _(t):
            t.wait_ge(cB.h, ebar)
            t.wait_ge(sG.h, f_sg1)
            t.matmul(PB[0][:, :], ones1b[0:1, 0:128], brw[0:1, 1, 0:512], start=True, stop=True)
            t.matmul(PB[1][:, :], ones1b[0:1, 0:128], brw[0:1, 1, 512:1024], start=True, stop=True)
            t.matmul(PB[2][:, :], ones1b[0:1, 0:128], brw[0:1, 2, 0:512], start=True, stop=True)
            mm = t.matmul(PB[3][:, :], ones1b[0:1, 0:128], brw[0:1, 2, 512:1024], start=True, stop=True)
            cP.inc(mm)
        f_p1 = cP.n

        @block.vector
        def _(v):
            v.drain()
            v.wait_ge(cP.h, f_p1)
            v.tensor_copy(bcG2[:, 0:512], PB[0][:, :])
            v.tensor_copy(bcG2[:, 512:1024], PB[1][:, :])
            v.tensor_copy(bcE2[:, 0:512], PB[2][:, :])
            cc = v.tensor_copy(bcE2[:, 512:1024], PB[3][:, :])
            cV.inc(cc)
        f_v1 = cV.n

        TP, TV, TA, TG = 97, 40, 4, 96
        f_p2, f_v2, f_a2, f_g2 = cP.n, cV.n, cA.n, cG.n
        for tsb in range(NSL):
            tp = f_p2 + TP * tsb
            tv = f_v2 + TV * tsb
            ta = f_a2 + TA * tsb
            tg = f_g2 + TG * tsb

            @block.sync
            def _(s):
                if tsb == 0:
                    s.wait_ge(cV.h, f_v1)
                else:
                    s.wait_ge(tY.h, 512 * tsb)
                    s.wait_ge(cV.h, f_v2 + TV * tsb)
                for tc in range(4):
                    r0 = 512 * tsb + 128 * tc
                    qY.inc(s.dma_start(out=ytm[:, tc, :], in_=x1sp[r0:r0 + 128, :]), tsb)
                    qY.inc(s.dma_start(out=ytm[:, 4 + tc, :], in_=x2sp[r0:r0 + 128, :]), tsb)
                for hg in range(8):
                    for cin in range(8):
                        gg = 64 * tsb + 8 * hg + cin
                        if gg >= 2:
                            gge = gg - 2
                            s.wait_ge(cG.h, f_g2 + TG * (gge // 64) + (gge % 64) + 1)
                        qW1.inc(s.dma_start(
                            out=w1f[gg % 2],
                            in_=wff1_in[128 * cin:128 * (cin + 1), 512 * hg:512 * (hg + 1)]), gg)
                for hc in range(32):
                    g2 = 32 * tsb + hc
                    if g2 >= 2:
                        g2e = g2 - 2
                        s.wait_ge(cG.h, f_g2 + TG * (g2e // 32) + 64 + (g2e % 32) + 1)
                    qW2.inc(s.dma_start(out=w2f[g2 % 2],
                                        in_=wff2_in[128 * hc:128 * (hc + 1), :]), g2)

            @block.gpsimd
            def _(g):
                for hg in range(8):
                    for cin in range(8):
                        gg = 64 * tsb + 8 * hg + cin
                        qW1.wait(g, gg, 16 * (gg // 2 + 1))
                        if gg >= 2:
                            gge = gg - 2
                            g.wait_ge(cP.h, f_p2 + TP * (gge // 64) + (gge % 64) + 1)
                        cc = g.tensor_copy(w1b[gg % 2], w1f[gg % 2])
                        cG.inc(cc)
                for hc in range(32):
                    g2 = 32 * tsb + hc
                    qW2.wait(g, g2, 16 * (g2 // 2 + 1))
                    if g2 >= 2:
                        g2e = g2 - 2
                        g.wait_ge(cP.h, f_p2 + TP * (g2e // 32) + 64 + (g2e % 32) + 1)
                    cc = g.tensor_copy(w2b[g2 % 2], w2f[g2 % 2])
                    cG.inc(cc)

            @block.scalar
            def _(sc):
                qY.wait(sc, tsb, 128 * (tsb // 2 + 1))
                sc.wait_ge(cP.h, f_p2 + TP * tsb)
                for tc in range(4):
                    for k in range(4):
                        sc.dma_start_transpose(
                            ytr[:, k, 128 * tc:128 * (tc + 1)],
                            ytm[:, tc, 128 * k:128 * (k + 1)]).then_inc(tY.h, 16)
                        tY.bump()
                    for k in range(4):
                        sc.dma_start_transpose(
                            ytr[:, 4 + k, 128 * tc:128 * (tc + 1)],
                            ytm[:, 4 + tc, 128 * k:128 * (k + 1)]).then_inc(tY.h, 16)
                        tY.bump()
                for tc in range(4):
                    sc.wait_ge(cV.h, tv + 32 + 2 * tc + 1)
                    a = sc.activation(stat[:, 16:17], stat[:, 13:14], AF.Sqrt,
                                      bias=epscol[:, :])
                    cA.inc(a)

            @block.tensor
            def _(t):
                t.wait_ge(tY.h, 512 * (tsb + 1))
                if tsb >= 1:
                    t.wait_ge(cV.h, f_v2 + TV * tsb)
                for hg in range(8):
                    if hg >= 1:
                        t.wait_ge(cV.h, tv + 4 * hg)
                    for cin in range(8):
                        gg = 64 * tsb + 8 * hg + cin
                        t.wait_ge(cG.h, f_g2 + TG * tsb + 8 * hg + cin + 1)
                        for j in range(4):
                            mm = t.matmul(PSQ[:, 512 * j:512 * (j + 1)],
                                          w1b[gg % 2][:, 128 * j:128 * (j + 1)],
                                          ytr[:, cin, :],
                                          start=(cin == 0), stop=(cin == 7))
                        cP.inc(mm)
                t.wait_ge(cV.h, tv + 32)
                for hc in range(32):
                    t.wait_ge(cG.h, f_g2 + TG * tsb + 64 + hc + 1)
                    for tc in range(4):
                        for ch in range(2):
                            bank = PB[2 * (tc % 2) + ch] if tc < 2 else PSQ[:, 512 * (2 * (tc - 2) + ch):512 * (2 * (tc - 2) + ch + 1)]
                            dst = bank[:, :] if tc < 2 else bank
                            mm = t.matmul(dst, hsl[:, hc, 128 * tc:128 * (tc + 1)],
                                          w2b[hc % 2][:, 512 * ch:512 * (ch + 1)],
                                          start=(hc == 0), stop=False)
                    cP.inc(mm)
                for tc in range(4):
                    for ch in range(2):
                        bank = PB[2 * (tc % 2) + ch] if tc < 2 else PSQ[:, 512 * (2 * (tc - 2) + ch):512 * (2 * (tc - 2) + ch + 1)]
                        dst = bank[:, :] if tc < 2 else bank
                        mm = t.matmul(dst, ones1b[0:1, 0:128],
                                      brw[0:1, 0, 512 * ch:512 * (ch + 1)],
                                      start=False, stop=True)
                cP.inc(mm)

            @block.vector
            def _(v):
                v.drain()
                if tsb == 0:
                    v.wait_ge(sQ.h, f_q1)
                for hg in range(8):
                    for j in range(4):
                        v.wait_ge(cP.h, tp + 8 * hg + 8)
                        cc = v.tensor_scalar(hsl[:, 4 * hg + j, :], PSQ[:, 512 * j:512 * (j + 1)],
                                             bf1c[:, 4 * hg + j, :], 0.0, ALU.add, ALU.max)
                        cV.inc(cc)
                v.wait_ge(cP.h, tp + TP)
                for tc in range(4):
                    if tc < 2:
                        b0, b1 = PB[2 * tc][:, :], PB[2 * tc + 1][:, :]
                    else:
                        b0 = PSQ[:, 512 * (2 * (tc - 2)):512 * (2 * (tc - 2) + 1)]
                        b1 = PSQ[:, 512 * (2 * (tc - 2) + 1):512 * (2 * (tc - 2) + 2)]
                    v.drain()
                    v.tensor_tensor(sfp[:, 0:512], ytm[:, tc, :], b0, ALU.add)
                    v.tensor_tensor(sfp[:, 512:1024], ytm[:, 4 + tc, :], b1, ALU.add)
                    v.drain()
                    v.bn_stats(stat[:, 0:6], sfp[:, 0:512])
                    v.bn_stats(stat[:, 6:12], sfp[:, 512:1024])
                    v.drain()
                    v.bn_aggr(stat[:, 12:14], stat[:, 0:12])
                    v.drain().then_inc(cV.h, 1)
                    cV.bump()
                    v.wait_ge(cA.h, ta + tc + 1)
                    v.reciprocal(stat[:, 17:18], stat[:, 16:17])
                    v.drain()
                    v.tensor_scalar(sfp, sfp, stat[:, 12:13], stat[:, 17:18],
                                    ALU.subtract, ALU.mult)
                    v.drain()
                    v.tensor_tensor(sfp, sfp, bcG2, ALU.mult)
                    v.drain()
                    v.tensor_tensor(sfp, sfp, bcE2, ALU.add)
                    v.drain()
                    cc = v.tensor_tensor(XBF[:, 4 * tsb + tc, :], XBF[:, 4 * tsb + tc, :],
                                         sfp, ALU.add)
                    cV.inc(cc)
        bfa.pop()
        fpa.pop()
        assert cP.n == f_p2 + TP * NSL
        assert cV.n == f_v2 + TV * NSL
        assert cA.n == f_a2 + TA * NSL

    emit_ffn0()

    def emit_rwkv():
        bfa.push()
        fpa.push()
        H1T = bfa.alloc([128, 8, S])
        slA = bfa.alloc([128, 8, 512])
        slC = bfa.alloc([128, 8, 512])
        slD = bfa.alloc([128, 8, 512])
        h1tm = [bfa.alloc([128, 1024]) for _ in range(2)]
        wchb = [bfa.alloc([128, 1024]) for _ in range(2)]
        otm = [bfa.alloc([128, 1024]) for _ in range(2)]
        wchf = [fpa.alloc([128, 1024]) for _ in range(2)]
        muc = fpa.alloc([128, 8, 4])
        sfp = fpa.alloc([128, 1024])
        stat = fpa.alloc([128, 20])

        qWc = PSem(es, nc, "qWc")
        tH = Sem(es, nc, "tH", 16)
        tO = PSem(es, nc, "tO")

        ebar = phase_barrier()

        @block.sync
        def _(s):
            s.wait_ge(cB.h, ebar)
            for c in range(8):
                s.dma_start(out=muc[:, c, 0:1],
                            in_=mur_in[128 * c:128 * (c + 1)].unsqueeze(1)).then_inc(sQ.h, 16)
                s.dma_start(out=muc[:, c, 2:3],
                            in_=muk_in[128 * c:128 * (c + 1)].unsqueeze(1)).then_inc(sQ.h, 16)
            sQ.bump(16)
        r_q1 = sQ.n

        @block.vector
        def _(v):
            v.drain()
            v.wait_ge(cB.h, ebar)
            v.wait_ge(sQ.h, r_q1)
            v.tensor_scalar(muc[:, :, 1:2], muc[:, :, 0:1], -1.0, 1.0, ALU.mult, ALU.add)
            cc = v.tensor_scalar(muc[:, :, 3:4], muc[:, :, 2:3], -1.0, 1.0, ALU.mult, ALU.add)
            cV.inc(cc)
        r_v1 = cV.n

        r_v2, r_a2, r_g2 = cV.n, cA.n, cG.n
        @block.vector
        def _(v):
            v.drain()
            for c in range(NCH):
                if c >= 2:
                    v.wait_ge(tH.h, 128 * (c - 1))
                v.bn_stats(stat[:, 0:6], XBF[:, c, 0:512])
                v.bn_stats(stat[:, 6:12], XBF[:, c, 512:1024])
                v.drain()
                v.bn_aggr(stat[:, 12:14], stat[:, 0:12])
                v.drain().then_inc(cV.h, 1)
                cV.bump()
                v.wait_ge(cA.h, r_a2 + c + 1)
                v.reciprocal(stat[:, 17:18], stat[:, 16:17])
                v.drain()
                cc = v.tensor_scalar(h1tm[c % 2], XBF[:, c, :], stat[:, 12:13],
                                     stat[:, 17:18], ALU.subtract, ALU.mult)
                cV.inc(cc)
        r_v3 = cV.n

        @block.scalar
        def _(sc):
            for c in range(NCH):
                sc.wait_ge(cV.h, r_v2 + 2 * c + 1)
                a = sc.activation(stat[:, 16:17], stat[:, 13:14], AF.Sqrt,
                                  bias=epscol[:, :])
                cA.inc(a)
                if c >= 1:
                    sc.wait_ge(cV.h, r_v2 + 2 * c)
                    for k in range(8):
                        sc.dma_start_transpose(
                            H1T[:, k, 128 * (c - 1):128 * c],
                            h1tm[(c - 1) % 2][:, 128 * k:128 * (k + 1)]).then_inc(tH.h, 16)
                        tH.bump()
            sc.wait_ge(cV.h, r_v2 + 2 * NCH)
            for k in range(8):
                sc.dma_start_transpose(
                    H1T[:, k, 128 * (NCH - 1):128 * NCH],
                    h1tm[(NCH - 1) % 2][:, 128 * k:128 * (k + 1)]).then_inc(tH.h, 16)
                tH.bump()
        r_th = tH.n

        TPr = 24
        NVX = 22
        r_p4, r_v4, r_a4, r_g4 = cP.n, cV.n, cA.n, cG.n
        for ts in range(NSL):
            tp = r_p4 + TPr * ts
            tv = r_v4 + NVX * ts
            ta = r_a4 + 16 * ts

            @block.sync
            def _(s):
                for i3 in range(24):
                    gg = 24 * ts + i3
                    if gg >= 2:
                        s.wait_ge(cG.h, r_g4 + gg - 1)
                    w_ = (wr_in, wkk_in, wvv_in)[i3 // 8]
                    cin = i3 % 8
                    qWc.inc(s.dma_start(out=wchf[gg % 2],
                                        in_=w_[128 * cin:128 * (cin + 1), :]), gg)

            @block.gpsimd
            def _(g):
                for i3 in range(24):
                    gg = 24 * ts + i3
                    qWc.wait(g, gg, 16 * (gg // 2 + 1))
                    if gg >= 2:
                        gge = gg - 2
                        g.wait_ge(cP.h, r_p4 + TPr * (gge // 24) + (gge % 24) + 1)
                    cc = g.tensor_copy(wchb[gg % 2], wchf[gg % 2])
                    cG.inc(cc)

            def mix(v, mu_i, mu1_i):
                for c in range(8):
                    if ts == 0:
                        v.tensor_scalar(slA[:, c, :], H1T[:, c, 0:512],
                                        muc[:, c, mu_i:mu_i + 1], None, ALU.mult)
                        v.drain()
                        cc = v.scalar_tensor_tensor(slA[:, c, 1:512], H1T[:, c, 0:511],
                                                    muc[:, c, mu1_i:mu1_i + 1],
                                                    slA[:, c, 1:512], ALU.mult, ALU.add)
                    else:
                        v.tensor_scalar(slA[:, c, :], H1T[:, c, 512 * ts:512 * (ts + 1)],
                                        muc[:, c, mu_i:mu_i + 1], None, ALU.mult)
                        v.drain()
                        cc = v.scalar_tensor_tensor(slA[:, c, :],
                                                    H1T[:, c, 512 * ts - 1:512 * (ts + 1) - 1],
                                                    muc[:, c, mu1_i:mu1_i + 1],
                                                    slA[:, c, :], ALU.mult, ALU.add)
                v.drain().then_inc(cV.h, 1)
                cV.bump()

            @block.vector
            def _(v):
                v.drain()
                if ts == 0:
                    v.wait_ge(tH.h, r_th)
                    v.wait_ge(cV.h, r_v1)
                else:
                    v.wait_ge(cP.h, tp)
                    v.wait_ge(tO.s[0].h, 256 * ts)
                    v.wait_ge(tO.s[1].h, 256 * ts)
                mix(v, 0, 1)
                v.wait_ge(cP.h, tp + 8)
                mix(v, 2, 3)
                for co in range(8):
                    v.wait_ge(cP.h, tp + 16)
                    bank = PB[co][:, :] if co < 4 else PSQ[:, 512 * (co - 4):512 * (co - 3)]
                    cc = v.tensor_scalar(slC[:, co, :], bank, 0.0, None, ALU.max)
                    cV.inc(cc)
                v.wait_ge(cP.h, tp + 24)
                v.wait_ge(cA.h, ta + 16)
                for co in range(8):
                    bank = PB[co][:, :] if co < 4 else PSQ[:, 512 * (co - 4):512 * (co - 3)]
                    cc = v.tensor_tensor(slA[:, co, :], slD[:, co, :], bank, ALU.mult)
                    cV.inc(cc)
                for tc in range(4):
                    v.wait_ge(tO.s[tc % 2].h, 256 * ts + 128 * (tc // 2 + 1))
                    v.drain()
                    cc = v.tensor_tensor(XBF[:, 4 * ts + tc, :], XBF[:, 4 * ts + tc, :],
                                         otm[tc % 2], ALU.add)
                    cV.inc(cc)

            @block.tensor
            def _(t):
                if ts == 0:
                    t.wait_ge(tH.h, r_th)
                for ph in range(3):
                    if ph == 0:
                        t.wait_ge(cV.h, tv + 1)
                    elif ph == 1:
                        t.wait_ge(cV.h, tv + 2)
                        t.wait_ge(cA.h, ta + 8)
                    else:
                        t.wait_ge(cV.h, tv + 10)
                        t.wait_ge(cA.h, ta + 16)
                    sl_ = slC if ph == 2 else slA
                    for cin in range(8):
                        gg = 24 * ts + 8 * ph + cin
                        t.wait_ge(cG.h, r_g4 + gg + 1)
                        for co in range(8):
                            bank = PB[co][:, :] if co < 4 else PSQ[:, 512 * (co - 4):512 * (co - 3)]
                            mm = t.matmul(bank, wchb[gg % 2][:, 128 * co:128 * (co + 1)],
                                          sl_[:, cin, :], start=(cin == 0), stop=(cin == 7))
                        cP.inc(mm)

            @block.scalar
            def _(sc):
                for co in range(8):
                    sc.wait_ge(cP.h, tp + 8)
                    bank = PB[co][:, :] if co < 4 else PSQ[:, 512 * (co - 4):512 * (co - 3)]
                    a = sc.activation(slD[:, co, :], bank, AF.Sigmoid)
                    cA.inc(a)
                for co in range(8):
                    sc.wait_ge(cV.h, tv + 2 + co + 1)
                    a = sc.activation(slC[:, co, :], slC[:, co, :], AF.Square)
                    cA.inc(a)
                for tc in range(4):
                    sc.wait_ge(cV.h, tv + 18)
                    if tc >= 2:
                        sc.wait_ge(cV.h, tv + 18 + tc - 1)
                    for k in range(8):
                        sc.dma_start_transpose(
                            otm[tc % 2][:, 128 * k:128 * (k + 1)],
                            slA[:, k, 128 * tc:128 * (tc + 1)]).then_inc(tO.s[tc % 2].h, 16)
                        tO.s[tc % 2].bump()
        bfa.pop()
        fpa.pop()

    emit_rwkv()


    def emit_fft():
        bfa.push()
        fpa.push()
        Cb = bfa.alloc([128, 8, 1024])
        Sb = bfa.alloc([128, 8, 1024])
        h2tm = [bfa.alloc([128, 1024]) for _ in range(2)]
        h2T = bfa.alloc([128, 8, 512])
        asl = bfa.alloc([128, 8, 512])
        bsl = bfa.alloc([128, 8, 512])
        x1T = bfa.alloc([128, 8, 512])
        o2T = bfa.alloc([128, 8, 512])
        o2tm = [bfa.alloc([128, 1024]) for _ in range(2)]
        h3st = bfa.alloc([128, 4, 1024])
        bitsT = bfa.alloc([128, 512])
        synb = bfa.alloc([128, 512])
        corT = bfa.alloc([128, 512])
        fixb = bfa.alloc([1, 512])
        welb = bfa.alloc([128, 8, 24])
        wleb = bfa.alloc([128, 1024])
        hmtb = bfa.alloc([128, 12])
        xrg = bfa.alloc([128, 512])
        wgrb = bfa.alloc([128, 8, 16])
        wgtb = bfa.alloc([128, 8])
        zrow = fpa.alloc([128, 24])
        ccol = fpa.alloc([128, 8, 8])
        belc = fpa.alloc([128, 1])
        blec = fpa.alloc([128, 8, 1])
        rnrm = fpa.alloc([128, 1])
        logf = fpa.alloc([128, 512])
        x3f = fpa.alloc([128, 1024])
        mint = fpa.alloc([128, 512])
        stat = fpa.alloc([128, 20])
        pi2c = fpa.alloc([128, 1])
        wgmf = fpa.alloc([128, 16])
        mi32 = es.enter_context(nc.sbuf_tensor("mi32", [128, 512], mybir.dt.int32))
        krow = x3f[0:1, :]

        tH2 = PSem(es, nc, "tH2")
        tO2 = PSem(es, nc, "tO2")
        tH3 = PSem(es, nc, "tH3")
        qX3 = PSem(es, nc, "qX3")

        ebar = phase_barrier()

        @block.gpsimd
        def _(g):
            g.wait_ge(cB.h, ebar)
            g.iota(krow[0:1, :], pattern=[[1, 1024]], base=0, channel_multiplier=0,
                   allow_small_or_imprecise_dtypes=True)
            for c in range(8):
                cc = g.iota(ccol[:, c, 6:7], pattern=[[0, 1]], base=128 * c,
                            channel_multiplier=1,
                            allow_small_or_imprecise_dtypes=True)
            cc.then_inc(cG.h, 1)
            cG.bump()
            g.dma_start(out=welb[:, :, :],
                        in_=wel_in.rearrange("(a p) o -> p a o", p=128)).then_inc(sG.h, 16)
            g.dma_start(out=wleb[0:24, :], in_=wle_in[:, :]).then_inc(sG.h, 16)
            g.dma_start(out=hmtb[0:24, :], in_=hmt_in[:, :]).then_inc(sG.h, 16)
            g.dma_start(out=wgrb[:, :, :],
                        in_=wgred_in.rearrange("(a p) o -> p a o", p=128)).then_inc(sG.h, 16)
            g.dma_start(out=wgtb[0:16, :], in_=wgt_in[:, :]).then_inc(sG.h, 16)
            sG.bump(5)
        c_g1 = cG.n
        c_sg1 = sG.n

        @block.sync
        def _(s):
            s.wait_ge(cB.h, ebar)
            s.dma_start(out=belc[0:24, :], in_=bel_in.unsqueeze(1)).then_inc(sQ.h, 16)
            for c in range(8):
                s.dma_start(out=blec[:, c, :],
                            in_=ble_in[128 * c:128 * (c + 1)].unsqueeze(1)).then_inc(sQ.h, 16)
            s.dma_start(out=wgmf[0:8, :], in_=wgm_in[:, :]).then_inc(sQ.h, 16)
            sQ.bump(10)
        c_q1 = sQ.n

        @block.vector
        def _(v):
            v.drain()
            v.wait_ge(cG.h, c_g1)
            v.wait_ge(sQ.h, c_q1)
            v.memset(pi2c[:, :], float(np.pi / 2))
            for c in range(8):
                v.tensor_scalar(ccol[:, c, 1:2], ccol[:, c, 6:7], -1.0, 1024.0,
                                ALU.mult, ALU.add)
                v.drain()
                v.tensor_tensor(ccol[:, c, 0:1], ccol[:, c, 6:7], ccol[:, c, 1:2], ALU.min)
                v.drain()
                v.tensor_scalar(ccol[:, c, 0:1], ccol[:, c, 0:1], 1.0 / 1024.0, 1e-08,
                                ALU.mult, ALU.add)
            v.drain().then_inc(cV.h, 1)
            cV.bump()
        c_v1 = cV.n

        @block.scalar
        def _(sc):
            sc.wait_ge(cV.h, c_v1)
            for c in range(8):
                a = sc.activation(ccol[:, c, 2:3], ccol[:, c, 0:1], AF.Square)
                cA.inc(a)
        c_a1 = cA.n

        @block.vector
        def _(v):
            v.wait_ge(cA.h, c_a1)
            OD = (1.0 + 0.2 * 1.0) * 1.0
            for c in range(8):
                v.tensor_scalar(ccol[:, c, 3:4], ccol[:, c, 0:1], 1.5, OD,
                                ALU.mult, ALU.add)
                v.tensor_scalar(ccol[:, c, 4:5], ccol[:, c, 2:3], 0.1, OD,
                                ALU.mult, ALU.add)
                v.drain()
                v.tensor_tensor(ccol[:, c, 4:5], ccol[:, c, 4:5], ccol[:, c, 0:1],
                                ALU.subtract)
            v.drain().then_inc(cV.h, 1)
            cV.bump()
        c_v2 = cV.n

        @block.scalar
        def _(sc):
            sc.wait_ge(cV.h, c_v2)
            for c in range(8):
                sc.activation(ccol[:, c, 5:6], ccol[:, c, 3:4], AF.Sin)
                sc.drain()
                sc.activation(ccol[:, c, 3:4], ccol[:, c, 4:5], AF.Sin)
                sc.drain()
                a = sc.activation(ccol[:, c, 4:5], ccol[:, c, 4:5], AF.Sin,
                                  scale=-1.0, bias=pi2c[:, :])
                cA.inc(a)
                sc.drain()
        c_a2 = cA.n

        @block.vector
        def _(v):
            v.wait_ge(cA.h, c_a2)
            for c in range(8):
                v.tensor_tensor(ccol[:, c, 6:7], ccol[:, c, 5:6], ccol[:, c, 4:5], ALU.mult)
                v.tensor_tensor(ccol[:, c, 7:8], ccol[:, c, 5:6], ccol[:, c, 3:4], ALU.mult)
            v.drain()
            v.tensor_tensor(wgmf[0:8, :], wgmf[0:8, :], wgmf[0:8, :], ALU.mult)
            v.drain()
            v.reduce_sum(rnrm[0:8, :], wgmf[0:8, :], axis=AX.X)
            v.drain().then_inc(cV.h, 1)
            cV.bump()
        c_v3 = cV.n

        @block.scalar
        def _(sc):
            sc.wait_ge(cV.h, c_v3)
            a = sc.activation(rnrm[0:8, :], rnrm[0:8, :], AF.Sqrt)
            cA.inc(a)
        c_a3 = cA.n

        @block.vector
        def _(v):
            v.wait_ge(cA.h, c_a3)
            v.tensor_single_scalar(rnrm[0:8, :], rnrm[0:8, :], 1e-4, ALU.max)
            v.drain()
            cc = v.reciprocal(rnrm[0:8, :], rnrm[0:8, :])
            cV.inc(cc)
        c_v4 = cV.n

        c_p1 = cP.n
        @block.tensor
        def _(t):
            t.wait_ge(cG.h, c_g1)
            for i in range(16):
                kc, ns = i // 2, i % 2
                if i >= 2:
                    t.wait_ge(cA.h, c_a3 + 2 * (i - 1))
                mm = t.matmul(PB[i % 2][:, :], krow[0:1, 128 * kc:128 * (kc + 1)],
                              krow[0:1, 512 * ns:512 * (ns + 1)],
                              start=True, stop=True)
                cP.inc(mm)

        @block.vector
        def _(v):
            v.drain()
            for i in range(16):
                v.wait_ge(cP.h, c_p1 + i + 1)
                if i >= 1:
                    v.wait_ge(cA.h, c_a3 + 2 * i)
                v.tensor_copy(mi32[:, :], PB[i % 2][:, :])
                v.drain()
                v.tensor_single_scalar(mi32[:, :], mi32[:, :], 1023, ALU.bitwise_and)
                v.drain()
                v.tensor_copy(mint[:, :], mi32[:, :])
                v.drain()
                v.tensor_single_scalar(logf[:, :], mint[:, :], 511.5, ALU.is_gt)
                v.drain()
                v.scalar_tensor_tensor(mint[:, :], logf[:, :], -1024.0, mint[:, :],
                                       ALU.mult, ALU.add)
                v.drain()
                v.tensor_scalar(logf[:, :], mint[:, :], -1.0, None, ALU.mult)
                v.drain()
                cc = v.tensor_tensor(logf[:, :], logf[:, :], mint[:, :], ALU.max)
                cV.inc(cc)
        c_v5 = cV.n

        @block.scalar
        def _(sc):
            TWO_PI = float(2.0 * np.pi)
            for i in range(16):
                kc, ns = i // 2, i % 2
                sc.wait_ge(cV.h, c_v4 + i + 1)
                sc.activation(Sb[:, kc, 512 * ns:512 * (ns + 1)], mint[:, :], AF.Sin,
                              scale=TWO_PI / 1024.0)
                a = sc.activation(Cb[:, kc, 512 * ns:512 * (ns + 1)], logf[:, :], AF.Sin,
                                  scale=-TWO_PI / 1024.0, bias=pi2c[:, :])
                cA.inc(a, 2)
        c_a4 = cA.n


        latb = bfa.alloc([128, 512])
        zsc = fpa.alloc([128, 8])
        x3f2 = fpa.alloc([128, 1024])
        gsc = fpa.alloc([128, 8])
        gsc2 = fpa.alloc([128, 4])
        gsc3 = fpa.alloc([128, 16])

        @block.vector
        def _(v):
            v.drain()
            v.memset(zsc[:, :], 0.0)
            v.drain().then_inc(cV.h, 1)
            cV.bump()
        c_v5b = cV.n

        VS, AS_, PS_ = 66, 21, 42
        c_p2, c_v6, c_a5 = cP.n, cV.n, cA.n
        for ts2 in range(NSL):
            pv0 = c_p2 + PS_ * ts2
            vv0 = c_v6 + VS * ts2
            av0 = c_a5 + AS_ * ts2
            xfpair = [x3f, x3f2]

            @block.vector
            def _(v):
                v.drain()
                if ts2 == 0:
                    v.wait_ge(cV.h, c_v5b)
                    v.wait_ge(cA.h, c_a4)
                for tc in range(4):
                    if ts2 > 0 or tc >= 2:
                        nfull = 2 * ts2 + tc // 2
                        v.wait_ge(tH2.s[tc % 2].h, 128 * nfull)
                    c = 4 * ts2 + tc
                    v.bn_stats(stat[:, 0:6], XBF[:, c, 0:512])
                    v.bn_stats(stat[:, 6:12], XBF[:, c, 512:1024])
                    v.drain()
                    v.bn_aggr(stat[:, 12:14], stat[:, 0:12])
                    v.drain().then_inc(cV.h, 1)
                    cV.bump()
                    v.wait_ge(cA.h, av0 + tc + 1)
                    v.reciprocal(stat[:, 17:18], stat[:, 16:17])
                    v.drain()
                    cc = v.tensor_scalar(h2tm[tc % 2], XBF[:, c, :], stat[:, 12:13],
                                         stat[:, 17:18], ALU.subtract, ALU.mult)
                    cV.inc(cc)
                for c in range(8):
                    v.wait_ge(cP.h, pv0 + c + 1)
                    bank = PB[c][:, :] if c < 4 else PSQ[:, 512 * (c - 4):512 * (c - 3)]
                    cc = v.tensor_copy(asl[:, c, :], bank)
                    cV.inc(cc)
                for c in range(8):
                    v.wait_ge(cP.h, pv0 + 8 + c + 1)
                    bank = PB[c][:, :] if c < 4 else PSQ[:, 512 * (c - 4):512 * (c - 3)]
                    cc = v.tensor_copy(bsl[:, c, :], bank)
                    cV.inc(cc)
                if ts2 >= 1:
                    v.wait_ge(qX3.s[0].h, 32 * ts2)
                    v.wait_ge(qX3.s[1].h, 32 * ts2)
                for c in range(8):
                    v.drain()
                    v.tensor_scalar(x3f[:, 0:512],
                                    asl[:, c, :], ccol[:, c, 7:8], None, ALU.mult)
                    v.tensor_scalar(x3f[:, 512:1024], bsl[:, c, :], ccol[:, c, 7:8],
                                    None, ALU.mult)
                    v.drain()
                    v.scalar_tensor_tensor(asl[:, c, :], asl[:, c, :], ccol[:, c, 6:7],
                                           x3f[:, 512:1024], ALU.mult, ALU.add)
                    v.drain()
                    cc = v.scalar_tensor_tensor(bsl[:, c, :], bsl[:, c, :], ccol[:, c, 6:7],
                                                x3f[:, 0:512], ALU.mult, ALU.subtract)
                    cV.inc(cc)
                v.wait_ge(cA.h, av0 + 13)
                v.drain()
                cc = v.tensor_single_scalar(bitsT[0:24, :], latb[0:24, :], 0.0, ALU.is_gt)
                cV.inc(cc)
                v.wait_ge(cP.h, pv0 + 26)
                cc = v.tensor_single_scalar(synb[0:12, :], PB[1][0:12, :], 0.5, ALU.is_gt)
                cV.inc(cc)
                v.wait_ge(cP.h, pv0 + 27)
                v.drain()
                v.tensor_single_scalar(latb[0:1, :], PB[2][0:1, :], 0.5, ALU.is_gt)
                v.tensor_single_scalar(x3f[0:1, 0:512], PB[2][0:1, :], 3.5, ALU.is_lt)
                v.drain()
                cc = v.tensor_tensor(fixb[0:1, :], latb[0:1, :], x3f[0:1, 0:512], ALU.mult)
                cV.inc(cc)
                v.wait_ge(cP.h, pv0 + 28)
                v.drain()
                v.tensor_tensor(x3f[0:12, 0:512], bitsT[0:12, :], synb[0:12, :], ALU.mult)
                v.drain()
                v.scalar_tensor_tensor(x3f[0:12, 0:512], x3f[0:12, 0:512], -2.0,
                                       synb[0:12, :], ALU.mult, ALU.add)
                v.drain()
                v.tensor_tensor(synb[0:12, :], x3f[0:12, 0:512], PB[3][0:12, :], ALU.mult)
                v.drain()
                v.tensor_copy(corT[0:24, :], bitsT[0:24, :])
                v.drain()
                cc = v.tensor_tensor(corT[0:12, :], corT[0:12, :], synb[0:12, :], ALU.add)
                cV.inc(cc)
                for c in range(8):
                    v.wait_ge(cP.h, pv0 + 28 + c + 1)
                    bank = PB[c][:, :] if c < 4 else PSQ[:, 512 * (c - 4):512 * (c - 3)]
                    cc = v.scalar_tensor_tensor(o2T[:, c, :], bank, blec[:, c, 0:1],
                                                x1T[:, c, :], ALU.add, ALU.add)
                    cV.inc(cc)
                for tc in range(4):
                    i2 = 4 * ts2 + tc
                    v.wait_ge(tO2.s[tc % 2].h, 128 * (2 * ts2 + tc // 2 + 1))
                    if tc == 0 and ts2 >= 1:
                        v.wait_ge(tH3.s[(ts2 - 1) % 2].h,
                                  512 * ((ts2 - 1) // 2 + 1))
                    if i2 >= 2:
                        qX3.wait(v, i2, 16 * ((i2 - 2) // 2 + 1))
                    v.drain()
                    v.tensor_tensor(xfpair[tc % 2], XBF[:, 4 * ts2 + tc, :],
                                    o2tm[tc % 2], ALU.add)
                    v.drain()
                    v.bn_stats(stat[:, 0:6], xfpair[tc % 2][:, 0:512])
                    v.bn_stats(stat[:, 6:12], xfpair[tc % 2][:, 512:1024])
                    v.drain()
                    v.bn_aggr(stat[:, 12:14], stat[:, 0:12])
                    v.drain().then_inc(cV.h, 1)
                    cV.bump()
                    v.drain().then_inc(cV.h, 1)
                    cV.bump()
                    v.wait_ge(cA.h, av0 + 14 + tc)
                    v.reciprocal(stat[:, 17:18], stat[:, 16:17])
                    v.drain()
                    cc = v.tensor_scalar(h3st[:, tc, :], xfpair[tc % 2], stat[:, 12:13],
                                         stat[:, 17:18], ALU.subtract, ALU.mult)
                    cV.inc(cc)
                v.wait_ge(cP.h, pv0 + 37)
                cc = v.tensor_copy(xrg[0:16, :], PB[0][0:16, :])
                cV.inc(cc)
                v.wait_ge(cP.h, pv0 + 38)
                cc = v.tensor_scalar(logf[0:8, :], PB[1][0:8, :], rnrm[0:8, 0:1],
                                     None, ALU.mult)
                cV.inc(cc)
                for tc in range(4):
                    bank = PB[2] if tc % 2 == 0 else PB[3]
                    v.wait_ge(cP.h, pv0 + 38 + tc + 1)
                    v.drain()
                    v.reduce_max(gsc2[:, 0:1], bank[:, 0:8], axis=AX.X)
                    v.drain()
                    cc = v.tensor_scalar(gsc2[:, 1:2], gsc2[:, 0:1], -1.0, None, ALU.mult)
                    cV.inc(cc)
                    v.wait_ge(cA.h, av0 + 17 + tc + 1)
                    v.drain()
                    v.reciprocal(gsc2[:, 3:4], gsc2[:, 2:3])
                    v.tensor_scalar(gsc[:, :], bank[:, 0:8], gsc2[:, 0:1], None, ALU.is_ge)
                    v.drain()
                    v.tensor_tensor_scan(gsc3[:, 0:8], gsc[:, :], zsc[:, :], 0.0,
                                         ALU.add, ALU.add)
                    v.drain()
                    v.tensor_single_scalar(gsc3[:, 8:16], gsc3[:, 0:8], 1.0, ALU.is_equal)
                    v.drain()
                    v.tensor_tensor(gsc[:, :], gsc[:, :], gsc3[:, 8:16], ALU.mult)
                    v.drain()
                    cc = v.tensor_scalar(gatG[:, 4 * ts2 + tc, :], gsc[:, :],
                                         gsc2[:, 3:4], None, ALU.mult)
                    cV.inc(cc)

            @block.scalar
            def _(sc):
                for tc in range(4):
                    sc.wait_ge(cV.h, vv0 + 2 * tc + 1)
                    a = sc.activation(stat[:, 16:17], stat[:, 13:14], AF.Sqrt,
                                      bias=epscol[:, :])
                    cA.inc(a)
                    sc.wait_ge(cV.h, vv0 + 2 * tc + 2)
                    if ts2 > 0 and tc == 0:
                        sc.wait_ge(cP.h, pv0 - PS_ + 16)
                    for k in range(8):
                        sc.dma_start_transpose(
                            h2T[:, k, 128 * tc:128 * (tc + 1)],
                            h2tm[tc % 2][:, 128 * k:128 * (k + 1)]).then_inc(tH2.s[tc % 2].h, 16)
                        tH2.s[tc % 2].bump()
                for c in range(8):
                    sc.wait_ge(cP.h, pv0 + 16 + c + 1)
                    bank = PB[c][:, :] if c < 4 else PSQ[:, 512 * (c - 4):512 * (c - 3)]
                    a = sc.activation(x1T[:, c, :], bank, AF.Copy, scale=float(2.0 ** -10))
                    cA.inc(a)
                sc.wait_ge(cP.h, pv0 + 25)
                a = sc.activation(latb[0:24, :], PB[0][0:24, :], AF.Identity,
                                  bias=belc[0:24, :])
                cA.inc(a)
                for tc in range(4):
                    sc.wait_ge(cV.h, vv0 + 44)
                    if tc >= 2:
                        sc.wait_ge(cV.h, vv0 + 44 + 3 * (tc - 2) + 1)
                    for k in range(8):
                        sc.dma_start_transpose(
                            o2tm[tc % 2][:, 128 * k:128 * (k + 1)],
                            o2T[:, k, 128 * tc:128 * (tc + 1)]).then_inc(tO2.s[tc % 2].h, 16)
                        tO2.s[tc % 2].bump()
                    sc.wait_ge(cV.h, vv0 + 44 + 3 * tc + 2)
                    a = sc.activation(stat[:, 16:17], stat[:, 13:14], AF.Sqrt,
                                      bias=epscol[:, :])
                    cA.inc(a)
                sc.wait_ge(cV.h, vv0 + 56)
                H3Ts = bfa.at(4096 * ts2, [128, 8, 512])
                for tc in range(4):
                    for k in range(8):
                        sc.dma_start_transpose(
                            H3Ts[:, k, 128 * tc:128 * (tc + 1)],
                            h3st[:, tc, 128 * k:128 * (k + 1)]).then_inc(tH3.s[ts2 % 2].h, 16)
                        tH3.s[ts2 % 2].bump()
                for tc in range(4):
                    bank = PB[2] if tc % 2 == 0 else PB[3]
                    sc.wait_ge(cV.h, vv0 + 58 + 2 * tc + 1)
                    a = sc.activation(gsc[:, :], bank[:, 0:8], AF.Exp,
                                      bias=gsc2[:, 1:2], accum_out=gsc2[:, 2:3])
                    cA.inc(a)

            @block.tensor
            def _(t):
                t.wait_ge(tH2.s[0].h, 128 * (2 * ts2 + 2))
                t.wait_ge(tH2.s[1].h, 128 * (2 * ts2 + 2))
                if ts2 == 0:
                    t.wait_ge(sG.h, c_sg1)
                    t.wait_ge(sQ.h, c_q1)
                if ts2 >= 1:
                    t.wait_ge(cV.h, vv0)
                for c in range(8):
                    bank = PB[c][:, :] if c < 4 else PSQ[:, 512 * (c - 4):512 * (c - 3)]
                    for cin in range(8):
                        mm = t.matmul(bank, Cb[:, cin, 128 * c:128 * (c + 1)],
                                      h2T[:, cin, :], start=(cin == 0), stop=(cin == 7))
                    cP.inc(mm)
                for c in range(8):
                    t.wait_ge(cV.h, vv0 + 8 + c + 1)
                    bank = PB[c][:, :] if c < 4 else PSQ[:, 512 * (c - 4):512 * (c - 3)]
                    for cin in range(8):
                        mm = t.matmul(bank, Sb[:, cin, 128 * c:128 * (c + 1)],
                                      h2T[:, cin, :], start=(cin == 0), stop=(cin == 7))
                    cP.inc(mm)
                t.wait_ge(cV.h, vv0 + 32)
                for c in range(8):
                    bank = PB[c][:, :] if c < 4 else PSQ[:, 512 * (c - 4):512 * (c - 3)]
                    for cin in range(8):
                        t.matmul(bank, Cb[:, cin, 128 * c:128 * (c + 1)],
                                 asl[:, cin, :], start=(cin == 0), stop=False)
                    for cin in range(8):
                        mm = t.matmul(bank, Sb[:, cin, 128 * c:128 * (c + 1)],
                                      bsl[:, cin, :], start=False, stop=(cin == 7))
                    cP.inc(mm)
                t.wait_ge(cA.h, av0 + 12)
                for cin in range(8):
                    mm = t.matmul(PB[0][0:24, :], welb[:, cin, :], x1T[:, cin, :],
                                  start=(cin == 0), stop=(cin == 7))
                cP.inc(mm)
                t.wait_ge(cV.h, vv0 + 33)
                mm = t.matmul(PB[1][0:12, :], hmtb[0:24, :], bitsT[0:24, :],
                              start=True, stop=True)
                cP.inc(mm)
                t.wait_ge(cV.h, vv0 + 34)
                mm = t.matmul(PB[2][0:1, :], onescb[0:12, :], synb[0:12, :],
                              start=True, stop=True)
                cP.inc(mm)
                t.wait_ge(cV.h, vv0 + 35)
                mm = t.matmul(PB[3][0:12, :], ones1b[0:1, 0:12], fixb[0:1, :],
                              start=True, stop=True)
                cP.inc(mm)
                t.wait_ge(cV.h, vv0 + 36)
                t.wait_ge(cA.h, av0 + 13)
                for c in range(8):
                    bank = PB[c][:, :] if c < 4 else PSQ[:, 512 * (c - 4):512 * (c - 3)]
                    mm = t.matmul(bank, wleb[0:24, 128 * c:128 * (c + 1)],
                                  corT[0:24, :], start=True, stop=True)
                    cP.inc(mm)
                t.wait_ge(tH3.s[ts2 % 2].h, 512 * (ts2 // 2 + 1))
                t.wait_ge(cV.h, vv0 + 44 + 1)
                H3Ts = bfa.at(4096 * ts2, [128, 8, 512])
                for cin in range(8):
                    mm = t.matmul(PB[0][0:16, :], wgrb[:, cin, :], H3Ts[:, cin, :],
                                  start=(cin == 0), stop=(cin == 7))
                cP.inc(mm)
                t.wait_ge(cV.h, vv0 + 57)
                mm = t.matmul(PB[1][0:8, :], wgtb[0:16, :], xrg[0:16, :],
                              start=True, stop=True)
                cP.inc(mm)
                t.wait_ge(cV.h, vv0 + 58)
                for tc in range(4):
                    bank = PB[2] if tc % 2 == 0 else PB[3]
                    if tc >= 2:
                        t.wait_ge(cV.h, vv0 + 60 + 2 * (tc - 2))
                    mm = t.transpose(bank[:, 0:8], logf[0:8, 128 * tc:128 * (tc + 1)],
                                     ident[0:8, 0:8])
                    cP.inc(mm)

            @block.sync
            def _(s):
                for tc in range(4):
                    i2 = 4 * ts2 + tc
                    s.wait_ge(cV.h, vv0 + 44 + 3 * tc + 1)
                    r0 = 512 * ts2 + 128 * tc
                    qX3.inc(s.dma_start(out=out_hbm[r0:r0 + 128, :],
                                        in_=xfpair[tc % 2]), i2)

        assert cP.n == c_p2 + PS_ * NSL, (cP.n, c_p2)
        assert cV.n == c_v6 + VS * NSL, (cV.n, c_v6)
        assert cA.n == c_a5 + AS_ * NSL, (cA.n, c_a5)
        bfa.pop()
        fpa.pop()
        return qX3

    qX3g = emit_fft()


    def emit_moe():
        bfa.push()
        fpa.push()
        w1hb = bfa.alloc([128, 8, 2048])
        w2hb = bfa.alloc([128, 16, 1024])
        Hh = bfa.alloc([128, 16, 512])
        b2rb = bfa.alloc([1, 1024])
        wstg = [fpa.alloc([128, 2048]) for _ in range(2)]
        stgf = [fpa.alloc([128, 512]) for _ in range(4)]
        b1c = fpa.alloc([128, 32, 1])

        qD1 = PSem(es, nc, "qD1")
        qD = [Sem(es, nc, f"qD{i}", 16) for i in range(4)]

        dbar = phase_barrier()

        d_p0, d_v0, d_a0, d_g0 = cP.n, cV.n, cA.n, cG.n
        d_q0 = sQ.n
        d_sg0 = sG.n
        PR, VR, AR, GR = 264, 64, 128, 24
        nacc = [0]

        for e in range(NE):
            @block.sync
            def _(s):
                if e == 0:
                    s.wait_ge(cB.h, dbar)
                else:
                    s.wait_ge(cA.h, d_a0 + AR * 2 * e)
                for k in range(32):
                    s.dma_start(out=b1c[:, k, :],
                                in_=b1m_in[e, 128 * k:128 * (k + 1)].unsqueeze(1)).then_inc(sQ.h, 16)
                sQ.bump(32)

            @block.gpsimd
            def _(g):
                if e == 0:
                    g.wait_ge(cB.h, dbar)
                else:
                    g.wait_ge(cP.h, d_p0 + PR * 2 * e)
                g.dma_start(out=b2rb[0:1, :], in_=b2m_in[e, :].unsqueeze(0)).then_inc(sG.h, 16)
                sG.bump()

            for hh in range(2):
                r = 2 * e + hh
                rp = d_p0 + PR * r
                rv = d_v0 + VR * r
                ra = d_a0 + AR * r
                rg = d_g0 + GR * r

                @block.sync
                def _(s):
                    if e == 0 and hh == 0:
                        s.wait_ge(cB.h, dbar)
                    for i in range(24):
                        gi = 24 * r + i
                        if gi >= 2:
                            gie = gi - 2
                            s.wait_ge(cG.h, d_g0 + GR * (gie // 24) + (gie % 24) + 1)
                        if i < 8:
                            dd = s.dma_start(
                                out=wstg[gi % 2],
                                in_=w1m_in[e, 128 * i:128 * (i + 1),
                                           2048 * hh:2048 * (hh + 1)])
                        else:
                            j = i - 8
                            dd = s.dma_start(
                                out=wstg[gi % 2][:, 0:1024],
                                in_=w2m_in[e, 2048 * hh + 128 * j:2048 * hh + 128 * (j + 1), :])
                        qD1.inc(dd, gi)

                @block.gpsimd
                def _(g):
                    if r >= 1:
                        g.wait_ge(cP.h, d_p0 + PR * r)
                    for i in range(24):
                        gi = 24 * r + i
                        qD1.wait(g, gi, 16 * (gi // 2 + 1))
                        if i < 8:
                            cc = g.tensor_copy(w1hb[:, i, :], wstg[gi % 2])
                        else:
                            j = i - 8
                            cc = g.tensor_copy(w2hb[:, j, :], wstg[gi % 2][:, 0:1024])
                        cG.inc(cc)

                @block.tensor
                def _(t):
                    t.wait_ge(cG.h, d_g0 + GR * r + 24)
                    if hh == 0:
                        t.wait_ge(sG.h, d_sg0 + 16 * (e + 1))
                    if r >= 1:
                        t.wait_ge(cV.h, d_v0 + VR * r)
                    for ts3 in range(NSL):
                        tsp = rp + 33 * ts3
                        H3Ts = bfa.at(4096 * ts3, [128, 8, 512])
                        if ts3 >= 1:
                            t.wait_ge(cV.h, rv + 8 * ts3)
                        for hc in range(16):
                            if ts3 > 0 or hc >= 2:
                                gidx = 16 * ts3 + hc - 2
                                t.wait_ge(cA.h, ra + gidx + 1)
                            bank = PB[hc % 2]
                            for cin in range(8):
                                mm = t.matmul(bank[:, :], w1hb[:, cin, 128 * hc:128 * (hc + 1)],
                                              H3Ts[:, cin, :], start=(cin == 0), stop=(cin == 7))
                            cP.inc(mm)
                        t.wait_ge(cA.h, ra + 16 * (ts3 + 1))
                        if ts3 >= 1:
                            t.wait_ge(cV.h, rv + 8 * ts3)
                        for hc in range(16):
                            for tc in range(4):
                                for ch in range(2):
                                    g8 = 2 * tc + ch
                                    bank = (PB[g8][:, :] if g8 < 4
                                            else PSQ[:, 512 * (g8 - 4):512 * (g8 - 3)])
                                    mm = t.matmul(bank, Hh[:, hc, 128 * tc:128 * (tc + 1)],
                                                  w2hb[:, hc, 512 * ch:512 * (ch + 1)],
                                                  start=(hc == 0),
                                                  stop=(hh == 1 and hc == 15))
                            if hh == 1 and hc == 15:
                                cP.inc(mm, 2)
                            else:
                                cP.inc(mm)
                        if hh == 0:
                            for g8 in range(8):
                                ch = g8 % 2
                                bank = (PB[g8][:, :] if g8 < 4
                                        else PSQ[:, 512 * (g8 - 4):512 * (g8 - 3)])
                                mm = t.matmul(bank, ones1b[0:1, 0:128],
                                              b2rb[0:1, 512 * ch:512 * (ch + 1)],
                                              start=False, stop=True)
                            cP.inc(mm)

                @block.scalar
                def _(sc):
                    if e > 0 or hh > 0:
                        pass
                    for ts3 in range(NSL):
                        tsp = rp + 33 * ts3
                        if ts3 == 0 and hh == 0:
                            sc.wait_ge(sQ.h, d_q0 + 32 * 16 * (e + 1))
                        for hc in range(16):
                            sc.wait_ge(cP.h, tsp + hc + 1)
                            bank = PB[hc % 2]
                            a = sc.activation(Hh[:, hc, :], bank[:, :], AF.Gelu,
                                              bias=b1c[:, 16 * hh + hc, :])
                            cA.inc(a)

                @block.vector
                def _(v):
                    v.drain()
                    for ts3 in range(NSL):
                        tsp = rp + 33 * ts3
                        v.wait_ge(cP.h, tsp + 33)
                        for tc in range(4):
                            for ch in range(2):
                                g8 = 2 * tc + ch
                                i4 = nacc[0] + 8 * ts3 + g8
                                bank = (PB[g8][:, :] if g8 < 4
                                        else PSQ[:, 512 * (g8 - 4):512 * (g8 - 3)])
                                if i4 >= 4:
                                    v.wait_ge(qD[i4 % 4].h, 16 * ((i4 - 4) // 4 + 1))
                                cc = v.tensor_scalar(stgf[i4 % 4], bank,
                                                     gatG[:, 4 * ts3 + tc, e:e + 1],
                                                     None, ALU.mult)
                                cV.inc(cc)

                @block.gpsimd
                def _(g):
                    for ts3 in range(NSL):
                        for tc in range(4):
                            for ch in range(2):
                                g8 = 2 * tc + ch
                                i4 = nacc[0] + 8 * ts3 + g8
                                g.wait_ge(cV.h, rv + 8 * ts3 + g8 + 1)
                                if i4 >= 4:
                                    g.wait_ge(qD[i4 % 4].h, 16 * ((i4 - 4) // 4 + 1))
                                r0 = 512 * ts3 + 128 * tc
                                dd = g.dma_start(out=out_hbm[r0:r0 + 128,
                                                             512 * ch:512 * (ch + 1)],
                                                 in_=stgf[i4 % 4],
                                                 accum_op=ALU.add)
                                dd.then_inc(qD[i4 % 4].h, 16)
                                qD[i4 % 4].bump()
                nacc[0] += 64

        @block.gpsimd
        def _(g):
            for i in range(4):
                g.wait_ge(qD[i].h, qD[i].n)
            g.wait_ge(qX3g.s[0].h, qX3g.s[0].n)
            g.wait_ge(qX3g.s[1].h, qX3g.s[1].n)
        bfa.pop()
        fpa.pop()

    emit_moe()

    @block.gpsimd
    def _(g):
        g.wait_ge(sQ.h, sQ.n)
        g.wait_ge(sG.h, sG.n)
        g.wait_ge(cV.h, cV.n)
        g.wait_ge(cP.h, cP.n)
        g.wait_ge(cA.h, cA.n)

    es.close()
    return nc, dbg


def _host_inputs(core_inputs):
    return core_inputs


_GOLAY_A = np.array(
    [[1, 1, 1, 0, 1, 1, 1, 0, 0, 0, 1, 1], [1, 1, 0, 1, 1, 1, 0, 0, 0, 1, 1, 0],
     [1, 0, 1, 1, 1, 0, 0, 0, 1, 1, 0, 1], [0, 1, 1, 1, 0, 0, 0, 1, 1, 0, 1, 1],
     [0, 0, 0, 0, 1, 1, 1, 1, 1, 1, 1, 1], [1, 1, 1, 1, 0, 0, 0, 0, 0, 0, 0, 0],
     [0, 0, 0, 0, 1, 1, 1, 1, 0, 0, 0, 0], [1, 1, 0, 0, 1, 1, 0, 0, 1, 1, 0, 0],
     [1, 0, 1, 0, 1, 0, 1, 0, 1, 0, 1, 0], [0, 1, 0, 1, 0, 1, 0, 1, 0, 1, 0, 1],
     [1, 0, 0, 1, 1, 0, 0, 1, 1, 0, 0, 1], [0, 1, 1, 0, 0, 1, 1, 0, 0, 1, 1, 0]],
    dtype=np.float32)
_HMT = np.concatenate([_GOLAY_A.T, np.eye(12, dtype=np.float32)], axis=1).T.copy()


def host_maps(inputs):
    f = np.float32
    def T(a):
        return np.ascontiguousarray(np.asarray(a, dtype=f).T)
    def C(a):
        return np.ascontiguousarray(np.asarray(a, dtype=f))
    common = {
        "wq1": T(inputs["L0_attn_wq1"]), "wk1": T(inputs["L0_attn_wk1"]),
        "wv1": T(inputs["L0_attn_wv1"]), "wo1": T(inputs["L0_attn_wo1"]),
        "wq2": T(inputs["L0_attn_wq2"]), "wk2": T(inputs["L0_attn_wk2"]),
        "wv2": T(inputs["L0_attn_wv2"]), "wo2": T(inputs["L0_attn_wo2"]),
        "bq1": C(inputs["L0_attn_bq1"]), "bk1": C(inputs["L0_attn_bk1"]),
        "bv1": C(inputs["L0_attn_bv1"]), "bo1": C(inputs["L0_attn_bo1"]),
        "bq2": C(inputs["L0_attn_bq2"]), "bk2": C(inputs["L0_attn_bk2"]),
        "bv2": C(inputs["L0_attn_bv2"]), "bo2": C(inputs["L0_attn_bo2"]),
        "g1": C(inputs["L0_attn_g1"]), "be1": C(inputs["L0_attn_be1"]),
        "g2": C(inputs["L0_attn_g2"]), "be2": C(inputs["L0_attn_be2"]),
        "wff1": T(inputs["L0_attn_w_ff1"]), "bff1": C(inputs["L0_attn_b_ff1"]),
        "wff2": T(inputs["L0_attn_w_ff2"]), "bff2": C(inputs["L0_attn_b_ff2"]),
        "gff": C(inputs["L0_attn_g_ff"]), "beff": C(inputs["L0_attn_be_ff"]),
        "wr": T(inputs["L0_ffn_W_r"]), "wkk": T(inputs["L0_ffn_W_k"]),
        "wvv": T(inputs["L0_ffn_W_v"]),
        "mur": C(inputs["L0_ffn_mu_r"]), "muk": C(inputs["L0_ffn_mu_k"]),
        "wel": C(inputs["L1_attn_W_el"]), "bel": C(inputs["L1_attn_b_el"]),
        "wle": C(inputs["L1_attn_W_le"]), "ble": C(inputs["L1_attn_b_le"]),
        "hmt": np.ascontiguousarray(_HMT),
        "wgred": T(inputs["L1_ffn_wg_red"]), "wgm": C(inputs["L1_ffn_wg"]),
        "wgt": T(inputs["L1_ffn_wg"]),
        "b1m": C(inputs["L1_ffn_b1"]), "b2m": C(inputs["L1_ffn_b2"]),
        "w1m": C(inputs["L1_ffn_w1"]), "w2m": C(inputs["L1_ffn_w2"]),
        "ident": np.eye(128, dtype=f),
    }
    x = np.asarray(inputs["L0_attn_x"], dtype=f)
    return [dict(common, x=np.ascontiguousarray(x[b])) for b in range(x.shape[0])]


def kernel(**inputs):
    maps = host_maps(inputs)
    nc, _ = build_program()
    res = run_bass_kernel_spmd(nc, maps, core_ids=list(range(len(maps))))
    return np.stack([np.asarray(r["out"], dtype=np.float32) for r in res.results])


# version 03
# speedup vs baseline: 1.0020x; 1.0020x over previous
import numpy as np
import concourse.bass as bass
import concourse.mybir as mybir
from concourse.bass_utils import run_bass_kernel_spmd

F32 = mybir.dt.float32
BF16 = mybir.dt.bfloat16
AF = mybir.ActivationFunctionType
ALU = mybir.AluOpType
AX = mybir.AxisListType

S, D, DH, H, DK, DFF, NE = 4096, 1024, 512, 8, 64, 4096, 8
NCH, NSL = 32, 8

ALPHA = 0.1
BETA = 0.1
A11 = 1.0 - ALPHA
A12 = 1.0 - ALPHA
A21 = -(ALPHA * BETA)
A22 = 1.0 - ALPHA * BETA
B1 = ALPHA
B2 = ALPHA * BETA
LNEPS = 1e-5


class Sem:
    REG = None

    def __init__(self, es, nc, name, step=1):
        self.h = es.enter_context(nc.semaphore(name))
        self.n = 0
        self.step = step
        if Sem.REG is not None:
            Sem.REG.append(self)

    def inc(self, inst, k=1):
        inst.then_inc(self.h, self.step * k)
        self.n += self.step * k
        return self.n

    def bump(self, k=1):
        self.n += self.step * k
        return self.n

    def wait(self, eng, val=None):
        eng.wait_ge(self.h, self.n if val is None else val)


class PSem:

    def __init__(self, es, nc, name):
        self.s = [Sem(es, nc, name + "a", 16), Sem(es, nc, name + "b", 16)]

    def inc(self, inst, i):
        return self.s[i % 2].inc(inst)

    def bump(self, i, k=1):
        return self.s[i % 2].bump(k)

    def wait(self, eng, i, val):
        eng.wait_ge(self.s[i % 2].h, val)


class Arena:
    def __init__(self, ap, elems):
        self.ap, self.elems, self.off, self.stack = ap, elems, 0, []

    def alloc(self, shape):
        n = int(np.prod(shape[1:]))
        assert self.off + n <= self.elems, (self.off, n, self.elems)
        flat = self.ap[:, self.off:self.off + n]
        self.off += n
        if len(shape) == 3:
            flat = flat.rearrange("p (a b) -> p a b", a=shape[1], b=shape[2])
        return flat

    def at(self, off, shape):
        n = int(np.prod(shape[1:]))
        assert off + n <= self.elems
        flat = self.ap[:, off:off + n]
        if len(shape) == 3:
            flat = flat.rearrange("p (a b) -> p a b", a=shape[1], b=shape[2])
        return flat

    def push(self):
        self.stack.append(self.off)

    def pop(self):
        self.off = self.stack.pop()


def build_program(debug=(), stop_after=None):
    nc = bass.Bass("TRN2", target_bir_lowering=False)

    x_in = nc.dram_tensor("x", [S, D], F32, kind="ExternalInput").ap()
    out_hbm = nc.dram_tensor("out", [S, D], F32, kind="ExternalOutput").ap()

    def din(name, shape):
        return nc.dram_tensor(name, shape, F32, kind="ExternalInput").ap()

    wq = [din("wq1", [DH, DH]), din("wq2", [DH, DH])]
    wk = [din("wk1", [DH, DH]), din("wk2", [DH, DH])]
    wv = [din("wv1", [DH, DH]), din("wv2", [DH, DH])]
    wo = [din("wo1", [DH, DH]), din("wo2", [DH, DH])]
    bqi = [din("bq1", [DH]), din("bq2", [DH])]
    bki = [din("bk1", [DH]), din("bk2", [DH])]
    bvi = [din("bv1", [DH]), din("bv2", [DH])]
    boi = [din("bo1", [DH]), din("bo2", [DH])]
    gbi = [(din("g1", [DH]), din("be1", [DH])), (din("g2", [DH]), din("be2", [DH]))]
    wff1_in = din("wff1", [D, DFF]); bff1_in = din("bff1", [DFF])
    wff2_in = din("wff2", [DFF, D]); bff2_in = din("bff2", [D])
    gff_in = din("gff", [D]); beff_in = din("beff", [D])
    wr_in = din("wr", [D, D]); wkk_in = din("wkk", [D, D]); wvv_in = din("wvv", [D, D])
    mur_in = din("mur", [D]); muk_in = din("muk", [D])
    wel_in = din("wel", [D, 24]); bel_in = din("bel", [24])
    wle_in = din("wle", [24, D]); ble_in = din("ble", [D])
    hmt_in = din("hmt", [24, 12])
    wgred_in = din("wgred", [D, 16])
    wgm_in = din("wgm", [NE, 16]); wgt_in = din("wgt", [16, NE])
    b1m_in = din("b1m", [NE, DFF]); b2m_in = din("b2m", [NE, D])
    w1m_in = din("w1m", [NE, D, DFF]); w2m_in = din("w2m", [NE, DFF, D])
    ident_in = din("ident", [128, 128])
    x1sp = nc.dram_tensor("x1sp", [S, DH], BF16, kind="Internal").ap()
    x2sp = nc.dram_tensor("x2sp", [S, DH], BF16, kind="Internal").ap()
    xshsp = nc.dram_tensor("xshsp", [S, DH], BF16, kind="Internal").ap()

    dbg = {}
    for spec in debug:
        name, shape = spec[0], spec[1]
        ddt = BF16 if (len(spec) > 2 and spec[2] == "bf16") else F32
        dbg[name] = nc.dram_tensor("dbg_" + name, list(shape), ddt,
                                   kind="ExternalOutput").ap()

    from contextlib import ExitStack
    es = ExitStack()

    BFA_N, FPA_N = 84480, 7936
    BFAt = es.enter_context(nc.sbuf_tensor("BFA", [128, BFA_N], BF16))
    FPAt = es.enter_context(nc.sbuf_tensor("FPA", [128, FPA_N], F32))
    bfa = Arena(BFAt[:, :], BFA_N)
    fpa = Arena(FPAt[:, :], FPA_N)
    carry = fpa.alloc([128, 2, 512])

    ident = es.enter_context(nc.sbuf_tensor("idents", [128, 128], F32))
    ones1 = es.enter_context(nc.sbuf_tensor("ones1r", [1, 512], F32))
    ones1b = es.enter_context(nc.sbuf_tensor("ones1b", [1, 512], BF16))
    onescb = es.enter_context(nc.sbuf_tensor("onescb", [128, 1], BF16))
    srow = es.enter_context(nc.sbuf_tensor("srow", [1, 64], F32))
    spow = es.enter_context(nc.sbuf_tensor("spow", [2, 8, 2], F32))
    tpow = es.enter_context(nc.sbuf_tensor("tpow", [2, 8, 2], F32))
    cpair = es.enter_context(nc.sbuf_tensor("cpair", [2, 520], F32))
    rseq = es.enter_context(nc.sbuf_tensor("rseq", [2, 128], F32))
    e1seq = es.enter_context(nc.sbuf_tensor("e1seq", [2, 128], F32))
    cpt = es.enter_context(nc.sbuf_tensor("cpt", [2, 128], F32))
    w2t = es.enter_context(nc.sbuf_tensor("w2t", [128, 2], F32))
    w2ct = es.enter_context(nc.sbuf_tensor("w2ct", [128, 2], F32))
    t0t = es.enter_context(nc.sbuf_tensor("t0t", [128, 128], BF16))
    t0ct = es.enter_context(nc.sbuf_tensor("t0ct", [128, 128], BF16))
    wrow = es.enter_context(nc.sbuf_tensor("wrow", [1, 128], F32))
    epscol = es.enter_context(nc.sbuf_tensor("epscol", [128, 1], F32))

    ln0 = es.enter_context(nc.sbuf_tensor("ln0s", [128, NCH, 2], F32))
    gatG = es.enter_context(nc.sbuf_tensor("gatG", [128, NCH, 8], F32))

    PSQ = es.enter_context(nc.psum_tensor("PSQ", [128, 2048], F32))
    PB = [es.enter_context(nc.psum_tensor(f"PB{i}", [128, 512], F32)) for i in range(4)]

    Sem.REG = []
    sQ = Sem(es, nc, "sQ", 16)
    sT = Sem(es, nc, "sT", 16)
    sG = Sem(es, nc, "sG", 16)
    cP = Sem(es, nc, "cP")
    cV = Sem(es, nc, "cV")
    cA = Sem(es, nc, "cA")
    cG = Sem(es, nc, "cG")
    cB = Sem(es, nc, "cB")

    def phase_barrier(taps=()):
        sems = [(s_.h, s_.n) for s_ in Sem.REG if s_.n > 0 and s_ is not cB]
        live = [(dbg[n_], a_) for n_, a_ in taps if n_ in dbg]
        for _ in live:
            sQ.bump()
        sqn = sQ.n
        @block.sync
        def _(s):
            for h_, n_ in sems:
                s.wait_ge(h_, n_)
            for d_, a_ in live:
                s.dma_start(out=d_, in_=a_).then_inc(sQ.h, 16)
            if live:
                s.wait_ge(sQ.h, sqn)
            s.nop().then_inc(cB.h, 1)
        cB.bump()
        return cB.n

    block = es.enter_context(nc.Block())

    @block.sync
    def _(s):
        s.dma_start(out=ident[:, :], in_=ident_in[:, :]).then_inc(sQ.h, 16)
        sQ.bump()
    sq_ident = sQ.n

    @block.vector
    def _(v):
        v.memset(ones1[:, :], 1.0)
        v.memset(ones1b[:, :], 1.0)
        v.memset(onescb[:, :], 1.0)
        vals = [A11, A12, A21, A22, A11, A21, A12, A22,
                1.0, 0.0, 0.0, 1.0, B1, B2]
        for i, val in enumerate(vals):
            v.memset(srow[0:1, i:i + 1], val)
        v.drain()
        for i, val in enumerate(vals):
            lo = float(np.float64(val) - np.float64(np.float32(val)))
            if lo != 0.0:
                v.tensor_scalar(srow[0:1, i:i + 1], srow[0:1, i:i + 1], lo,
                                None, ALU.add)
        v.drain()
        v.memset(cpair[:, :], 0.0)
        v.memset(rseq[:, :], 0.0)
        v.memset(e1seq[:, :], 0.0)
        v.memset(cpt[:, :], 0.0)
        v.memset(carry[:, :, :], 0.0)
        v.memset(t0t[:, :], 0.0)
        v.memset(epscol[:, :], LNEPS)
        v.drain().then_inc(cV.h, 1)
        cV.bump()
    cv0 = cV.n

    @block.gpsimd
    def _(g):
        g.wait_ge(cV.h, cv0)
        g.dma_start(out=spow[:, 0, :], in_=srow[0:1, 0:4]).then_inc(sG.h, 16)
        g.dma_start(out=tpow[:, 0, :], in_=srow[0:1, 4:8]).then_inc(sG.h, 16)
        g.dma_start(out=cpair[:, 0:2], in_=srow[0:1, 8:12]).then_inc(sG.h, 16)
        g.dma_start(out=rseq[:, 127:128], in_=srow[0:1, 12:14]).then_inc(sG.h, 16)
        sG.bump(4)
    sg0 = sG.n

    cp_base = cP.n
    @block.tensor
    def _(t):
        t.wait_ge(sG.h, sg0)
        for k in range(7):
            if k > 0:
                t.wait_ge(cV.h, cv0 + k)
            m = 1 << k
            t.matmul(PB[0][0:2, 0:2], tpow[:, k, :], spow[:, k, :], start=True, stop=True)
            t.matmul(PB[0][0:2, 4:6], spow[:, k, :], tpow[:, k, :], start=True, stop=True)
            t.matmul(PB[1][0:2, 0:2 * m], tpow[:, k, :], cpair[:, 0:2 * m],
                     start=True, stop=True)
            mm = t.matmul(PB[2][0:2, 0:m], tpow[:, k, :], rseq[:, 128 - m:128],
                          start=True, stop=True)
            cP.inc(mm)

    @block.vector
    def _(v):
        for k in range(7):
            v.wait_ge(cP.h, cp_base + k + 1)
            m = 1 << k
            v.tensor_copy(spow[:, k + 1, :], PB[0][0:2, 0:2])
            v.tensor_copy(tpow[:, k + 1, :], PB[0][0:2, 4:6])
            v.tensor_copy(cpair[:, 2 * m:4 * m], PB[1][0:2, 0:2 * m])
            v.tensor_copy(rseq[:, 128 - 2 * m:128 - m], PB[2][0:2, 0:m])
            v.drain().then_inc(cV.h, 1)
            cV.bump()
    cv_dbl = cV.n

    @block.tensor
    def _(t):
        t.wait_ge(cV.h, cv_dbl)
        mm = t.matmul(PB[1][0:2, 0:2], tpow[:, 7, :], cpair[:, 0:2], start=True, stop=True)
        cP.inc(mm)
    cp_a128 = cP.n

    @block.vector
    def _(v):
        v.wait_ge(cP.h, cp_a128)
        v.tensor_copy(cpair[:, 256:258], PB[1][0:2, 0:2])
        v.tensor_scalar(wrow[0:1, :], cpair[0:1, 0:256:2], B1, None, ALU.mult)
        v.drain()
        v.scalar_tensor_tensor(wrow[0:1, :], cpair[0:1, 1:257:2], B2, wrow[0:1, :],
                               ALU.mult, ALU.add)
        v.tensor_copy(cpt[0:1, :], cpair[0:1, 2:258:2])
        v.tensor_copy(e1seq[0:1, :], cpair[0:1, 0:256:2])
        v.drain().then_inc(cV.h, 1)
        cV.bump()
    cv_rows = cV.n

    @block.tensor
    def _(t):
        t.wait_ge(cV.h, cv_rows)
        t.wait_ge(sQ.h, sq_ident)
        t.matmul(PB[0][0:2, 0:128], srow[0:1, 8:10], cpair[0:1, 2:258:2],
                 start=True, stop=False)
        t.matmul(PB[0][0:2, 0:128], srow[0:1, 10:12], cpair[0:1, 3:259:2],
                 start=False, stop=True)
        t.matmul(PB[1][0:2, 0:128], srow[0:1, 8:10], cpair[0:1, 0:256:2],
                 start=True, stop=False)
        t.matmul(PB[1][0:2, 0:128], srow[0:1, 10:12], cpair[0:1, 1:257:2],
                 start=False, stop=True)
        t.transpose(PB[2][0:128, 0:2], rseq[0:2, :], ident[0:2, 0:2])
        mm = t.matmul(PB[3][0:1, 0:2], cpair[:, 254:255], ident[0:2, 0:2],
                      start=True, stop=True)
        cP.inc(mm)
    cp_rows2 = cP.n

    @block.vector
    def _(v):
        v.wait_ge(cP.h, cp_rows2)
        v.tensor_copy(cpt[:, :], PB[0][0:2, 0:128])
        v.tensor_copy(e1seq[:, :], PB[1][0:2, 0:128])
        v.tensor_copy(w2t[:, :], PB[2][0:128, 0:2])
        v.tensor_copy(srow[0:1, 16:18], PB[3][0:1, 0:2])
        cc = v.tensor_copy(t0t[0:1, 0:1], wrow[0:1, 0:1])
        cV.inc(cc)
    sg_rows = sG.n
    cv_rows2 = cV.n

    cp_t0b = cP.n
    sg_t0b = sG.n
    cv_t0b = cV.n
    @block.tensor
    def _(t):
        t.wait_ge(cV.h, cv_rows2)
        for k in range(7):
            m = 1 << k
            if k > 0:
                t.wait_ge(sG.h, sg_t0b + 16 * k)
            mm = t.matmul(PB[3][0:m, 0:m], rseq[:, 127 - m:127], e1seq[:, 0:m],
                          start=True, stop=True)
            cP.inc(mm)

    @block.vector
    def _(v):
        for k in range(7):
            m = 1 << k
            v.wait_ge(cP.h, cp_t0b + k + 1)
            if k > 0:
                v.wait_ge(sG.h, sg_t0b + 16 * k)
            cc = v.tensor_copy(t0t[0:m, m:2 * m], PB[3][0:m, 0:m])
            cV.inc(cc)

    @block.gpsimd
    def _(g):
        for k in range(7):
            m = 1 << k
            g.wait_ge(cV.h, cv_t0b + k + 1)
            g.dma_start(out=t0t[m:2 * m, m:2 * m], in_=t0t[0:m, 0:m]).then_inc(sG.h, 16)
            sG.bump()
    sg_t0d = sG.n

    @block.vector
    def _(v):
        v.wait_ge(sG.h, sg_t0d)
        v.tensor_copy(t0ct[:, :], t0t[:, :])
        v.drain()
        v.tensor_copy(t0ct[0:1, :], cpair[0:1, 0:256:2])
        v.tensor_copy(w2ct[:, :], w2t[:, :])
        v.drain()
        v.tensor_copy(w2ct[0:1, 0:2], srow[0:1, 16:18])
        v.drain().then_inc(cV.h, 1)
        cV.bump()
    cv_t0c = cV.n
    sg_c0 = sG.n
    cv_c0 = cV.n

    def tap_small(name, src_ap):
        if name not in dbg:
            return
        @block.sync
        def _(s):
            s.wait_ge(sG.h, sg_c0)
            s.wait_ge(cV.h, cv_c0)
            s.dma_start(out=dbg[name][:, :], in_=src_ap).then_inc(sQ.h, 16)
            sQ.bump()

    tap_small("cpair", cpair[:, :])
    tap_small("rseq", rseq[:, :])
    tap_small("w2t", w2t[:, :])
    tap_small("w2ct", w2ct[:, :])
    tap_small("cpt", cpt[:, :])
    tap_small("wrow", wrow[:, :])

    if "t0t" in dbg or "t0ct" in dbg:
        fpa.push()
        tf = fpa.alloc([128, 128])
        tf2 = fpa.alloc([128, 128])
        @block.scalar
        def _(sc):
            sc.wait_ge(cV.h, cv_c0)
            sc.wait_ge(sG.h, sg_c0)
            sc.copy(tf, t0t[:, :])
            a = sc.copy(tf2, t0ct[:, :])
            cA.inc(a)
        ca_t0 = cA.n
        @block.sync
        def _(s):
            s.wait_ge(cA.h, ca_t0)
            if "t0t" in dbg:
                s.dma_start(out=dbg["t0t"][:, :], in_=tf).then_inc(sQ.h, 16)
                sQ.bump()
            if "t0ct" in dbg:
                s.dma_start(out=dbg["t0ct"][:, :], in_=tf2).then_inc(sQ.h, 16)
                sQ.bump()
        fpa.pop()

    XBF = bfa.at(0, [128, NCH, 1024])
    R1 = 32768

    fpa.push()
    xfp = [fpa.alloc([128, 1024]) for _ in range(2)]
    bnst = fpa.alloc([128, 2, 12])
    bnag = fpa.alloc([128, 2, 2])
    sdc = fpa.alloc([128, 2, 1])

    qx = PSem(es, nc, "qx")
    a1_v0, a1_a0, a1_g0 = cV.n, cA.n, cG.n

    @block.sync
    def _(s):
        s.wait_ge(sQ.h, sQ.n)
        for c in range(NCH):
            if c >= 2:
                s.wait_ge(cV.h, a1_v0 + 2 * (c - 1))
                s.wait_ge(cG.h, a1_g0 + (c - 1))
            qx.inc(s.dma_start(out=xfp[c % 2], in_=x_in[128 * c:128 * (c + 1), :]), c)

    @block.vector
    def _(v):
        for c in range(NCH):
            qx.wait(v, c, 16 * (c // 2 + 1))
            p = c % 2
            v.bn_stats(bnst[:, p, 0:6], xfp[p][:, 0:512])
            v.bn_stats(bnst[:, p, 6:12], xfp[p][:, 512:1024])
            v.drain()
            v.bn_aggr(bnag[:, p, :], bnst[:, p, 0:12])
            v.drain()
            v.tensor_copy(ln0[:, c, 0:1], bnag[:, p, 0:1])
            v.drain().then_inc(cV.h, 1)
            cV.bump()
            if c >= 1:
                v.wait_ge(cA.h, a1_a0 + c)
                rr = v.reciprocal(ln0[:, c - 1, 1:2], sdc[:, (c - 1) % 2, :])
                cV.inc(rr)
        v.wait_ge(cA.h, a1_a0 + NCH)
        rr = v.reciprocal(ln0[:, NCH - 1, 1:2], sdc[:, (NCH - 1) % 2, :])
        cV.inc(rr)
    a1_v1 = cV.n

    @block.scalar
    def _(sc):
        for c in range(NCH):
            sc.wait_ge(cV.h, a1_v0 + 2 * c + 1)
            a = sc.activation(sdc[:, c % 2, :], bnag[:, c % 2, 1:2], AF.Sqrt,
                              bias=epscol[:, :])
            cA.inc(a)

    @block.gpsimd
    def _(g):
        for c in range(NCH):
            qx.wait(g, c, 16 * (c // 2 + 1))
            cc = g.tensor_copy(XBF[:, c, :], xfp[c % 2])
            cG.inc(cc)
    a1_g1 = cG.n
    fpa.pop()


    if "ln0m" in dbg:
        fpa.push()
        tf = fpa.alloc([128, 32])
        tf2 = fpa.alloc([128, 32])
        @block.vector
        def _(v):
            v.wait_ge(cV.h, a1_v1)
            v.wait_ge(cG.h, a1_g1)
            v.tensor_copy(tf, ln0[:, :, 0])
            cc = v.tensor_copy(tf2, ln0[:, :, 1])
            cV.inc(cc)
        vv = cV.n
        @block.sync
        def _(s):
            s.wait_ge(cV.h, vv)
            s.dma_start(out=dbg["ln0m"][:, :], in_=tf).then_inc(sQ.h, 16)
            s.dma_start(out=dbg["ln0r"][:, :], in_=tf2).then_inc(sQ.h, 16)
            sQ.bump(2)
        fpa.pop()

    bfa.off = R1
    bfa.push()
    ubf = [bfa.alloc([128, 512]) for _ in range(2)]
    xst = [bfa.alloc([128, 512]) for _ in range(2)]
    fpa.push()
    ufp = [fpa.alloc([128, 512]) for _ in range(2)]
    qH = PSem(es, nc, "qH")

    h_v0, h_p0 = cV.n, cP.n
    h_q0 = sQ.n

    @block.vector
    def _(v):
        v.drain()
        v.wait_ge(sQ.h, h_q0)
        v.wait_ge(cG.h, a1_g1)
        for c in range(NCH):
            if c >= 1:
                v.wait_ge(cP.h, h_p0 + 2 * c)
                cc = v.tensor_copy(carry[0:2, c % 2, :], PB[1][0:2, :])
                cV.inc(cc)
                if c >= 3:
                    qH.wait(v, c - 1, 16 * ((c - 3) // 2 + 1))
                cc = v.tensor_copy(xst[(c - 1) % 2], PB[0][:, :])
                cV.inc(cc)
            p = c % 2
            v.tensor_scalar(ufp[p], XBF[:, c, 0:512], ln0[:, c, 0:1],
                            ln0[:, c, 1:2], ALU.subtract, ALU.mult)
            v.drain()
            cc = v.tensor_copy(ubf[p], ufp[p])
            cV.inc(cc)
        v.wait_ge(cP.h, h_p0 + 2 * NCH)
        qH.wait(v, NCH - 1, 16 * ((NCH - 3) // 2 + 1))
        cc = v.tensor_copy(xst[(NCH - 1) % 2], PB[0][:, :])
        cV.inc(cc)
    h_v2 = cV.n

    @block.gpsimd
    def _(g):
        for c in range(NCH):
            if c < NCH - 1:
                g.wait_ge(cV.h, h_v0 + 3 * (c + 1))
            else:
                g.wait_ge(cV.h, h_v2)
            qH.inc(g.dma_start(out=xshsp[128 * c:128 * (c + 1), :],
                               in_=xst[c % 2]), c)
    

    @block.tensor
    def _(t):
        for c in range(NCH):
            t.wait_ge(cV.h, h_v0 + 3 * c + 1)
            p = c % 2
            tt = t0ct if c == 0 else t0t
            ww = w2ct if c == 0 else w2t
            t.matmul(PB[0][:, :], tt[:, :], ubf[p], start=True, stop=False)
            mm = t.matmul(PB[0][:, :], cpt[:, :], carry[0:2, c % 2, :],
                          start=False, stop=True)
            cP.inc(mm)
            t.matmul(PB[1][0:2, :], ww[:, :], ufp[p], start=True, stop=False)
            mm = t.matmul(PB[1][0:2, :], tpow[:, 7, :], carry[0:2, c % 2, :],
                          start=False, stop=True)
            cP.inc(mm)
    h_p1 = cP.n

    bfa.pop()
    fpa.pop()


    def emit_mha(li, dst_sp):
        bfa.push()
        fpa.push()
        K1T = bfa.alloc([128, 4, S])
        V1 = bfa.alloc([128, NCH, 520])
        wqb = bfa.alloc([128, 4, 512])
        wob = bfa.alloc([128, 8, 512])
        tmsl = [bfa.alloc([128, 4, 512])]
        tsl = [bfa.alloc([128, 4, 512])]
        wcb = [bfa.alloc([128, 1024]) for _ in range(2)]
        browb = bfa.alloc([1, 4, 512])
        qtt = bfa.alloc([128, 512])
        aq = [bfa.alloc([128, 1024]) for _ in range(2)]
        ohb = bfa.alloc([128, 512])
        stg = [bfa.alloc([128, 512]) for _ in range(2)]

        wfs = [fpa.alloc([128, 1024]) for _ in range(2)]
        bqkc = fpa.alloc([128, 4, 2])
        bcB = fpa.alloc([128, 512])
        bcG = fpa.alloc([128, 512])
        bcE = fpa.alloc([128, 512])
        oacc = fpa.alloc([128, 4, 512])
        denA = fpa.alloc([128, 512])
        rcq = fpa.alloc([128, 4])
        sfp = fpa.alloc([128, 512])
        stat = fpa.alloc([128, 18])

        qW = PSem(es, nc, "qW" + str(li))
        qK = PSem(es, nc, "qK" + str(li))
        qM = PSem(es, nc, "qM" + str(li))
        tT = PSem(es, nc, "tT" + str(li))
        tQ = Sem(es, nc, "tQ" + str(li), 16)
        qO = PSem(es, nc, "qO" + str(li))

        wqi, wki, wvi, woi = wq[li], wk[li], wv[li], wo[li]
        bqI, bkI, bvI, boI = bqi[li], bki[li], bvi[li], boi[li]
        gI, beI = gbi[li]

        ebar = phase_barrier()

        mw_g0 = cG.n
        @block.sync
        def _(s):
            s.wait_ge(cB.h, ebar)
            for i in range(12):
                if i >= 2:
                    s.wait_ge(cG.h, mw_g0 + i - 1)
                if i < 4:
                    dd = s.dma_start(out=wfs[i % 2][:, 0:512],
                                     in_=wqi[128 * i:128 * (i + 1), :])
                else:
                    hh = i - 4
                    dd = s.dma_start(out=wfs[i % 2][0:64, 0:512],
                                     in_=woi[64 * hh:64 * (hh + 1), :])
                qW.inc(dd, i)
            for c_ in range(4):
                s.dma_start(out=bqkc[:, c_, 0:1],
                            in_=bqI[128 * c_:128 * (c_ + 1)].unsqueeze(1)).then_inc(sQ.h, 16)
                s.dma_start(out=bqkc[:, c_, 1:2],
                            in_=bkI[128 * c_:128 * (c_ + 1)].unsqueeze(1)).then_inc(sQ.h, 16)
            sQ.bump(8)
        mw_q1 = sQ.n

        @block.gpsimd
        def _(g):
            g.wait_ge(cB.h, ebar)
            g.dma_start(out=browb[0:1, 0, :], in_=bvI.unsqueeze(0)).then_inc(sG.h, 16)
            g.dma_start(out=browb[0:1, 1, :], in_=boI.unsqueeze(0)).then_inc(sG.h, 16)
            g.dma_start(out=browb[0:1, 2, :], in_=gI.unsqueeze(0)).then_inc(sG.h, 16)
            g.dma_start(out=browb[0:1, 3, :], in_=beI.unsqueeze(0)).then_inc(sG.h, 16)
            sG.bump(4)
            for i in range(12):
                qW.wait(g, i, 16 * (i // 2 + 1))
                if i < 4:
                    cc = g.tensor_copy(wqb[:, i, :], wfs[i % 2][:, 0:512])
                else:
                    hh = i - 4
                    cc = g.tensor_copy(wob[0:64, hh, :], wfs[i % 2][0:64, 0:512])
                cG.inc(cc)
        mw_g1 = cG.n
        mw_sg1 = sG.n

        @block.vector
        def _(v):
            v.drain()
            v.wait_ge(cB.h, ebar)
            v.wait_ge(sQ.h, mw_q1)
            v.tensor_scalar(bqkc[:, :, 0:1], bqkc[:, :, 0:1], 0.125, None, ALU.mult)
            v.memset(V1[:, :, 64:520:65], 1.0)
            v.drain().then_inc(cV.h, 1)
            cV.bump()
        mw_v1 = cV.n

        @block.tensor
        def _(t):
            t.wait_ge(cB.h, ebar)
            t.wait_ge(sG.h, mw_sg1)
            t.matmul(PB[0][:, :], ones1b[0:1, 0:128], browb[0:1, 1, :], start=True, stop=True)
            t.matmul(PB[1][:, :], ones1b[0:1, 0:128], browb[0:1, 2, :], start=True, stop=True)
            mm = t.matmul(PB[2][:, :], ones1b[0:1, 0:128], browb[0:1, 3, :], start=True, stop=True)
            cP.inc(mm)
        mw_p1 = cP.n

        @block.vector
        def _(v):
            v.wait_ge(cP.h, mw_p1)
            v.tensor_copy(bcB, PB[0][:, :])
            v.tensor_copy(bcG, PB[1][:, :])
            cc = v.tensor_copy(bcE, PB[2][:, :])
            cV.inc(cc)
        mw_v2 = cV.n

        kv_p0, kv_v0, kv_a0, kv_g0 = cP.n, cV.n, cA.n, cG.n
        NVI = 5 if li == 0 else 4
        for ts in range(NSL):
            if li == 0:
                @block.vector
                def _(v):
                    v.drain()
                    if ts == 0:
                        v.wait_ge(cV.h, mw_v2)
                    if ts >= 1:
                        v.wait_ge(tT.s[0].h, 256 * ts)
                    for tc in range(4):
                        c = 4 * ts + tc
                        cc = v.tensor_scalar(tmsl[0][:, tc, :], XBF[:, c, 0:512],
                                             ln0[:, c, 0:1], ln0[:, c, 1:2],
                                             ALU.subtract, ALU.mult)
                    cV.inc(cc)
            else:
                @block.sync
                def _(s):
                    if ts >= 1:
                        s.wait_ge(tT.s[0].h, 256 * ts)
                    for tc in range(4):
                        r0 = 512 * ts + 128 * tc
                        qM.s[0].inc(s.dma_start(out=tmsl[0][:, tc, :],
                                                in_=xshsp[r0:r0 + 128, :]))

            @block.sync
            def _(s):
                for cin in range(4):
                    gidx = 4 * ts + cin
                    s.wait_ge(cG.h, kv_g0 + max(0, gidx - 1))
                    d1 = s.dma_start(out=wfs[gidx % 2][:, 0:512],
                                     in_=wki[128 * cin:128 * (cin + 1), :])
                    qK.inc(d1, gidx)
                    d2 = s.dma_start(out=wfs[gidx % 2][:, 512:1024],
                                     in_=wvi[128 * cin:128 * (cin + 1), :])
                    qK.inc(d2, gidx)

            @block.gpsimd
            def _(g):
                for cin in range(4):
                    gidx = 4 * ts + cin
                    qK.wait(g, gidx, 32 * (gidx // 2 + 1))
                    if gidx >= 2:
                        ts_, cin_ = (gidx - 2) // 4, (gidx - 2) % 4
                        g.wait_ge(cP.h, kv_p0 + 8 * ts_ + cin_ + 1)
                    cc = g.tensor_copy(wcb[gidx % 2], wfs[gidx % 2][:, 0:1024])
                    cG.inc(cc)

            @block.scalar
            def _(sc):
                if ts == 0:
                    sc.wait_ge(cB.h, ebar)
                if li == 0:
                    sc.wait_ge(cV.h, kv_v0 + NVI * ts + 1)
                else:
                    sc.wait_ge(qM.s[0].h, 64 * (ts + 1))
                if ts >= 1:
                    sc.wait_ge(cP.h, kv_p0 + 8 * ts)
                for tc in range(4):
                    for k in range(4):
                        tT.s[0].inc(sc.dma_start_transpose(
                            tsl[0][:, k, 128 * tc:128 * (tc + 1)],
                            tmsl[0][:, tc, 128 * k:128 * (k + 1)]))

            @block.tensor
            def _(t):
                t.wait_ge(tT.s[0].h, 256 * (ts + 1))
                if ts == 0:
                    t.wait_ge(cG.h, mw_g1)
                    t.wait_ge(cV.h, mw_v2)
                if ts >= 1:
                    t.wait_ge(cA.h, kv_a0 + 4 * ts)
                    t.wait_ge(cV.h, kv_v0 + NVI * ts)
                for cin in range(4):
                    gidx = 4 * ts + cin
                    t.wait_ge(cG.h, kv_g0 + gidx + 1)
                    for kc in range(4):
                        t.matmul(PSQ[:, 512 * kc:512 * (kc + 1)],
                                 wcb[gidx % 2][:, 128 * kc:128 * (kc + 1)],
                                 tsl[0][:, cin, :],
                                 start=(cin == 0), stop=(cin == 3))
                    for tc in range(4):
                        mm = t.matmul(PB[tc][:, :],
                                      tsl[0][:, cin, 128 * tc:128 * (tc + 1)],
                                      wcb[gidx % 2][:, 512:1024],
                                      start=(cin == 0), stop=False)
                    cP.inc(mm)
                for tc in range(4):
                    mm = t.matmul(PB[tc][:, :], ones1b[0:1, 0:128], browb[0:1, 0, :],
                                  start=False, stop=True)
                    cP.inc(mm)

            @block.scalar
            def _(sc):
                for kc in range(4):
                    sc.wait_ge(cP.h, kv_p0 + 8 * ts + 4)
                    a = sc.activation(K1T[:, kc, 512 * ts:512 * (ts + 1)],
                                      PSQ[:, 512 * kc:512 * (kc + 1)], AF.Identity,
                                      bias=bqkc[:, kc, 1:2])
                    cA.inc(a)

            @block.vector
            def _(v):
                v.drain()
                for tc in range(4):
                    v.wait_ge(cP.h, kv_p0 + 8 * ts + 4 + tc + 1)
                    vv = V1[:, 4 * ts + tc, 0:520].rearrange("p (h w) -> p h w", h=8, w=65)
                    cc = v.tensor_copy(vv[:, :, 0:64],
                                       PB[tc][:, :].rearrange("p (h w) -> p h w", h=8, w=64))
                    cV.inc(cc)
        kv_p1, kv_v1, kv_a1 = cP.n, cV.n, cA.n

        at_p0, at_v0, at_a0 = cP.n, cV.n, cA.n
        at_g0 = cG.n
        VQ0 = 1 if li == 0 else 0
        VQ = VQ0 + 48 + 8
        PQS = 184
        AQS = 140
        for qs in range(NSL):
            pq = at_p0 + PQS * qs
            vq = at_v0 + VQ * qs
            aqb = at_a0 + AQS * qs
            vas = vq + VQ0 + 48

            if li == 0:
                @block.vector
                def _(v):
                    v.drain()
                    if qs == 0:
                        v.wait_ge(tT.s[0].h, 256 * NSL)
                    if qs >= 1:
                        v.wait_ge(tQ.h, 256 * qs)
                    for tc in range(4):
                        c = 4 * qs + tc
                        cc = v.tensor_scalar(tmsl[0][:, tc, :], XBF[:, c, 0:512],
                                             ln0[:, c, 0:1], ln0[:, c, 1:2],
                                             ALU.subtract, ALU.mult)
                    cV.inc(cc)
            else:
                @block.sync
                def _(s):
                    s.wait_ge(cV.h, vq)
                    s.wait_ge(tQ.h, 256 * qs)
                    if qs == 0:
                        s.wait_ge(tT.s[0].h, 256 * NSL)
                    for tc in range(4):
                        r0 = 512 * qs + 128 * tc
                        qM.s[0].inc(s.dma_start(out=tmsl[0][:, tc, :],
                                                in_=xshsp[r0:r0 + 128, :]))

            @block.scalar
            def _(sc):
                if li == 0:
                    sc.wait_ge(cV.h, vq + 1)
                else:
                    sc.wait_ge(qM.s[0].h, 64 * (NSL + qs + 1))
                sc.wait_ge(cP.h, pq)
                for tc in range(4):
                    for k in range(4):
                        sc.dma_start_transpose(
                            tsl[0][:, k, 128 * tc:128 * (tc + 1)],
                            tmsl[0][:, tc, 128 * k:128 * (k + 1)]).then_inc(tQ.h, 16)
                        tQ.bump()

            @block.tensor
            def _(t):
                t.wait_ge(tQ.h, 256 * (qs + 1))
                if qs == 0:
                    t.wait_ge(cP.h, kv_p1)
                    t.wait_ge(cA.h, kv_a1)
                    t.wait_ge(cV.h, kv_v1)
                for h in range(H):
                    ph = pq + 23 * h
                    ah = aqb + 17 * h
                    vh = vq + VQ0 + 6 * h
                    bp = 64 * (h % 2)
                    hc = h // 2
                    for cin in range(4):
                        mm = t.matmul(PB[1][bp:bp + 64, :],
                                      wqb[:, cin, 64 * h:64 * h + 64],
                                      tsl[0][:, cin, :],
                                      start=(cin == 0), stop=(cin == 3))
                    cP.inc(mm)
                    for d2 in range(16):
                        dp = d2 % 2
                        if d2 >= 2:
                            t.wait_ge(cA.h, ah + d2)
                        if d2 == 0:
                            t.wait_ge(cA.h, ah + 1)
                        for j in range(2):
                            kt = 2 * d2 + j
                            mm = t.matmul(PSQ[:, 1024 * dp + 512 * j:1024 * dp + 512 * (j + 1)],
                                          K1T[bp:bp + 64, hc, 128 * kt:128 * (kt + 1)],
                                          qtt[bp:bp + 64, :], start=True, stop=True)
                        if d2 >= 1:
                            t.wait_ge(cA.h, ah + 2 + (d2 - 1))
                            if d2 == 1:
                                if h >= 1:
                                    t.wait_ge(cV.h, vh - 6 + 1)
                                elif qs >= 1:
                                    t.wait_ge(cV.h, vq - VQ + VQ0 + 6 * 7 + 1)
                            for j in range(2):
                                kt = 2 * (d2 - 1) + j
                                mm = t.matmul(PB[0][0:65, :], V1[:, kt, 65 * h:65 * h + 65],
                                              aq[(d2 - 1) % 2][:, 512 * j:512 * (j + 1)],
                                              start=(kt == 0), stop=False)
                        cP.inc(mm)
                    t.wait_ge(cA.h, ah + 2 + 15)
                    for j in range(2):
                        kt = 30 + j
                        mm = t.matmul(PB[0][0:65, :], V1[:, kt, 65 * h:65 * h + 65],
                                      aq[1][:, 512 * j:512 * (j + 1)],
                                      start=False, stop=(j == 1))
                    cP.inc(mm)
                    t.wait_ge(cV.h, vh + 1)
                    for tc in range(4):
                        mm = t.transpose(PB[2][:, tc:tc + 1],
                                         denA[64:65, 128 * tc:128 * (tc + 1)],
                                         ident[64:65, 64:65])
                    cP.inc(mm)
                    for tc in range(4):
                        if tc == 1:
                            t.wait_ge(cV.h, vh + 2)
                        elif tc >= 2:
                            t.wait_ge(cV.h, vh + 1 + tc)
                        bank = PB[3] if tc % 2 == 0 else PB[2]
                        mm = t.matmul(bank[:, :], ohb[0:64, 128 * tc:128 * (tc + 1)],
                                      wob[0:64, h, :], start=True, stop=True)
                        cP.inc(mm)

            @block.scalar
            def _(sc):
                for h in range(H):
                    ph = pq + 23 * h
                    ah = aqb + 17 * h
                    bp = 64 * (h % 2)
                    sc.wait_ge(cP.h, ph + 1)
                    a = sc.activation(qtt[bp:bp + 64, :], PB[1][bp:bp + 64, :], AF.Identity,
                                      bias=bqkc[bp:bp + 64, h // 2, 0:1], scale=0.125)
                    cA.inc(a)
                    for d2 in range(16):
                        dp = d2 % 2
                        sc.wait_ge(cP.h, ph + 2 + d2)
                        a = sc.activation(aq[dp], PSQ[:, 1024 * dp:1024 * (dp + 1)],
                                          AF.Exp)
                        cA.inc(a)
                for tc in range(4):
                    sc.wait_ge(cV.h, vas + 2 * tc + 1)
                    a = sc.activation(stat[:, 16:17], stat[:, 13:14], AF.Sqrt,
                                      bias=epscol[:, :])
                    cA.inc(a)

            @block.vector
            def _(v):
                v.drain()
                for h in range(H):
                    ph = pq + 23 * h
                    vh = vq + VQ0 + 6 * h
                    v.wait_ge(cP.h, ph + 18)
                    v.tensor_copy(ohb[0:64, :], PB[0][0:64, :])
                    cc = v.tensor_copy(denA[64:65, :], PB[0][64:65, :])
                    cV.inc(cc)
                    v.wait_ge(cP.h, ph + 19)
                    cc = v.reciprocal(rcq[:, 0:4], PB[2][:, 0:4])
                    cV.inc(cc)
                    v.drain()
                    for tc in range(4):
                        v.wait_ge(cP.h, ph + 20 + tc)
                        bank = PB[3] if tc % 2 == 0 else PB[2]
                        if h == 0:
                            cc = v.tensor_scalar(oacc[:, tc, :], bank[:, :],
                                                 rcq[:, tc:tc + 1], None, ALU.mult)
                        else:
                            cc = v.scalar_tensor_tensor(oacc[:, tc, :], bank[:, :],
                                                        rcq[:, tc:tc + 1], oacc[:, tc, :],
                                                        ALU.mult, ALU.add)
                        cV.inc(cc)
                for tc in range(4):
                    i = 4 * qs + tc
                    if i >= 2:
                        qO.wait(v, i, 16 * ((i - 2) // 2 + 1))
                    v.drain()
                    v.tensor_tensor(sfp, tmsl[0][:, tc, :], oacc[:, tc, :], ALU.add)
                    v.drain()
                    v.tensor_tensor(sfp, sfp, bcB, ALU.add)
                    v.drain()
                    v.bn_stats(stat[:, 0:6], sfp)
                    v.drain()
                    v.bn_aggr(stat[:, 12:14], stat[:, 0:6])
                    v.drain().then_inc(cV.h, 1)
                    cV.bump()
                    v.wait_ge(cA.h, aqb + 136 + tc + 1)
                    v.reciprocal(stat[:, 17:18], stat[:, 16:17])
                    v.drain()
                    v.tensor_scalar(sfp, sfp, stat[:, 12:13], stat[:, 17:18],
                                    ALU.subtract, ALU.mult)
                    v.drain()
                    v.tensor_tensor(sfp, sfp, bcG, ALU.mult)
                    v.drain()
                    cc = v.tensor_tensor(stg[tc % 2], sfp, bcE, ALU.add)
                    cV.inc(cc)

            @block.sync
            def _(s):
                for tc in range(4):
                    i = 4 * qs + tc
                    s.wait_ge(cV.h, vas + 2 * tc + 2)
                    r0 = 512 * qs + 128 * tc
                    qO.inc(s.dma_start(out=dst_sp[r0:r0 + 128, :], in_=stg[tc % 2]), i)

        at_p1, at_v1, at_a1 = cP.n, cV.n, cA.n
        assert at_p1 == at_p0 + PQS * NSL, (at_p1, at_p0)
        assert at_v1 == at_v0 + VQ * NSL, (at_v1, at_v0)
        assert at_a1 == at_a0 + AQS * NSL, (at_a1, at_a0)
        bfa.pop()
        fpa.pop()
        return qO

    def _stopped(name):
        return stop_after is not None and name == stop_after

    qO1 = emit_mha(0, x1sp)
    if not _stopped("mha0"):
        qO2 = emit_mha(1, x2sp)


    def emit_ffn0():
        bfa.push()
        fpa.push()
        ytm = bfa.alloc([128, 8, 512])
        ytr = bfa.alloc([128, 8, 512])
        hsl = bfa.alloc([128, 32, 512])
        w1b = [bfa.alloc([128, 512]) for _ in range(2)]
        w2b = [bfa.alloc([128, 1024]) for _ in range(2)]
        brw = bfa.alloc([1, 3, 1024])
        bf1c = fpa.alloc([128, 32, 1])
        w1f = [fpa.alloc([128, 512]) for _ in range(2)]
        w2f = [fpa.alloc([128, 1024]) for _ in range(2)]
        bcG2 = fpa.alloc([128, 1024])
        bcE2 = fpa.alloc([128, 1024])
        sfp = fpa.alloc([128, 1024])
        stat = fpa.alloc([128, 20])

        qY = PSem(es, nc, "qY")
        qW1 = PSem(es, nc, "qF1")
        qW2 = PSem(es, nc, "qF2")
        tY = Sem(es, nc, "tY", 16)

        ebar = phase_barrier()

        @block.gpsimd
        def _(g):
            g.wait_ge(cB.h, ebar)
            g.dma_start(out=brw[0:1, 0, :], in_=bff2_in.unsqueeze(0)).then_inc(sG.h, 16)
            g.dma_start(out=brw[0:1, 1, :], in_=gff_in.unsqueeze(0)).then_inc(sG.h, 16)
            g.dma_start(out=brw[0:1, 2, :], in_=beff_in.unsqueeze(0)).then_inc(sG.h, 16)
            sG.bump(3)
        f_sg1 = sG.n

        @block.sync
        def _(s):
            s.wait_ge(cB.h, ebar)
            for c in range(32):
                s.dma_start(out=bf1c[:, c, :],
                            in_=bff1_in[128 * c:128 * (c + 1)].unsqueeze(1)).then_inc(sQ.h, 16)
            sQ.bump(32)
        f_q1 = sQ.n

        @block.tensor
        def _(t):
            t.wait_ge(cB.h, ebar)
            t.wait_ge(sG.h, f_sg1)
            t.matmul(PB[0][:, :], ones1b[0:1, 0:128], brw[0:1, 1, 0:512], start=True, stop=True)
            t.matmul(PB[1][:, :], ones1b[0:1, 0:128], brw[0:1, 1, 512:1024], start=True, stop=True)
            t.matmul(PB[2][:, :], ones1b[0:1, 0:128], brw[0:1, 2, 0:512], start=True, stop=True)
            mm = t.matmul(PB[3][:, :], ones1b[0:1, 0:128], brw[0:1, 2, 512:1024], start=True, stop=True)
            cP.inc(mm)
        f_p1 = cP.n

        @block.vector
        def _(v):
            v.drain()
            v.wait_ge(cP.h, f_p1)
            v.tensor_copy(bcG2[:, 0:512], PB[0][:, :])
            v.tensor_copy(bcG2[:, 512:1024], PB[1][:, :])
            v.tensor_copy(bcE2[:, 0:512], PB[2][:, :])
            cc = v.tensor_copy(bcE2[:, 512:1024], PB[3][:, :])
            cV.inc(cc)
        f_v1 = cV.n

        TP, TV, TA, TG = 97, 40, 4, 96
        f_p2, f_v2, f_a2, f_g2 = cP.n, cV.n, cA.n, cG.n
        for tsb in range(NSL):
            tp = f_p2 + TP * tsb
            tv = f_v2 + TV * tsb
            ta = f_a2 + TA * tsb
            tg = f_g2 + TG * tsb

            @block.sync
            def _(s):
                if tsb == 0:
                    s.wait_ge(cV.h, f_v1)
                else:
                    s.wait_ge(tY.h, 512 * tsb)
                    s.wait_ge(cV.h, f_v2 + TV * tsb)
                for tc in range(4):
                    r0 = 512 * tsb + 128 * tc
                    qY.inc(s.dma_start(out=ytm[:, tc, :], in_=x1sp[r0:r0 + 128, :]), tsb)
                    qY.inc(s.dma_start(out=ytm[:, 4 + tc, :], in_=x2sp[r0:r0 + 128, :]), tsb)
                for hg in range(8):
                    for cin in range(8):
                        gg = 64 * tsb + 8 * hg + cin
                        if gg >= 2:
                            gge = gg - 2
                            s.wait_ge(cG.h, f_g2 + TG * (gge // 64) + (gge % 64) + 1)
                        qW1.inc(s.dma_start(
                            out=w1f[gg % 2],
                            in_=wff1_in[128 * cin:128 * (cin + 1), 512 * hg:512 * (hg + 1)]), gg)
                for hc in range(32):
                    g2 = 32 * tsb + hc
                    if g2 >= 2:
                        g2e = g2 - 2
                        s.wait_ge(cG.h, f_g2 + TG * (g2e // 32) + 64 + (g2e % 32) + 1)
                    qW2.inc(s.dma_start(out=w2f[g2 % 2],
                                        in_=wff2_in[128 * hc:128 * (hc + 1), :]), g2)

            @block.gpsimd
            def _(g):
                for hg in range(8):
                    for cin in range(8):
                        gg = 64 * tsb + 8 * hg + cin
                        qW1.wait(g, gg, 16 * (gg // 2 + 1))
                        if gg >= 2:
                            gge = gg - 2
                            g.wait_ge(cP.h, f_p2 + TP * (gge // 64) + (gge % 64) + 1)
                        cc = g.tensor_copy(w1b[gg % 2], w1f[gg % 2])
                        cG.inc(cc)
                for hc in range(32):
                    g2 = 32 * tsb + hc
                    qW2.wait(g, g2, 16 * (g2 // 2 + 1))
                    if g2 >= 2:
                        g2e = g2 - 2
                        g.wait_ge(cP.h, f_p2 + TP * (g2e // 32) + 64 + (g2e % 32) + 1)
                    cc = g.tensor_copy(w2b[g2 % 2], w2f[g2 % 2])
                    cG.inc(cc)

            @block.scalar
            def _(sc):
                qY.wait(sc, tsb, 128 * (tsb // 2 + 1))
                sc.wait_ge(cP.h, f_p2 + TP * tsb)
                for tc in range(4):
                    for k in range(4):
                        sc.dma_start_transpose(
                            ytr[:, k, 128 * tc:128 * (tc + 1)],
                            ytm[:, tc, 128 * k:128 * (k + 1)]).then_inc(tY.h, 16)
                        tY.bump()
                    for k in range(4):
                        sc.dma_start_transpose(
                            ytr[:, 4 + k, 128 * tc:128 * (tc + 1)],
                            ytm[:, 4 + tc, 128 * k:128 * (k + 1)]).then_inc(tY.h, 16)
                        tY.bump()
                for tc in range(4):
                    sc.wait_ge(cV.h, tv + 32 + 2 * tc + 1)
                    a = sc.activation(stat[:, 16:17], stat[:, 13:14], AF.Sqrt,
                                      bias=epscol[:, :])
                    cA.inc(a)

            @block.tensor
            def _(t):
                t.wait_ge(tY.h, 512 * (tsb + 1))
                if tsb >= 1:
                    t.wait_ge(cV.h, f_v2 + TV * tsb)
                for hg in range(8):
                    if hg >= 1:
                        t.wait_ge(cV.h, tv + 4 * hg)
                    for cin in range(8):
                        gg = 64 * tsb + 8 * hg + cin
                        t.wait_ge(cG.h, f_g2 + TG * tsb + 8 * hg + cin + 1)
                        for j in range(4):
                            mm = t.matmul(PSQ[:, 512 * j:512 * (j + 1)],
                                          w1b[gg % 2][:, 128 * j:128 * (j + 1)],
                                          ytr[:, cin, :],
                                          start=(cin == 0), stop=(cin == 7))
                        cP.inc(mm)
                t.wait_ge(cV.h, tv + 32)
                for hc in range(32):
                    t.wait_ge(cG.h, f_g2 + TG * tsb + 64 + hc + 1)
                    for tc in range(4):
                        for ch in range(2):
                            bank = PB[2 * (tc % 2) + ch] if tc < 2 else PSQ[:, 512 * (2 * (tc - 2) + ch):512 * (2 * (tc - 2) + ch + 1)]
                            dst = bank[:, :] if tc < 2 else bank
                            mm = t.matmul(dst, hsl[:, hc, 128 * tc:128 * (tc + 1)],
                                          w2b[hc % 2][:, 512 * ch:512 * (ch + 1)],
                                          start=(hc == 0), stop=False)
                    cP.inc(mm)
                for tc in range(4):
                    for ch in range(2):
                        bank = PB[2 * (tc % 2) + ch] if tc < 2 else PSQ[:, 512 * (2 * (tc - 2) + ch):512 * (2 * (tc - 2) + ch + 1)]
                        dst = bank[:, :] if tc < 2 else bank
                        mm = t.matmul(dst, ones1b[0:1, 0:128],
                                      brw[0:1, 0, 512 * ch:512 * (ch + 1)],
                                      start=False, stop=True)
                cP.inc(mm)

            @block.vector
            def _(v):
                v.drain()
                if tsb == 0:
                    v.wait_ge(sQ.h, f_q1)
                for hg in range(8):
                    for j in range(4):
                        v.wait_ge(cP.h, tp + 8 * hg + 8)
                        cc = v.tensor_scalar(hsl[:, 4 * hg + j, :], PSQ[:, 512 * j:512 * (j + 1)],
                                             bf1c[:, 4 * hg + j, :], 0.0, ALU.add, ALU.max)
                        cV.inc(cc)
                v.wait_ge(cP.h, tp + TP)
                for tc in range(4):
                    if tc < 2:
                        b0, b1 = PB[2 * tc][:, :], PB[2 * tc + 1][:, :]
                    else:
                        b0 = PSQ[:, 512 * (2 * (tc - 2)):512 * (2 * (tc - 2) + 1)]
                        b1 = PSQ[:, 512 * (2 * (tc - 2) + 1):512 * (2 * (tc - 2) + 2)]
                    v.drain()
                    v.tensor_tensor(sfp[:, 0:512], ytm[:, tc, :], b0, ALU.add)
                    v.tensor_tensor(sfp[:, 512:1024], ytm[:, 4 + tc, :], b1, ALU.add)
                    v.drain()
                    v.bn_stats(stat[:, 0:6], sfp[:, 0:512])
                    v.bn_stats(stat[:, 6:12], sfp[:, 512:1024])
                    v.drain()
                    v.bn_aggr(stat[:, 12:14], stat[:, 0:12])
                    v.drain().then_inc(cV.h, 1)
                    cV.bump()
                    v.wait_ge(cA.h, ta + tc + 1)
                    v.reciprocal(stat[:, 17:18], stat[:, 16:17])
                    v.drain()
                    v.tensor_scalar(sfp, sfp, stat[:, 12:13], stat[:, 17:18],
                                    ALU.subtract, ALU.mult)
                    v.drain()
                    v.tensor_tensor(sfp, sfp, bcG2, ALU.mult)
                    v.drain()
                    v.tensor_tensor(sfp, sfp, bcE2, ALU.add)
                    v.drain()
                    cc = v.tensor_tensor(XBF[:, 4 * tsb + tc, :], XBF[:, 4 * tsb + tc, :],
                                         sfp, ALU.add)
                    cV.inc(cc)
        bfa.pop()
        fpa.pop()
        assert cP.n == f_p2 + TP * NSL
        assert cV.n == f_v2 + TV * NSL
        assert cA.n == f_a2 + TA * NSL

    if stop_after not in ("mha0", "mha1"):
        emit_ffn0()

    def emit_rwkv():
        bfa.push()
        fpa.push()
        H1T = bfa.alloc([128, 8, S])
        slA = bfa.alloc([128, 8, 512])
        slC = bfa.alloc([128, 8, 512])
        slD = bfa.alloc([128, 8, 512])
        h1tm = [bfa.alloc([128, 1024]) for _ in range(2)]
        wchb = [bfa.alloc([128, 1024]) for _ in range(2)]
        otm = [bfa.alloc([128, 1024]) for _ in range(2)]
        wchf = [fpa.alloc([128, 1024]) for _ in range(2)]
        muc = fpa.alloc([128, 8, 4])
        sfp = fpa.alloc([128, 1024])
        stat = fpa.alloc([128, 20])

        qWc = PSem(es, nc, "qWc")
        tH = Sem(es, nc, "tH", 16)
        tO = PSem(es, nc, "tO")

        ebar = phase_barrier(taps=[("xbfA5", bfa.at(0, [128, 32768]))])

        @block.sync
        def _(s):
            s.wait_ge(cB.h, ebar)
            for c in range(8):
                s.dma_start(out=muc[:, c, 0:1],
                            in_=mur_in[128 * c:128 * (c + 1)].unsqueeze(1)).then_inc(sQ.h, 16)
                s.dma_start(out=muc[:, c, 2:3],
                            in_=muk_in[128 * c:128 * (c + 1)].unsqueeze(1)).then_inc(sQ.h, 16)
            sQ.bump(16)
        r_q1 = sQ.n

        @block.vector
        def _(v):
            v.drain()
            v.wait_ge(cB.h, ebar)
            v.wait_ge(sQ.h, r_q1)
            v.tensor_scalar(muc[:, :, 1:2], muc[:, :, 0:1], -1.0, 1.0, ALU.mult, ALU.add)
            cc = v.tensor_scalar(muc[:, :, 3:4], muc[:, :, 2:3], -1.0, 1.0, ALU.mult, ALU.add)
            cV.inc(cc)
        r_v1 = cV.n

        r_v2, r_a2, r_g2 = cV.n, cA.n, cG.n
        @block.vector
        def _(v):
            v.drain()
            for c in range(NCH):
                if c >= 2:
                    v.wait_ge(tH.h, 128 * (c - 1))
                v.bn_stats(stat[:, 0:6], XBF[:, c, 0:512])
                v.bn_stats(stat[:, 6:12], XBF[:, c, 512:1024])
                v.drain()
                v.bn_aggr(stat[:, 12:14], stat[:, 0:12])
                v.drain().then_inc(cV.h, 1)
                cV.bump()
                v.wait_ge(cA.h, r_a2 + c + 1)
                v.reciprocal(stat[:, 17:18], stat[:, 16:17])
                v.drain()
                cc = v.tensor_scalar(h1tm[c % 2], XBF[:, c, :], stat[:, 12:13],
                                     stat[:, 17:18], ALU.subtract, ALU.mult)
                cV.inc(cc)
        r_v3 = cV.n

        @block.scalar
        def _(sc):
            for c in range(NCH):
                sc.wait_ge(cV.h, r_v2 + 2 * c + 1)
                a = sc.activation(stat[:, 16:17], stat[:, 13:14], AF.Sqrt,
                                  bias=epscol[:, :])
                cA.inc(a)
                if c >= 1:
                    sc.wait_ge(cV.h, r_v2 + 2 * c)
                    for k in range(8):
                        sc.dma_start_transpose(
                            H1T[:, k, 128 * (c - 1):128 * c],
                            h1tm[(c - 1) % 2][:, 128 * k:128 * (k + 1)]).then_inc(tH.h, 16)
                        tH.bump()
            sc.wait_ge(cV.h, r_v2 + 2 * NCH)
            for k in range(8):
                sc.dma_start_transpose(
                    H1T[:, k, 128 * (NCH - 1):128 * NCH],
                    h1tm[(NCH - 1) % 2][:, 128 * k:128 * (k + 1)]).then_inc(tH.h, 16)
                tH.bump()
        r_th = tH.n

        TPr = 24
        NVX = 22
        r_p4, r_v4, r_a4, r_g4 = cP.n, cV.n, cA.n, cG.n
        for ts in range(NSL):
            tp = r_p4 + TPr * ts
            tv = r_v4 + NVX * ts
            ta = r_a4 + 16 * ts

            @block.sync
            def _(s):
                for i3 in range(24):
                    gg = 24 * ts + i3
                    if gg >= 2:
                        s.wait_ge(cG.h, r_g4 + gg - 1)
                    w_ = (wr_in, wkk_in, wvv_in)[i3 // 8]
                    cin = i3 % 8
                    qWc.inc(s.dma_start(out=wchf[gg % 2],
                                        in_=w_[128 * cin:128 * (cin + 1), :]), gg)

            @block.gpsimd
            def _(g):
                for i3 in range(24):
                    gg = 24 * ts + i3
                    qWc.wait(g, gg, 16 * (gg // 2 + 1))
                    if gg >= 2:
                        gge = gg - 2
                        g.wait_ge(cP.h, r_p4 + TPr * (gge // 24) + (gge % 24) + 1)
                    cc = g.tensor_copy(wchb[gg % 2], wchf[gg % 2])
                    cG.inc(cc)

            def mix(v, mu_i, mu1_i):
                for c in range(8):
                    if ts == 0:
                        v.tensor_scalar(slA[:, c, :], H1T[:, c, 0:512],
                                        muc[:, c, mu_i:mu_i + 1], None, ALU.mult)
                        v.drain()
                        cc = v.scalar_tensor_tensor(slA[:, c, 1:512], H1T[:, c, 0:511],
                                                    muc[:, c, mu1_i:mu1_i + 1],
                                                    slA[:, c, 1:512], ALU.mult, ALU.add)
                    else:
                        v.tensor_scalar(slA[:, c, :], H1T[:, c, 512 * ts:512 * (ts + 1)],
                                        muc[:, c, mu_i:mu_i + 1], None, ALU.mult)
                        v.drain()
                        cc = v.scalar_tensor_tensor(slA[:, c, :],
                                                    H1T[:, c, 512 * ts - 1:512 * (ts + 1) - 1],
                                                    muc[:, c, mu1_i:mu1_i + 1],
                                                    slA[:, c, :], ALU.mult, ALU.add)
                v.drain().then_inc(cV.h, 1)
                cV.bump()

            @block.vector
            def _(v):
                v.drain()
                if ts == 0:
                    v.wait_ge(tH.h, r_th)
                    v.wait_ge(cV.h, r_v1)
                else:
                    v.wait_ge(cP.h, tp)
                    v.wait_ge(tO.s[0].h, 256 * ts)
                    v.wait_ge(tO.s[1].h, 256 * ts)
                mix(v, 0, 1)
                v.wait_ge(cP.h, tp + 8)
                mix(v, 2, 3)
                for co in range(8):
                    v.wait_ge(cP.h, tp + 16)
                    bank = PB[co][:, :] if co < 4 else PSQ[:, 512 * (co - 4):512 * (co - 3)]
                    cc = v.tensor_scalar(slC[:, co, :], bank, 0.0, None, ALU.max)
                    cV.inc(cc)
                v.wait_ge(cP.h, tp + 24)
                v.wait_ge(cA.h, ta + 16)
                for co in range(8):
                    bank = PB[co][:, :] if co < 4 else PSQ[:, 512 * (co - 4):512 * (co - 3)]
                    cc = v.tensor_tensor(slA[:, co, :], slD[:, co, :], bank, ALU.mult)
                    cV.inc(cc)
                for tc in range(4):
                    v.wait_ge(tO.s[tc % 2].h, 256 * ts + 128 * (tc // 2 + 1))
                    v.drain()
                    cc = v.tensor_tensor(XBF[:, 4 * ts + tc, :], XBF[:, 4 * ts + tc, :],
                                         otm[tc % 2], ALU.add)
                    cV.inc(cc)

            @block.tensor
            def _(t):
                if ts == 0:
                    t.wait_ge(tH.h, r_th)
                for ph in range(3):
                    if ph == 0:
                        t.wait_ge(cV.h, tv + 1)
                    elif ph == 1:
                        t.wait_ge(cV.h, tv + 2)
                        t.wait_ge(cA.h, ta + 8)
                    else:
                        t.wait_ge(cV.h, tv + 10)
                        t.wait_ge(cA.h, ta + 16)
                    sl_ = slC if ph == 2 else slA
                    for cin in range(8):
                        gg = 24 * ts + 8 * ph + cin
                        t.wait_ge(cG.h, r_g4 + gg + 1)
                        for co in range(8):
                            bank = PB[co][:, :] if co < 4 else PSQ[:, 512 * (co - 4):512 * (co - 3)]
                            mm = t.matmul(bank, wchb[gg % 2][:, 128 * co:128 * (co + 1)],
                                          sl_[:, cin, :], start=(cin == 0), stop=(cin == 7))
                        cP.inc(mm)

            @block.scalar
            def _(sc):
                for co in range(8):
                    sc.wait_ge(cP.h, tp + 8)
                    bank = PB[co][:, :] if co < 4 else PSQ[:, 512 * (co - 4):512 * (co - 3)]
                    a = sc.activation(slD[:, co, :], bank, AF.Sigmoid)
                    cA.inc(a)
                for co in range(8):
                    sc.wait_ge(cV.h, tv + 2 + co + 1)
                    a = sc.activation(slC[:, co, :], slC[:, co, :], AF.Square)
                    cA.inc(a)
                for tc in range(4):
                    sc.wait_ge(cV.h, tv + 18)
                    if tc >= 2:
                        sc.wait_ge(cV.h, tv + 18 + tc - 1)
                    for k in range(8):
                        sc.dma_start_transpose(
                            otm[tc % 2][:, 128 * k:128 * (k + 1)],
                            slA[:, k, 128 * tc:128 * (tc + 1)]).then_inc(tO.s[tc % 2].h, 16)
                        tO.s[tc % 2].bump()
        bfa.pop()
        fpa.pop()

    if stop_after not in ("mha0", "mha1", "ffn0"):
        emit_rwkv()


    def emit_fft():
        bfa.push()
        fpa.push()
        Cb = bfa.alloc([128, 8, 1024])
        Sb = bfa.alloc([128, 8, 1024])
        h2tm = [bfa.alloc([128, 1024]) for _ in range(2)]
        h2T = bfa.alloc([128, 8, 512])
        asl = bfa.alloc([128, 8, 512])
        bsl = bfa.alloc([128, 8, 512])
        x1T = bfa.alloc([128, 8, 512])
        o2T = bfa.alloc([128, 8, 512])
        o2tm = [bfa.alloc([128, 1024]) for _ in range(2)]
        h3st = bfa.alloc([128, 4, 1024])
        bitsT = bfa.alloc([128, 512])
        synb = bfa.alloc([128, 512])
        corT = bfa.alloc([128, 512])
        fixb = bfa.alloc([1, 512])
        welb = bfa.alloc([128, 8, 24])
        wleb = bfa.alloc([128, 1024])
        hmtb = bfa.alloc([128, 12])
        xrg = bfa.alloc([128, 512])
        wgrb = bfa.alloc([128, 8, 16])
        wgtb = bfa.alloc([128, 8])
        zrow = fpa.alloc([128, 24])
        ccol = fpa.alloc([128, 8, 8])
        belc = fpa.alloc([128, 1])
        blec = fpa.alloc([128, 8, 1])
        rnrm = fpa.alloc([128, 1])
        logf = fpa.alloc([128, 512])
        x3f = fpa.alloc([128, 1024])
        mint = fpa.alloc([128, 512])
        stat = fpa.alloc([128, 20])
        pi2c = fpa.alloc([128, 1])
        wgmf = fpa.alloc([128, 16])
        mi32 = es.enter_context(nc.sbuf_tensor("mi32", [128, 512], mybir.dt.int32))
        krow = x3f[0:1, :]

        tH2 = PSem(es, nc, "tH2")
        tO2 = PSem(es, nc, "tO2")
        tH3 = PSem(es, nc, "tH3")
        qX3 = PSem(es, nc, "qX3")

        ebar = phase_barrier(taps=[("xbfB", bfa.at(0, [128, 32768]))])

        @block.gpsimd
        def _(g):
            g.wait_ge(cB.h, ebar)
            g.iota(krow[0:1, :], pattern=[[1, 1024]], base=0, channel_multiplier=0,
                   allow_small_or_imprecise_dtypes=True)
            for c in range(8):
                cc = g.iota(ccol[:, c, 6:7], pattern=[[0, 1]], base=128 * c,
                            channel_multiplier=1,
                            allow_small_or_imprecise_dtypes=True)
            cc.then_inc(cG.h, 1)
            cG.bump()
            g.dma_start(out=welb[:, :, :],
                        in_=wel_in.rearrange("(a p) o -> p a o", p=128)).then_inc(sG.h, 16)
            g.dma_start(out=wleb[0:24, :], in_=wle_in[:, :]).then_inc(sG.h, 16)
            g.dma_start(out=hmtb[0:24, :], in_=hmt_in[:, :]).then_inc(sG.h, 16)
            g.dma_start(out=wgrb[:, :, :],
                        in_=wgred_in.rearrange("(a p) o -> p a o", p=128)).then_inc(sG.h, 16)
            g.dma_start(out=wgtb[0:16, :], in_=wgt_in[:, :]).then_inc(sG.h, 16)
            sG.bump(5)
        c_g1 = cG.n
        c_sg1 = sG.n

        @block.sync
        def _(s):
            s.wait_ge(cB.h, ebar)
            s.dma_start(out=belc[0:24, :], in_=bel_in.unsqueeze(1)).then_inc(sQ.h, 16)
            for c in range(8):
                s.dma_start(out=blec[:, c, :],
                            in_=ble_in[128 * c:128 * (c + 1)].unsqueeze(1)).then_inc(sQ.h, 16)
            s.dma_start(out=wgmf[0:8, :], in_=wgm_in[:, :]).then_inc(sQ.h, 16)
            sQ.bump(10)
        c_q1 = sQ.n

        @block.vector
        def _(v):
            v.drain()
            v.wait_ge(cG.h, c_g1)
            v.wait_ge(sQ.h, c_q1)
            v.memset(pi2c[:, :], float(np.pi / 2))
            v.drain()
            v.tensor_scalar(pi2c[:, :], pi2c[:, :],
                            float(np.float64(np.pi / 2) - np.float64(np.float32(np.pi / 2))),
                            None, ALU.add)
            v.drain()
            for c in range(8):
                v.tensor_scalar(ccol[:, c, 1:2], ccol[:, c, 6:7], -1.0, 1024.0,
                                ALU.mult, ALU.add)
                v.drain()
                v.tensor_tensor(ccol[:, c, 0:1], ccol[:, c, 6:7], ccol[:, c, 1:2], ALU.min)
                v.drain()
                v.tensor_scalar(ccol[:, c, 0:1], ccol[:, c, 0:1], 1.0 / 1024.0, 1e-08,
                                ALU.mult, ALU.add)
            v.drain().then_inc(cV.h, 1)
            cV.bump()
        c_v1 = cV.n

        @block.scalar
        def _(sc):
            sc.wait_ge(cV.h, c_v1)
            for c in range(8):
                a = sc.activation(ccol[:, c, 2:3], ccol[:, c, 0:1], AF.Square)
                cA.inc(a)
        c_a1 = cA.n

        @block.vector
        def _(v):
            v.wait_ge(cA.h, c_a1)
            OD = (1.0 + 0.2 * 1.0) * 1.0
            for c in range(8):
                v.tensor_scalar(ccol[:, c, 3:4], ccol[:, c, 0:1], 1.5, OD,
                                ALU.mult, ALU.add)
                v.tensor_scalar(ccol[:, c, 4:5], ccol[:, c, 2:3], 0.1, OD,
                                ALU.mult, ALU.add)
                v.drain()
                v.tensor_tensor(ccol[:, c, 4:5], ccol[:, c, 4:5], ccol[:, c, 0:1],
                                ALU.subtract)
            v.drain().then_inc(cV.h, 1)
            cV.bump()
        c_v2 = cV.n

        @block.scalar
        def _(sc):
            sc.wait_ge(cV.h, c_v2)
            for c in range(8):
                sc.activation(ccol[:, c, 5:6], ccol[:, c, 3:4], AF.Sin)
                sc.drain()
                sc.activation(ccol[:, c, 3:4], ccol[:, c, 4:5], AF.Sin)
                sc.drain()
                a = sc.activation(ccol[:, c, 4:5], ccol[:, c, 4:5], AF.Sin,
                                  scale=-1.0, bias=pi2c[:, :])
                cA.inc(a)
                sc.drain()
        c_a2 = cA.n

        @block.vector
        def _(v):
            v.wait_ge(cA.h, c_a2)
            for c in range(8):
                v.tensor_tensor(ccol[:, c, 6:7], ccol[:, c, 5:6], ccol[:, c, 4:5], ALU.mult)
                v.tensor_tensor(ccol[:, c, 7:8], ccol[:, c, 5:6], ccol[:, c, 3:4], ALU.mult)
            v.drain()
            v.tensor_tensor(wgmf[0:8, :], wgmf[0:8, :], wgmf[0:8, :], ALU.mult)
            v.drain()
            v.reduce_sum(rnrm[0:8, :], wgmf[0:8, :], axis=AX.X)
            v.drain().then_inc(cV.h, 1)
            cV.bump()
        c_v3 = cV.n

        @block.scalar
        def _(sc):
            sc.wait_ge(cV.h, c_v3)
            a = sc.activation(rnrm[0:8, :], rnrm[0:8, :], AF.Sqrt)
            cA.inc(a)
        c_a3 = cA.n

        @block.vector
        def _(v):
            v.wait_ge(cA.h, c_a3)
            v.tensor_single_scalar(rnrm[0:8, :], rnrm[0:8, :], 1e-4, ALU.max)
            v.drain()
            cc = v.reciprocal(rnrm[0:8, :], rnrm[0:8, :])
            cV.inc(cc)
        c_v4 = cV.n

        c_p1 = cP.n
        @block.tensor
        def _(t):
            t.wait_ge(cG.h, c_g1)
            for i in range(16):
                kc, ns = i // 2, i % 2
                if i >= 2:
                    t.wait_ge(cA.h, c_a3 + 2 * (i - 1))
                mm = t.matmul(PB[i % 2][:, :], krow[0:1, 128 * kc:128 * (kc + 1)],
                              krow[0:1, 512 * ns:512 * (ns + 1)],
                              start=True, stop=True)
                cP.inc(mm)

        @block.vector
        def _(v):
            v.drain()
            for i in range(16):
                v.wait_ge(cP.h, c_p1 + i + 1)
                if i >= 1:
                    v.wait_ge(cA.h, c_a3 + 2 * i)
                v.tensor_copy(mi32[:, :], PB[i % 2][:, :])
                v.drain()
                v.tensor_single_scalar(mi32[:, :], mi32[:, :], 1023, ALU.bitwise_and)
                v.drain()
                v.tensor_copy(mint[:, :], mi32[:, :])
                v.drain()
                v.tensor_single_scalar(logf[:, :], mint[:, :], 511.5, ALU.is_gt)
                v.drain()
                v.scalar_tensor_tensor(mint[:, :], logf[:, :], -1024.0, mint[:, :],
                                       ALU.mult, ALU.add)
                v.drain()
                v.tensor_scalar(logf[:, :], mint[:, :], -1.0, None, ALU.mult)
                v.drain()
                cc = v.tensor_tensor(logf[:, :], logf[:, :], mint[:, :], ALU.max)
                cV.inc(cc)
        c_v5 = cV.n

        @block.scalar
        def _(sc):
            TWO_PI = float(2.0 * np.pi)
            for i in range(16):
                kc, ns = i // 2, i % 2
                sc.wait_ge(cV.h, c_v4 + i + 1)
                sc.activation(Sb[:, kc, 512 * ns:512 * (ns + 1)], mint[:, :], AF.Sin,
                              scale=TWO_PI / 1024.0)
                a = sc.activation(Cb[:, kc, 512 * ns:512 * (ns + 1)], logf[:, :], AF.Sin,
                                  scale=-TWO_PI / 1024.0, bias=pi2c[:, :])
                cA.inc(a, 2)
        c_a4 = cA.n


        latb = bfa.alloc([128, 512])
        zsc = fpa.alloc([128, 8])
        x3f2 = fpa.alloc([128, 1024])
        gsc = fpa.alloc([128, 8])
        gsc2 = fpa.alloc([128, 4])
        gsc3 = fpa.alloc([128, 16])

        @block.vector
        def _(v):
            v.drain()
            v.memset(zsc[:, :], 0.0)
            v.drain().then_inc(cV.h, 1)
            cV.bump()
        c_v5b = cV.n

        VS, AS_, PS_ = 66, 21, 42
        c_p2, c_v6, c_a5 = cP.n, cV.n, cA.n
        for ts2 in range(NSL):
            pv0 = c_p2 + PS_ * ts2
            vv0 = c_v6 + VS * ts2
            av0 = c_a5 + AS_ * ts2
            xfpair = [x3f, x3f2]

            @block.vector
            def _(v):
                v.drain()
                if ts2 == 0:
                    v.wait_ge(cV.h, c_v5b)
                    v.wait_ge(cA.h, c_a4)
                for tc in range(4):
                    if ts2 > 0 or tc >= 2:
                        nfull = 2 * ts2 + tc // 2
                        v.wait_ge(tH2.s[tc % 2].h, 128 * nfull)
                    c = 4 * ts2 + tc
                    v.bn_stats(stat[:, 0:6], XBF[:, c, 0:512])
                    v.bn_stats(stat[:, 6:12], XBF[:, c, 512:1024])
                    v.drain()
                    v.bn_aggr(stat[:, 12:14], stat[:, 0:12])
                    v.drain().then_inc(cV.h, 1)
                    cV.bump()
                    v.wait_ge(cA.h, av0 + tc + 1)
                    v.reciprocal(stat[:, 17:18], stat[:, 16:17])
                    v.drain()
                    cc = v.tensor_scalar(h2tm[tc % 2], XBF[:, c, :], stat[:, 12:13],
                                         stat[:, 17:18], ALU.subtract, ALU.mult)
                    cV.inc(cc)
                for c in range(8):
                    v.wait_ge(cP.h, pv0 + c + 1)
                    bank = PB[c][:, :] if c < 4 else PSQ[:, 512 * (c - 4):512 * (c - 3)]
                    cc = v.tensor_copy(asl[:, c, :], bank)
                    cV.inc(cc)
                for c in range(8):
                    v.wait_ge(cP.h, pv0 + 8 + c + 1)
                    bank = PB[c][:, :] if c < 4 else PSQ[:, 512 * (c - 4):512 * (c - 3)]
                    cc = v.tensor_copy(bsl[:, c, :], bank)
                    cV.inc(cc)
                if ts2 >= 1:
                    v.wait_ge(qX3.s[0].h, 32 * ts2)
                    v.wait_ge(qX3.s[1].h, 32 * ts2)
                for c in range(8):
                    v.drain()
                    v.tensor_scalar(x3f[:, 0:512],
                                    asl[:, c, :], ccol[:, c, 7:8], None, ALU.mult)
                    v.tensor_scalar(x3f[:, 512:1024], bsl[:, c, :], ccol[:, c, 7:8],
                                    None, ALU.mult)
                    v.drain()
                    v.scalar_tensor_tensor(asl[:, c, :], asl[:, c, :], ccol[:, c, 6:7],
                                           x3f[:, 512:1024], ALU.mult, ALU.add)
                    v.drain()
                    cc = v.scalar_tensor_tensor(bsl[:, c, :], bsl[:, c, :], ccol[:, c, 6:7],
                                                x3f[:, 0:512], ALU.mult, ALU.subtract)
                    cV.inc(cc)
                v.wait_ge(cA.h, av0 + 13)
                v.drain()
                cc = v.tensor_single_scalar(bitsT[0:24, :], latb[0:24, :], 0.0, ALU.is_gt)
                cV.inc(cc)
                v.wait_ge(cP.h, pv0 + 26)
                cc = v.tensor_single_scalar(synb[0:12, :], PB[1][0:12, :], 0.5, ALU.is_gt)
                cV.inc(cc)
                v.wait_ge(cP.h, pv0 + 27)
                v.drain()
                v.tensor_single_scalar(latb[0:1, :], PB[2][0:1, :], 0.5, ALU.is_gt)
                v.tensor_single_scalar(x3f[0:1, 0:512], PB[2][0:1, :], 3.5, ALU.is_lt)
                v.drain()
                cc = v.tensor_tensor(fixb[0:1, :], latb[0:1, :], x3f[0:1, 0:512], ALU.mult)
                cV.inc(cc)
                v.wait_ge(cP.h, pv0 + 28)
                v.drain()
                v.tensor_tensor(x3f[0:12, 0:512], bitsT[0:12, :], synb[0:12, :], ALU.mult)
                v.drain()
                v.scalar_tensor_tensor(x3f[0:12, 0:512], x3f[0:12, 0:512], -2.0,
                                       synb[0:12, :], ALU.mult, ALU.add)
                v.drain()
                v.tensor_tensor(synb[0:12, :], x3f[0:12, 0:512], PB[3][0:12, :], ALU.mult)
                v.drain()
                v.tensor_copy(corT[0:24, :], bitsT[0:24, :])
                v.drain()
                cc = v.tensor_tensor(corT[0:12, :], corT[0:12, :], synb[0:12, :], ALU.add)
                cV.inc(cc)
                for c in range(8):
                    v.wait_ge(cP.h, pv0 + 28 + c + 1)
                    bank = PB[c][:, :] if c < 4 else PSQ[:, 512 * (c - 4):512 * (c - 3)]
                    cc = v.scalar_tensor_tensor(o2T[:, c, :], bank, blec[:, c, 0:1],
                                                x1T[:, c, :], ALU.add, ALU.add)
                    cV.inc(cc)
                for tc in range(4):
                    i2 = 4 * ts2 + tc
                    v.wait_ge(tO2.s[tc % 2].h, 128 * (2 * ts2 + tc // 2 + 1))
                    if tc == 0 and ts2 >= 1:
                        v.wait_ge(tH3.s[(ts2 - 1) % 2].h,
                                  512 * ((ts2 - 1) // 2 + 1))
                    if i2 >= 2:
                        qX3.wait(v, i2, 16 * ((i2 - 2) // 2 + 1))
                    v.drain()
                    v.tensor_tensor(xfpair[tc % 2], XBF[:, 4 * ts2 + tc, :],
                                    o2tm[tc % 2], ALU.add)
                    v.drain()
                    v.bn_stats(stat[:, 0:6], xfpair[tc % 2][:, 0:512])
                    v.bn_stats(stat[:, 6:12], xfpair[tc % 2][:, 512:1024])
                    v.drain()
                    v.bn_aggr(stat[:, 12:14], stat[:, 0:12])
                    v.drain().then_inc(cV.h, 1)
                    cV.bump()
                    v.drain().then_inc(cV.h, 1)
                    cV.bump()
                    v.wait_ge(cA.h, av0 + 14 + tc)
                    v.reciprocal(stat[:, 17:18], stat[:, 16:17])
                    v.drain()
                    cc = v.tensor_scalar(h3st[:, tc, :], xfpair[tc % 2], stat[:, 12:13],
                                         stat[:, 17:18], ALU.subtract, ALU.mult)
                    cV.inc(cc)
                v.wait_ge(cP.h, pv0 + 37)
                cc = v.tensor_copy(xrg[0:16, :], PB[0][0:16, :])
                cV.inc(cc)
                v.wait_ge(cP.h, pv0 + 38)
                cc = v.tensor_scalar(logf[0:8, :], PB[1][0:8, :], rnrm[0:8, 0:1],
                                     None, ALU.mult)
                cV.inc(cc)
                for tc in range(4):
                    bank = PB[2] if tc % 2 == 0 else PB[3]
                    v.wait_ge(cP.h, pv0 + 38 + tc + 1)
                    v.drain()
                    v.reduce_max(gsc2[:, 0:1], bank[:, 0:8], axis=AX.X)
                    v.drain()
                    cc = v.tensor_scalar(gsc2[:, 1:2], gsc2[:, 0:1], -1.0, None, ALU.mult)
                    cV.inc(cc)
                    v.wait_ge(cA.h, av0 + 17 + tc + 1)
                    v.drain()
                    v.reciprocal(gsc2[:, 3:4], gsc2[:, 2:3])
                    v.tensor_scalar(gsc[:, :], bank[:, 0:8], gsc2[:, 0:1], None, ALU.is_ge)
                    v.drain()
                    v.tensor_tensor_scan(gsc3[:, 0:8], gsc[:, :], zsc[:, :], 0.0,
                                         ALU.add, ALU.add)
                    v.drain()
                    v.tensor_single_scalar(gsc3[:, 8:16], gsc3[:, 0:8], 1.0, ALU.is_equal)
                    v.drain()
                    v.tensor_tensor(gsc[:, :], gsc[:, :], gsc3[:, 8:16], ALU.mult)
                    v.drain()
                    cc = v.tensor_scalar(gatG[:, 4 * ts2 + tc, :], gsc[:, :],
                                         gsc2[:, 3:4], None, ALU.mult)
                    cV.inc(cc)

            @block.scalar
            def _(sc):
                for tc in range(4):
                    sc.wait_ge(cV.h, vv0 + 2 * tc + 1)
                    a = sc.activation(stat[:, 16:17], stat[:, 13:14], AF.Sqrt,
                                      bias=epscol[:, :])
                    cA.inc(a)
                    sc.wait_ge(cV.h, vv0 + 2 * tc + 2)
                    if ts2 > 0 and tc == 0:
                        sc.wait_ge(cP.h, pv0 - PS_ + 16)
                    for k in range(8):
                        sc.dma_start_transpose(
                            h2T[:, k, 128 * tc:128 * (tc + 1)],
                            h2tm[tc % 2][:, 128 * k:128 * (k + 1)]).then_inc(tH2.s[tc % 2].h, 16)
                        tH2.s[tc % 2].bump()
                for c in range(8):
                    sc.wait_ge(cP.h, pv0 + 16 + c + 1)
                    bank = PB[c][:, :] if c < 4 else PSQ[:, 512 * (c - 4):512 * (c - 3)]
                    a = sc.activation(x1T[:, c, :], bank, AF.Copy, scale=float(2.0 ** -10))
                    cA.inc(a)
                sc.wait_ge(cP.h, pv0 + 25)
                a = sc.activation(latb[0:24, :], PB[0][0:24, :], AF.Identity,
                                  bias=belc[0:24, :])
                cA.inc(a)
                for tc in range(4):
                    sc.wait_ge(cV.h, vv0 + 44)
                    if tc >= 2:
                        sc.wait_ge(cV.h, vv0 + 44 + 3 * (tc - 2) + 1)
                    for k in range(8):
                        sc.dma_start_transpose(
                            o2tm[tc % 2][:, 128 * k:128 * (k + 1)],
                            o2T[:, k, 128 * tc:128 * (tc + 1)]).then_inc(tO2.s[tc % 2].h, 16)
                        tO2.s[tc % 2].bump()
                    sc.wait_ge(cV.h, vv0 + 44 + 3 * tc + 2)
                    a = sc.activation(stat[:, 16:17], stat[:, 13:14], AF.Sqrt,
                                      bias=epscol[:, :])
                    cA.inc(a)
                sc.wait_ge(cV.h, vv0 + 56)
                H3Ts = bfa.at(4096 * ts2, [128, 8, 512])
                for tc in range(4):
                    for k in range(8):
                        sc.dma_start_transpose(
                            H3Ts[:, k, 128 * tc:128 * (tc + 1)],
                            h3st[:, tc, 128 * k:128 * (k + 1)]).then_inc(tH3.s[ts2 % 2].h, 16)
                        tH3.s[ts2 % 2].bump()
                for tc in range(4):
                    bank = PB[2] if tc % 2 == 0 else PB[3]
                    sc.wait_ge(cV.h, vv0 + 58 + 2 * tc + 1)
                    a = sc.activation(gsc[:, :], bank[:, 0:8], AF.Exp,
                                      bias=gsc2[:, 1:2], accum_out=gsc2[:, 2:3])
                    cA.inc(a)

            @block.tensor
            def _(t):
                t.wait_ge(tH2.s[0].h, 128 * (2 * ts2 + 2))
                t.wait_ge(tH2.s[1].h, 128 * (2 * ts2 + 2))
                if ts2 == 0:
                    t.wait_ge(sG.h, c_sg1)
                    t.wait_ge(sQ.h, c_q1)
                if ts2 >= 1:
                    t.wait_ge(cV.h, vv0)
                for c in range(8):
                    bank = PB[c][:, :] if c < 4 else PSQ[:, 512 * (c - 4):512 * (c - 3)]
                    for cin in range(8):
                        mm = t.matmul(bank, Cb[:, cin, 128 * c:128 * (c + 1)],
                                      h2T[:, cin, :], start=(cin == 0), stop=(cin == 7))
                    cP.inc(mm)
                for c in range(8):
                    t.wait_ge(cV.h, vv0 + 8 + c + 1)
                    bank = PB[c][:, :] if c < 4 else PSQ[:, 512 * (c - 4):512 * (c - 3)]
                    for cin in range(8):
                        mm = t.matmul(bank, Sb[:, cin, 128 * c:128 * (c + 1)],
                                      h2T[:, cin, :], start=(cin == 0), stop=(cin == 7))
                    cP.inc(mm)
                t.wait_ge(cV.h, vv0 + 32)
                for c in range(8):
                    bank = PB[c][:, :] if c < 4 else PSQ[:, 512 * (c - 4):512 * (c - 3)]
                    for cin in range(8):
                        t.matmul(bank, Cb[:, cin, 128 * c:128 * (c + 1)],
                                 asl[:, cin, :], start=(cin == 0), stop=False)
                    for cin in range(8):
                        mm = t.matmul(bank, Sb[:, cin, 128 * c:128 * (c + 1)],
                                      bsl[:, cin, :], start=False, stop=(cin == 7))
                    cP.inc(mm)
                t.wait_ge(cA.h, av0 + 12)
                for cin in range(8):
                    mm = t.matmul(PB[0][0:24, :], welb[:, cin, :], x1T[:, cin, :],
                                  start=(cin == 0), stop=(cin == 7))
                cP.inc(mm)
                t.wait_ge(cV.h, vv0 + 33)
                mm = t.matmul(PB[1][0:12, :], hmtb[0:24, :], bitsT[0:24, :],
                              start=True, stop=True)
                cP.inc(mm)
                t.wait_ge(cV.h, vv0 + 34)
                mm = t.matmul(PB[2][0:1, :], onescb[0:12, :], synb[0:12, :],
                              start=True, stop=True)
                cP.inc(mm)
                t.wait_ge(cV.h, vv0 + 35)
                mm = t.matmul(PB[3][0:12, :], ones1b[0:1, 0:12], fixb[0:1, :],
                              start=True, stop=True)
                cP.inc(mm)
                t.wait_ge(cV.h, vv0 + 36)
                t.wait_ge(cA.h, av0 + 13)
                for c in range(8):
                    bank = PB[c][:, :] if c < 4 else PSQ[:, 512 * (c - 4):512 * (c - 3)]
                    mm = t.matmul(bank, wleb[0:24, 128 * c:128 * (c + 1)],
                                  corT[0:24, :], start=True, stop=True)
                    cP.inc(mm)
                t.wait_ge(tH3.s[ts2 % 2].h, 512 * (ts2 // 2 + 1))
                t.wait_ge(cV.h, vv0 + 44 + 1)
                H3Ts = bfa.at(4096 * ts2, [128, 8, 512])
                for cin in range(8):
                    mm = t.matmul(PB[0][0:16, :], wgrb[:, cin, :], H3Ts[:, cin, :],
                                  start=(cin == 0), stop=(cin == 7))
                cP.inc(mm)
                t.wait_ge(cV.h, vv0 + 57)
                mm = t.matmul(PB[1][0:8, :], wgtb[0:16, :], xrg[0:16, :],
                              start=True, stop=True)
                cP.inc(mm)
                t.wait_ge(cV.h, vv0 + 58)
                for tc in range(4):
                    bank = PB[2] if tc % 2 == 0 else PB[3]
                    if tc >= 2:
                        t.wait_ge(cV.h, vv0 + 60 + 2 * (tc - 2))
                    mm = t.transpose(bank[:, 0:8], logf[0:8, 128 * tc:128 * (tc + 1)],
                                     ident[0:8, 0:8])
                    cP.inc(mm)

            @block.sync
            def _(s):
                for tc in range(4):
                    i2 = 4 * ts2 + tc
                    s.wait_ge(cV.h, vv0 + 44 + 3 * tc + 1)
                    r0 = 512 * ts2 + 128 * tc
                    qX3.inc(s.dma_start(out=out_hbm[r0:r0 + 128, :],
                                        in_=xfpair[tc % 2]), i2)

        assert cP.n == c_p2 + PS_ * NSL, (cP.n, c_p2)
        assert cV.n == c_v6 + VS * NSL, (cV.n, c_v6)
        assert cA.n == c_a5 + AS_ * NSL, (cA.n, c_a5)
        bfa.pop()
        fpa.pop()
        return qX3

    if stop_after not in ("mha0", "mha1", "ffn0", "rwkv"):
        qX3g = emit_fft()


    def emit_moe():
        bfa.push()
        fpa.push()
        w1hb = bfa.alloc([128, 8, 2048])
        w2hb = bfa.alloc([128, 16, 1024])
        Hh = bfa.alloc([128, 16, 512])
        b2rb = bfa.alloc([1, 1024])
        wstg = [fpa.alloc([128, 2048]) for _ in range(2)]
        stgf = [fpa.alloc([128, 512]) for _ in range(4)]
        b1c = fpa.alloc([128, 32, 1])

        qD1 = PSem(es, nc, "qD1")
        qD = [Sem(es, nc, f"qD{i}", 16) for i in range(4)]

        dbar = phase_barrier(taps=[("x3c", out_hbm[:, :]),
                                   ("gatGd", gatG[:, :, :])])

        d_p0, d_v0, d_a0, d_g0 = cP.n, cV.n, cA.n, cG.n
        d_q0 = sQ.n
        d_sg0 = sG.n
        PR, VR, AR, GR = 264, 64, 128, 24
        nacc = [0]

        for e in range(NE):
            @block.sync
            def _(s):
                if e == 0:
                    s.wait_ge(cB.h, dbar)
                else:
                    s.wait_ge(cA.h, d_a0 + AR * 2 * e)
                for k in range(32):
                    s.dma_start(out=b1c[:, k, :],
                                in_=b1m_in[e, 128 * k:128 * (k + 1)].unsqueeze(1)).then_inc(sQ.h, 16)
                sQ.bump(32)

            @block.gpsimd
            def _(g):
                if e == 0:
                    g.wait_ge(cB.h, dbar)
                else:
                    g.wait_ge(cP.h, d_p0 + PR * 2 * e)
                g.dma_start(out=b2rb[0:1, :], in_=b2m_in[e, :].unsqueeze(0)).then_inc(sG.h, 16)
                sG.bump()

            for hh in range(2):
                r = 2 * e + hh
                rp = d_p0 + PR * r
                rv = d_v0 + VR * r
                ra = d_a0 + AR * r
                rg = d_g0 + GR * r

                @block.sync
                def _(s):
                    if e == 0 and hh == 0:
                        s.wait_ge(cB.h, dbar)
                    for i in range(24):
                        gi = 24 * r + i
                        if gi >= 2:
                            gie = gi - 2
                            s.wait_ge(cG.h, d_g0 + GR * (gie // 24) + (gie % 24) + 1)
                        if i < 8:
                            dd = s.dma_start(
                                out=wstg[gi % 2],
                                in_=w1m_in[e, 128 * i:128 * (i + 1),
                                           2048 * hh:2048 * (hh + 1)])
                        else:
                            j = i - 8
                            dd = s.dma_start(
                                out=wstg[gi % 2][:, 0:1024],
                                in_=w2m_in[e, 2048 * hh + 128 * j:2048 * hh + 128 * (j + 1), :])
                        qD1.inc(dd, gi)

                @block.gpsimd
                def _(g):
                    if r >= 1:
                        g.wait_ge(cP.h, d_p0 + PR * r)
                    for i in range(24):
                        gi = 24 * r + i
                        qD1.wait(g, gi, 16 * (gi // 2 + 1))
                        if i < 8:
                            cc = g.tensor_copy(w1hb[:, i, :], wstg[gi % 2])
                        else:
                            j = i - 8
                            cc = g.tensor_copy(w2hb[:, j, :], wstg[gi % 2][:, 0:1024])
                        cG.inc(cc)

                @block.tensor
                def _(t):
                    t.wait_ge(cG.h, d_g0 + GR * r + 24)
                    if hh == 0:
                        t.wait_ge(sG.h, d_sg0 + 16 * (e + 1))
                    if r >= 1:
                        t.wait_ge(cV.h, d_v0 + VR * r)
                    for ts3 in range(NSL):
                        tsp = rp + 33 * ts3
                        H3Ts = bfa.at(4096 * ts3, [128, 8, 512])
                        if ts3 >= 1:
                            t.wait_ge(cV.h, rv + 8 * ts3)
                        for hc in range(16):
                            if ts3 > 0 or hc >= 2:
                                gidx = 16 * ts3 + hc - 2
                                t.wait_ge(cA.h, ra + gidx + 1)
                            bank = PB[hc % 2]
                            for cin in range(8):
                                mm = t.matmul(bank[:, :], w1hb[:, cin, 128 * hc:128 * (hc + 1)],
                                              H3Ts[:, cin, :], start=(cin == 0), stop=(cin == 7))
                            cP.inc(mm)
                        t.wait_ge(cA.h, ra + 16 * (ts3 + 1))
                        if ts3 >= 1:
                            t.wait_ge(cV.h, rv + 8 * ts3)
                        for hc in range(16):
                            for tc in range(4):
                                for ch in range(2):
                                    g8 = 2 * tc + ch
                                    bank = (PB[g8][:, :] if g8 < 4
                                            else PSQ[:, 512 * (g8 - 4):512 * (g8 - 3)])
                                    mm = t.matmul(bank, Hh[:, hc, 128 * tc:128 * (tc + 1)],
                                                  w2hb[:, hc, 512 * ch:512 * (ch + 1)],
                                                  start=(hc == 0),
                                                  stop=(hh == 1 and hc == 15))
                            if hh == 1 and hc == 15:
                                cP.inc(mm, 2)
                            else:
                                cP.inc(mm)
                        if hh == 0:
                            for g8 in range(8):
                                ch = g8 % 2
                                bank = (PB[g8][:, :] if g8 < 4
                                        else PSQ[:, 512 * (g8 - 4):512 * (g8 - 3)])
                                mm = t.matmul(bank, ones1b[0:1, 0:128],
                                              b2rb[0:1, 512 * ch:512 * (ch + 1)],
                                              start=False, stop=True)
                            cP.inc(mm)

                @block.scalar
                def _(sc):
                    if e > 0 or hh > 0:
                        pass
                    for ts3 in range(NSL):
                        tsp = rp + 33 * ts3
                        if ts3 == 0 and hh == 0:
                            sc.wait_ge(sQ.h, d_q0 + 32 * 16 * (e + 1))
                        for hc in range(16):
                            sc.wait_ge(cP.h, tsp + hc + 1)
                            bank = PB[hc % 2]
                            a = sc.activation(Hh[:, hc, :], bank[:, :], AF.Gelu,
                                              bias=b1c[:, 16 * hh + hc, :])
                            cA.inc(a)

                @block.vector
                def _(v):
                    v.drain()
                    for ts3 in range(NSL):
                        tsp = rp + 33 * ts3
                        v.wait_ge(cP.h, tsp + 33)
                        for tc in range(4):
                            for ch in range(2):
                                g8 = 2 * tc + ch
                                i4 = nacc[0] + 8 * ts3 + g8
                                bank = (PB[g8][:, :] if g8 < 4
                                        else PSQ[:, 512 * (g8 - 4):512 * (g8 - 3)])
                                if i4 >= 4:
                                    v.wait_ge(qD[i4 % 4].h, 16 * ((i4 - 4) // 4 + 1))
                                cc = v.tensor_scalar(stgf[i4 % 4], bank,
                                                     gatG[:, 4 * ts3 + tc, e:e + 1],
                                                     None, ALU.mult)
                                cV.inc(cc)

                @block.gpsimd
                def _(g):
                    for ts3 in range(NSL):
                        for tc in range(4):
                            for ch in range(2):
                                g8 = 2 * tc + ch
                                i4 = nacc[0] + 8 * ts3 + g8
                                g.wait_ge(cV.h, rv + 8 * ts3 + g8 + 1)
                                if i4 >= 4:
                                    g.wait_ge(qD[i4 % 4].h, 16 * ((i4 - 4) // 4 + 1))
                                r0 = 512 * ts3 + 128 * tc
                                dd = g.dma_start(out=out_hbm[r0:r0 + 128,
                                                             512 * ch:512 * (ch + 1)],
                                                 in_=stgf[i4 % 4],
                                                 accum_op=ALU.add)
                                dd.then_inc(qD[i4 % 4].h, 16)
                                qD[i4 % 4].bump()
                nacc[0] += 64

        @block.gpsimd
        def _(g):
            for i in range(4):
                g.wait_ge(qD[i].h, qD[i].n)
            g.wait_ge(qX3g.s[0].h, qX3g.s[0].n)
            g.wait_ge(qX3g.s[1].h, qX3g.s[1].n)
        bfa.pop()
        fpa.pop()

    if stop_after not in ("mha0", "mha1", "ffn0", "rwkv", "fft"):
        emit_moe()

    @block.gpsimd
    def _(g):
        g.wait_ge(sQ.h, sQ.n)
        g.wait_ge(sG.h, sG.n)
        g.wait_ge(cV.h, cV.n)
        g.wait_ge(cP.h, cP.n)
        g.wait_ge(cA.h, cA.n)

    es.close()
    return nc, dbg


def _host_inputs(core_inputs):
    return core_inputs


_GOLAY_A = np.array(
    [[1, 1, 1, 0, 1, 1, 1, 0, 0, 0, 1, 1], [1, 1, 0, 1, 1, 1, 0, 0, 0, 1, 1, 0],
     [1, 0, 1, 1, 1, 0, 0, 0, 1, 1, 0, 1], [0, 1, 1, 1, 0, 0, 0, 1, 1, 0, 1, 1],
     [0, 0, 0, 0, 1, 1, 1, 1, 1, 1, 1, 1], [1, 1, 1, 1, 0, 0, 0, 0, 0, 0, 0, 0],
     [0, 0, 0, 0, 1, 1, 1, 1, 0, 0, 0, 0], [1, 1, 0, 0, 1, 1, 0, 0, 1, 1, 0, 0],
     [1, 0, 1, 0, 1, 0, 1, 0, 1, 0, 1, 0], [0, 1, 0, 1, 0, 1, 0, 1, 0, 1, 0, 1],
     [1, 0, 0, 1, 1, 0, 0, 1, 1, 0, 0, 1], [0, 1, 1, 0, 0, 1, 1, 0, 0, 1, 1, 0]],
    dtype=np.float32)
_HMT = np.concatenate([_GOLAY_A.T, np.eye(12, dtype=np.float32)], axis=1).T.copy()


def host_maps(inputs):
    f = np.float32
    def T(a):
        return np.ascontiguousarray(np.asarray(a, dtype=f).T)
    def C(a):
        return np.ascontiguousarray(np.asarray(a, dtype=f))
    common = {
        "wq1": T(inputs["L0_attn_wq1"]), "wk1": T(inputs["L0_attn_wk1"]),
        "wv1": T(inputs["L0_attn_wv1"]), "wo1": T(inputs["L0_attn_wo1"]),
        "wq2": T(inputs["L0_attn_wq2"]), "wk2": T(inputs["L0_attn_wk2"]),
        "wv2": T(inputs["L0_attn_wv2"]), "wo2": T(inputs["L0_attn_wo2"]),
        "bq1": C(inputs["L0_attn_bq1"]), "bk1": C(inputs["L0_attn_bk1"]),
        "bv1": C(inputs["L0_attn_bv1"]), "bo1": C(inputs["L0_attn_bo1"]),
        "bq2": C(inputs["L0_attn_bq2"]), "bk2": C(inputs["L0_attn_bk2"]),
        "bv2": C(inputs["L0_attn_bv2"]), "bo2": C(inputs["L0_attn_bo2"]),
        "g1": C(inputs["L0_attn_g1"]), "be1": C(inputs["L0_attn_be1"]),
        "g2": C(inputs["L0_attn_g2"]), "be2": C(inputs["L0_attn_be2"]),
        "wff1": T(inputs["L0_attn_w_ff1"]), "bff1": C(inputs["L0_attn_b_ff1"]),
        "wff2": T(inputs["L0_attn_w_ff2"]), "bff2": C(inputs["L0_attn_b_ff2"]),
        "gff": C(inputs["L0_attn_g_ff"]), "beff": C(inputs["L0_attn_be_ff"]),
        "wr": T(inputs["L0_ffn_W_r"]), "wkk": T(inputs["L0_ffn_W_k"]),
        "wvv": T(inputs["L0_ffn_W_v"]),
        "mur": C(inputs["L0_ffn_mu_r"]), "muk": C(inputs["L0_ffn_mu_k"]),
        "wel": C(inputs["L1_attn_W_el"]), "bel": C(inputs["L1_attn_b_el"]),
        "wle": C(inputs["L1_attn_W_le"]), "ble": C(inputs["L1_attn_b_le"]),
        "hmt": np.ascontiguousarray(_HMT),
        "wgred": T(inputs["L1_ffn_wg_red"]), "wgm": C(inputs["L1_ffn_wg"]),
        "wgt": T(inputs["L1_ffn_wg"]),
        "b1m": C(inputs["L1_ffn_b1"]), "b2m": C(inputs["L1_ffn_b2"]),
        "w1m": C(inputs["L1_ffn_w1"]), "w2m": C(inputs["L1_ffn_w2"]),
        "ident": np.eye(128, dtype=f),
    }
    x = np.asarray(inputs["L0_attn_x"], dtype=f)
    return [dict(common, x=np.ascontiguousarray(x[b])) for b in range(x.shape[0])]


def kernel(**inputs):
    maps = host_maps(inputs)
    nc, _ = build_program()
    res = run_bass_kernel_spmd(nc, maps, core_ids=list(range(len(maps))))
    return np.stack([np.asarray(r["out"], dtype=np.float32) for r in res.results])


# version 04
# speedup vs baseline: 1.1090x; 1.1069x over previous
import numpy as np
import concourse.bass as bass
import concourse.mybir as mybir
from concourse.bass_utils import run_bass_kernel_spmd

F32 = mybir.dt.float32
BF16 = mybir.dt.bfloat16
AF = mybir.ActivationFunctionType
ALU = mybir.AluOpType
AX = mybir.AxisListType

S, D, DH, H, DK, DFF, NE = 4096, 1024, 512, 8, 64, 4096, 8
NCH, NSL = 32, 8

ALPHA = 0.1
BETA = 0.1
A11 = 1.0 - ALPHA
A12 = 1.0 - ALPHA
A21 = -(ALPHA * BETA)
A22 = 1.0 - ALPHA * BETA
B1 = ALPHA
B2 = ALPHA * BETA
LNEPS = 1e-5


class Sem:
    REG = None

    def __init__(self, es, nc, name, step=1):
        self.h = es.enter_context(nc.semaphore(name))
        self.n = 0
        self.step = step
        if Sem.REG is not None:
            Sem.REG.append(self)

    def inc(self, inst, k=1):
        inst.then_inc(self.h, self.step * k)
        self.n += self.step * k
        return self.n

    def bump(self, k=1):
        self.n += self.step * k
        return self.n

    def wait(self, eng, val=None):
        eng.wait_ge(self.h, self.n if val is None else val)


class PSem:

    def __init__(self, es, nc, name):
        self.s = [Sem(es, nc, name + "a", 16), Sem(es, nc, name + "b", 16)]

    def inc(self, inst, i):
        return self.s[i % 2].inc(inst)

    def bump(self, i, k=1):
        return self.s[i % 2].bump(k)

    def wait(self, eng, i, val):
        eng.wait_ge(self.s[i % 2].h, val)


class Arena:
    def __init__(self, ap, elems):
        self.ap, self.elems, self.off, self.stack = ap, elems, 0, []

    def alloc(self, shape):
        n = int(np.prod(shape[1:]))
        assert self.off + n <= self.elems, (self.off, n, self.elems)
        flat = self.ap[:, self.off:self.off + n]
        self.off += n
        if len(shape) == 3:
            flat = flat.rearrange("p (a b) -> p a b", a=shape[1], b=shape[2])
        return flat

    def at(self, off, shape):
        n = int(np.prod(shape[1:]))
        assert off + n <= self.elems
        flat = self.ap[:, off:off + n]
        if len(shape) == 3:
            flat = flat.rearrange("p (a b) -> p a b", a=shape[1], b=shape[2])
        return flat

    def push(self):
        self.stack.append(self.off)

    def pop(self):
        self.off = self.stack.pop()


def build_program(debug=(), stop_after=None):
    nc = bass.Bass("TRN2", target_bir_lowering=False)

    x_in = nc.dram_tensor("x", [S, D], F32, kind="ExternalInput").ap()
    out_hbm = nc.dram_tensor("out", [S, D], F32, kind="ExternalOutput").ap()

    def din(name, shape):
        return nc.dram_tensor(name, shape, F32, kind="ExternalInput").ap()

    wq = [din("wq1", [DH, DH]), din("wq2", [DH, DH])]
    wk = [din("wk1", [DH, DH]), din("wk2", [DH, DH])]
    wv = [din("wv1", [DH, DH]), din("wv2", [DH, DH])]
    wo = [din("wo1", [DH, DH]), din("wo2", [DH, DH])]
    bqi = [din("bq1", [DH]), din("bq2", [DH])]
    bki = [din("bk1", [DH]), din("bk2", [DH])]
    bvi = [din("bv1", [DH]), din("bv2", [DH])]
    boi = [din("bo1", [DH]), din("bo2", [DH])]
    gbi = [(din("g1", [DH]), din("be1", [DH])), (din("g2", [DH]), din("be2", [DH]))]
    wff1_in = din("wff1", [D, DFF]); bff1_in = din("bff1", [DFF])
    wff2_in = din("wff2", [DFF, D]); bff2_in = din("bff2", [D])
    gff_in = din("gff", [D]); beff_in = din("beff", [D])
    wr_in = din("wr", [D, D]); wkk_in = din("wkk", [D, D]); wvv_in = din("wvv", [D, D])
    mur_in = din("mur", [D]); muk_in = din("muk", [D])
    wel_in = din("wel", [D, 24]); bel_in = din("bel", [24])
    wle_in = din("wle", [24, D]); ble_in = din("ble", [D])
    hmt_in = din("hmt", [24, 12])
    wgred_in = din("wgred", [D, 16])
    wgm_in = din("wgm", [NE, 16]); wgt_in = din("wgt", [16, NE])
    b1m_in = din("b1m", [NE, DFF]); b2m_in = din("b2m", [NE, D])
    w1m_in = din("w1m", [NE, D, DFF]); w2m_in = din("w2m", [NE, DFF, D])
    ident_in = din("ident", [128, 128])
    x1sp = nc.dram_tensor("x1sp", [S, DH], BF16, kind="Internal").ap()
    x2sp = nc.dram_tensor("x2sp", [S, DH], BF16, kind="Internal").ap()
    xshsp = nc.dram_tensor("xshsp", [S, DH], BF16, kind="Internal").ap()

    dbg = {}
    for spec in debug:
        name, shape = spec[0], spec[1]
        ddt = BF16 if (len(spec) > 2 and spec[2] == "bf16") else F32
        dbg[name] = nc.dram_tensor("dbg_" + name, list(shape), ddt,
                                   kind="ExternalOutput").ap()

    from contextlib import ExitStack
    es = ExitStack()

    BFA_N, FPA_N = 84480, 7936
    BFAt = es.enter_context(nc.sbuf_tensor("BFA", [128, BFA_N], BF16))
    FPAt = es.enter_context(nc.sbuf_tensor("FPA", [128, FPA_N], F32))
    bfa = Arena(BFAt[:, :], BFA_N)
    fpa = Arena(FPAt[:, :], FPA_N)
    carry = fpa.alloc([128, 2, 512])

    ident = es.enter_context(nc.sbuf_tensor("idents", [128, 128], F32))
    ones1 = es.enter_context(nc.sbuf_tensor("ones1r", [1, 512], F32))
    ones1b = es.enter_context(nc.sbuf_tensor("ones1b", [1, 512], BF16))
    onescb = es.enter_context(nc.sbuf_tensor("onescb", [128, 1], BF16))
    srow = es.enter_context(nc.sbuf_tensor("srow", [1, 64], F32))
    spow = es.enter_context(nc.sbuf_tensor("spow", [2, 8, 2], F32))
    tpow = es.enter_context(nc.sbuf_tensor("tpow", [2, 8, 2], F32))
    cpair = es.enter_context(nc.sbuf_tensor("cpair", [2, 520], F32))
    rseq = es.enter_context(nc.sbuf_tensor("rseq", [2, 128], F32))
    e1seq = es.enter_context(nc.sbuf_tensor("e1seq", [2, 128], F32))
    cpt = es.enter_context(nc.sbuf_tensor("cpt", [2, 128], F32))
    w2t = es.enter_context(nc.sbuf_tensor("w2t", [128, 2], F32))
    w2ct = es.enter_context(nc.sbuf_tensor("w2ct", [128, 2], F32))
    t0t = es.enter_context(nc.sbuf_tensor("t0t", [128, 128], BF16))
    t0ct = es.enter_context(nc.sbuf_tensor("t0ct", [128, 128], BF16))
    wrow = es.enter_context(nc.sbuf_tensor("wrow", [1, 128], F32))
    epscol = es.enter_context(nc.sbuf_tensor("epscol", [128, 1], F32))

    ln0 = es.enter_context(nc.sbuf_tensor("ln0s", [128, NCH, 2], F32))
    gatG = es.enter_context(nc.sbuf_tensor("gatG", [128, NCH, 8], F32))

    PSQ = es.enter_context(nc.psum_tensor("PSQ", [128, 2048], F32))
    PB = [es.enter_context(nc.psum_tensor(f"PB{i}", [128, 512], F32)) for i in range(4)]

    Sem.REG = []
    sQ = Sem(es, nc, "sQ", 16)
    sT = Sem(es, nc, "sT", 16)
    sG = Sem(es, nc, "sG", 16)
    cP = Sem(es, nc, "cP")
    cV = Sem(es, nc, "cV")
    cA = Sem(es, nc, "cA")
    cG = Sem(es, nc, "cG")
    cB = Sem(es, nc, "cB")

    def phase_barrier(taps=()):
        sems = [(s_.h, s_.n) for s_ in Sem.REG if s_.n > 0 and s_ is not cB]
        live = [(dbg[n_], a_) for n_, a_ in taps if n_ in dbg]
        for _ in live:
            sQ.bump()
        sqn = sQ.n
        @block.sync
        def _(s):
            for h_, n_ in sems:
                s.wait_ge(h_, n_)
            for d_, a_ in live:
                s.dma_start(out=d_, in_=a_).then_inc(sQ.h, 16)
            if live:
                s.wait_ge(sQ.h, sqn)
            s.nop().then_inc(cB.h, 1)
        cB.bump()
        return cB.n

    block = es.enter_context(nc.Block())

    @block.sync
    def _(s):
        s.dma_start(out=ident[:, :], in_=ident_in[:, :]).then_inc(sQ.h, 16)
        sQ.bump()
    sq_ident = sQ.n

    @block.vector
    def _(v):
        v.memset(ones1[:, :], 1.0)
        v.memset(ones1b[:, :], 1.0)
        v.memset(onescb[:, :], 1.0)
        vals = [A11, A12, A21, A22, A11, A21, A12, A22,
                1.0, 0.0, 0.0, 1.0, B1, B2]
        for i, val in enumerate(vals):
            v.memset(srow[0:1, i:i + 1], val)
        v.drain()
        for i, val in enumerate(vals):
            lo = float(np.float64(val) - np.float64(np.float32(val)))
            if lo != 0.0:
                v.tensor_scalar(srow[0:1, i:i + 1], srow[0:1, i:i + 1], lo,
                                None, ALU.add)
        v.drain()
        v.memset(cpair[:, :], 0.0)
        v.memset(rseq[:, :], 0.0)
        v.memset(e1seq[:, :], 0.0)
        v.memset(cpt[:, :], 0.0)
        v.memset(carry[:, :, :], 0.0)
        v.memset(t0t[:, :], 0.0)
        v.memset(epscol[:, :], LNEPS)
        v.drain().then_inc(cV.h, 1)
        cV.bump()
    cv0 = cV.n

    @block.gpsimd
    def _(g):
        g.wait_ge(cV.h, cv0)
        g.dma_start(out=spow[:, 0, :], in_=srow[0:1, 0:4]).then_inc(sG.h, 16)
        g.dma_start(out=tpow[:, 0, :], in_=srow[0:1, 4:8]).then_inc(sG.h, 16)
        g.dma_start(out=cpair[:, 0:2], in_=srow[0:1, 8:12]).then_inc(sG.h, 16)
        g.dma_start(out=rseq[:, 127:128], in_=srow[0:1, 12:14]).then_inc(sG.h, 16)
        sG.bump(4)
    sg0 = sG.n

    cp_base = cP.n
    @block.tensor
    def _(t):
        t.wait_ge(sG.h, sg0)
        for k in range(7):
            if k > 0:
                t.wait_ge(cV.h, cv0 + k)
            m = 1 << k
            t.matmul(PB[0][0:2, 0:2], tpow[:, k, :], spow[:, k, :], start=True, stop=True)
            t.matmul(PB[0][0:2, 4:6], spow[:, k, :], tpow[:, k, :], start=True, stop=True)
            t.matmul(PB[1][0:2, 0:2 * m], tpow[:, k, :], cpair[:, 0:2 * m],
                     start=True, stop=True)
            mm = t.matmul(PB[2][0:2, 0:m], tpow[:, k, :], rseq[:, 128 - m:128],
                          start=True, stop=True)
            cP.inc(mm)

    @block.vector
    def _(v):
        for k in range(7):
            v.wait_ge(cP.h, cp_base + k + 1)
            m = 1 << k
            v.tensor_copy(spow[:, k + 1, :], PB[0][0:2, 0:2])
            v.tensor_copy(tpow[:, k + 1, :], PB[0][0:2, 4:6])
            v.tensor_copy(cpair[:, 2 * m:4 * m], PB[1][0:2, 0:2 * m])
            v.tensor_copy(rseq[:, 128 - 2 * m:128 - m], PB[2][0:2, 0:m])
            v.drain().then_inc(cV.h, 1)
            cV.bump()
    cv_dbl = cV.n

    @block.tensor
    def _(t):
        t.wait_ge(cV.h, cv_dbl)
        mm = t.matmul(PB[1][0:2, 0:2], tpow[:, 7, :], cpair[:, 0:2], start=True, stop=True)
        cP.inc(mm)
    cp_a128 = cP.n

    @block.vector
    def _(v):
        v.wait_ge(cP.h, cp_a128)
        v.tensor_copy(cpair[:, 256:258], PB[1][0:2, 0:2])
        v.tensor_scalar(wrow[0:1, :], cpair[0:1, 0:256:2], B1, None, ALU.mult)
        v.drain()
        v.scalar_tensor_tensor(wrow[0:1, :], cpair[0:1, 1:257:2], B2, wrow[0:1, :],
                               ALU.mult, ALU.add)
        v.tensor_copy(cpt[0:1, :], cpair[0:1, 2:258:2])
        v.tensor_copy(e1seq[0:1, :], cpair[0:1, 0:256:2])
        v.drain().then_inc(cV.h, 1)
        cV.bump()
    cv_rows = cV.n

    @block.tensor
    def _(t):
        t.wait_ge(cV.h, cv_rows)
        t.wait_ge(sQ.h, sq_ident)
        t.matmul(PB[0][0:2, 0:128], srow[0:1, 8:10], cpair[0:1, 2:258:2],
                 start=True, stop=False)
        t.matmul(PB[0][0:2, 0:128], srow[0:1, 10:12], cpair[0:1, 3:259:2],
                 start=False, stop=True)
        t.matmul(PB[1][0:2, 0:128], srow[0:1, 8:10], cpair[0:1, 0:256:2],
                 start=True, stop=False)
        t.matmul(PB[1][0:2, 0:128], srow[0:1, 10:12], cpair[0:1, 1:257:2],
                 start=False, stop=True)
        t.transpose(PB[2][0:128, 0:2], rseq[0:2, :], ident[0:2, 0:2])
        mm = t.matmul(PB[3][0:1, 0:2], cpair[:, 254:255], ident[0:2, 0:2],
                      start=True, stop=True)
        cP.inc(mm)
    cp_rows2 = cP.n

    @block.vector
    def _(v):
        v.wait_ge(cP.h, cp_rows2)
        v.tensor_copy(cpt[:, :], PB[0][0:2, 0:128])
        v.tensor_copy(e1seq[:, :], PB[1][0:2, 0:128])
        v.tensor_copy(w2t[:, :], PB[2][0:128, 0:2])
        v.tensor_copy(srow[0:1, 16:18], PB[3][0:1, 0:2])
        cc = v.tensor_copy(t0t[0:1, 0:1], wrow[0:1, 0:1])
        cV.inc(cc)
    sg_rows = sG.n
    cv_rows2 = cV.n

    cp_t0b = cP.n
    sg_t0b = sG.n
    cv_t0b = cV.n
    @block.tensor
    def _(t):
        t.wait_ge(cV.h, cv_rows2)
        for k in range(7):
            m = 1 << k
            if k > 0:
                t.wait_ge(sG.h, sg_t0b + 16 * k)
            mm = t.matmul(PB[3][0:m, 0:m], rseq[:, 127 - m:127], e1seq[:, 0:m],
                          start=True, stop=True)
            cP.inc(mm)

    @block.vector
    def _(v):
        for k in range(7):
            m = 1 << k
            v.wait_ge(cP.h, cp_t0b + k + 1)
            if k > 0:
                v.wait_ge(sG.h, sg_t0b + 16 * k)
            cc = v.tensor_copy(t0t[0:m, m:2 * m], PB[3][0:m, 0:m])
            cV.inc(cc)

    @block.gpsimd
    def _(g):
        for k in range(7):
            m = 1 << k
            g.wait_ge(cV.h, cv_t0b + k + 1)
            g.dma_start(out=t0t[m:2 * m, m:2 * m], in_=t0t[0:m, 0:m]).then_inc(sG.h, 16)
            sG.bump()
    sg_t0d = sG.n

    @block.vector
    def _(v):
        v.wait_ge(sG.h, sg_t0d)
        v.tensor_copy(t0ct[:, :], t0t[:, :])
        v.drain()
        v.tensor_copy(t0ct[0:1, :], cpair[0:1, 0:256:2])
        v.tensor_copy(w2ct[:, :], w2t[:, :])
        v.drain()
        v.tensor_copy(w2ct[0:1, 0:2], srow[0:1, 16:18])
        v.drain().then_inc(cV.h, 1)
        cV.bump()
    cv_t0c = cV.n
    sg_c0 = sG.n
    cv_c0 = cV.n

    def tap_small(name, src_ap):
        if name not in dbg:
            return
        @block.sync
        def _(s):
            s.wait_ge(sG.h, sg_c0)
            s.wait_ge(cV.h, cv_c0)
            s.dma_start(out=dbg[name][:, :], in_=src_ap).then_inc(sQ.h, 16)
            sQ.bump()

    tap_small("cpair", cpair[:, :])
    tap_small("rseq", rseq[:, :])
    tap_small("w2t", w2t[:, :])
    tap_small("w2ct", w2ct[:, :])
    tap_small("cpt", cpt[:, :])
    tap_small("wrow", wrow[:, :])

    if "t0t" in dbg or "t0ct" in dbg:
        fpa.push()
        tf = fpa.alloc([128, 128])
        tf2 = fpa.alloc([128, 128])
        @block.scalar
        def _(sc):
            sc.wait_ge(cV.h, cv_c0)
            sc.wait_ge(sG.h, sg_c0)
            sc.copy(tf, t0t[:, :])
            a = sc.copy(tf2, t0ct[:, :])
            cA.inc(a)
        ca_t0 = cA.n
        @block.sync
        def _(s):
            s.wait_ge(cA.h, ca_t0)
            if "t0t" in dbg:
                s.dma_start(out=dbg["t0t"][:, :], in_=tf).then_inc(sQ.h, 16)
                sQ.bump()
            if "t0ct" in dbg:
                s.dma_start(out=dbg["t0ct"][:, :], in_=tf2).then_inc(sQ.h, 16)
                sQ.bump()
        fpa.pop()

    XBF = bfa.at(0, [128, NCH, 1024])
    R1 = 32768

    fpa.push()
    xfp = [fpa.alloc([128, 1024]) for _ in range(2)]
    bnst = fpa.alloc([128, 2, 12])
    bnag = fpa.alloc([128, 2, 2])
    sdc = fpa.alloc([128, 2, 1])

    qx = PSem(es, nc, "qx")
    a1_v0, a1_a0, a1_g0 = cV.n, cA.n, cG.n

    @block.sync
    def _(s):
        s.wait_ge(sQ.h, sQ.n)
        for c in range(NCH):
            if c >= 2:
                s.wait_ge(cV.h, a1_v0 + 2 * (c - 1))
                s.wait_ge(cG.h, a1_g0 + (c - 1))
            qx.inc(s.dma_start(out=xfp[c % 2], in_=x_in[128 * c:128 * (c + 1), :]), c)

    @block.vector
    def _(v):
        for c in range(NCH):
            qx.wait(v, c, 16 * (c // 2 + 1))
            p = c % 2
            v.bn_stats(bnst[:, p, 0:6], xfp[p][:, 0:512])
            v.bn_stats(bnst[:, p, 6:12], xfp[p][:, 512:1024])
            v.drain()
            v.bn_aggr(bnag[:, p, :], bnst[:, p, 0:12])
            v.drain()
            v.tensor_copy(ln0[:, c, 0:1], bnag[:, p, 0:1])
            v.drain().then_inc(cV.h, 1)
            cV.bump()
            if c >= 1:
                v.wait_ge(cA.h, a1_a0 + c)
                rr = v.reciprocal(ln0[:, c - 1, 1:2], sdc[:, (c - 1) % 2, :])
                cV.inc(rr)
        v.wait_ge(cA.h, a1_a0 + NCH)
        rr = v.reciprocal(ln0[:, NCH - 1, 1:2], sdc[:, (NCH - 1) % 2, :])
        cV.inc(rr)
    a1_v1 = cV.n

    @block.scalar
    def _(sc):
        for c in range(NCH):
            sc.wait_ge(cV.h, a1_v0 + 2 * c + 1)
            a = sc.activation(sdc[:, c % 2, :], bnag[:, c % 2, 1:2], AF.Sqrt,
                              bias=epscol[:, :])
            cA.inc(a)

    @block.gpsimd
    def _(g):
        for c in range(NCH):
            qx.wait(g, c, 16 * (c // 2 + 1))
            cc = g.tensor_copy(XBF[:, c, :], xfp[c % 2])
            cG.inc(cc)
    a1_g1 = cG.n
    fpa.pop()


    if "ln0m" in dbg:
        fpa.push()
        tf = fpa.alloc([128, 32])
        tf2 = fpa.alloc([128, 32])
        @block.vector
        def _(v):
            v.wait_ge(cV.h, a1_v1)
            v.wait_ge(cG.h, a1_g1)
            v.tensor_copy(tf, ln0[:, :, 0])
            cc = v.tensor_copy(tf2, ln0[:, :, 1])
            cV.inc(cc)
        vv = cV.n
        @block.sync
        def _(s):
            s.wait_ge(cV.h, vv)
            s.dma_start(out=dbg["ln0m"][:, :], in_=tf).then_inc(sQ.h, 16)
            s.dma_start(out=dbg["ln0r"][:, :], in_=tf2).then_inc(sQ.h, 16)
            sQ.bump(2)
        fpa.pop()

    bfa.off = R1
    bfa.push()
    ubf = [bfa.alloc([128, 512]) for _ in range(2)]
    xst = [bfa.alloc([128, 512]) for _ in range(2)]
    fpa.push()
    ufp = [fpa.alloc([128, 512]) for _ in range(2)]
    qH = PSem(es, nc, "qH")

    h_v0, h_p0 = cV.n, cP.n
    h_q0 = sQ.n

    @block.vector
    def _(v):
        v.drain()
        v.wait_ge(sQ.h, h_q0)
        v.wait_ge(cG.h, a1_g1)
        for c in range(NCH):
            if c >= 1:
                v.wait_ge(cP.h, h_p0 + 2 * c)
                cc = v.tensor_copy(carry[0:2, c % 2, :], PB[1][0:2, :])
                cV.inc(cc)
                if c >= 3:
                    qH.wait(v, c - 1, 16 * ((c - 3) // 2 + 1))
                cc = v.tensor_copy(xst[(c - 1) % 2], PB[0][:, :])
                cV.inc(cc)
            p = c % 2
            v.tensor_scalar(ufp[p], XBF[:, c, 0:512], ln0[:, c, 0:1],
                            ln0[:, c, 1:2], ALU.subtract, ALU.mult)
            v.drain()
            cc = v.tensor_copy(ubf[p], ufp[p])
            cV.inc(cc)
        v.wait_ge(cP.h, h_p0 + 2 * NCH)
        qH.wait(v, NCH - 1, 16 * ((NCH - 3) // 2 + 1))
        cc = v.tensor_copy(xst[(NCH - 1) % 2], PB[0][:, :])
        cV.inc(cc)
    h_v2 = cV.n

    @block.gpsimd
    def _(g):
        for c in range(NCH):
            if c < NCH - 1:
                g.wait_ge(cV.h, h_v0 + 3 * (c + 1))
            else:
                g.wait_ge(cV.h, h_v2)
            qH.inc(g.dma_start(out=xshsp[128 * c:128 * (c + 1), :],
                               in_=xst[c % 2]), c)
    

    @block.tensor
    def _(t):
        for c in range(NCH):
            t.wait_ge(cV.h, h_v0 + 3 * c + 1)
            p = c % 2
            tt = t0ct if c == 0 else t0t
            ww = w2ct if c == 0 else w2t
            t.matmul(PB[0][:, :], tt[:, :], ubf[p], start=True, stop=False)
            mm = t.matmul(PB[0][:, :], cpt[:, :], carry[0:2, c % 2, :],
                          start=False, stop=True)
            cP.inc(mm)
            t.matmul(PB[1][0:2, :], ww[:, :], ufp[p], start=True, stop=False)
            mm = t.matmul(PB[1][0:2, :], tpow[:, 7, :], carry[0:2, c % 2, :],
                          start=False, stop=True)
            cP.inc(mm)
    h_p1 = cP.n

    bfa.pop()
    fpa.pop()


    def emit_mha(li, dst_sp):
        bfa.push()
        fpa.push()
        K1T = bfa.alloc([128, 4, S])
        V1 = bfa.alloc([128, NCH, 520])
        wqb = bfa.alloc([128, 4, 512])
        wob = bfa.alloc([128, 8, 512])
        tmsl = [bfa.alloc([128, 4, 512])]
        tsl = [bfa.alloc([128, 4, 512])]
        wcb = [bfa.alloc([128, 1024]) for _ in range(2)]
        browb = bfa.alloc([1, 4, 512])
        qtt = bfa.alloc([128, 512])
        aq = [bfa.alloc([128, 1024]) for _ in range(2)]
        ohb = bfa.alloc([128, 512])
        stg = [bfa.alloc([128, 512]) for _ in range(2)]

        wfs = [fpa.alloc([128, 1024]) for _ in range(2)]
        bqkc = fpa.alloc([128, 4, 2])
        bcB = fpa.alloc([128, 512])
        bcG = fpa.alloc([128, 512])
        bcE = fpa.alloc([128, 512])
        oacc = fpa.alloc([128, 4, 512])
        denA = fpa.alloc([128, 512])
        rcq = fpa.alloc([128, 4])
        sfp = fpa.alloc([128, 512])
        stat = fpa.alloc([128, 18])

        qW = PSem(es, nc, "qW" + str(li))
        qK = PSem(es, nc, "qK" + str(li))
        qM = PSem(es, nc, "qM" + str(li))
        tT = PSem(es, nc, "tT" + str(li))
        tQ = Sem(es, nc, "tQ" + str(li), 16)
        qO = PSem(es, nc, "qO" + str(li))

        wqi, wki, wvi, woi = wq[li], wk[li], wv[li], wo[li]
        bqI, bkI, bvI, boI = bqi[li], bki[li], bvi[li], boi[li]
        gI, beI = gbi[li]

        ebar = phase_barrier()

        mw_g0 = cG.n
        @block.sync
        def _(s):
            s.wait_ge(cB.h, ebar)
            for i in range(12):
                if i >= 2:
                    s.wait_ge(cG.h, mw_g0 + i - 1)
                if i < 4:
                    dd = s.dma_start(out=wfs[i % 2][:, 0:512],
                                     in_=wqi[128 * i:128 * (i + 1), :])
                else:
                    hh = i - 4
                    dd = s.dma_start(out=wfs[i % 2][0:64, 0:512],
                                     in_=woi[64 * hh:64 * (hh + 1), :])
                qW.inc(dd, i)
            for c_ in range(4):
                s.dma_start(out=bqkc[:, c_, 0:1],
                            in_=bqI[128 * c_:128 * (c_ + 1)].unsqueeze(1)).then_inc(sQ.h, 16)
                s.dma_start(out=bqkc[:, c_, 1:2],
                            in_=bkI[128 * c_:128 * (c_ + 1)].unsqueeze(1)).then_inc(sQ.h, 16)
            sQ.bump(8)
        mw_q1 = sQ.n

        @block.gpsimd
        def _(g):
            g.wait_ge(cB.h, ebar)
            g.dma_start(out=browb[0:1, 0, :], in_=bvI.unsqueeze(0)).then_inc(sG.h, 16)
            g.dma_start(out=browb[0:1, 1, :], in_=boI.unsqueeze(0)).then_inc(sG.h, 16)
            g.dma_start(out=browb[0:1, 2, :], in_=gI.unsqueeze(0)).then_inc(sG.h, 16)
            g.dma_start(out=browb[0:1, 3, :], in_=beI.unsqueeze(0)).then_inc(sG.h, 16)
            sG.bump(4)
            for i in range(12):
                qW.wait(g, i, 16 * (i // 2 + 1))
                if i < 4:
                    cc = g.tensor_copy(wqb[:, i, :], wfs[i % 2][:, 0:512])
                else:
                    hh = i - 4
                    cc = g.tensor_copy(wob[0:64, hh, :], wfs[i % 2][0:64, 0:512])
                cG.inc(cc)
        mw_g1 = cG.n
        mw_sg1 = sG.n

        @block.vector
        def _(v):
            v.drain()
            v.wait_ge(cB.h, ebar)
            v.wait_ge(sQ.h, mw_q1)
            v.tensor_scalar(bqkc[:, :, 0:1], bqkc[:, :, 0:1], 0.125, None, ALU.mult)
            v.memset(V1[:, :, 64:520:65], 1.0)
            v.drain().then_inc(cV.h, 1)
            cV.bump()
        mw_v1 = cV.n

        @block.tensor
        def _(t):
            t.wait_ge(cB.h, ebar)
            t.wait_ge(sG.h, mw_sg1)
            t.matmul(PB[0][:, :], ones1b[0:1, 0:128], browb[0:1, 1, :], start=True, stop=True)
            t.matmul(PB[1][:, :], ones1b[0:1, 0:128], browb[0:1, 2, :], start=True, stop=True)
            mm = t.matmul(PB[2][:, :], ones1b[0:1, 0:128], browb[0:1, 3, :], start=True, stop=True)
            cP.inc(mm)
        mw_p1 = cP.n

        @block.vector
        def _(v):
            v.wait_ge(cP.h, mw_p1)
            v.tensor_copy(bcB, PB[0][:, :])
            v.tensor_copy(bcG, PB[1][:, :])
            cc = v.tensor_copy(bcE, PB[2][:, :])
            cV.inc(cc)
        mw_v2 = cV.n

        kv_p0, kv_v0, kv_a0, kv_g0 = cP.n, cV.n, cA.n, cG.n
        NVI = 5 if li == 0 else 4
        for ts in range(NSL):
            if li == 0:
                @block.vector
                def _(v):
                    v.drain()
                    if ts == 0:
                        v.wait_ge(cV.h, mw_v2)
                    if ts >= 1:
                        v.wait_ge(tT.s[0].h, 256 * ts)
                    for tc in range(4):
                        c = 4 * ts + tc
                        cc = v.tensor_scalar(tmsl[0][:, tc, :], XBF[:, c, 0:512],
                                             ln0[:, c, 0:1], ln0[:, c, 1:2],
                                             ALU.subtract, ALU.mult)
                    cV.inc(cc)
            else:
                @block.sync
                def _(s):
                    if ts >= 1:
                        s.wait_ge(tT.s[0].h, 256 * ts)
                    for tc in range(4):
                        r0 = 512 * ts + 128 * tc
                        qM.s[0].inc(s.dma_start(out=tmsl[0][:, tc, :],
                                                in_=xshsp[r0:r0 + 128, :]))

            @block.sync
            def _(s):
                for cin in range(4):
                    gidx = 4 * ts + cin
                    s.wait_ge(cG.h, kv_g0 + max(0, gidx - 1))
                    d1 = s.dma_start(out=wfs[gidx % 2][:, 0:512],
                                     in_=wki[128 * cin:128 * (cin + 1), :])
                    qK.inc(d1, gidx)
                    d2 = s.dma_start(out=wfs[gidx % 2][:, 512:1024],
                                     in_=wvi[128 * cin:128 * (cin + 1), :])
                    qK.inc(d2, gidx)

            @block.gpsimd
            def _(g):
                for cin in range(4):
                    gidx = 4 * ts + cin
                    qK.wait(g, gidx, 32 * (gidx // 2 + 1))
                    if gidx >= 2:
                        ts_, cin_ = (gidx - 2) // 4, (gidx - 2) % 4
                        g.wait_ge(cP.h, kv_p0 + 8 * ts_ + cin_ + 1)
                    cc = g.tensor_copy(wcb[gidx % 2], wfs[gidx % 2][:, 0:1024])
                    cG.inc(cc)

            @block.scalar
            def _(sc):
                if ts == 0:
                    sc.wait_ge(cB.h, ebar)
                if li == 0:
                    sc.wait_ge(cV.h, kv_v0 + NVI * ts + 1)
                else:
                    sc.wait_ge(qM.s[0].h, 64 * (ts + 1))
                if ts >= 1:
                    sc.wait_ge(cP.h, kv_p0 + 8 * ts)
                for tc in range(4):
                    for k in range(4):
                        tT.s[0].inc(sc.dma_start_transpose(
                            tsl[0][:, k, 128 * tc:128 * (tc + 1)],
                            tmsl[0][:, tc, 128 * k:128 * (k + 1)]))

            @block.tensor
            def _(t):
                t.wait_ge(tT.s[0].h, 256 * (ts + 1))
                if ts == 0:
                    t.wait_ge(cG.h, mw_g1)
                    t.wait_ge(cV.h, mw_v2)
                if ts >= 1:
                    t.wait_ge(cA.h, kv_a0 + 4 * ts)
                    t.wait_ge(cV.h, kv_v0 + NVI * ts)
                for cin in range(4):
                    gidx = 4 * ts + cin
                    t.wait_ge(cG.h, kv_g0 + gidx + 1)
                    for kc in range(4):
                        t.matmul(PSQ[:, 512 * kc:512 * (kc + 1)],
                                 wcb[gidx % 2][:, 128 * kc:128 * (kc + 1)],
                                 tsl[0][:, cin, :],
                                 start=(cin == 0), stop=(cin == 3))
                    for tc in range(4):
                        mm = t.matmul(PB[tc][:, :],
                                      tsl[0][:, cin, 128 * tc:128 * (tc + 1)],
                                      wcb[gidx % 2][:, 512:1024],
                                      start=(cin == 0), stop=False)
                    cP.inc(mm)
                for tc in range(4):
                    mm = t.matmul(PB[tc][:, :], ones1b[0:1, 0:128], browb[0:1, 0, :],
                                  start=False, stop=True)
                    cP.inc(mm)

            @block.scalar
            def _(sc):
                for kc in range(4):
                    sc.wait_ge(cP.h, kv_p0 + 8 * ts + 4)
                    a = sc.activation(K1T[:, kc, 512 * ts:512 * (ts + 1)],
                                      PSQ[:, 512 * kc:512 * (kc + 1)], AF.Identity,
                                      bias=bqkc[:, kc, 1:2])
                    cA.inc(a)

            @block.vector
            def _(v):
                v.drain()
                for tc in range(4):
                    v.wait_ge(cP.h, kv_p0 + 8 * ts + 4 + tc + 1)
                    vv = V1[:, 4 * ts + tc, 0:520].rearrange("p (h w) -> p h w", h=8, w=65)
                    cc = v.tensor_copy(vv[:, :, 0:64],
                                       PB[tc][:, :].rearrange("p (h w) -> p h w", h=8, w=64))
                    cV.inc(cc)
        kv_p1, kv_v1, kv_a1 = cP.n, cV.n, cA.n

        at_p0, at_v0, at_a0 = cP.n, cV.n, cA.n
        at_g0 = cG.n
        VQ0 = 1 if li == 0 else 0
        VQ = VQ0 + 48 + 8
        PQS = 184
        AQS = 140
        for qs in range(NSL):
            pq = at_p0 + PQS * qs
            vq = at_v0 + VQ * qs
            aqb = at_a0 + AQS * qs
            vas = vq + VQ0 + 48

            if li == 0:
                @block.vector
                def _(v):
                    v.drain()
                    if qs == 0:
                        v.wait_ge(tT.s[0].h, 256 * NSL)
                    if qs >= 1:
                        v.wait_ge(tQ.h, 256 * qs)
                    for tc in range(4):
                        c = 4 * qs + tc
                        cc = v.tensor_scalar(tmsl[0][:, tc, :], XBF[:, c, 0:512],
                                             ln0[:, c, 0:1], ln0[:, c, 1:2],
                                             ALU.subtract, ALU.mult)
                    cV.inc(cc)
            else:
                @block.sync
                def _(s):
                    s.wait_ge(cV.h, vq)
                    s.wait_ge(tQ.h, 256 * qs)
                    if qs == 0:
                        s.wait_ge(tT.s[0].h, 256 * NSL)
                    for tc in range(4):
                        r0 = 512 * qs + 128 * tc
                        qM.s[0].inc(s.dma_start(out=tmsl[0][:, tc, :],
                                                in_=xshsp[r0:r0 + 128, :]))

            @block.scalar
            def _(sc):
                if li == 0:
                    sc.wait_ge(cV.h, vq + 1)
                else:
                    sc.wait_ge(qM.s[0].h, 64 * (NSL + qs + 1))
                sc.wait_ge(cP.h, pq)
                for tc in range(4):
                    for k in range(4):
                        sc.dma_start_transpose(
                            tsl[0][:, k, 128 * tc:128 * (tc + 1)],
                            tmsl[0][:, tc, 128 * k:128 * (k + 1)]).then_inc(tQ.h, 16)
                        tQ.bump()

            @block.tensor
            def _(t):
                t.wait_ge(tQ.h, 256 * (qs + 1))
                if qs == 0:
                    t.wait_ge(cP.h, kv_p1)
                    t.wait_ge(cA.h, kv_a1)
                    t.wait_ge(cV.h, kv_v1)
                for h in range(H):
                    ph = pq + 23 * h
                    ah = aqb + 17 * h
                    vh = vq + VQ0 + 6 * h
                    bp = 64 * (h % 2)
                    hc = h // 2
                    for cin in range(4):
                        mm = t.matmul(PB[1][bp:bp + 64, :],
                                      wqb[:, cin, 64 * h:64 * h + 64],
                                      tsl[0][:, cin, :],
                                      start=(cin == 0), stop=(cin == 3))
                    cP.inc(mm)
                    for d2 in range(16):
                        dp = d2 % 2
                        if d2 >= 2:
                            t.wait_ge(cA.h, ah + d2)
                        if d2 == 0:
                            t.wait_ge(cA.h, ah + 1)
                        for j in range(2):
                            kt = 2 * d2 + j
                            mm = t.matmul(PSQ[:, 1024 * dp + 512 * j:1024 * dp + 512 * (j + 1)],
                                          K1T[bp:bp + 64, hc, 128 * kt:128 * (kt + 1)],
                                          qtt[bp:bp + 64, :], start=True, stop=True)
                        if d2 >= 1:
                            t.wait_ge(cA.h, ah + 2 + (d2 - 1))
                            if d2 == 1:
                                if h >= 1:
                                    t.wait_ge(cV.h, vh - 6 + 1)
                                elif qs >= 1:
                                    t.wait_ge(cV.h, vq - VQ + VQ0 + 6 * 7 + 1)
                            for j in range(2):
                                kt = 2 * (d2 - 1) + j
                                mm = t.matmul(PB[0][0:65, :], V1[:, kt, 65 * h:65 * h + 65],
                                              aq[(d2 - 1) % 2][:, 512 * j:512 * (j + 1)],
                                              start=(kt == 0), stop=False)
                        cP.inc(mm)
                    t.wait_ge(cA.h, ah + 2 + 15)
                    for j in range(2):
                        kt = 30 + j
                        mm = t.matmul(PB[0][0:65, :], V1[:, kt, 65 * h:65 * h + 65],
                                      aq[1][:, 512 * j:512 * (j + 1)],
                                      start=False, stop=(j == 1))
                    cP.inc(mm)
                    t.wait_ge(cV.h, vh + 1)
                    for tc in range(4):
                        mm = t.transpose(PB[2][:, tc:tc + 1],
                                         denA[64:65, 128 * tc:128 * (tc + 1)],
                                         ident[64:65, 64:65])
                    cP.inc(mm)
                    for tc in range(4):
                        if tc == 1:
                            t.wait_ge(cV.h, vh + 2)
                        elif tc >= 2:
                            t.wait_ge(cV.h, vh + 1 + tc)
                        bank = PB[3] if tc % 2 == 0 else PB[2]
                        mm = t.matmul(bank[:, :], ohb[0:64, 128 * tc:128 * (tc + 1)],
                                      wob[0:64, h, :], start=True, stop=True)
                        cP.inc(mm)

            @block.scalar
            def _(sc):
                for h in range(H):
                    ph = pq + 23 * h
                    ah = aqb + 17 * h
                    bp = 64 * (h % 2)
                    sc.wait_ge(cP.h, ph + 1)
                    a = sc.activation(qtt[bp:bp + 64, :], PB[1][bp:bp + 64, :], AF.Identity,
                                      bias=bqkc[bp:bp + 64, h // 2, 0:1], scale=0.125)
                    cA.inc(a)
                    for d2 in range(16):
                        dp = d2 % 2
                        sc.wait_ge(cP.h, ph + 2 + d2)
                        a = sc.activation(aq[dp], PSQ[:, 1024 * dp:1024 * (dp + 1)],
                                          AF.Exp)
                        cA.inc(a)
                for tc in range(4):
                    sc.wait_ge(cV.h, vas + 2 * tc + 1)
                    a = sc.activation(stat[:, 16:17], stat[:, 13:14], AF.Sqrt,
                                      bias=epscol[:, :])
                    cA.inc(a)

            @block.vector
            def _(v):
                v.drain()
                for h in range(H):
                    ph = pq + 23 * h
                    vh = vq + VQ0 + 6 * h
                    v.wait_ge(cP.h, ph + 18)
                    v.tensor_copy(ohb[0:64, :], PB[0][0:64, :])
                    cc = v.tensor_copy(denA[64:65, :], PB[0][64:65, :])
                    cV.inc(cc)
                    v.wait_ge(cP.h, ph + 19)
                    cc = v.reciprocal(rcq[:, 0:4], PB[2][:, 0:4])
                    cV.inc(cc)
                    v.drain()
                    for tc in range(4):
                        v.wait_ge(cP.h, ph + 20 + tc)
                        bank = PB[3] if tc % 2 == 0 else PB[2]
                        if h == 0:
                            cc = v.tensor_scalar(oacc[:, tc, :], bank[:, :],
                                                 rcq[:, tc:tc + 1], None, ALU.mult)
                        else:
                            cc = v.scalar_tensor_tensor(oacc[:, tc, :], bank[:, :],
                                                        rcq[:, tc:tc + 1], oacc[:, tc, :],
                                                        ALU.mult, ALU.add)
                        cV.inc(cc)
                for tc in range(4):
                    i = 4 * qs + tc
                    if i >= 2:
                        qO.wait(v, i, 16 * ((i - 2) // 2 + 1))
                    v.drain()
                    v.tensor_tensor(sfp, tmsl[0][:, tc, :], oacc[:, tc, :], ALU.add)
                    v.drain()
                    v.tensor_tensor(sfp, sfp, bcB, ALU.add)
                    v.drain()
                    v.bn_stats(stat[:, 0:6], sfp)
                    v.drain()
                    v.bn_aggr(stat[:, 12:14], stat[:, 0:6])
                    v.drain().then_inc(cV.h, 1)
                    cV.bump()
                    v.wait_ge(cA.h, aqb + 136 + tc + 1)
                    v.reciprocal(stat[:, 17:18], stat[:, 16:17])
                    v.drain()
                    v.tensor_scalar(sfp, sfp, stat[:, 12:13], stat[:, 17:18],
                                    ALU.subtract, ALU.mult)
                    v.drain()
                    v.tensor_tensor(sfp, sfp, bcG, ALU.mult)
                    v.drain()
                    cc = v.tensor_tensor(stg[tc % 2], sfp, bcE, ALU.add)
                    cV.inc(cc)

            @block.sync
            def _(s):
                for tc in range(4):
                    i = 4 * qs + tc
                    s.wait_ge(cV.h, vas + 2 * tc + 2)
                    r0 = 512 * qs + 128 * tc
                    qO.inc(s.dma_start(out=dst_sp[r0:r0 + 128, :], in_=stg[tc % 2]), i)

        at_p1, at_v1, at_a1 = cP.n, cV.n, cA.n
        assert at_p1 == at_p0 + PQS * NSL, (at_p1, at_p0)
        assert at_v1 == at_v0 + VQ * NSL, (at_v1, at_v0)
        assert at_a1 == at_a0 + AQS * NSL, (at_a1, at_a0)
        bfa.pop()
        fpa.pop()
        return qO

    def _stopped(name):
        return stop_after is not None and name == stop_after

    qO1 = emit_mha(0, x1sp)
    if not _stopped("mha0"):
        qO2 = emit_mha(1, x2sp)


    def emit_ffn0():
        bfa.push()
        fpa.push()
        ytm = bfa.alloc([128, 8, 512])
        ytr = bfa.alloc([128, 8, 512])
        hsl = bfa.alloc([128, 32, 512])
        w1b = [bfa.alloc([128, 512]) for _ in range(2)]
        w2b = [bfa.alloc([128, 1024]) for _ in range(2)]
        brw = bfa.alloc([1, 3, 1024])
        bf1c = fpa.alloc([128, 32, 1])
        w1f = [fpa.alloc([128, 512]) for _ in range(2)]
        w2f = [fpa.alloc([128, 1024]) for _ in range(2)]
        bcG2 = fpa.alloc([128, 1024])
        bcE2 = fpa.alloc([128, 1024])
        sfp = fpa.alloc([128, 1024])
        stat = fpa.alloc([128, 20])

        qY = PSem(es, nc, "qY")
        qW1 = PSem(es, nc, "qF1")
        qW2 = PSem(es, nc, "qF2")
        tY = Sem(es, nc, "tY", 16)

        ebar = phase_barrier()

        @block.gpsimd
        def _(g):
            g.wait_ge(cB.h, ebar)
            g.dma_start(out=brw[0:1, 0, :], in_=bff2_in.unsqueeze(0)).then_inc(sG.h, 16)
            g.dma_start(out=brw[0:1, 1, :], in_=gff_in.unsqueeze(0)).then_inc(sG.h, 16)
            g.dma_start(out=brw[0:1, 2, :], in_=beff_in.unsqueeze(0)).then_inc(sG.h, 16)
            sG.bump(3)
        f_sg1 = sG.n

        @block.sync
        def _(s):
            s.wait_ge(cB.h, ebar)
            for c in range(32):
                s.dma_start(out=bf1c[:, c, :],
                            in_=bff1_in[128 * c:128 * (c + 1)].unsqueeze(1)).then_inc(sQ.h, 16)
            sQ.bump(32)
        f_q1 = sQ.n

        @block.tensor
        def _(t):
            t.wait_ge(cB.h, ebar)
            t.wait_ge(sG.h, f_sg1)
            t.matmul(PB[0][:, :], ones1b[0:1, 0:128], brw[0:1, 1, 0:512], start=True, stop=True)
            t.matmul(PB[1][:, :], ones1b[0:1, 0:128], brw[0:1, 1, 512:1024], start=True, stop=True)
            t.matmul(PB[2][:, :], ones1b[0:1, 0:128], brw[0:1, 2, 0:512], start=True, stop=True)
            mm = t.matmul(PB[3][:, :], ones1b[0:1, 0:128], brw[0:1, 2, 512:1024], start=True, stop=True)
            cP.inc(mm)
        f_p1 = cP.n

        @block.vector
        def _(v):
            v.drain()
            v.wait_ge(cP.h, f_p1)
            v.tensor_copy(bcG2[:, 0:512], PB[0][:, :])
            v.tensor_copy(bcG2[:, 512:1024], PB[1][:, :])
            v.tensor_copy(bcE2[:, 0:512], PB[2][:, :])
            cc = v.tensor_copy(bcE2[:, 512:1024], PB[3][:, :])
            cV.inc(cc)
        f_v1 = cV.n

        TP, TV, TA, TG = 97, 40, 4, 96
        f_p2, f_v2, f_a2, f_g2 = cP.n, cV.n, cA.n, cG.n
        for tsb in range(NSL):
            tp = f_p2 + TP * tsb
            tv = f_v2 + TV * tsb
            ta = f_a2 + TA * tsb
            tg = f_g2 + TG * tsb

            @block.sync
            def _(s):
                if tsb == 0:
                    s.wait_ge(cV.h, f_v1)
                else:
                    s.wait_ge(tY.h, 512 * tsb)
                    s.wait_ge(cV.h, f_v2 + TV * tsb)
                for tc in range(4):
                    r0 = 512 * tsb + 128 * tc
                    qY.inc(s.dma_start(out=ytm[:, tc, :], in_=x1sp[r0:r0 + 128, :]), tsb)
                    qY.inc(s.dma_start(out=ytm[:, 4 + tc, :], in_=x2sp[r0:r0 + 128, :]), tsb)
                for hg in range(8):
                    for cin in range(8):
                        gg = 64 * tsb + 8 * hg + cin
                        if gg >= 2:
                            gge = gg - 2
                            s.wait_ge(cG.h, f_g2 + TG * (gge // 64) + (gge % 64) + 1)
                        qW1.inc(s.dma_start(
                            out=w1f[gg % 2],
                            in_=wff1_in[128 * cin:128 * (cin + 1), 512 * hg:512 * (hg + 1)]), gg)
                for hc in range(32):
                    g2 = 32 * tsb + hc
                    if g2 >= 2:
                        g2e = g2 - 2
                        s.wait_ge(cG.h, f_g2 + TG * (g2e // 32) + 64 + (g2e % 32) + 1)
                    qW2.inc(s.dma_start(out=w2f[g2 % 2],
                                        in_=wff2_in[128 * hc:128 * (hc + 1), :]), g2)

            @block.gpsimd
            def _(g):
                for hg in range(8):
                    for cin in range(8):
                        gg = 64 * tsb + 8 * hg + cin
                        qW1.wait(g, gg, 16 * (gg // 2 + 1))
                        if gg >= 2:
                            gge = gg - 2
                            g.wait_ge(cP.h, f_p2 + TP * (gge // 64) + (gge % 64) + 1)
                        cc = g.tensor_copy(w1b[gg % 2], w1f[gg % 2])
                        cG.inc(cc)
                for hc in range(32):
                    g2 = 32 * tsb + hc
                    qW2.wait(g, g2, 16 * (g2 // 2 + 1))
                    if g2 >= 2:
                        g2e = g2 - 2
                        g.wait_ge(cP.h, f_p2 + TP * (g2e // 32) + 64 + (g2e % 32) + 1)
                    cc = g.tensor_copy(w2b[g2 % 2], w2f[g2 % 2])
                    cG.inc(cc)

            @block.scalar
            def _(sc):
                qY.wait(sc, tsb, 128 * (tsb // 2 + 1))
                sc.wait_ge(cP.h, f_p2 + TP * tsb)
                for tc in range(4):
                    for k in range(4):
                        sc.dma_start_transpose(
                            ytr[:, k, 128 * tc:128 * (tc + 1)],
                            ytm[:, tc, 128 * k:128 * (k + 1)]).then_inc(tY.h, 16)
                        tY.bump()
                    for k in range(4):
                        sc.dma_start_transpose(
                            ytr[:, 4 + k, 128 * tc:128 * (tc + 1)],
                            ytm[:, 4 + tc, 128 * k:128 * (k + 1)]).then_inc(tY.h, 16)
                        tY.bump()
                for tc in range(4):
                    sc.wait_ge(cV.h, tv + 32 + 2 * tc + 1)
                    a = sc.activation(stat[:, 16:17], stat[:, 13:14], AF.Sqrt,
                                      bias=epscol[:, :])
                    cA.inc(a)

            @block.tensor
            def _(t):
                t.wait_ge(tY.h, 512 * (tsb + 1))
                if tsb >= 1:
                    t.wait_ge(cV.h, f_v2 + TV * tsb)
                for hg in range(8):
                    if hg >= 1:
                        t.wait_ge(cV.h, tv + 4 * hg)
                    for cin in range(8):
                        gg = 64 * tsb + 8 * hg + cin
                        t.wait_ge(cG.h, f_g2 + TG * tsb + 8 * hg + cin + 1)
                        for j in range(4):
                            mm = t.matmul(PSQ[:, 512 * j:512 * (j + 1)],
                                          w1b[gg % 2][:, 128 * j:128 * (j + 1)],
                                          ytr[:, cin, :],
                                          start=(cin == 0), stop=(cin == 7))
                        cP.inc(mm)
                t.wait_ge(cV.h, tv + 32)
                for hc in range(32):
                    t.wait_ge(cG.h, f_g2 + TG * tsb + 64 + hc + 1)
                    for tc in range(4):
                        for ch in range(2):
                            bank = PB[2 * (tc % 2) + ch] if tc < 2 else PSQ[:, 512 * (2 * (tc - 2) + ch):512 * (2 * (tc - 2) + ch + 1)]
                            dst = bank[:, :] if tc < 2 else bank
                            mm = t.matmul(dst, hsl[:, hc, 128 * tc:128 * (tc + 1)],
                                          w2b[hc % 2][:, 512 * ch:512 * (ch + 1)],
                                          start=(hc == 0), stop=False)
                    cP.inc(mm)
                for tc in range(4):
                    for ch in range(2):
                        bank = PB[2 * (tc % 2) + ch] if tc < 2 else PSQ[:, 512 * (2 * (tc - 2) + ch):512 * (2 * (tc - 2) + ch + 1)]
                        dst = bank[:, :] if tc < 2 else bank
                        mm = t.matmul(dst, ones1b[0:1, 0:128],
                                      brw[0:1, 0, 512 * ch:512 * (ch + 1)],
                                      start=False, stop=True)
                cP.inc(mm)

            @block.vector
            def _(v):
                v.drain()
                if tsb == 0:
                    v.wait_ge(sQ.h, f_q1)
                for hg in range(8):
                    for j in range(4):
                        v.wait_ge(cP.h, tp + 8 * hg + 8)
                        cc = v.tensor_scalar(hsl[:, 4 * hg + j, :], PSQ[:, 512 * j:512 * (j + 1)],
                                             bf1c[:, 4 * hg + j, :], 0.0, ALU.add, ALU.max)
                        cV.inc(cc)
                v.wait_ge(cP.h, tp + TP)
                for tc in range(4):
                    if tc < 2:
                        b0, b1 = PB[2 * tc][:, :], PB[2 * tc + 1][:, :]
                    else:
                        b0 = PSQ[:, 512 * (2 * (tc - 2)):512 * (2 * (tc - 2) + 1)]
                        b1 = PSQ[:, 512 * (2 * (tc - 2) + 1):512 * (2 * (tc - 2) + 2)]
                    v.drain()
                    v.tensor_tensor(sfp[:, 0:512], ytm[:, tc, :], b0, ALU.add)
                    v.tensor_tensor(sfp[:, 512:1024], ytm[:, 4 + tc, :], b1, ALU.add)
                    v.drain()
                    v.bn_stats(stat[:, 0:6], sfp[:, 0:512])
                    v.bn_stats(stat[:, 6:12], sfp[:, 512:1024])
                    v.drain()
                    v.bn_aggr(stat[:, 12:14], stat[:, 0:12])
                    v.drain().then_inc(cV.h, 1)
                    cV.bump()
                    v.wait_ge(cA.h, ta + tc + 1)
                    v.reciprocal(stat[:, 17:18], stat[:, 16:17])
                    v.drain()
                    v.tensor_scalar(sfp, sfp, stat[:, 12:13], stat[:, 17:18],
                                    ALU.subtract, ALU.mult)
                    v.drain()
                    v.tensor_tensor(sfp, sfp, bcG2, ALU.mult)
                    v.drain()
                    v.tensor_tensor(sfp, sfp, bcE2, ALU.add)
                    v.drain()
                    cc = v.tensor_tensor(XBF[:, 4 * tsb + tc, :], XBF[:, 4 * tsb + tc, :],
                                         sfp, ALU.add)
                    cV.inc(cc)
        bfa.pop()
        fpa.pop()
        assert cP.n == f_p2 + TP * NSL
        assert cV.n == f_v2 + TV * NSL
        assert cA.n == f_a2 + TA * NSL

    if stop_after not in ("mha0", "mha1"):
        emit_ffn0()

    def emit_rwkv():
        bfa.push()
        fpa.push()
        H1T = bfa.alloc([128, 8, S])
        slA = bfa.alloc([128, 8, 512])
        slC = bfa.alloc([128, 8, 512])
        slD = bfa.alloc([128, 8, 512])
        h1tm = [bfa.alloc([128, 1024]) for _ in range(2)]
        wchb = [bfa.alloc([128, 1024]) for _ in range(2)]
        otm = [bfa.alloc([128, 1024]) for _ in range(2)]
        wchf = [fpa.alloc([128, 1024]) for _ in range(2)]
        muc = fpa.alloc([128, 8, 4])
        sfp = fpa.alloc([128, 1024])
        stat = fpa.alloc([128, 20])

        qWc = PSem(es, nc, "qWc")
        tH = Sem(es, nc, "tH", 16)
        tO = PSem(es, nc, "tO")

        ebar = phase_barrier(taps=[("xbfA5", bfa.at(0, [128, 32768]))])

        @block.sync
        def _(s):
            s.wait_ge(cB.h, ebar)
            for c in range(8):
                s.dma_start(out=muc[:, c, 0:1],
                            in_=mur_in[128 * c:128 * (c + 1)].unsqueeze(1)).then_inc(sQ.h, 16)
                s.dma_start(out=muc[:, c, 2:3],
                            in_=muk_in[128 * c:128 * (c + 1)].unsqueeze(1)).then_inc(sQ.h, 16)
            sQ.bump(16)
        r_q1 = sQ.n

        @block.vector
        def _(v):
            v.drain()
            v.wait_ge(cB.h, ebar)
            v.wait_ge(sQ.h, r_q1)
            v.tensor_scalar(muc[:, :, 1:2], muc[:, :, 0:1], -1.0, 1.0, ALU.mult, ALU.add)
            cc = v.tensor_scalar(muc[:, :, 3:4], muc[:, :, 2:3], -1.0, 1.0, ALU.mult, ALU.add)
            cV.inc(cc)
        r_v1 = cV.n

        r_v2, r_a2, r_g2 = cV.n, cA.n, cG.n
        @block.vector
        def _(v):
            v.drain()
            for c in range(NCH):
                if c >= 2:
                    v.wait_ge(tH.h, 128 * (c - 1))
                v.bn_stats(stat[:, 0:6], XBF[:, c, 0:512])
                v.bn_stats(stat[:, 6:12], XBF[:, c, 512:1024])
                v.drain()
                v.bn_aggr(stat[:, 12:14], stat[:, 0:12])
                v.drain().then_inc(cV.h, 1)
                cV.bump()
                v.wait_ge(cA.h, r_a2 + c + 1)
                v.reciprocal(stat[:, 17:18], stat[:, 16:17])
                v.drain()
                cc = v.tensor_scalar(h1tm[c % 2], XBF[:, c, :], stat[:, 12:13],
                                     stat[:, 17:18], ALU.subtract, ALU.mult)
                cV.inc(cc)
        r_v3 = cV.n

        @block.scalar
        def _(sc):
            for c in range(NCH):
                sc.wait_ge(cV.h, r_v2 + 2 * c + 1)
                a = sc.activation(stat[:, 16:17], stat[:, 13:14], AF.Sqrt,
                                  bias=epscol[:, :])
                cA.inc(a)
                if c >= 1:
                    sc.wait_ge(cV.h, r_v2 + 2 * c)
                    for k in range(8):
                        sc.dma_start_transpose(
                            H1T[:, k, 128 * (c - 1):128 * c],
                            h1tm[(c - 1) % 2][:, 128 * k:128 * (k + 1)]).then_inc(tH.h, 16)
                        tH.bump()
            sc.wait_ge(cV.h, r_v2 + 2 * NCH)
            for k in range(8):
                sc.dma_start_transpose(
                    H1T[:, k, 128 * (NCH - 1):128 * NCH],
                    h1tm[(NCH - 1) % 2][:, 128 * k:128 * (k + 1)]).then_inc(tH.h, 16)
                tH.bump()
        r_th = tH.n

        TPr = 24
        NVX = 22
        r_p4, r_v4, r_a4, r_g4 = cP.n, cV.n, cA.n, cG.n
        for ts in range(NSL):
            tp = r_p4 + TPr * ts
            tv = r_v4 + NVX * ts
            ta = r_a4 + 16 * ts

            @block.sync
            def _(s):
                for i3 in range(24):
                    gg = 24 * ts + i3
                    if gg >= 2:
                        s.wait_ge(cG.h, r_g4 + gg - 1)
                    w_ = (wr_in, wkk_in, wvv_in)[i3 // 8]
                    cin = i3 % 8
                    qWc.inc(s.dma_start(out=wchf[gg % 2],
                                        in_=w_[128 * cin:128 * (cin + 1), :]), gg)

            @block.gpsimd
            def _(g):
                for i3 in range(24):
                    gg = 24 * ts + i3
                    qWc.wait(g, gg, 16 * (gg // 2 + 1))
                    if gg >= 2:
                        gge = gg - 2
                        g.wait_ge(cP.h, r_p4 + TPr * (gge // 24) + (gge % 24) + 1)
                    cc = g.tensor_copy(wchb[gg % 2], wchf[gg % 2])
                    cG.inc(cc)

            def mix(v, mu_i, mu1_i):
                for c in range(8):
                    if ts == 0:
                        v.tensor_scalar(slA[:, c, :], H1T[:, c, 0:512],
                                        muc[:, c, mu_i:mu_i + 1], None, ALU.mult)
                        v.drain()
                        cc = v.scalar_tensor_tensor(slA[:, c, 1:512], H1T[:, c, 0:511],
                                                    muc[:, c, mu1_i:mu1_i + 1],
                                                    slA[:, c, 1:512], ALU.mult, ALU.add)
                    else:
                        v.tensor_scalar(slA[:, c, :], H1T[:, c, 512 * ts:512 * (ts + 1)],
                                        muc[:, c, mu_i:mu_i + 1], None, ALU.mult)
                        v.drain()
                        cc = v.scalar_tensor_tensor(slA[:, c, :],
                                                    H1T[:, c, 512 * ts - 1:512 * (ts + 1) - 1],
                                                    muc[:, c, mu1_i:mu1_i + 1],
                                                    slA[:, c, :], ALU.mult, ALU.add)
                v.drain().then_inc(cV.h, 1)
                cV.bump()

            @block.vector
            def _(v):
                v.drain()
                if ts == 0:
                    v.wait_ge(tH.h, r_th)
                    v.wait_ge(cV.h, r_v1)
                else:
                    v.wait_ge(cP.h, tp)
                    v.wait_ge(tO.s[0].h, 256 * ts)
                    v.wait_ge(tO.s[1].h, 256 * ts)
                mix(v, 0, 1)
                v.wait_ge(cP.h, tp + 8)
                mix(v, 2, 3)
                for co in range(8):
                    v.wait_ge(cP.h, tp + 16)
                    bank = PB[co][:, :] if co < 4 else PSQ[:, 512 * (co - 4):512 * (co - 3)]
                    cc = v.tensor_scalar(slC[:, co, :], bank, 0.0, None, ALU.max)
                    cV.inc(cc)
                v.wait_ge(cP.h, tp + 24)
                v.wait_ge(cA.h, ta + 16)
                for co in range(8):
                    bank = PB[co][:, :] if co < 4 else PSQ[:, 512 * (co - 4):512 * (co - 3)]
                    cc = v.tensor_tensor(slA[:, co, :], slD[:, co, :], bank, ALU.mult)
                    cV.inc(cc)
                for tc in range(4):
                    v.wait_ge(tO.s[tc % 2].h, 256 * ts + 128 * (tc // 2 + 1))
                    v.drain()
                    cc = v.tensor_tensor(XBF[:, 4 * ts + tc, :], XBF[:, 4 * ts + tc, :],
                                         otm[tc % 2], ALU.add)
                    cV.inc(cc)

            @block.tensor
            def _(t):
                if ts == 0:
                    t.wait_ge(tH.h, r_th)
                for ph in range(3):
                    if ph == 0:
                        t.wait_ge(cV.h, tv + 1)
                    elif ph == 1:
                        t.wait_ge(cV.h, tv + 2)
                        t.wait_ge(cA.h, ta + 8)
                    else:
                        t.wait_ge(cV.h, tv + 10)
                        t.wait_ge(cA.h, ta + 16)
                    sl_ = slC if ph == 2 else slA
                    for cin in range(8):
                        gg = 24 * ts + 8 * ph + cin
                        t.wait_ge(cG.h, r_g4 + gg + 1)
                        for co in range(8):
                            bank = PB[co][:, :] if co < 4 else PSQ[:, 512 * (co - 4):512 * (co - 3)]
                            mm = t.matmul(bank, wchb[gg % 2][:, 128 * co:128 * (co + 1)],
                                          sl_[:, cin, :], start=(cin == 0), stop=(cin == 7))
                        cP.inc(mm)

            @block.scalar
            def _(sc):
                for co in range(8):
                    sc.wait_ge(cP.h, tp + 8)
                    bank = PB[co][:, :] if co < 4 else PSQ[:, 512 * (co - 4):512 * (co - 3)]
                    a = sc.activation(slD[:, co, :], bank, AF.Sigmoid)
                    cA.inc(a)
                for co in range(8):
                    sc.wait_ge(cV.h, tv + 2 + co + 1)
                    a = sc.activation(slC[:, co, :], slC[:, co, :], AF.Square)
                    cA.inc(a)
                for tc in range(4):
                    sc.wait_ge(cV.h, tv + 18)
                    if tc >= 2:
                        sc.wait_ge(cV.h, tv + 18 + tc - 1)
                    for k in range(8):
                        sc.dma_start_transpose(
                            otm[tc % 2][:, 128 * k:128 * (k + 1)],
                            slA[:, k, 128 * tc:128 * (tc + 1)]).then_inc(tO.s[tc % 2].h, 16)
                        tO.s[tc % 2].bump()
        bfa.pop()
        fpa.pop()

    if stop_after not in ("mha0", "mha1", "ffn0"):
        emit_rwkv()


    def emit_fft():
        bfa.push()
        fpa.push()
        Cb = bfa.alloc([128, 8, 1024])
        Sb = bfa.alloc([128, 8, 1024])
        h2tm = [bfa.alloc([128, 1024]) for _ in range(2)]
        h2T = bfa.alloc([128, 8, 512])
        asl = bfa.alloc([128, 8, 512])
        bsl = bfa.alloc([128, 8, 512])
        x1T = bfa.alloc([128, 8, 512])
        o2T = bfa.alloc([128, 8, 512])
        o2tm = [bfa.alloc([128, 1024]) for _ in range(2)]
        h3st = bfa.alloc([128, 4, 1024])
        bitsT = bfa.alloc([128, 512])
        synb = bfa.alloc([128, 512])
        corT = bfa.alloc([128, 512])
        fixb = bfa.alloc([1, 512])
        welb = bfa.alloc([128, 8, 24])
        wleb = bfa.alloc([128, 1024])
        hmtb = bfa.alloc([128, 12])
        xrg = bfa.alloc([128, 512])
        wgrb = bfa.alloc([128, 8, 16])
        wgtb = bfa.alloc([128, 8])
        zrow = fpa.alloc([128, 24])
        ccol = fpa.alloc([128, 8, 8])
        belc = fpa.alloc([128, 1])
        blec = fpa.alloc([128, 8, 1])
        rnrm = fpa.alloc([128, 1])
        logf = fpa.alloc([128, 512])
        x3f = fpa.alloc([128, 1024])
        mint = fpa.alloc([128, 512])
        stat = fpa.alloc([128, 20])
        pi2c = fpa.alloc([128, 1])
        wgmf = fpa.alloc([128, 16])
        mi32 = es.enter_context(nc.sbuf_tensor("mi32", [128, 512], mybir.dt.int32))
        krow = x3f[0:1, :]

        tH2 = PSem(es, nc, "tH2")
        tO2 = PSem(es, nc, "tO2")
        tH3 = PSem(es, nc, "tH3")
        qX3 = PSem(es, nc, "qX3")

        ebar = phase_barrier(taps=[("xbfB", bfa.at(0, [128, 32768]))])

        @block.gpsimd
        def _(g):
            g.wait_ge(cB.h, ebar)
            g.iota(krow[0:1, :], pattern=[[1, 1024]], base=0, channel_multiplier=0,
                   allow_small_or_imprecise_dtypes=True)
            for c in range(8):
                cc = g.iota(ccol[:, c, 6:7], pattern=[[0, 1]], base=128 * c,
                            channel_multiplier=1,
                            allow_small_or_imprecise_dtypes=True)
            cc.then_inc(cG.h, 1)
            cG.bump()
            g.dma_start(out=welb[:, :, :],
                        in_=wel_in.rearrange("(a p) o -> p a o", p=128)).then_inc(sG.h, 16)
            g.dma_start(out=wleb[0:24, :], in_=wle_in[:, :]).then_inc(sG.h, 16)
            g.dma_start(out=hmtb[0:24, :], in_=hmt_in[:, :]).then_inc(sG.h, 16)
            g.dma_start(out=wgrb[:, :, :],
                        in_=wgred_in.rearrange("(a p) o -> p a o", p=128)).then_inc(sG.h, 16)
            g.dma_start(out=wgtb[0:16, :], in_=wgt_in[:, :]).then_inc(sG.h, 16)
            sG.bump(5)
        c_g1 = cG.n
        c_sg1 = sG.n

        @block.sync
        def _(s):
            s.wait_ge(cB.h, ebar)
            s.dma_start(out=belc[0:24, :], in_=bel_in.unsqueeze(1)).then_inc(sQ.h, 16)
            for c in range(8):
                s.dma_start(out=blec[:, c, :],
                            in_=ble_in[128 * c:128 * (c + 1)].unsqueeze(1)).then_inc(sQ.h, 16)
            s.dma_start(out=wgmf[0:8, :], in_=wgm_in[:, :]).then_inc(sQ.h, 16)
            sQ.bump(10)
        c_q1 = sQ.n

        @block.vector
        def _(v):
            v.drain()
            v.wait_ge(cG.h, c_g1)
            v.wait_ge(sQ.h, c_q1)
            v.memset(pi2c[:, :], float(np.pi / 2))
            v.drain()
            v.tensor_scalar(pi2c[:, :], pi2c[:, :],
                            float(np.float64(np.pi / 2) - np.float64(np.float32(np.pi / 2))),
                            None, ALU.add)
            v.drain()
            for c in range(8):
                v.tensor_scalar(ccol[:, c, 1:2], ccol[:, c, 6:7], -1.0, 1024.0,
                                ALU.mult, ALU.add)
                v.drain()
                v.tensor_tensor(ccol[:, c, 0:1], ccol[:, c, 6:7], ccol[:, c, 1:2], ALU.min)
                v.drain()
                v.tensor_scalar(ccol[:, c, 0:1], ccol[:, c, 0:1], 1.0 / 1024.0, 1e-08,
                                ALU.mult, ALU.add)
            v.drain().then_inc(cV.h, 1)
            cV.bump()
        c_v1 = cV.n

        @block.scalar
        def _(sc):
            sc.wait_ge(cV.h, c_v1)
            for c in range(8):
                a = sc.activation(ccol[:, c, 2:3], ccol[:, c, 0:1], AF.Square)
                cA.inc(a)
        c_a1 = cA.n

        @block.vector
        def _(v):
            v.wait_ge(cA.h, c_a1)
            OD = (1.0 + 0.2 * 1.0) * 1.0
            for c in range(8):
                v.tensor_scalar(ccol[:, c, 3:4], ccol[:, c, 0:1], 1.5, OD,
                                ALU.mult, ALU.add)
                v.tensor_scalar(ccol[:, c, 4:5], ccol[:, c, 2:3], 0.1, OD,
                                ALU.mult, ALU.add)
                v.drain()
                v.tensor_tensor(ccol[:, c, 4:5], ccol[:, c, 4:5], ccol[:, c, 0:1],
                                ALU.subtract)
            v.drain().then_inc(cV.h, 1)
            cV.bump()
        c_v2 = cV.n

        @block.scalar
        def _(sc):
            sc.wait_ge(cV.h, c_v2)
            for c in range(8):
                sc.activation(ccol[:, c, 5:6], ccol[:, c, 3:4], AF.Sin)
                sc.drain()
                sc.activation(ccol[:, c, 3:4], ccol[:, c, 4:5], AF.Sin)
                sc.drain()
                a = sc.activation(ccol[:, c, 4:5], ccol[:, c, 4:5], AF.Sin,
                                  scale=-1.0, bias=pi2c[:, :])
                cA.inc(a)
                sc.drain()
        c_a2 = cA.n

        @block.vector
        def _(v):
            v.wait_ge(cA.h, c_a2)
            for c in range(8):
                v.tensor_tensor(ccol[:, c, 6:7], ccol[:, c, 5:6], ccol[:, c, 4:5], ALU.mult)
                v.tensor_tensor(ccol[:, c, 7:8], ccol[:, c, 5:6], ccol[:, c, 3:4], ALU.mult)
            v.drain()
            v.tensor_tensor(wgmf[0:8, :], wgmf[0:8, :], wgmf[0:8, :], ALU.mult)
            v.drain()
            v.reduce_sum(rnrm[0:8, :], wgmf[0:8, :], axis=AX.X)
            v.drain().then_inc(cV.h, 1)
            cV.bump()
        c_v3 = cV.n

        @block.scalar
        def _(sc):
            sc.wait_ge(cV.h, c_v3)
            a = sc.activation(rnrm[0:8, :], rnrm[0:8, :], AF.Sqrt)
            cA.inc(a)
        c_a3 = cA.n

        @block.vector
        def _(v):
            v.wait_ge(cA.h, c_a3)
            v.tensor_single_scalar(rnrm[0:8, :], rnrm[0:8, :], 1e-4, ALU.max)
            v.drain()
            cc = v.reciprocal(rnrm[0:8, :], rnrm[0:8, :])
            cV.inc(cc)
        c_v4 = cV.n

        c_p1 = cP.n
        @block.tensor
        def _(t):
            t.wait_ge(cG.h, c_g1)
            for i in range(16):
                kc, ns = i // 2, i % 2
                if i >= 2:
                    t.wait_ge(cA.h, c_a3 + 2 * (i - 1))
                mm = t.matmul(PB[i % 2][:, :], krow[0:1, 128 * kc:128 * (kc + 1)],
                              krow[0:1, 512 * ns:512 * (ns + 1)],
                              start=True, stop=True)
                cP.inc(mm)

        @block.vector
        def _(v):
            v.drain()
            for i in range(16):
                v.wait_ge(cP.h, c_p1 + i + 1)
                if i >= 1:
                    v.wait_ge(cA.h, c_a3 + 2 * i)
                v.tensor_copy(mi32[:, :], PB[i % 2][:, :])
                v.drain()
                v.tensor_single_scalar(mi32[:, :], mi32[:, :], 1023, ALU.bitwise_and)
                v.drain()
                v.tensor_copy(mint[:, :], mi32[:, :])
                v.drain()
                v.tensor_single_scalar(logf[:, :], mint[:, :], 511.5, ALU.is_gt)
                v.drain()
                v.scalar_tensor_tensor(mint[:, :], logf[:, :], -1024.0, mint[:, :],
                                       ALU.mult, ALU.add)
                v.drain()
                v.tensor_scalar(logf[:, :], mint[:, :], -1.0, None, ALU.mult)
                v.drain()
                cc = v.tensor_tensor(logf[:, :], logf[:, :], mint[:, :], ALU.max)
                cV.inc(cc)
        c_v5 = cV.n

        @block.scalar
        def _(sc):
            TWO_PI = float(2.0 * np.pi)
            for i in range(16):
                kc, ns = i // 2, i % 2
                sc.wait_ge(cV.h, c_v4 + i + 1)
                sc.activation(Sb[:, kc, 512 * ns:512 * (ns + 1)], mint[:, :], AF.Sin,
                              scale=TWO_PI / 1024.0)
                a = sc.activation(Cb[:, kc, 512 * ns:512 * (ns + 1)], logf[:, :], AF.Sin,
                                  scale=-TWO_PI / 1024.0, bias=pi2c[:, :])
                cA.inc(a, 2)
        c_a4 = cA.n


        latb = bfa.alloc([128, 512])
        zsc = fpa.alloc([128, 8])
        x3f2 = fpa.alloc([128, 1024])
        gsc = fpa.alloc([128, 8])
        gsc2 = fpa.alloc([128, 4])
        gsc3 = fpa.alloc([128, 16])

        @block.vector
        def _(v):
            v.drain()
            v.memset(zsc[:, :], 0.0)
            v.drain().then_inc(cV.h, 1)
            cV.bump()
        c_v5b = cV.n

        VS, AS_, PS_ = 66, 21, 42
        c_p2, c_v6, c_a5 = cP.n, cV.n, cA.n
        for ts2 in range(NSL):
            pv0 = c_p2 + PS_ * ts2
            vv0 = c_v6 + VS * ts2
            av0 = c_a5 + AS_ * ts2
            xfpair = [x3f, x3f2]

            @block.vector
            def _(v):
                v.drain()
                if ts2 == 0:
                    v.wait_ge(cV.h, c_v5b)
                    v.wait_ge(cA.h, c_a4)
                for tc in range(4):
                    if ts2 > 0 or tc >= 2:
                        nfull = 2 * ts2 + tc // 2
                        v.wait_ge(tH2.s[tc % 2].h, 128 * nfull)
                    c = 4 * ts2 + tc
                    v.bn_stats(stat[:, 0:6], XBF[:, c, 0:512])
                    v.bn_stats(stat[:, 6:12], XBF[:, c, 512:1024])
                    v.drain()
                    v.bn_aggr(stat[:, 12:14], stat[:, 0:12])
                    v.drain().then_inc(cV.h, 1)
                    cV.bump()
                    v.wait_ge(cA.h, av0 + tc + 1)
                    v.reciprocal(stat[:, 17:18], stat[:, 16:17])
                    v.drain()
                    cc = v.tensor_scalar(h2tm[tc % 2], XBF[:, c, :], stat[:, 12:13],
                                         stat[:, 17:18], ALU.subtract, ALU.mult)
                    cV.inc(cc)
                for c in range(8):
                    v.wait_ge(cP.h, pv0 + c + 1)
                    bank = PB[c][:, :] if c < 4 else PSQ[:, 512 * (c - 4):512 * (c - 3)]
                    cc = v.tensor_copy(asl[:, c, :], bank)
                    cV.inc(cc)
                for c in range(8):
                    v.wait_ge(cP.h, pv0 + 8 + c + 1)
                    bank = PB[c][:, :] if c < 4 else PSQ[:, 512 * (c - 4):512 * (c - 3)]
                    cc = v.tensor_copy(bsl[:, c, :], bank)
                    cV.inc(cc)
                if ts2 >= 1:
                    v.wait_ge(qX3.s[0].h, 32 * ts2)
                    v.wait_ge(qX3.s[1].h, 32 * ts2)
                for c in range(8):
                    v.drain()
                    v.tensor_scalar(x3f[:, 0:512],
                                    asl[:, c, :], ccol[:, c, 7:8], None, ALU.mult)
                    v.tensor_scalar(x3f[:, 512:1024], bsl[:, c, :], ccol[:, c, 7:8],
                                    None, ALU.mult)
                    v.drain()
                    v.scalar_tensor_tensor(asl[:, c, :], asl[:, c, :], ccol[:, c, 6:7],
                                           x3f[:, 512:1024], ALU.mult, ALU.add)
                    v.drain()
                    cc = v.scalar_tensor_tensor(bsl[:, c, :], bsl[:, c, :], ccol[:, c, 6:7],
                                                x3f[:, 0:512], ALU.mult, ALU.subtract)
                    cV.inc(cc)
                v.wait_ge(cA.h, av0 + 13)
                v.drain()
                cc = v.tensor_single_scalar(bitsT[0:24, :], latb[0:24, :], 0.0, ALU.is_gt)
                cV.inc(cc)
                v.wait_ge(cP.h, pv0 + 26)
                cc = v.tensor_single_scalar(synb[0:12, :], PB[1][0:12, :], 0.5, ALU.is_gt)
                cV.inc(cc)
                v.wait_ge(cP.h, pv0 + 27)
                v.drain()
                v.tensor_single_scalar(latb[0:1, :], PB[2][0:1, :], 0.5, ALU.is_gt)
                v.tensor_single_scalar(x3f[0:1, 0:512], PB[2][0:1, :], 3.5, ALU.is_lt)
                v.drain()
                cc = v.tensor_tensor(fixb[0:1, :], latb[0:1, :], x3f[0:1, 0:512], ALU.mult)
                cV.inc(cc)
                v.wait_ge(cP.h, pv0 + 28)
                v.drain()
                v.tensor_tensor(x3f[0:12, 0:512], bitsT[0:12, :], synb[0:12, :], ALU.mult)
                v.drain()
                v.scalar_tensor_tensor(x3f[0:12, 0:512], x3f[0:12, 0:512], -2.0,
                                       synb[0:12, :], ALU.mult, ALU.add)
                v.drain()
                v.tensor_tensor(synb[0:12, :], x3f[0:12, 0:512], PB[3][0:12, :], ALU.mult)
                v.drain()
                v.tensor_copy(corT[0:24, :], bitsT[0:24, :])
                v.drain()
                cc = v.tensor_tensor(corT[0:12, :], corT[0:12, :], synb[0:12, :], ALU.add)
                cV.inc(cc)
                for c in range(8):
                    v.wait_ge(cP.h, pv0 + 28 + c + 1)
                    bank = PB[c][:, :] if c < 4 else PSQ[:, 512 * (c - 4):512 * (c - 3)]
                    cc = v.scalar_tensor_tensor(o2T[:, c, :], bank, blec[:, c, 0:1],
                                                x1T[:, c, :], ALU.add, ALU.add)
                    cV.inc(cc)
                for tc in range(4):
                    i2 = 4 * ts2 + tc
                    v.wait_ge(tO2.s[tc % 2].h, 128 * (2 * ts2 + tc // 2 + 1))
                    if tc == 0 and ts2 >= 1:
                        v.wait_ge(tH3.s[(ts2 - 1) % 2].h,
                                  512 * ((ts2 - 1) // 2 + 1))
                    if i2 >= 2:
                        qX3.wait(v, i2, 16 * ((i2 - 2) // 2 + 1))
                    v.drain()
                    v.tensor_tensor(xfpair[tc % 2], XBF[:, 4 * ts2 + tc, :],
                                    o2tm[tc % 2], ALU.add)
                    v.drain()
                    v.bn_stats(stat[:, 0:6], xfpair[tc % 2][:, 0:512])
                    v.bn_stats(stat[:, 6:12], xfpair[tc % 2][:, 512:1024])
                    v.drain()
                    v.bn_aggr(stat[:, 12:14], stat[:, 0:12])
                    v.drain().then_inc(cV.h, 1)
                    cV.bump()
                    v.drain().then_inc(cV.h, 1)
                    cV.bump()
                    v.wait_ge(cA.h, av0 + 14 + tc)
                    v.reciprocal(stat[:, 17:18], stat[:, 16:17])
                    v.drain()
                    cc = v.tensor_scalar(h3st[:, tc, :], xfpair[tc % 2], stat[:, 12:13],
                                         stat[:, 17:18], ALU.subtract, ALU.mult)
                    cV.inc(cc)
                v.wait_ge(cP.h, pv0 + 37)
                cc = v.tensor_copy(xrg[0:16, :], PB[0][0:16, :])
                cV.inc(cc)
                v.wait_ge(cP.h, pv0 + 38)
                cc = v.tensor_scalar(logf[0:8, :], PB[1][0:8, :], rnrm[0:8, 0:1],
                                     None, ALU.mult)
                cV.inc(cc)
                for tc in range(4):
                    bank = PB[2] if tc % 2 == 0 else PB[3]
                    v.wait_ge(cP.h, pv0 + 38 + tc + 1)
                    v.drain()
                    v.reduce_max(gsc2[:, 0:1], bank[:, 0:8], axis=AX.X)
                    v.drain()
                    cc = v.tensor_scalar(gsc2[:, 1:2], gsc2[:, 0:1], -1.0, None, ALU.mult)
                    cV.inc(cc)
                    v.wait_ge(cA.h, av0 + 17 + tc + 1)
                    v.drain()
                    v.reciprocal(gsc2[:, 3:4], gsc2[:, 2:3])
                    v.tensor_scalar(gsc[:, :], bank[:, 0:8], gsc2[:, 0:1], None, ALU.is_ge)
                    v.drain()
                    v.tensor_tensor_scan(gsc3[:, 0:8], gsc[:, :], zsc[:, :], 0.0,
                                         ALU.add, ALU.add)
                    v.drain()
                    v.tensor_single_scalar(gsc3[:, 8:16], gsc3[:, 0:8], 1.0, ALU.is_equal)
                    v.drain()
                    v.tensor_tensor(gsc[:, :], gsc[:, :], gsc3[:, 8:16], ALU.mult)
                    v.drain()
                    cc = v.tensor_scalar(gatG[:, 4 * ts2 + tc, :], gsc[:, :],
                                         gsc2[:, 3:4], None, ALU.mult)
                    cV.inc(cc)

            @block.scalar
            def _(sc):
                for tc in range(4):
                    sc.wait_ge(cV.h, vv0 + 2 * tc + 1)
                    a = sc.activation(stat[:, 16:17], stat[:, 13:14], AF.Sqrt,
                                      bias=epscol[:, :])
                    cA.inc(a)
                    sc.wait_ge(cV.h, vv0 + 2 * tc + 2)
                    if ts2 > 0 and tc == 0:
                        sc.wait_ge(cP.h, pv0 - PS_ + 16)
                    for k in range(8):
                        sc.dma_start_transpose(
                            h2T[:, k, 128 * tc:128 * (tc + 1)],
                            h2tm[tc % 2][:, 128 * k:128 * (k + 1)]).then_inc(tH2.s[tc % 2].h, 16)
                        tH2.s[tc % 2].bump()
                for c in range(8):
                    sc.wait_ge(cP.h, pv0 + 16 + c + 1)
                    bank = PB[c][:, :] if c < 4 else PSQ[:, 512 * (c - 4):512 * (c - 3)]
                    a = sc.activation(x1T[:, c, :], bank, AF.Copy, scale=float(2.0 ** -10))
                    cA.inc(a)
                sc.wait_ge(cP.h, pv0 + 25)
                a = sc.activation(latb[0:24, :], PB[0][0:24, :], AF.Identity,
                                  bias=belc[0:24, :])
                cA.inc(a)
                for tc in range(4):
                    sc.wait_ge(cV.h, vv0 + 44)
                    if tc >= 2:
                        sc.wait_ge(cV.h, vv0 + 44 + 3 * (tc - 2) + 1)
                    for k in range(8):
                        sc.dma_start_transpose(
                            o2tm[tc % 2][:, 128 * k:128 * (k + 1)],
                            o2T[:, k, 128 * tc:128 * (tc + 1)]).then_inc(tO2.s[tc % 2].h, 16)
                        tO2.s[tc % 2].bump()
                    sc.wait_ge(cV.h, vv0 + 44 + 3 * tc + 2)
                    a = sc.activation(stat[:, 16:17], stat[:, 13:14], AF.Sqrt,
                                      bias=epscol[:, :])
                    cA.inc(a)
                sc.wait_ge(cV.h, vv0 + 56)
                H3Ts = bfa.at(4096 * ts2, [128, 8, 512])
                for tc in range(4):
                    for k in range(8):
                        sc.dma_start_transpose(
                            H3Ts[:, k, 128 * tc:128 * (tc + 1)],
                            h3st[:, tc, 128 * k:128 * (k + 1)]).then_inc(tH3.s[ts2 % 2].h, 16)
                        tH3.s[ts2 % 2].bump()
                for tc in range(4):
                    bank = PB[2] if tc % 2 == 0 else PB[3]
                    sc.wait_ge(cV.h, vv0 + 58 + 2 * tc + 1)
                    a = sc.activation(gsc[:, :], bank[:, 0:8], AF.Exp,
                                      bias=gsc2[:, 1:2], accum_out=gsc2[:, 2:3])
                    cA.inc(a)

            @block.tensor
            def _(t):
                t.wait_ge(tH2.s[0].h, 128 * (2 * ts2 + 2))
                t.wait_ge(tH2.s[1].h, 128 * (2 * ts2 + 2))
                if ts2 == 0:
                    t.wait_ge(sG.h, c_sg1)
                    t.wait_ge(sQ.h, c_q1)
                if ts2 >= 1:
                    t.wait_ge(cV.h, vv0)
                for c in range(8):
                    bank = PB[c][:, :] if c < 4 else PSQ[:, 512 * (c - 4):512 * (c - 3)]
                    for cin in range(8):
                        mm = t.matmul(bank, Cb[:, cin, 128 * c:128 * (c + 1)],
                                      h2T[:, cin, :], start=(cin == 0), stop=(cin == 7))
                    cP.inc(mm)
                for c in range(8):
                    t.wait_ge(cV.h, vv0 + 8 + c + 1)
                    bank = PB[c][:, :] if c < 4 else PSQ[:, 512 * (c - 4):512 * (c - 3)]
                    for cin in range(8):
                        mm = t.matmul(bank, Sb[:, cin, 128 * c:128 * (c + 1)],
                                      h2T[:, cin, :], start=(cin == 0), stop=(cin == 7))
                    cP.inc(mm)
                t.wait_ge(cV.h, vv0 + 32)
                for c in range(8):
                    bank = PB[c][:, :] if c < 4 else PSQ[:, 512 * (c - 4):512 * (c - 3)]
                    for cin in range(8):
                        t.matmul(bank, Cb[:, cin, 128 * c:128 * (c + 1)],
                                 asl[:, cin, :], start=(cin == 0), stop=False)
                    for cin in range(8):
                        mm = t.matmul(bank, Sb[:, cin, 128 * c:128 * (c + 1)],
                                      bsl[:, cin, :], start=False, stop=(cin == 7))
                    cP.inc(mm)
                t.wait_ge(cA.h, av0 + 12)
                for cin in range(8):
                    mm = t.matmul(PB[0][0:24, :], welb[:, cin, :], x1T[:, cin, :],
                                  start=(cin == 0), stop=(cin == 7))
                cP.inc(mm)
                t.wait_ge(cV.h, vv0 + 33)
                mm = t.matmul(PB[1][0:12, :], hmtb[0:24, :], bitsT[0:24, :],
                              start=True, stop=True)
                cP.inc(mm)
                t.wait_ge(cV.h, vv0 + 34)
                mm = t.matmul(PB[2][0:1, :], onescb[0:12, :], synb[0:12, :],
                              start=True, stop=True)
                cP.inc(mm)
                t.wait_ge(cV.h, vv0 + 35)
                mm = t.matmul(PB[3][0:12, :], ones1b[0:1, 0:12], fixb[0:1, :],
                              start=True, stop=True)
                cP.inc(mm)
                t.wait_ge(cV.h, vv0 + 36)
                t.wait_ge(cA.h, av0 + 13)
                for c in range(8):
                    bank = PB[c][:, :] if c < 4 else PSQ[:, 512 * (c - 4):512 * (c - 3)]
                    mm = t.matmul(bank, wleb[0:24, 128 * c:128 * (c + 1)],
                                  corT[0:24, :], start=True, stop=True)
                    cP.inc(mm)
                t.wait_ge(tH3.s[ts2 % 2].h, 512 * (ts2 // 2 + 1))
                t.wait_ge(cV.h, vv0 + 44 + 1)
                H3Ts = bfa.at(4096 * ts2, [128, 8, 512])
                for cin in range(8):
                    mm = t.matmul(PB[0][0:16, :], wgrb[:, cin, :], H3Ts[:, cin, :],
                                  start=(cin == 0), stop=(cin == 7))
                cP.inc(mm)
                t.wait_ge(cV.h, vv0 + 57)
                mm = t.matmul(PB[1][0:8, :], wgtb[0:16, :], xrg[0:16, :],
                              start=True, stop=True)
                cP.inc(mm)
                t.wait_ge(cV.h, vv0 + 58)
                for tc in range(4):
                    bank = PB[2] if tc % 2 == 0 else PB[3]
                    if tc >= 2:
                        t.wait_ge(cV.h, vv0 + 60 + 2 * (tc - 2))
                    mm = t.transpose(bank[:, 0:8], logf[0:8, 128 * tc:128 * (tc + 1)],
                                     ident[0:8, 0:8])
                    cP.inc(mm)

            @block.sync
            def _(s):
                for tc in range(4):
                    i2 = 4 * ts2 + tc
                    s.wait_ge(cV.h, vv0 + 44 + 3 * tc + 1)
                    r0 = 512 * ts2 + 128 * tc
                    qX3.inc(s.dma_start(out=out_hbm[r0:r0 + 128, :],
                                        in_=xfpair[tc % 2]), i2)

        assert cP.n == c_p2 + PS_ * NSL, (cP.n, c_p2)
        assert cV.n == c_v6 + VS * NSL, (cV.n, c_v6)
        assert cA.n == c_a5 + AS_ * NSL, (cA.n, c_a5)
        bfa.pop()
        fpa.pop()
        return qX3

    if stop_after not in ("mha0", "mha1", "ffn0", "rwkv"):
        qX3g = emit_fft()


    def emit_moe():
        bfa.push()
        fpa.push()
        w1hb = bfa.alloc([128, 8, 2048])
        w2hb = bfa.alloc([128, 16, 1024])
        Hh = bfa.alloc([128, 16, 512])
        b2rb = bfa.alloc([1, 1024])
        wstg = [fpa.alloc([128, 2048]) for _ in range(2)]
        stgf = [fpa.alloc([128, 512]) for _ in range(4)]
        b1c = fpa.alloc([128, 32, 1])

        qD1 = PSem(es, nc, "qD1")
        qD = [Sem(es, nc, f"qD{i}", 16) for i in range(4)]
        cC0 = Sem(es, nc, "cC0")
        cC1 = Sem(es, nc, "cC1")

        dbar = phase_barrier(taps=[("x3c", out_hbm[:, :]),
                                   ("gatGd", gatG[:, :, :])])

        d_p0, d_v0, d_a0 = cP.n, cV.n, cA.n
        d_q0 = sQ.n
        d_sg0 = sG.n
        PR, VR, AR = 160, 64, 128
        nacc = [0]

        def w2bank(tc, ch):
            if tc < 2:
                return PB[2 * tc + ch][:, :]
            if tc == 3:
                return PB[ch][:, :]
            return PSQ[:, 1024 + 512 * ch:1536 + 512 * ch]

        for e in range(NE):
            @block.sync
            def _(s):
                if e == 0:
                    s.wait_ge(cB.h, dbar)
                else:
                    s.wait_ge(cA.h, d_a0 + AR * 2 * e)
                for k in range(32):
                    s.dma_start(out=b1c[:, k, :],
                                in_=b1m_in[e, 128 * k:128 * (k + 1)].unsqueeze(1)).then_inc(sQ.h, 16)
                sQ.bump(32)

            @block.gpsimd
            def _(g):
                if e == 0:
                    g.wait_ge(cB.h, dbar)
                else:
                    g.wait_ge(cP.h, d_p0 + PR * 2 * e)
                g.dma_start(out=b2rb[0:1, :], in_=b2m_in[e, :].unsqueeze(0)).then_inc(sG.h, 16)
                sG.bump()

            for hh in range(2):
                r = 2 * e + hh
                rp = d_p0 + PR * r
                rv = d_v0 + VR * r
                ra = d_a0 + AR * r

                @block.sync
                def _(s):
                    if e == 0 and hh == 0:
                        s.wait_ge(cB.h, dbar)
                    for i in range(24):
                        gi = 24 * r + i
                        if gi >= 2:
                            s.wait_ge(cC0.h if gi % 2 == 0 else cC1.h, gi // 2)
                        if i < 8:
                            dd = s.dma_start(
                                out=wstg[gi % 2],
                                in_=w1m_in[e, 128 * i:128 * (i + 1),
                                           2048 * hh:2048 * (hh + 1)])
                        else:
                            j = i - 8
                            dd = s.dma_start(
                                out=wstg[gi % 2][:, 0:1024],
                                in_=w2m_in[e, 2048 * hh + 128 * j:2048 * hh + 128 * (j + 1), :])
                        qD1.inc(dd, gi)

                @block.tensor
                def _(t):
                    t.wait_ge(cC0.h, 12 * r + 4)
                    t.wait_ge(cC1.h, 12 * r + 4)
                    if hh == 0:
                        t.wait_ge(sG.h, d_sg0 + 16 * (e + 1))
                    if r >= 1:
                        t.wait_ge(cV.h, d_v0 + VR * r)
                    for ts3 in range(NSL):
                        tsp = rp + 20 * ts3
                        H3Ts = bfa.at(4096 * ts3, [128, 8, 512])
                        for hc in range(16):
                            if ts3 > 0 or hc >= 2:
                                gidx = 16 * ts3 + hc - 2
                                t.wait_ge(cA.h, ra + gidx + 1)
                            bank = PSQ[:, 512 * (hc % 2):512 * (hc % 2 + 1)]
                            for cin in range(8):
                                mm = t.matmul(bank, w1hb[:, cin, 128 * hc:128 * (hc + 1)],
                                              H3Ts[:, cin, :], start=(cin == 0), stop=(cin == 7))
                            cP.inc(mm)
                        t.wait_ge(cA.h, ra + 16 * (ts3 + 1))
                        if ts3 == 0:
                            t.wait_ge(cC0.h, 12 * (r + 1))
                            t.wait_ge(cC1.h, 12 * (r + 1))
                        for tc in range(4):
                            if tc == 3:
                                t.wait_ge(cV.h, rv + 8 * ts3 + 2)
                            elif ts3 >= 1:
                                prev = rv + 8 * (ts3 - 1)
                                t.wait_ge(cV.h, prev + (8, 4, 6)[tc])
                            for hc in range(16):
                                for ch in range(2):
                                    mm = t.matmul(w2bank(tc, ch),
                                                  Hh[:, hc, 128 * tc:128 * (tc + 1)],
                                                  w2hb[:, hc, 512 * ch:512 * (ch + 1)],
                                                  start=(hc == 0),
                                                  stop=(hh == 1 and hc == 15))
                            if hh == 0:
                                for ch in range(2):
                                    mm = t.matmul(w2bank(tc, ch), ones1b[0:1, 0:128],
                                                  b2rb[0:1, 512 * ch:512 * (ch + 1)],
                                                  start=False, stop=True)
                            cP.inc(mm)

                @block.scalar
                def _(sc):
                    if r == 0:
                        sc.wait_ge(cB.h, dbar)
                    else:
                        sc.wait_ge(cP.h, d_p0 + PR * r)
                    for i in range(12):
                        gi = 24 * r + 2 * i
                        ii = 2 * i
                        qD1.wait(sc, gi, 16 * (gi // 2 + 1))
                        if ii < 8:
                            a = sc.activation(w1hb[:, ii, :], wstg[0],
                                              AF.Identity, bias=0.0, scale=1.0)
                        else:
                            a = sc.activation(w2hb[:, ii - 8, :], wstg[0][:, 0:1024],
                                              AF.Identity, bias=0.0, scale=1.0)
                        cC0.inc(a)
                    for ts3 in range(NSL):
                        tsp = rp + 20 * ts3
                        if ts3 == 0 and hh == 0:
                            sc.wait_ge(sQ.h, d_q0 + 32 * 16 * (e + 1))
                        for hc in range(16):
                            sc.wait_ge(cP.h, tsp + hc + 1)
                            bank = PSQ[:, 512 * (hc % 2):512 * (hc % 2 + 1)]
                            a = sc.activation(Hh[:, hc, :], bank, AF.Gelu,
                                              bias=b1c[:, 16 * hh + hc, :])
                            cA.inc(a)

                @block.vector
                def _(v):
                    v.drain()
                    if r == 0:
                        v.wait_ge(cB.h, dbar)
                    else:
                        v.wait_ge(cP.h, d_p0 + PR * r)
                    for i in range(12):
                        gi = 24 * r + 2 * i + 1
                        ii = 2 * i + 1
                        qD1.wait(v, gi, 16 * (gi // 2 + 1))
                        if ii < 8:
                            cc = v.tensor_copy(w1hb[:, ii, :], wstg[1])
                        else:
                            cc = v.tensor_copy(w2hb[:, ii - 8, :], wstg[1][:, 0:1024])
                        cC1.inc(cc)
                    v.drain()
                    for ts3 in range(NSL):
                        tsp = rp + 20 * ts3
                        for tc in range(4):
                            v.wait_ge(cP.h, tsp + 16 + tc + 1)
                            for ch in range(2):
                                g8 = 2 * tc + ch
                                i4 = nacc[0] + 8 * ts3 + g8
                                if i4 >= 4:
                                    v.wait_ge(qD[i4 % 4].h, 16 * ((i4 - 4) // 4 + 1))
                                cc = v.tensor_scalar(stgf[i4 % 4], w2bank(tc, ch),
                                                     gatG[:, 4 * ts3 + tc, e:e + 1],
                                                     None, ALU.mult)
                                cV.inc(cc)

                @block.gpsimd
                def _(g):
                    for ts3 in range(NSL):
                        for tc in range(4):
                            for ch in range(2):
                                g8 = 2 * tc + ch
                                i4 = nacc[0] + 8 * ts3 + g8
                                g.wait_ge(cV.h, rv + 8 * ts3 + g8 + 1)
                                if i4 >= 4:
                                    g.wait_ge(qD[i4 % 4].h, 16 * ((i4 - 4) // 4 + 1))
                                r0 = 512 * ts3 + 128 * tc
                                dd = g.dma_start(out=out_hbm[r0:r0 + 128,
                                                             512 * ch:512 * (ch + 1)],
                                                 in_=stgf[i4 % 4],
                                                 accum_op=ALU.add)
                                dd.then_inc(qD[i4 % 4].h, 16)
                                qD[i4 % 4].bump()
                nacc[0] += 64

        @block.gpsimd
        def _(g):
            for i in range(4):
                g.wait_ge(qD[i].h, qD[i].n)
            g.wait_ge(qX3g.s[0].h, qX3g.s[0].n)
            g.wait_ge(qX3g.s[1].h, qX3g.s[1].n)
        bfa.pop()
        fpa.pop()

    if stop_after not in ("mha0", "mha1", "ffn0", "rwkv", "fft"):
        emit_moe()

    @block.gpsimd
    def _(g):
        g.wait_ge(sQ.h, sQ.n)
        g.wait_ge(sG.h, sG.n)
        g.wait_ge(cV.h, cV.n)
        g.wait_ge(cP.h, cP.n)
        g.wait_ge(cA.h, cA.n)

    es.close()
    return nc, dbg


def _host_inputs(core_inputs):
    return core_inputs


_GOLAY_A = np.array(
    [[1, 1, 1, 0, 1, 1, 1, 0, 0, 0, 1, 1], [1, 1, 0, 1, 1, 1, 0, 0, 0, 1, 1, 0],
     [1, 0, 1, 1, 1, 0, 0, 0, 1, 1, 0, 1], [0, 1, 1, 1, 0, 0, 0, 1, 1, 0, 1, 1],
     [0, 0, 0, 0, 1, 1, 1, 1, 1, 1, 1, 1], [1, 1, 1, 1, 0, 0, 0, 0, 0, 0, 0, 0],
     [0, 0, 0, 0, 1, 1, 1, 1, 0, 0, 0, 0], [1, 1, 0, 0, 1, 1, 0, 0, 1, 1, 0, 0],
     [1, 0, 1, 0, 1, 0, 1, 0, 1, 0, 1, 0], [0, 1, 0, 1, 0, 1, 0, 1, 0, 1, 0, 1],
     [1, 0, 0, 1, 1, 0, 0, 1, 1, 0, 0, 1], [0, 1, 1, 0, 0, 1, 1, 0, 0, 1, 1, 0]],
    dtype=np.float32)
_HMT = np.concatenate([_GOLAY_A.T, np.eye(12, dtype=np.float32)], axis=1).T.copy()


def host_maps(inputs):
    f = np.float32
    def T(a):
        return np.ascontiguousarray(np.asarray(a, dtype=f).T)
    def C(a):
        return np.ascontiguousarray(np.asarray(a, dtype=f))
    common = {
        "wq1": T(inputs["L0_attn_wq1"]), "wk1": T(inputs["L0_attn_wk1"]),
        "wv1": T(inputs["L0_attn_wv1"]), "wo1": T(inputs["L0_attn_wo1"]),
        "wq2": T(inputs["L0_attn_wq2"]), "wk2": T(inputs["L0_attn_wk2"]),
        "wv2": T(inputs["L0_attn_wv2"]), "wo2": T(inputs["L0_attn_wo2"]),
        "bq1": C(inputs["L0_attn_bq1"]), "bk1": C(inputs["L0_attn_bk1"]),
        "bv1": C(inputs["L0_attn_bv1"]), "bo1": C(inputs["L0_attn_bo1"]),
        "bq2": C(inputs["L0_attn_bq2"]), "bk2": C(inputs["L0_attn_bk2"]),
        "bv2": C(inputs["L0_attn_bv2"]), "bo2": C(inputs["L0_attn_bo2"]),
        "g1": C(inputs["L0_attn_g1"]), "be1": C(inputs["L0_attn_be1"]),
        "g2": C(inputs["L0_attn_g2"]), "be2": C(inputs["L0_attn_be2"]),
        "wff1": T(inputs["L0_attn_w_ff1"]), "bff1": C(inputs["L0_attn_b_ff1"]),
        "wff2": T(inputs["L0_attn_w_ff2"]), "bff2": C(inputs["L0_attn_b_ff2"]),
        "gff": C(inputs["L0_attn_g_ff"]), "beff": C(inputs["L0_attn_be_ff"]),
        "wr": T(inputs["L0_ffn_W_r"]), "wkk": T(inputs["L0_ffn_W_k"]),
        "wvv": T(inputs["L0_ffn_W_v"]),
        "mur": C(inputs["L0_ffn_mu_r"]), "muk": C(inputs["L0_ffn_mu_k"]),
        "wel": C(inputs["L1_attn_W_el"]), "bel": C(inputs["L1_attn_b_el"]),
        "wle": C(inputs["L1_attn_W_le"]), "ble": C(inputs["L1_attn_b_le"]),
        "hmt": np.ascontiguousarray(_HMT),
        "wgred": T(inputs["L1_ffn_wg_red"]), "wgm": C(inputs["L1_ffn_wg"]),
        "wgt": T(inputs["L1_ffn_wg"]),
        "b1m": C(inputs["L1_ffn_b1"]), "b2m": C(inputs["L1_ffn_b2"]),
        "w1m": C(inputs["L1_ffn_w1"]), "w2m": C(inputs["L1_ffn_w2"]),
        "ident": np.eye(128, dtype=f),
    }
    x = np.asarray(inputs["L0_attn_x"], dtype=f)
    return [dict(common, x=np.ascontiguousarray(x[b])) for b in range(x.shape[0])]


def kernel(**inputs):
    maps = host_maps(inputs)
    nc, _ = build_program()
    res = run_bass_kernel_spmd(nc, maps, core_ids=list(range(len(maps))))
    return np.stack([np.asarray(r["out"], dtype=np.float32) for r in res.results])


# version 05
# speedup vs baseline: 1.1383x; 1.0264x over previous
import numpy as np
import concourse.bass as bass
import concourse.mybir as mybir
from concourse.bass_utils import run_bass_kernel_spmd

F32 = mybir.dt.float32
BF16 = mybir.dt.bfloat16
AF = mybir.ActivationFunctionType
ALU = mybir.AluOpType
AX = mybir.AxisListType

S, D, DH, H, DK, DFF, NE = 4096, 1024, 512, 8, 64, 4096, 8
NCH, NSL = 32, 8

ALPHA = 0.1
BETA = 0.1
A11 = 1.0 - ALPHA
A12 = 1.0 - ALPHA
A21 = -(ALPHA * BETA)
A22 = 1.0 - ALPHA * BETA
B1 = ALPHA
B2 = ALPHA * BETA
LNEPS = 1e-5


class Sem:
    REG = None

    def __init__(self, es, nc, name, step=1):
        self.h = es.enter_context(nc.semaphore(name))
        self.n = 0
        self.step = step
        if Sem.REG is not None:
            Sem.REG.append(self)

    def inc(self, inst, k=1):
        inst.then_inc(self.h, self.step * k)
        self.n += self.step * k
        return self.n

    def bump(self, k=1):
        self.n += self.step * k
        return self.n

    def wait(self, eng, val=None):
        eng.wait_ge(self.h, self.n if val is None else val)


class PSem:

    def __init__(self, es, nc, name):
        self.s = [Sem(es, nc, name + "a", 16), Sem(es, nc, name + "b", 16)]

    def inc(self, inst, i):
        return self.s[i % 2].inc(inst)

    def bump(self, i, k=1):
        return self.s[i % 2].bump(k)

    def wait(self, eng, i, val):
        eng.wait_ge(self.s[i % 2].h, val)


class Arena:
    def __init__(self, ap, elems):
        self.ap, self.elems, self.off, self.stack = ap, elems, 0, []

    def alloc(self, shape):
        n = int(np.prod(shape[1:]))
        assert self.off + n <= self.elems, (self.off, n, self.elems)
        flat = self.ap[:, self.off:self.off + n]
        self.off += n
        if len(shape) == 3:
            flat = flat.rearrange("p (a b) -> p a b", a=shape[1], b=shape[2])
        return flat

    def at(self, off, shape):
        n = int(np.prod(shape[1:]))
        assert off + n <= self.elems
        flat = self.ap[:, off:off + n]
        if len(shape) == 3:
            flat = flat.rearrange("p (a b) -> p a b", a=shape[1], b=shape[2])
        return flat

    def push(self):
        self.stack.append(self.off)

    def pop(self):
        self.off = self.stack.pop()


def build_program(debug=(), stop_after=None):
    nc = bass.Bass("TRN2", target_bir_lowering=False)

    x_in = nc.dram_tensor("x", [S, D], F32, kind="ExternalInput").ap()
    out_hbm = nc.dram_tensor("out", [S, D], F32, kind="ExternalOutput").ap()

    def din(name, shape):
        return nc.dram_tensor(name, shape, F32, kind="ExternalInput").ap()

    wq = [din("wq1", [DH, DH]), din("wq2", [DH, DH])]
    wk = [din("wk1", [DH, DH]), din("wk2", [DH, DH])]
    wv = [din("wv1", [DH, DH]), din("wv2", [DH, DH])]
    wo = [din("wo1", [DH, DH]), din("wo2", [DH, DH])]
    bqi = [din("bq1", [DH]), din("bq2", [DH])]
    bki = [din("bk1", [DH]), din("bk2", [DH])]
    bvi = [din("bv1", [DH]), din("bv2", [DH])]
    boi = [din("bo1", [DH]), din("bo2", [DH])]
    gbi = [(din("g1", [DH]), din("be1", [DH])), (din("g2", [DH]), din("be2", [DH]))]
    wff1_in = din("wff1", [D, DFF]); bff1_in = din("bff1", [DFF])
    wff2_in = din("wff2", [DFF, D]); bff2_in = din("bff2", [D])
    gff_in = din("gff", [D]); beff_in = din("beff", [D])
    wr_in = din("wr", [D, D]); wkk_in = din("wkk", [D, D]); wvv_in = din("wvv", [D, D])
    mur_in = din("mur", [D]); muk_in = din("muk", [D])
    wel_in = din("wel", [D, 24]); bel_in = din("bel", [24])
    wle_in = din("wle", [24, D]); ble_in = din("ble", [D])
    hmt_in = din("hmt", [24, 12])
    wgred_in = din("wgred", [D, 16])
    wgm_in = din("wgm", [NE, 16]); wgt_in = din("wgt", [16, NE])
    b1m_in = din("b1m", [NE, DFF]); b2m_in = din("b2m", [NE, D])
    w1m_in = din("w1m", [NE, D, DFF]); w2m_in = din("w2m", [NE, DFF, D])
    ident_in = din("ident", [128, 128])
    x1sp = nc.dram_tensor("x1sp", [S, DH], BF16, kind="Internal").ap()
    x2sp = nc.dram_tensor("x2sp", [S, DH], BF16, kind="Internal").ap()
    xshsp = nc.dram_tensor("xshsp", [S, DH], BF16, kind="Internal").ap()

    dbg = {}
    for spec in debug:
        name, shape = spec[0], spec[1]
        ddt = BF16 if (len(spec) > 2 and spec[2] == "bf16") else F32
        dbg[name] = nc.dram_tensor("dbg_" + name, list(shape), ddt,
                                   kind="ExternalOutput").ap()

    from contextlib import ExitStack
    es = ExitStack()

    BFA_N, FPA_N = 84480, 7936
    BFAt = es.enter_context(nc.sbuf_tensor("BFA", [128, BFA_N], BF16))
    FPAt = es.enter_context(nc.sbuf_tensor("FPA", [128, FPA_N], F32))
    bfa = Arena(BFAt[:, :], BFA_N)
    fpa = Arena(FPAt[:, :], FPA_N)
    carry = fpa.alloc([128, 2, 512])

    ident = es.enter_context(nc.sbuf_tensor("idents", [128, 128], F32))
    ones1 = es.enter_context(nc.sbuf_tensor("ones1r", [1, 512], F32))
    ones1b = es.enter_context(nc.sbuf_tensor("ones1b", [1, 512], BF16))
    onescb = es.enter_context(nc.sbuf_tensor("onescb", [128, 1], BF16))
    srow = es.enter_context(nc.sbuf_tensor("srow", [1, 64], F32))
    spow = es.enter_context(nc.sbuf_tensor("spow", [2, 8, 2], F32))
    tpow = es.enter_context(nc.sbuf_tensor("tpow", [2, 8, 2], F32))
    cpair = es.enter_context(nc.sbuf_tensor("cpair", [2, 520], F32))
    rseq = es.enter_context(nc.sbuf_tensor("rseq", [2, 128], F32))
    e1seq = es.enter_context(nc.sbuf_tensor("e1seq", [2, 128], F32))
    cpt = es.enter_context(nc.sbuf_tensor("cpt", [2, 128], F32))
    w2t = es.enter_context(nc.sbuf_tensor("w2t", [128, 2], F32))
    w2ct = es.enter_context(nc.sbuf_tensor("w2ct", [128, 2], F32))
    t0t = es.enter_context(nc.sbuf_tensor("t0t", [128, 128], BF16))
    t0ct = es.enter_context(nc.sbuf_tensor("t0ct", [128, 128], BF16))
    wrow = es.enter_context(nc.sbuf_tensor("wrow", [1, 128], F32))
    epscol = es.enter_context(nc.sbuf_tensor("epscol", [128, 1], F32))

    ln0 = es.enter_context(nc.sbuf_tensor("ln0s", [128, NCH, 2], F32))
    gatG = es.enter_context(nc.sbuf_tensor("gatG", [128, NCH, 8], F32))

    PSQ = es.enter_context(nc.psum_tensor("PSQ", [128, 2048], F32))
    PB = [es.enter_context(nc.psum_tensor(f"PB{i}", [128, 512], F32)) for i in range(4)]

    Sem.REG = []
    sQ = Sem(es, nc, "sQ", 16)
    sT = Sem(es, nc, "sT", 16)
    sG = Sem(es, nc, "sG", 16)
    cP = Sem(es, nc, "cP")
    cV = Sem(es, nc, "cV")
    cA = Sem(es, nc, "cA")
    cG = Sem(es, nc, "cG")
    cB = Sem(es, nc, "cB")

    def phase_barrier(taps=()):
        sems = [(s_.h, s_.n) for s_ in Sem.REG if s_.n > 0 and s_ is not cB]
        live = [(dbg[n_], a_) for n_, a_ in taps if n_ in dbg]
        for _ in live:
            sQ.bump()
        sqn = sQ.n
        @block.sync
        def _(s):
            for h_, n_ in sems:
                s.wait_ge(h_, n_)
            for d_, a_ in live:
                s.dma_start(out=d_, in_=a_).then_inc(sQ.h, 16)
            if live:
                s.wait_ge(sQ.h, sqn)
            s.nop().then_inc(cB.h, 1)
        cB.bump()
        return cB.n

    block = es.enter_context(nc.Block())

    @block.sync
    def _(s):
        s.dma_start(out=ident[:, :], in_=ident_in[:, :]).then_inc(sQ.h, 16)
        sQ.bump()
    sq_ident = sQ.n

    @block.vector
    def _(v):
        v.memset(ones1[:, :], 1.0)
        v.memset(ones1b[:, :], 1.0)
        v.memset(onescb[:, :], 1.0)
        vals = [A11, A12, A21, A22, A11, A21, A12, A22,
                1.0, 0.0, 0.0, 1.0, B1, B2]
        for i, val in enumerate(vals):
            v.memset(srow[0:1, i:i + 1], val)
        v.drain()
        for i, val in enumerate(vals):
            lo = float(np.float64(val) - np.float64(np.float32(val)))
            if lo != 0.0:
                v.tensor_scalar(srow[0:1, i:i + 1], srow[0:1, i:i + 1], lo,
                                None, ALU.add)
        v.drain()
        v.memset(cpair[:, :], 0.0)
        v.memset(rseq[:, :], 0.0)
        v.memset(e1seq[:, :], 0.0)
        v.memset(cpt[:, :], 0.0)
        v.memset(carry[:, :, :], 0.0)
        v.memset(t0t[:, :], 0.0)
        v.memset(epscol[:, :], LNEPS)
        v.drain().then_inc(cV.h, 1)
        cV.bump()
    cv0 = cV.n

    @block.gpsimd
    def _(g):
        g.wait_ge(cV.h, cv0)
        g.dma_start(out=spow[:, 0, :], in_=srow[0:1, 0:4]).then_inc(sG.h, 16)
        g.dma_start(out=tpow[:, 0, :], in_=srow[0:1, 4:8]).then_inc(sG.h, 16)
        g.dma_start(out=cpair[:, 0:2], in_=srow[0:1, 8:12]).then_inc(sG.h, 16)
        g.dma_start(out=rseq[:, 127:128], in_=srow[0:1, 12:14]).then_inc(sG.h, 16)
        sG.bump(4)
    sg0 = sG.n

    cp_base = cP.n
    @block.tensor
    def _(t):
        t.wait_ge(sG.h, sg0)
        for k in range(7):
            if k > 0:
                t.wait_ge(cV.h, cv0 + k)
            m = 1 << k
            t.matmul(PB[0][0:2, 0:2], tpow[:, k, :], spow[:, k, :], start=True, stop=True)
            t.matmul(PB[0][0:2, 4:6], spow[:, k, :], tpow[:, k, :], start=True, stop=True)
            t.matmul(PB[1][0:2, 0:2 * m], tpow[:, k, :], cpair[:, 0:2 * m],
                     start=True, stop=True)
            mm = t.matmul(PB[2][0:2, 0:m], tpow[:, k, :], rseq[:, 128 - m:128],
                          start=True, stop=True)
            cP.inc(mm)

    @block.vector
    def _(v):
        for k in range(7):
            v.wait_ge(cP.h, cp_base + k + 1)
            m = 1 << k
            v.tensor_copy(spow[:, k + 1, :], PB[0][0:2, 0:2])
            v.tensor_copy(tpow[:, k + 1, :], PB[0][0:2, 4:6])
            v.tensor_copy(cpair[:, 2 * m:4 * m], PB[1][0:2, 0:2 * m])
            v.tensor_copy(rseq[:, 128 - 2 * m:128 - m], PB[2][0:2, 0:m])
            v.drain().then_inc(cV.h, 1)
            cV.bump()
    cv_dbl = cV.n

    @block.tensor
    def _(t):
        t.wait_ge(cV.h, cv_dbl)
        mm = t.matmul(PB[1][0:2, 0:2], tpow[:, 7, :], cpair[:, 0:2], start=True, stop=True)
        cP.inc(mm)
    cp_a128 = cP.n

    @block.vector
    def _(v):
        v.wait_ge(cP.h, cp_a128)
        v.tensor_copy(cpair[:, 256:258], PB[1][0:2, 0:2])
        v.tensor_scalar(wrow[0:1, :], cpair[0:1, 0:256:2], B1, None, ALU.mult)
        v.drain()
        v.scalar_tensor_tensor(wrow[0:1, :], cpair[0:1, 1:257:2], B2, wrow[0:1, :],
                               ALU.mult, ALU.add)
        v.tensor_copy(cpt[0:1, :], cpair[0:1, 2:258:2])
        v.tensor_copy(e1seq[0:1, :], cpair[0:1, 0:256:2])
        v.drain().then_inc(cV.h, 1)
        cV.bump()
    cv_rows = cV.n

    @block.tensor
    def _(t):
        t.wait_ge(cV.h, cv_rows)
        t.wait_ge(sQ.h, sq_ident)
        t.matmul(PB[0][0:2, 0:128], srow[0:1, 8:10], cpair[0:1, 2:258:2],
                 start=True, stop=False)
        t.matmul(PB[0][0:2, 0:128], srow[0:1, 10:12], cpair[0:1, 3:259:2],
                 start=False, stop=True)
        t.matmul(PB[1][0:2, 0:128], srow[0:1, 8:10], cpair[0:1, 0:256:2],
                 start=True, stop=False)
        t.matmul(PB[1][0:2, 0:128], srow[0:1, 10:12], cpair[0:1, 1:257:2],
                 start=False, stop=True)
        t.transpose(PB[2][0:128, 0:2], rseq[0:2, :], ident[0:2, 0:2])
        mm = t.matmul(PB[3][0:1, 0:2], cpair[:, 254:255], ident[0:2, 0:2],
                      start=True, stop=True)
        cP.inc(mm)
    cp_rows2 = cP.n

    @block.vector
    def _(v):
        v.wait_ge(cP.h, cp_rows2)
        v.tensor_copy(cpt[:, :], PB[0][0:2, 0:128])
        v.tensor_copy(e1seq[:, :], PB[1][0:2, 0:128])
        v.tensor_copy(w2t[:, :], PB[2][0:128, 0:2])
        v.tensor_copy(srow[0:1, 16:18], PB[3][0:1, 0:2])
        cc = v.tensor_copy(t0t[0:1, 0:1], wrow[0:1, 0:1])
        cV.inc(cc)
    sg_rows = sG.n
    cv_rows2 = cV.n

    cp_t0b = cP.n
    sg_t0b = sG.n
    cv_t0b = cV.n
    @block.tensor
    def _(t):
        t.wait_ge(cV.h, cv_rows2)
        for k in range(7):
            m = 1 << k
            if k > 0:
                t.wait_ge(sG.h, sg_t0b + 16 * k)
            mm = t.matmul(PB[3][0:m, 0:m], rseq[:, 127 - m:127], e1seq[:, 0:m],
                          start=True, stop=True)
            cP.inc(mm)

    @block.vector
    def _(v):
        for k in range(7):
            m = 1 << k
            v.wait_ge(cP.h, cp_t0b + k + 1)
            if k > 0:
                v.wait_ge(sG.h, sg_t0b + 16 * k)
            cc = v.tensor_copy(t0t[0:m, m:2 * m], PB[3][0:m, 0:m])
            cV.inc(cc)

    @block.gpsimd
    def _(g):
        for k in range(7):
            m = 1 << k
            g.wait_ge(cV.h, cv_t0b + k + 1)
            g.dma_start(out=t0t[m:2 * m, m:2 * m], in_=t0t[0:m, 0:m]).then_inc(sG.h, 16)
            sG.bump()
    sg_t0d = sG.n

    @block.vector
    def _(v):
        v.wait_ge(sG.h, sg_t0d)
        v.tensor_copy(t0ct[:, :], t0t[:, :])
        v.drain()
        v.tensor_copy(t0ct[0:1, :], cpair[0:1, 0:256:2])
        v.tensor_copy(w2ct[:, :], w2t[:, :])
        v.drain()
        v.tensor_copy(w2ct[0:1, 0:2], srow[0:1, 16:18])
        v.drain().then_inc(cV.h, 1)
        cV.bump()
    cv_t0c = cV.n
    sg_c0 = sG.n
    cv_c0 = cV.n

    def tap_small(name, src_ap):
        if name not in dbg:
            return
        @block.sync
        def _(s):
            s.wait_ge(sG.h, sg_c0)
            s.wait_ge(cV.h, cv_c0)
            s.dma_start(out=dbg[name][:, :], in_=src_ap).then_inc(sQ.h, 16)
            sQ.bump()

    tap_small("cpair", cpair[:, :])
    tap_small("rseq", rseq[:, :])
    tap_small("w2t", w2t[:, :])
    tap_small("w2ct", w2ct[:, :])
    tap_small("cpt", cpt[:, :])
    tap_small("wrow", wrow[:, :])

    if "t0t" in dbg or "t0ct" in dbg:
        fpa.push()
        tf = fpa.alloc([128, 128])
        tf2 = fpa.alloc([128, 128])
        @block.scalar
        def _(sc):
            sc.wait_ge(cV.h, cv_c0)
            sc.wait_ge(sG.h, sg_c0)
            sc.copy(tf, t0t[:, :])
            a = sc.copy(tf2, t0ct[:, :])
            cA.inc(a)
        ca_t0 = cA.n
        @block.sync
        def _(s):
            s.wait_ge(cA.h, ca_t0)
            if "t0t" in dbg:
                s.dma_start(out=dbg["t0t"][:, :], in_=tf).then_inc(sQ.h, 16)
                sQ.bump()
            if "t0ct" in dbg:
                s.dma_start(out=dbg["t0ct"][:, :], in_=tf2).then_inc(sQ.h, 16)
                sQ.bump()
        fpa.pop()

    XBF = bfa.at(0, [128, NCH, 1024])
    R1 = 32768

    fpa.push()
    xfp = [fpa.alloc([128, 1024]) for _ in range(2)]
    bnst = fpa.alloc([128, 2, 12])
    bnag = fpa.alloc([128, 2, 2])
    sdc = fpa.alloc([128, 2, 1])

    qx = PSem(es, nc, "qx")
    a1_v0, a1_a0, a1_g0 = cV.n, cA.n, cG.n

    @block.sync
    def _(s):
        s.wait_ge(sQ.h, sQ.n)
        for c in range(NCH):
            if c >= 2:
                s.wait_ge(cV.h, a1_v0 + 2 * (c - 1))
                s.wait_ge(cG.h, a1_g0 + (c - 1))
            qx.inc(s.dma_start(out=xfp[c % 2], in_=x_in[128 * c:128 * (c + 1), :]), c)

    @block.vector
    def _(v):
        for c in range(NCH):
            qx.wait(v, c, 16 * (c // 2 + 1))
            p = c % 2
            v.bn_stats(bnst[:, p, 0:6], xfp[p][:, 0:512])
            v.bn_stats(bnst[:, p, 6:12], xfp[p][:, 512:1024])
            v.drain()
            v.bn_aggr(bnag[:, p, :], bnst[:, p, 0:12])
            v.drain()
            v.tensor_copy(ln0[:, c, 0:1], bnag[:, p, 0:1])
            v.drain().then_inc(cV.h, 1)
            cV.bump()
            if c >= 1:
                v.wait_ge(cA.h, a1_a0 + c)
                rr = v.reciprocal(ln0[:, c - 1, 1:2], sdc[:, (c - 1) % 2, :])
                cV.inc(rr)
        v.wait_ge(cA.h, a1_a0 + NCH)
        rr = v.reciprocal(ln0[:, NCH - 1, 1:2], sdc[:, (NCH - 1) % 2, :])
        cV.inc(rr)
    a1_v1 = cV.n

    @block.scalar
    def _(sc):
        for c in range(NCH):
            sc.wait_ge(cV.h, a1_v0 + 2 * c + 1)
            a = sc.activation(sdc[:, c % 2, :], bnag[:, c % 2, 1:2], AF.Sqrt,
                              bias=epscol[:, :])
            cA.inc(a)

    @block.gpsimd
    def _(g):
        for c in range(NCH):
            qx.wait(g, c, 16 * (c // 2 + 1))
            cc = g.tensor_copy(XBF[:, c, :], xfp[c % 2])
            cG.inc(cc)
    a1_g1 = cG.n
    fpa.pop()


    if "ln0m" in dbg:
        fpa.push()
        tf = fpa.alloc([128, 32])
        tf2 = fpa.alloc([128, 32])
        @block.vector
        def _(v):
            v.wait_ge(cV.h, a1_v1)
            v.wait_ge(cG.h, a1_g1)
            v.tensor_copy(tf, ln0[:, :, 0])
            cc = v.tensor_copy(tf2, ln0[:, :, 1])
            cV.inc(cc)
        vv = cV.n
        @block.sync
        def _(s):
            s.wait_ge(cV.h, vv)
            s.dma_start(out=dbg["ln0m"][:, :], in_=tf).then_inc(sQ.h, 16)
            s.dma_start(out=dbg["ln0r"][:, :], in_=tf2).then_inc(sQ.h, 16)
            sQ.bump(2)
        fpa.pop()

    bfa.off = R1
    bfa.push()
    ubf = [bfa.alloc([128, 512]) for _ in range(2)]
    xst = [bfa.alloc([128, 512]) for _ in range(2)]
    fpa.push()
    ufp = [fpa.alloc([128, 512]) for _ in range(2)]
    qH = PSem(es, nc, "qH")

    h_v0, h_p0 = cV.n, cP.n
    h_q0 = sQ.n

    @block.vector
    def _(v):
        v.drain()
        v.wait_ge(sQ.h, h_q0)
        v.wait_ge(cG.h, a1_g1)
        for c in range(NCH):
            if c >= 1:
                v.wait_ge(cP.h, h_p0 + 2 * c)
                cc = v.tensor_copy(carry[0:2, c % 2, :], PB[1][0:2, :])
                cV.inc(cc)
                if c >= 3:
                    qH.wait(v, c - 1, 16 * ((c - 3) // 2 + 1))
                cc = v.tensor_copy(xst[(c - 1) % 2], PB[0][:, :])
                cV.inc(cc)
            p = c % 2
            v.tensor_scalar(ufp[p], XBF[:, c, 0:512], ln0[:, c, 0:1],
                            ln0[:, c, 1:2], ALU.subtract, ALU.mult)
            v.drain()
            cc = v.tensor_copy(ubf[p], ufp[p])
            cV.inc(cc)
        v.wait_ge(cP.h, h_p0 + 2 * NCH)
        qH.wait(v, NCH - 1, 16 * ((NCH - 3) // 2 + 1))
        cc = v.tensor_copy(xst[(NCH - 1) % 2], PB[0][:, :])
        cV.inc(cc)
    h_v2 = cV.n

    @block.gpsimd
    def _(g):
        for c in range(NCH):
            if c < NCH - 1:
                g.wait_ge(cV.h, h_v0 + 3 * (c + 1))
            else:
                g.wait_ge(cV.h, h_v2)
            qH.inc(g.dma_start(out=xshsp[128 * c:128 * (c + 1), :],
                               in_=xst[c % 2]), c)
    

    @block.tensor
    def _(t):
        for c in range(NCH):
            t.wait_ge(cV.h, h_v0 + 3 * c + 1)
            p = c % 2
            tt = t0ct if c == 0 else t0t
            ww = w2ct if c == 0 else w2t
            t.matmul(PB[0][:, :], tt[:, :], ubf[p], start=True, stop=False)
            mm = t.matmul(PB[0][:, :], cpt[:, :], carry[0:2, c % 2, :],
                          start=False, stop=True)
            cP.inc(mm)
            t.matmul(PB[1][0:2, :], ww[:, :], ufp[p], start=True, stop=False)
            mm = t.matmul(PB[1][0:2, :], tpow[:, 7, :], carry[0:2, c % 2, :],
                          start=False, stop=True)
            cP.inc(mm)
    h_p1 = cP.n

    bfa.pop()
    fpa.pop()


    def emit_mha(li, dst_sp):
        bfa.push()
        fpa.push()
        K1T = bfa.alloc([128, 4, S])
        V1 = bfa.alloc([128, NCH, 520])
        wqb = bfa.alloc([128, 4, 512])
        wob = bfa.alloc([128, 8, 512])
        tmsl = [bfa.alloc([128, 4, 512])]
        tsl = [bfa.alloc([128, 4, 512])]
        wcb = [bfa.alloc([128, 1024]) for _ in range(2)]
        browb = bfa.alloc([1, 4, 512])
        qtt = bfa.alloc([128, 512])
        aq = [bfa.alloc([128, 1024]) for _ in range(2)]
        ohb = bfa.alloc([128, 512])
        stg = [bfa.alloc([128, 512]) for _ in range(2)]

        wfs = [fpa.alloc([128, 1024]) for _ in range(2)]
        bqkc = fpa.alloc([128, 4, 2])
        bcB = fpa.alloc([128, 512])
        bcG = fpa.alloc([128, 512])
        bcE = fpa.alloc([128, 512])
        oacc = fpa.alloc([128, 4, 512])
        denA = fpa.alloc([128, 512])
        rcq = fpa.alloc([128, 4])
        sfp = fpa.alloc([128, 512])
        stat = fpa.alloc([128, 18])

        qW = PSem(es, nc, "qW" + str(li))
        qK = PSem(es, nc, "qK" + str(li))
        cKV = Sem(es, nc, "cKV" + str(li))
        qM = PSem(es, nc, "qM" + str(li))
        tT = PSem(es, nc, "tT" + str(li))
        tQ = Sem(es, nc, "tQ" + str(li), 16)
        qO = PSem(es, nc, "qO" + str(li))

        wqi, wki, wvi, woi = wq[li], wk[li], wv[li], wo[li]
        bqI, bkI, bvI, boI = bqi[li], bki[li], bvi[li], boi[li]
        gI, beI = gbi[li]

        ebar = phase_barrier()

        mw_g0 = cG.n
        @block.sync
        def _(s):
            s.wait_ge(cB.h, ebar)
            for i in range(12):
                if i >= 2:
                    s.wait_ge(cG.h, mw_g0 + i - 1)
                if i < 4:
                    dd = s.dma_start(out=wfs[i % 2][:, 0:512],
                                     in_=wqi[128 * i:128 * (i + 1), :])
                else:
                    hh = i - 4
                    dd = s.dma_start(out=wfs[i % 2][0:64, 0:512],
                                     in_=woi[64 * hh:64 * (hh + 1), :])
                qW.inc(dd, i)
            for c_ in range(4):
                s.dma_start(out=bqkc[:, c_, 0:1],
                            in_=bqI[128 * c_:128 * (c_ + 1)].unsqueeze(1)).then_inc(sQ.h, 16)
                s.dma_start(out=bqkc[:, c_, 1:2],
                            in_=bkI[128 * c_:128 * (c_ + 1)].unsqueeze(1)).then_inc(sQ.h, 16)
            sQ.bump(8)
        mw_q1 = sQ.n

        @block.gpsimd
        def _(g):
            g.wait_ge(cB.h, ebar)
            g.dma_start(out=browb[0:1, 0, :], in_=bvI.unsqueeze(0)).then_inc(sG.h, 16)
            g.dma_start(out=browb[0:1, 1, :], in_=boI.unsqueeze(0)).then_inc(sG.h, 16)
            g.dma_start(out=browb[0:1, 2, :], in_=gI.unsqueeze(0)).then_inc(sG.h, 16)
            g.dma_start(out=browb[0:1, 3, :], in_=beI.unsqueeze(0)).then_inc(sG.h, 16)
            sG.bump(4)
        mw_sg1 = sG.n

        @block.vector
        def _(v):
            v.drain()
            v.wait_ge(cB.h, ebar)
            for i in range(12):
                qW.wait(v, i, 16 * (i // 2 + 1))
                if i < 4:
                    cc = v.tensor_copy(wqb[:, i, :], wfs[i % 2][:, 0:512])
                else:
                    hh = i - 4
                    cc = v.tensor_copy(wob[0:64, hh, :], wfs[i % 2][0:64, 0:512])
                cG.inc(cc)
            v.wait_ge(sQ.h, mw_q1)
            v.tensor_scalar(bqkc[:, :, 0:1], bqkc[:, :, 0:1], 0.125, None, ALU.mult)
            v.memset(V1[:, :, 64:520:65], 1.0)
            v.drain().then_inc(cV.h, 1)
            cV.bump()
        mw_g1 = cG.n
        mw_v1 = cV.n

        @block.tensor
        def _(t):
            t.wait_ge(cB.h, ebar)
            t.wait_ge(sG.h, mw_sg1)
            t.matmul(PB[0][:, :], ones1b[0:1, 0:128], browb[0:1, 1, :], start=True, stop=True)
            t.matmul(PB[1][:, :], ones1b[0:1, 0:128], browb[0:1, 2, :], start=True, stop=True)
            mm = t.matmul(PB[2][:, :], ones1b[0:1, 0:128], browb[0:1, 3, :], start=True, stop=True)
            cP.inc(mm)
        mw_p1 = cP.n

        @block.vector
        def _(v):
            v.wait_ge(cP.h, mw_p1)
            v.tensor_copy(bcB, PB[0][:, :])
            v.tensor_copy(bcG, PB[1][:, :])
            cc = v.tensor_copy(bcE, PB[2][:, :])
            cV.inc(cc)
        mw_v2 = cV.n

        kv_p0, kv_v0, kv_a0, kv_g0 = cP.n, cV.n, cA.n, cG.n
        NVI = 5 if li == 0 else 4
        for ts in range(NSL):
            if li == 0:
                @block.vector
                def _(v):
                    v.drain()
                    if ts == 0:
                        v.wait_ge(cV.h, mw_v2)
                    if ts >= 1:
                        v.wait_ge(tT.s[0].h, 256 * ts)
                    for tc in range(4):
                        c = 4 * ts + tc
                        cc = v.tensor_scalar(tmsl[0][:, tc, :], XBF[:, c, 0:512],
                                             ln0[:, c, 0:1], ln0[:, c, 1:2],
                                             ALU.subtract, ALU.mult)
                    cV.inc(cc)
            else:
                @block.sync
                def _(s):
                    if ts >= 1:
                        s.wait_ge(tT.s[0].h, 256 * ts)
                    for tc in range(4):
                        r0 = 512 * ts + 128 * tc
                        qM.s[0].inc(s.dma_start(out=tmsl[0][:, tc, :],
                                                in_=xshsp[r0:r0 + 128, :]))

            @block.sync
            def _(s):
                for cin in range(4):
                    gidx = 4 * ts + cin
                    if gidx >= 2:
                        s.wait_ge(cKV.h, gidx - 1)
                    else:
                        s.wait_ge(cG.h, mw_g1)
                    d1 = s.dma_start(out=wfs[gidx % 2][:, 0:512],
                                     in_=wki[128 * cin:128 * (cin + 1), :])
                    qK.inc(d1, gidx)
                    d2 = s.dma_start(out=wfs[gidx % 2][:, 512:1024],
                                     in_=wvi[128 * cin:128 * (cin + 1), :])
                    qK.inc(d2, gidx)

            def kv_cast(sc, cins):
                for cin in cins:
                    gidx = 4 * ts + cin
                    qK.wait(sc, gidx, 32 * (gidx // 2 + 1))
                    if gidx >= 2:
                        ts_, cin_ = (gidx - 2) // 4, (gidx - 2) % 4
                        sc.wait_ge(cP.h, kv_p0 + 8 * ts_ + cin_ + 1)
                    a = sc.activation(wcb[gidx % 2], wfs[gidx % 2][:, 0:1024],
                                      AF.Identity, bias=0.0, scale=1.0)
                    cKV.inc(a)

            @block.scalar
            def _(sc):
                if ts == 0:
                    sc.wait_ge(cB.h, ebar)
                kv_cast(sc, (0, 1))
                if li == 0:
                    sc.wait_ge(cV.h, kv_v0 + NVI * ts + 1)
                else:
                    sc.wait_ge(qM.s[0].h, 64 * (ts + 1))
                if ts >= 1:
                    sc.wait_ge(cP.h, kv_p0 + 8 * ts)
                for tc in range(4):
                    for k in range(4):
                        tT.s[0].inc(sc.dma_start_transpose(
                            tsl[0][:, k, 128 * tc:128 * (tc + 1)],
                            tmsl[0][:, tc, 128 * k:128 * (k + 1)]))
                kv_cast(sc, (2, 3))

            @block.tensor
            def _(t):
                t.wait_ge(tT.s[0].h, 256 * (ts + 1))
                if ts == 0:
                    t.wait_ge(cG.h, mw_g1)
                    t.wait_ge(cV.h, mw_v2)
                if ts >= 1:
                    t.wait_ge(cA.h, kv_a0 + 4 * ts)
                    t.wait_ge(cV.h, kv_v0 + NVI * ts)
                for cin in range(4):
                    gidx = 4 * ts + cin
                    t.wait_ge(cKV.h, gidx + 1)
                    for kc in range(4):
                        t.matmul(PSQ[:, 512 * kc:512 * (kc + 1)],
                                 wcb[gidx % 2][:, 128 * kc:128 * (kc + 1)],
                                 tsl[0][:, cin, :],
                                 start=(cin == 0), stop=(cin == 3))
                    for tc in range(4):
                        mm = t.matmul(PB[tc][:, :],
                                      tsl[0][:, cin, 128 * tc:128 * (tc + 1)],
                                      wcb[gidx % 2][:, 512:1024],
                                      start=(cin == 0), stop=False)
                    cP.inc(mm)
                for tc in range(4):
                    mm = t.matmul(PB[tc][:, :], ones1b[0:1, 0:128], browb[0:1, 0, :],
                                  start=False, stop=True)
                    cP.inc(mm)

            @block.scalar
            def _(sc):
                for kc in range(4):
                    sc.wait_ge(cP.h, kv_p0 + 8 * ts + 4)
                    a = sc.activation(K1T[:, kc, 512 * ts:512 * (ts + 1)],
                                      PSQ[:, 512 * kc:512 * (kc + 1)], AF.Identity,
                                      bias=bqkc[:, kc, 1:2])
                    cA.inc(a)

            @block.vector
            def _(v):
                v.drain()
                for tc in range(4):
                    v.wait_ge(cP.h, kv_p0 + 8 * ts + 4 + tc + 1)
                    vv = V1[:, 4 * ts + tc, 0:520].rearrange("p (h w) -> p h w", h=8, w=65)
                    cc = v.tensor_copy(vv[:, :, 0:64],
                                       PB[tc][:, :].rearrange("p (h w) -> p h w", h=8, w=64))
                    cV.inc(cc)
        kv_p1, kv_v1, kv_a1 = cP.n, cV.n, cA.n

        at_p0, at_v0, at_a0 = cP.n, cV.n, cA.n
        at_g0 = cG.n
        VQ0 = 1 if li == 0 else 0
        VQ = VQ0 + 48 + 8
        PQS = 184
        AQS = 140
        for qs in range(NSL):
            pq = at_p0 + PQS * qs
            vq = at_v0 + VQ * qs
            aqb = at_a0 + AQS * qs
            vas = vq + VQ0 + 48

            if li == 0:
                @block.vector
                def _(v):
                    v.drain()
                    if qs == 0:
                        v.wait_ge(tT.s[0].h, 256 * NSL)
                    if qs >= 1:
                        v.wait_ge(tQ.h, 256 * qs)
                    for tc in range(4):
                        c = 4 * qs + tc
                        cc = v.tensor_scalar(tmsl[0][:, tc, :], XBF[:, c, 0:512],
                                             ln0[:, c, 0:1], ln0[:, c, 1:2],
                                             ALU.subtract, ALU.mult)
                    cV.inc(cc)
            else:
                @block.sync
                def _(s):
                    s.wait_ge(cV.h, vq)
                    s.wait_ge(tQ.h, 256 * qs)
                    if qs == 0:
                        s.wait_ge(tT.s[0].h, 256 * NSL)
                    for tc in range(4):
                        r0 = 512 * qs + 128 * tc
                        qM.s[0].inc(s.dma_start(out=tmsl[0][:, tc, :],
                                                in_=xshsp[r0:r0 + 128, :]))

            @block.scalar
            def _(sc):
                if li == 0:
                    sc.wait_ge(cV.h, vq + 1)
                else:
                    sc.wait_ge(qM.s[0].h, 64 * (NSL + qs + 1))
                sc.wait_ge(cP.h, pq)
                for tc in range(4):
                    for k in range(4):
                        sc.dma_start_transpose(
                            tsl[0][:, k, 128 * tc:128 * (tc + 1)],
                            tmsl[0][:, tc, 128 * k:128 * (k + 1)]).then_inc(tQ.h, 16)
                        tQ.bump()

            @block.tensor
            def _(t):
                t.wait_ge(tQ.h, 256 * (qs + 1))
                if qs == 0:
                    t.wait_ge(cP.h, kv_p1)
                    t.wait_ge(cA.h, kv_a1)
                    t.wait_ge(cV.h, kv_v1)
                for h in range(H):
                    ph = pq + 23 * h
                    ah = aqb + 17 * h
                    vh = vq + VQ0 + 6 * h
                    bp = 64 * (h % 2)
                    hc = h // 2
                    for cin in range(4):
                        mm = t.matmul(PB[1][bp:bp + 64, :],
                                      wqb[:, cin, 64 * h:64 * h + 64],
                                      tsl[0][:, cin, :],
                                      start=(cin == 0), stop=(cin == 3))
                    cP.inc(mm)
                    for d2 in range(16):
                        dp = d2 % 2
                        if d2 >= 2:
                            t.wait_ge(cA.h, ah + d2)
                        if d2 == 0:
                            t.wait_ge(cA.h, ah + 1)
                        for j in range(2):
                            kt = 2 * d2 + j
                            mm = t.matmul(PSQ[:, 1024 * dp + 512 * j:1024 * dp + 512 * (j + 1)],
                                          K1T[bp:bp + 64, hc, 128 * kt:128 * (kt + 1)],
                                          qtt[bp:bp + 64, :], start=True, stop=True)
                        if d2 >= 1:
                            t.wait_ge(cA.h, ah + 2 + (d2 - 1))
                            if d2 == 1:
                                if h >= 1:
                                    t.wait_ge(cV.h, vh - 6 + 1)
                                elif qs >= 1:
                                    t.wait_ge(cV.h, vq - VQ + VQ0 + 6 * 7 + 1)
                            for j in range(2):
                                kt = 2 * (d2 - 1) + j
                                mm = t.matmul(PB[0][0:65, :], V1[:, kt, 65 * h:65 * h + 65],
                                              aq[(d2 - 1) % 2][:, 512 * j:512 * (j + 1)],
                                              start=(kt == 0), stop=False)
                        cP.inc(mm)
                    t.wait_ge(cA.h, ah + 2 + 15)
                    for j in range(2):
                        kt = 30 + j
                        mm = t.matmul(PB[0][0:65, :], V1[:, kt, 65 * h:65 * h + 65],
                                      aq[1][:, 512 * j:512 * (j + 1)],
                                      start=False, stop=(j == 1))
                    cP.inc(mm)
                    t.wait_ge(cV.h, vh + 1)
                    for tc in range(4):
                        mm = t.transpose(PB[2][:, tc:tc + 1],
                                         denA[64:65, 128 * tc:128 * (tc + 1)],
                                         ident[64:65, 64:65])
                    cP.inc(mm)
                    for tc in range(4):
                        if tc == 1:
                            t.wait_ge(cV.h, vh + 2)
                        elif tc >= 2:
                            t.wait_ge(cV.h, vh + 1 + tc)
                        bank = PB[3] if tc % 2 == 0 else PB[2]
                        mm = t.matmul(bank[:, :], ohb[0:64, 128 * tc:128 * (tc + 1)],
                                      wob[0:64, h, :], start=True, stop=True)
                        cP.inc(mm)

            @block.scalar
            def _(sc):
                for h in range(H):
                    ph = pq + 23 * h
                    ah = aqb + 17 * h
                    bp = 64 * (h % 2)
                    sc.wait_ge(cP.h, ph + 1)
                    a = sc.activation(qtt[bp:bp + 64, :], PB[1][bp:bp + 64, :], AF.Identity,
                                      bias=bqkc[bp:bp + 64, h // 2, 0:1], scale=0.125)
                    cA.inc(a)
                    for d2 in range(16):
                        dp = d2 % 2
                        sc.wait_ge(cP.h, ph + 2 + d2)
                        a = sc.activation(aq[dp], PSQ[:, 1024 * dp:1024 * (dp + 1)],
                                          AF.Exp)
                        cA.inc(a)
                for tc in range(4):
                    sc.wait_ge(cV.h, vas + 2 * tc + 1)
                    a = sc.activation(stat[:, 16:17], stat[:, 13:14], AF.Sqrt,
                                      bias=epscol[:, :])
                    cA.inc(a)

            @block.vector
            def _(v):
                v.drain()
                for h in range(H):
                    ph = pq + 23 * h
                    vh = vq + VQ0 + 6 * h
                    v.wait_ge(cP.h, ph + 18)
                    v.tensor_copy(ohb[0:64, :], PB[0][0:64, :])
                    cc = v.tensor_copy(denA[64:65, :], PB[0][64:65, :])
                    cV.inc(cc)
                    v.wait_ge(cP.h, ph + 19)
                    cc = v.reciprocal(rcq[:, 0:4], PB[2][:, 0:4])
                    cV.inc(cc)
                    v.drain()
                    for tc in range(4):
                        v.wait_ge(cP.h, ph + 20 + tc)
                        bank = PB[3] if tc % 2 == 0 else PB[2]
                        if h == 0:
                            cc = v.tensor_scalar(oacc[:, tc, :], bank[:, :],
                                                 rcq[:, tc:tc + 1], None, ALU.mult)
                        else:
                            cc = v.scalar_tensor_tensor(oacc[:, tc, :], bank[:, :],
                                                        rcq[:, tc:tc + 1], oacc[:, tc, :],
                                                        ALU.mult, ALU.add)
                        cV.inc(cc)
                for tc in range(4):
                    i = 4 * qs + tc
                    if i >= 2:
                        qO.wait(v, i, 16 * ((i - 2) // 2 + 1))
                    v.drain()
                    v.tensor_tensor(sfp, tmsl[0][:, tc, :], oacc[:, tc, :], ALU.add)
                    v.drain()
                    v.tensor_tensor(sfp, sfp, bcB, ALU.add)
                    v.drain()
                    v.bn_stats(stat[:, 0:6], sfp)
                    v.drain()
                    v.bn_aggr(stat[:, 12:14], stat[:, 0:6])
                    v.drain().then_inc(cV.h, 1)
                    cV.bump()
                    v.wait_ge(cA.h, aqb + 136 + tc + 1)
                    v.reciprocal(stat[:, 17:18], stat[:, 16:17])
                    v.drain()
                    v.tensor_scalar(sfp, sfp, stat[:, 12:13], stat[:, 17:18],
                                    ALU.subtract, ALU.mult)
                    v.drain()
                    v.tensor_tensor(sfp, sfp, bcG, ALU.mult)
                    v.drain()
                    cc = v.tensor_tensor(stg[tc % 2], sfp, bcE, ALU.add)
                    cV.inc(cc)

            @block.sync
            def _(s):
                for tc in range(4):
                    i = 4 * qs + tc
                    s.wait_ge(cV.h, vas + 2 * tc + 2)
                    r0 = 512 * qs + 128 * tc
                    qO.inc(s.dma_start(out=dst_sp[r0:r0 + 128, :], in_=stg[tc % 2]), i)

        at_p1, at_v1, at_a1 = cP.n, cV.n, cA.n
        assert at_p1 == at_p0 + PQS * NSL, (at_p1, at_p0)
        assert at_v1 == at_v0 + VQ * NSL, (at_v1, at_v0)
        assert at_a1 == at_a0 + AQS * NSL, (at_a1, at_a0)
        bfa.pop()
        fpa.pop()
        return qO

    def _stopped(name):
        return stop_after is not None and name == stop_after

    qO1 = emit_mha(0, x1sp)
    if not _stopped("mha0"):
        qO2 = emit_mha(1, x2sp)


    def emit_ffn0():
        bfa.push()
        fpa.push()
        ytm = bfa.alloc([128, 8, 512])
        ytr = bfa.alloc([128, 8, 512])
        hsl = bfa.alloc([128, 32, 512])
        w1b = [bfa.alloc([128, 512]) for _ in range(4)]
        w2b = [bfa.alloc([128, 1024]) for _ in range(4)]
        brw = bfa.alloc([1, 3, 1024])
        bf1c = fpa.alloc([128, 32, 1])
        w1f = [fpa.alloc([128, 512]) for _ in range(2)]
        w2f = [fpa.alloc([128, 1024]) for _ in range(2)]
        bcG2 = fpa.alloc([128, 1024])
        bcE2 = fpa.alloc([128, 1024])
        sfp = fpa.alloc([128, 1024])
        stat = fpa.alloc([128, 20])

        qY = PSem(es, nc, "qY")
        qW1 = PSem(es, nc, "qF1")
        qW2 = PSem(es, nc, "qF2")
        tY = Sem(es, nc, "tY", 16)
        cF = Sem(es, nc, "cF")
        cW = Sem(es, nc, "cW")

        ebar = phase_barrier()

        @block.gpsimd
        def _(g):
            g.wait_ge(cB.h, ebar)
            g.dma_start(out=brw[0:1, 0, :], in_=bff2_in.unsqueeze(0)).then_inc(sG.h, 16)
            g.dma_start(out=brw[0:1, 1, :], in_=gff_in.unsqueeze(0)).then_inc(sG.h, 16)
            g.dma_start(out=brw[0:1, 2, :], in_=beff_in.unsqueeze(0)).then_inc(sG.h, 16)
            sG.bump(3)
        f_sg1 = sG.n

        @block.sync
        def _(s):
            s.wait_ge(cB.h, ebar)
            for c in range(32):
                s.dma_start(out=bf1c[:, c, :],
                            in_=bff1_in[128 * c:128 * (c + 1)].unsqueeze(1)).then_inc(sQ.h, 16)
            sQ.bump(32)
        f_q1 = sQ.n

        @block.tensor
        def _(t):
            t.wait_ge(cB.h, ebar)
            t.wait_ge(sG.h, f_sg1)
            t.matmul(PB[0][:, :], ones1b[0:1, 0:128], brw[0:1, 1, 0:512], start=True, stop=True)
            t.matmul(PB[1][:, :], ones1b[0:1, 0:128], brw[0:1, 1, 512:1024], start=True, stop=True)
            t.matmul(PB[2][:, :], ones1b[0:1, 0:128], brw[0:1, 2, 0:512], start=True, stop=True)
            mm = t.matmul(PB[3][:, :], ones1b[0:1, 0:128], brw[0:1, 2, 512:1024], start=True, stop=True)
            cP.inc(mm)
        f_p1 = cP.n

        @block.vector
        def _(v):
            v.drain()
            v.wait_ge(cP.h, f_p1)
            v.tensor_copy(bcG2[:, 0:512], PB[0][:, :])
            v.tensor_copy(bcG2[:, 512:1024], PB[1][:, :])
            v.tensor_copy(bcE2[:, 0:512], PB[2][:, :])
            cc = v.tensor_copy(bcE2[:, 512:1024], PB[3][:, :])
            cV.inc(cc)
        f_v1 = cV.n

        TP, TV, TA, TG = 97, 40, 4, 96
        f_p2, f_v2, f_a2, f_g2 = cP.n, cV.n, cA.n, cG.n
        for tsb in range(NSL):
            tp = f_p2 + TP * tsb
            tv = f_v2 + TV * tsb
            ta = f_a2 + TA * tsb
            tg = f_g2 + TG * tsb

            @block.sync
            def _(s):
                if tsb == 0:
                    s.wait_ge(cV.h, f_v1)
                else:
                    s.wait_ge(tY.h, 512 * tsb)
                    s.wait_ge(cV.h, f_v2 + TV * tsb)
                for tc in range(4):
                    r0 = 512 * tsb + 128 * tc
                    qY.inc(s.dma_start(out=ytm[:, tc, :], in_=x1sp[r0:r0 + 128, :]), tsb)
                    qY.inc(s.dma_start(out=ytm[:, 4 + tc, :], in_=x2sp[r0:r0 + 128, :]), tsb)
                for hg in range(8):
                    for cin in range(8):
                        gg = 64 * tsb + 8 * hg + cin
                        if gg >= 2:
                            s.wait_ge(cF.h, gg - 1)
                        qW1.inc(s.dma_start(
                            out=w1f[gg % 2],
                            in_=wff1_in[128 * cin:128 * (cin + 1), 512 * hg:512 * (hg + 1)]), gg)
                for hc in range(32):
                    g2 = 32 * tsb + hc
                    if g2 >= 2:
                        s.wait_ge(cW.h, g2 - 1)
                    qW2.inc(s.dma_start(out=w2f[g2 % 2],
                                        in_=wff2_in[128 * hc:128 * (hc + 1), :]), g2)

            @block.scalar
            def _(sc):
                qY.wait(sc, tsb, 128 * (tsb // 2 + 1))
                sc.wait_ge(cP.h, f_p2 + TP * tsb)
                for tc in range(4):
                    for k in range(4):
                        sc.dma_start_transpose(
                            ytr[:, k, 128 * tc:128 * (tc + 1)],
                            ytm[:, tc, 128 * k:128 * (k + 1)]).then_inc(tY.h, 16)
                        tY.bump()
                    for k in range(4):
                        sc.dma_start_transpose(
                            ytr[:, 4 + k, 128 * tc:128 * (tc + 1)],
                            ytm[:, 4 + tc, 128 * k:128 * (k + 1)]).then_inc(tY.h, 16)
                        tY.bump()
                for hg in range(8):
                    for cin in range(8):
                        gg = 64 * tsb + 8 * hg + cin
                        qW1.wait(sc, gg, 16 * (gg // 2 + 1))
                        if gg >= 4:
                            gge = gg - 4
                            sc.wait_ge(cP.h, f_p2 + TP * (gge // 64) + (gge % 64) + 1)
                        a = sc.activation(w1b[gg % 4], w1f[gg % 2],
                                          AF.Identity, bias=0.0, scale=1.0)
                        cF.inc(a)
                for tc in range(4):
                    sc.wait_ge(cV.h, tv + 32 + 2 * tc + 1)
                    a = sc.activation(stat[:, 16:17], stat[:, 13:14], AF.Sqrt,
                                      bias=epscol[:, :])
                    cA.inc(a)

            @block.tensor
            def _(t):
                t.wait_ge(tY.h, 512 * (tsb + 1))
                if tsb >= 1:
                    t.wait_ge(cV.h, f_v2 + TV * tsb)
                for hg in range(8):
                    if hg >= 1:
                        t.wait_ge(cV.h, tv + 4 * hg)
                    for cin in range(8):
                        gg = 64 * tsb + 8 * hg + cin
                        t.wait_ge(cF.h, gg + 1)
                        for j in range(4):
                            mm = t.matmul(PSQ[:, 512 * j:512 * (j + 1)],
                                          w1b[gg % 4][:, 128 * j:128 * (j + 1)],
                                          ytr[:, cin, :],
                                          start=(cin == 0), stop=(cin == 7))
                        cP.inc(mm)
                t.wait_ge(cV.h, tv + 32)
                for hc in range(32):
                    g2 = 32 * tsb + hc
                    t.wait_ge(cW.h, g2 + 1)
                    for tc in range(4):
                        for ch in range(2):
                            bank = PB[2 * (tc % 2) + ch] if tc < 2 else PSQ[:, 512 * (2 * (tc - 2) + ch):512 * (2 * (tc - 2) + ch + 1)]
                            dst = bank[:, :] if tc < 2 else bank
                            mm = t.matmul(dst, hsl[:, hc, 128 * tc:128 * (tc + 1)],
                                          w2b[g2 % 4][:, 512 * ch:512 * (ch + 1)],
                                          start=(hc == 0), stop=False)
                    cP.inc(mm)
                for tc in range(4):
                    for ch in range(2):
                        bank = PB[2 * (tc % 2) + ch] if tc < 2 else PSQ[:, 512 * (2 * (tc - 2) + ch):512 * (2 * (tc - 2) + ch + 1)]
                        dst = bank[:, :] if tc < 2 else bank
                        mm = t.matmul(dst, ones1b[0:1, 0:128],
                                      brw[0:1, 0, 512 * ch:512 * (ch + 1)],
                                      start=False, stop=True)
                cP.inc(mm)

            @block.vector
            def _(v):
                v.drain()
                if tsb == 0:
                    v.wait_ge(sQ.h, f_q1)
                for hg in range(8):
                    for j in range(4):
                        v.wait_ge(cP.h, tp + 8 * hg + 8)
                        cc = v.tensor_scalar(hsl[:, 4 * hg + j, :], PSQ[:, 512 * j:512 * (j + 1)],
                                             bf1c[:, 4 * hg + j, :], 0.0, ALU.add, ALU.max)
                        cV.inc(cc)
                for hc in range(32):
                    g2 = 32 * tsb + hc
                    qW2.wait(v, g2, 16 * (g2 // 2 + 1))
                    if g2 >= 4:
                        g2e = g2 - 4
                        v.wait_ge(cP.h, f_p2 + TP * (g2e // 32) + 64 + (g2e % 32) + 1)
                    cc = v.tensor_copy(w2b[g2 % 4], w2f[g2 % 2])
                    cW.inc(cc)
                v.drain()
                v.wait_ge(cP.h, tp + TP)
                for tc in range(4):
                    if tc < 2:
                        b0, b1 = PB[2 * tc][:, :], PB[2 * tc + 1][:, :]
                    else:
                        b0 = PSQ[:, 512 * (2 * (tc - 2)):512 * (2 * (tc - 2) + 1)]
                        b1 = PSQ[:, 512 * (2 * (tc - 2) + 1):512 * (2 * (tc - 2) + 2)]
                    v.drain()
                    v.tensor_tensor(sfp[:, 0:512], ytm[:, tc, :], b0, ALU.add)
                    v.tensor_tensor(sfp[:, 512:1024], ytm[:, 4 + tc, :], b1, ALU.add)
                    v.drain()
                    v.bn_stats(stat[:, 0:6], sfp[:, 0:512])
                    v.bn_stats(stat[:, 6:12], sfp[:, 512:1024])
                    v.drain()
                    v.bn_aggr(stat[:, 12:14], stat[:, 0:12])
                    v.drain().then_inc(cV.h, 1)
                    cV.bump()
                    v.wait_ge(cA.h, ta + tc + 1)
                    v.reciprocal(stat[:, 17:18], stat[:, 16:17])
                    v.drain()
                    v.tensor_scalar(sfp, sfp, stat[:, 12:13], stat[:, 17:18],
                                    ALU.subtract, ALU.mult)
                    v.drain()
                    v.tensor_tensor(sfp, sfp, bcG2, ALU.mult)
                    v.drain()
                    v.tensor_tensor(sfp, sfp, bcE2, ALU.add)
                    v.drain()
                    cc = v.tensor_tensor(XBF[:, 4 * tsb + tc, :], XBF[:, 4 * tsb + tc, :],
                                         sfp, ALU.add)
                    cV.inc(cc)
        bfa.pop()
        fpa.pop()
        assert cP.n == f_p2 + TP * NSL
        assert cV.n == f_v2 + TV * NSL
        assert cA.n == f_a2 + TA * NSL

    if stop_after not in ("mha0", "mha1"):
        emit_ffn0()

    def emit_rwkv():
        bfa.push()
        fpa.push()
        H1T = bfa.alloc([128, 8, S])
        slA = bfa.alloc([128, 8, 512])
        slC = bfa.alloc([128, 8, 512])
        slD = bfa.alloc([128, 8, 512])
        h1tm = [bfa.alloc([128, 1024]) for _ in range(2)]
        wchb = [bfa.alloc([128, 1024]) for _ in range(2)]
        otm = [bfa.alloc([128, 1024]) for _ in range(2)]
        wchf = [fpa.alloc([128, 1024]) for _ in range(2)]
        muc = fpa.alloc([128, 8, 4])
        sfp = fpa.alloc([128, 1024])
        stat = fpa.alloc([128, 20])

        qWc = PSem(es, nc, "qWc")
        cR = Sem(es, nc, "cR")
        tH = Sem(es, nc, "tH", 16)
        tO = PSem(es, nc, "tO")

        ebar = phase_barrier(taps=[("xbfA5", bfa.at(0, [128, 32768]))])

        @block.sync
        def _(s):
            s.wait_ge(cB.h, ebar)
            for c in range(8):
                s.dma_start(out=muc[:, c, 0:1],
                            in_=mur_in[128 * c:128 * (c + 1)].unsqueeze(1)).then_inc(sQ.h, 16)
                s.dma_start(out=muc[:, c, 2:3],
                            in_=muk_in[128 * c:128 * (c + 1)].unsqueeze(1)).then_inc(sQ.h, 16)
            sQ.bump(16)
        r_q1 = sQ.n

        @block.vector
        def _(v):
            v.drain()
            v.wait_ge(cB.h, ebar)
            v.wait_ge(sQ.h, r_q1)
            v.tensor_scalar(muc[:, :, 1:2], muc[:, :, 0:1], -1.0, 1.0, ALU.mult, ALU.add)
            cc = v.tensor_scalar(muc[:, :, 3:4], muc[:, :, 2:3], -1.0, 1.0, ALU.mult, ALU.add)
            cV.inc(cc)
        r_v1 = cV.n

        r_v2, r_a2, r_g2 = cV.n, cA.n, cG.n
        @block.vector
        def _(v):
            v.drain()
            for c in range(NCH):
                if c >= 2:
                    v.wait_ge(tH.h, 128 * (c - 1))
                v.bn_stats(stat[:, 0:6], XBF[:, c, 0:512])
                v.bn_stats(stat[:, 6:12], XBF[:, c, 512:1024])
                v.drain()
                v.bn_aggr(stat[:, 12:14], stat[:, 0:12])
                v.drain().then_inc(cV.h, 1)
                cV.bump()
                v.wait_ge(cA.h, r_a2 + c + 1)
                v.reciprocal(stat[:, 17:18], stat[:, 16:17])
                v.drain()
                cc = v.tensor_scalar(h1tm[c % 2], XBF[:, c, :], stat[:, 12:13],
                                     stat[:, 17:18], ALU.subtract, ALU.mult)
                cV.inc(cc)
        r_v3 = cV.n

        @block.scalar
        def _(sc):
            for c in range(NCH):
                sc.wait_ge(cV.h, r_v2 + 2 * c + 1)
                a = sc.activation(stat[:, 16:17], stat[:, 13:14], AF.Sqrt,
                                  bias=epscol[:, :])
                cA.inc(a)
                if c >= 1:
                    sc.wait_ge(cV.h, r_v2 + 2 * c)
                    for k in range(8):
                        sc.dma_start_transpose(
                            H1T[:, k, 128 * (c - 1):128 * c],
                            h1tm[(c - 1) % 2][:, 128 * k:128 * (k + 1)]).then_inc(tH.h, 16)
                        tH.bump()
            sc.wait_ge(cV.h, r_v2 + 2 * NCH)
            for k in range(8):
                sc.dma_start_transpose(
                    H1T[:, k, 128 * (NCH - 1):128 * NCH],
                    h1tm[(NCH - 1) % 2][:, 128 * k:128 * (k + 1)]).then_inc(tH.h, 16)
                tH.bump()
        r_th = tH.n

        TPr = 24
        NVX = 22
        r_p4, r_v4, r_a4, r_g4 = cP.n, cV.n, cA.n, cG.n
        for ts in range(NSL):
            tp = r_p4 + TPr * ts
            tv = r_v4 + NVX * ts
            ta = r_a4 + 16 * ts

            @block.sync
            def _(s):
                for i3 in range(24):
                    gg = 24 * ts + i3
                    if gg >= 2:
                        s.wait_ge(cR.h, gg - 1)
                    w_ = (wr_in, wkk_in, wvv_in)[i3 // 8]
                    cin = i3 % 8
                    qWc.inc(s.dma_start(out=wchf[gg % 2],
                                        in_=w_[128 * cin:128 * (cin + 1), :]), gg)

            def cast_grp(sc, ph):
                for cin in range(8):
                    i3 = 8 * ph + cin
                    gg = 24 * ts + i3
                    qWc.wait(sc, gg, 16 * (gg // 2 + 1))
                    if gg >= 2:
                        gge = gg - 2
                        sc.wait_ge(cP.h, r_p4 + TPr * (gge // 24) + (gge % 24) + 1)
                    a = sc.activation(wchb[gg % 2], wchf[gg % 2],
                                      AF.Identity, bias=0.0, scale=1.0)
                    cR.inc(a)

            def mix(v, mu_i, mu1_i):
                for c in range(8):
                    if ts == 0:
                        v.tensor_scalar(slA[:, c, :], H1T[:, c, 0:512],
                                        muc[:, c, mu_i:mu_i + 1], None, ALU.mult)
                        v.drain()
                        cc = v.scalar_tensor_tensor(slA[:, c, 1:512], H1T[:, c, 0:511],
                                                    muc[:, c, mu1_i:mu1_i + 1],
                                                    slA[:, c, 1:512], ALU.mult, ALU.add)
                    else:
                        v.tensor_scalar(slA[:, c, :], H1T[:, c, 512 * ts:512 * (ts + 1)],
                                        muc[:, c, mu_i:mu_i + 1], None, ALU.mult)
                        v.drain()
                        cc = v.scalar_tensor_tensor(slA[:, c, :],
                                                    H1T[:, c, 512 * ts - 1:512 * (ts + 1) - 1],
                                                    muc[:, c, mu1_i:mu1_i + 1],
                                                    slA[:, c, :], ALU.mult, ALU.add)
                v.drain().then_inc(cV.h, 1)
                cV.bump()

            @block.vector
            def _(v):
                v.drain()
                if ts == 0:
                    v.wait_ge(tH.h, r_th)
                    v.wait_ge(cV.h, r_v1)
                else:
                    v.wait_ge(cP.h, tp)
                    v.wait_ge(tO.s[0].h, 256 * ts)
                    v.wait_ge(tO.s[1].h, 256 * ts)
                mix(v, 0, 1)
                v.wait_ge(cP.h, tp + 8)
                mix(v, 2, 3)
                for co in range(8):
                    v.wait_ge(cP.h, tp + 16)
                    bank = PB[co][:, :] if co < 4 else PSQ[:, 512 * (co - 4):512 * (co - 3)]
                    cc = v.tensor_scalar(slC[:, co, :], bank, 0.0, None, ALU.max)
                    cV.inc(cc)
                v.wait_ge(cP.h, tp + 24)
                v.wait_ge(cA.h, ta + 16)
                for co in range(8):
                    bank = PB[co][:, :] if co < 4 else PSQ[:, 512 * (co - 4):512 * (co - 3)]
                    cc = v.tensor_tensor(slA[:, co, :], slD[:, co, :], bank, ALU.mult)
                    cV.inc(cc)
                for tc in range(4):
                    v.wait_ge(tO.s[tc % 2].h, 256 * ts + 128 * (tc // 2 + 1))
                    v.drain()
                    cc = v.tensor_tensor(XBF[:, 4 * ts + tc, :], XBF[:, 4 * ts + tc, :],
                                         otm[tc % 2], ALU.add)
                    cV.inc(cc)

            @block.tensor
            def _(t):
                if ts == 0:
                    t.wait_ge(tH.h, r_th)
                for ph in range(3):
                    if ph == 0:
                        t.wait_ge(cV.h, tv + 1)
                    elif ph == 1:
                        t.wait_ge(cV.h, tv + 2)
                        t.wait_ge(cA.h, ta + 8)
                    else:
                        t.wait_ge(cV.h, tv + 10)
                        t.wait_ge(cA.h, ta + 16)
                    sl_ = slC if ph == 2 else slA
                    for cin in range(8):
                        gg = 24 * ts + 8 * ph + cin
                        t.wait_ge(cR.h, gg + 1)
                        for co in range(8):
                            bank = PB[co][:, :] if co < 4 else PSQ[:, 512 * (co - 4):512 * (co - 3)]
                            mm = t.matmul(bank, wchb[gg % 2][:, 128 * co:128 * (co + 1)],
                                          sl_[:, cin, :], start=(cin == 0), stop=(cin == 7))
                        cP.inc(mm)

            @block.scalar
            def _(sc):
                cast_grp(sc, 0)
                for co in range(8):
                    sc.wait_ge(cP.h, tp + 8)
                    bank = PB[co][:, :] if co < 4 else PSQ[:, 512 * (co - 4):512 * (co - 3)]
                    a = sc.activation(slD[:, co, :], bank, AF.Sigmoid)
                    cA.inc(a)
                cast_grp(sc, 1)
                for co in range(8):
                    sc.wait_ge(cV.h, tv + 2 + co + 1)
                    a = sc.activation(slC[:, co, :], slC[:, co, :], AF.Square)
                    cA.inc(a)
                cast_grp(sc, 2)
                for tc in range(4):
                    sc.wait_ge(cV.h, tv + 18)
                    if tc >= 2:
                        sc.wait_ge(cV.h, tv + 18 + tc - 1)
                    for k in range(8):
                        sc.dma_start_transpose(
                            otm[tc % 2][:, 128 * k:128 * (k + 1)],
                            slA[:, k, 128 * tc:128 * (tc + 1)]).then_inc(tO.s[tc % 2].h, 16)
                        tO.s[tc % 2].bump()
        bfa.pop()
        fpa.pop()

    if stop_after not in ("mha0", "mha1", "ffn0"):
        emit_rwkv()


    def emit_fft():
        bfa.push()
        fpa.push()
        Cb = bfa.alloc([128, 8, 1024])
        Sb = bfa.alloc([128, 8, 1024])
        h2tm = [bfa.alloc([128, 1024]) for _ in range(2)]
        h2T = bfa.alloc([128, 8, 512])
        asl = bfa.alloc([128, 8, 512])
        bsl = bfa.alloc([128, 8, 512])
        x1T = bfa.alloc([128, 8, 512])
        o2T = bfa.alloc([128, 8, 512])
        o2tm = [bfa.alloc([128, 1024]) for _ in range(2)]
        h3st = bfa.alloc([128, 4, 1024])
        bitsT = bfa.alloc([128, 512])
        synb = bfa.alloc([128, 512])
        corT = bfa.alloc([128, 512])
        fixb = bfa.alloc([1, 512])
        welb = bfa.alloc([128, 8, 24])
        wleb = bfa.alloc([128, 1024])
        hmtb = bfa.alloc([128, 12])
        xrg = bfa.alloc([128, 512])
        wgrb = bfa.alloc([128, 8, 16])
        wgtb = bfa.alloc([128, 8])
        zrow = fpa.alloc([128, 24])
        ccol = fpa.alloc([128, 8, 8])
        belc = fpa.alloc([128, 1])
        blec = fpa.alloc([128, 8, 1])
        rnrm = fpa.alloc([128, 1])
        logf = fpa.alloc([128, 512])
        x3f = fpa.alloc([128, 1024])
        mint = fpa.alloc([128, 512])
        stat = fpa.alloc([128, 20])
        pi2c = fpa.alloc([128, 1])
        wgmf = fpa.alloc([128, 16])
        mi32 = es.enter_context(nc.sbuf_tensor("mi32", [128, 512], mybir.dt.int32))
        krow = x3f[0:1, :]

        tH2 = PSem(es, nc, "tH2")
        tO2 = PSem(es, nc, "tO2")
        tH3 = PSem(es, nc, "tH3")
        qX3 = PSem(es, nc, "qX3")

        ebar = phase_barrier(taps=[("xbfB", bfa.at(0, [128, 32768]))])

        @block.gpsimd
        def _(g):
            g.wait_ge(cB.h, ebar)
            g.iota(krow[0:1, :], pattern=[[1, 1024]], base=0, channel_multiplier=0,
                   allow_small_or_imprecise_dtypes=True)
            for c in range(8):
                cc = g.iota(ccol[:, c, 6:7], pattern=[[0, 1]], base=128 * c,
                            channel_multiplier=1,
                            allow_small_or_imprecise_dtypes=True)
            cc.then_inc(cG.h, 1)
            cG.bump()
            g.dma_start(out=welb[:, :, :],
                        in_=wel_in.rearrange("(a p) o -> p a o", p=128)).then_inc(sG.h, 16)
            g.dma_start(out=wleb[0:24, :], in_=wle_in[:, :]).then_inc(sG.h, 16)
            g.dma_start(out=hmtb[0:24, :], in_=hmt_in[:, :]).then_inc(sG.h, 16)
            g.dma_start(out=wgrb[:, :, :],
                        in_=wgred_in.rearrange("(a p) o -> p a o", p=128)).then_inc(sG.h, 16)
            g.dma_start(out=wgtb[0:16, :], in_=wgt_in[:, :]).then_inc(sG.h, 16)
            sG.bump(5)
        c_g1 = cG.n
        c_sg1 = sG.n

        @block.sync
        def _(s):
            s.wait_ge(cB.h, ebar)
            s.dma_start(out=belc[0:24, :], in_=bel_in.unsqueeze(1)).then_inc(sQ.h, 16)
            for c in range(8):
                s.dma_start(out=blec[:, c, :],
                            in_=ble_in[128 * c:128 * (c + 1)].unsqueeze(1)).then_inc(sQ.h, 16)
            s.dma_start(out=wgmf[0:8, :], in_=wgm_in[:, :]).then_inc(sQ.h, 16)
            sQ.bump(10)
        c_q1 = sQ.n

        @block.vector
        def _(v):
            v.drain()
            v.wait_ge(cG.h, c_g1)
            v.wait_ge(sQ.h, c_q1)
            v.memset(pi2c[:, :], float(np.pi / 2))
            v.drain()
            v.tensor_scalar(pi2c[:, :], pi2c[:, :],
                            float(np.float64(np.pi / 2) - np.float64(np.float32(np.pi / 2))),
                            None, ALU.add)
            v.drain()
            for c in range(8):
                v.tensor_scalar(ccol[:, c, 1:2], ccol[:, c, 6:7], -1.0, 1024.0,
                                ALU.mult, ALU.add)
                v.drain()
                v.tensor_tensor(ccol[:, c, 0:1], ccol[:, c, 6:7], ccol[:, c, 1:2], ALU.min)
                v.drain()
                v.tensor_scalar(ccol[:, c, 0:1], ccol[:, c, 0:1], 1.0 / 1024.0, 1e-08,
                                ALU.mult, ALU.add)
            v.drain().then_inc(cV.h, 1)
            cV.bump()
        c_v1 = cV.n

        @block.scalar
        def _(sc):
            sc.wait_ge(cV.h, c_v1)
            for c in range(8):
                a = sc.activation(ccol[:, c, 2:3], ccol[:, c, 0:1], AF.Square)
                cA.inc(a)
        c_a1 = cA.n

        @block.vector
        def _(v):
            v.wait_ge(cA.h, c_a1)
            OD = (1.0 + 0.2 * 1.0) * 1.0
            for c in range(8):
                v.tensor_scalar(ccol[:, c, 3:4], ccol[:, c, 0:1], 1.5, OD,
                                ALU.mult, ALU.add)
                v.tensor_scalar(ccol[:, c, 4:5], ccol[:, c, 2:3], 0.1, OD,
                                ALU.mult, ALU.add)
                v.drain()
                v.tensor_tensor(ccol[:, c, 4:5], ccol[:, c, 4:5], ccol[:, c, 0:1],
                                ALU.subtract)
            v.drain().then_inc(cV.h, 1)
            cV.bump()
        c_v2 = cV.n

        @block.scalar
        def _(sc):
            sc.wait_ge(cV.h, c_v2)
            for c in range(8):
                sc.activation(ccol[:, c, 5:6], ccol[:, c, 3:4], AF.Sin)
                sc.drain()
                sc.activation(ccol[:, c, 3:4], ccol[:, c, 4:5], AF.Sin)
                sc.drain()
                a = sc.activation(ccol[:, c, 4:5], ccol[:, c, 4:5], AF.Sin,
                                  scale=-1.0, bias=pi2c[:, :])
                cA.inc(a)
                sc.drain()
        c_a2 = cA.n

        @block.vector
        def _(v):
            v.wait_ge(cA.h, c_a2)
            for c in range(8):
                v.tensor_tensor(ccol[:, c, 6:7], ccol[:, c, 5:6], ccol[:, c, 4:5], ALU.mult)
                v.tensor_tensor(ccol[:, c, 7:8], ccol[:, c, 5:6], ccol[:, c, 3:4], ALU.mult)
            v.drain()
            v.tensor_tensor(wgmf[0:8, :], wgmf[0:8, :], wgmf[0:8, :], ALU.mult)
            v.drain()
            v.reduce_sum(rnrm[0:8, :], wgmf[0:8, :], axis=AX.X)
            v.drain().then_inc(cV.h, 1)
            cV.bump()
        c_v3 = cV.n

        @block.scalar
        def _(sc):
            sc.wait_ge(cV.h, c_v3)
            a = sc.activation(rnrm[0:8, :], rnrm[0:8, :], AF.Sqrt)
            cA.inc(a)
        c_a3 = cA.n

        @block.vector
        def _(v):
            v.wait_ge(cA.h, c_a3)
            v.tensor_single_scalar(rnrm[0:8, :], rnrm[0:8, :], 1e-4, ALU.max)
            v.drain()
            cc = v.reciprocal(rnrm[0:8, :], rnrm[0:8, :])
            cV.inc(cc)
        c_v4 = cV.n

        c_p1 = cP.n
        @block.tensor
        def _(t):
            t.wait_ge(cG.h, c_g1)
            for i in range(16):
                kc, ns = i // 2, i % 2
                if i >= 2:
                    t.wait_ge(cA.h, c_a3 + 2 * (i - 1))
                mm = t.matmul(PB[i % 2][:, :], krow[0:1, 128 * kc:128 * (kc + 1)],
                              krow[0:1, 512 * ns:512 * (ns + 1)],
                              start=True, stop=True)
                cP.inc(mm)

        @block.vector
        def _(v):
            v.drain()
            for i in range(16):
                v.wait_ge(cP.h, c_p1 + i + 1)
                if i >= 1:
                    v.wait_ge(cA.h, c_a3 + 2 * i)
                v.tensor_copy(mi32[:, :], PB[i % 2][:, :])
                v.drain()
                v.tensor_single_scalar(mi32[:, :], mi32[:, :], 1023, ALU.bitwise_and)
                v.drain()
                v.tensor_copy(mint[:, :], mi32[:, :])
                v.drain()
                v.tensor_single_scalar(logf[:, :], mint[:, :], 511.5, ALU.is_gt)
                v.drain()
                v.scalar_tensor_tensor(mint[:, :], logf[:, :], -1024.0, mint[:, :],
                                       ALU.mult, ALU.add)
                v.drain()
                v.tensor_scalar(logf[:, :], mint[:, :], -1.0, None, ALU.mult)
                v.drain()
                cc = v.tensor_tensor(logf[:, :], logf[:, :], mint[:, :], ALU.max)
                cV.inc(cc)
        c_v5 = cV.n

        @block.scalar
        def _(sc):
            TWO_PI = float(2.0 * np.pi)
            for i in range(16):
                kc, ns = i // 2, i % 2
                sc.wait_ge(cV.h, c_v4 + i + 1)
                sc.activation(Sb[:, kc, 512 * ns:512 * (ns + 1)], mint[:, :], AF.Sin,
                              scale=TWO_PI / 1024.0)
                a = sc.activation(Cb[:, kc, 512 * ns:512 * (ns + 1)], logf[:, :], AF.Sin,
                                  scale=-TWO_PI / 1024.0, bias=pi2c[:, :])
                cA.inc(a, 2)
        c_a4 = cA.n


        latb = bfa.alloc([128, 512])
        zsc = fpa.alloc([128, 8])
        x3f2 = fpa.alloc([128, 1024])
        gsc = fpa.alloc([128, 8])
        gsc2 = fpa.alloc([128, 4])
        gsc3 = fpa.alloc([128, 16])

        @block.vector
        def _(v):
            v.drain()
            v.memset(zsc[:, :], 0.0)
            v.drain().then_inc(cV.h, 1)
            cV.bump()
        c_v5b = cV.n

        VS, AS_, PS_ = 66, 21, 42
        c_p2, c_v6, c_a5 = cP.n, cV.n, cA.n
        for ts2 in range(NSL):
            pv0 = c_p2 + PS_ * ts2
            vv0 = c_v6 + VS * ts2
            av0 = c_a5 + AS_ * ts2
            xfpair = [x3f, x3f2]

            @block.vector
            def _(v):
                v.drain()
                if ts2 == 0:
                    v.wait_ge(cV.h, c_v5b)
                    v.wait_ge(cA.h, c_a4)
                for tc in range(4):
                    if ts2 > 0 or tc >= 2:
                        nfull = 2 * ts2 + tc // 2
                        v.wait_ge(tH2.s[tc % 2].h, 128 * nfull)
                    c = 4 * ts2 + tc
                    v.bn_stats(stat[:, 0:6], XBF[:, c, 0:512])
                    v.bn_stats(stat[:, 6:12], XBF[:, c, 512:1024])
                    v.drain()
                    v.bn_aggr(stat[:, 12:14], stat[:, 0:12])
                    v.drain().then_inc(cV.h, 1)
                    cV.bump()
                    v.wait_ge(cA.h, av0 + tc + 1)
                    v.reciprocal(stat[:, 17:18], stat[:, 16:17])
                    v.drain()
                    cc = v.tensor_scalar(h2tm[tc % 2], XBF[:, c, :], stat[:, 12:13],
                                         stat[:, 17:18], ALU.subtract, ALU.mult)
                    cV.inc(cc)
                for c in range(8):
                    v.wait_ge(cP.h, pv0 + c + 1)
                    bank = PB[c][:, :] if c < 4 else PSQ[:, 512 * (c - 4):512 * (c - 3)]
                    cc = v.tensor_copy(asl[:, c, :], bank)
                    cV.inc(cc)
                for c in range(8):
                    v.wait_ge(cP.h, pv0 + 8 + c + 1)
                    bank = PB[c][:, :] if c < 4 else PSQ[:, 512 * (c - 4):512 * (c - 3)]
                    cc = v.tensor_copy(bsl[:, c, :], bank)
                    cV.inc(cc)
                if ts2 >= 1:
                    v.wait_ge(qX3.s[0].h, 32 * ts2)
                    v.wait_ge(qX3.s[1].h, 32 * ts2)
                for c in range(8):
                    v.drain()
                    v.tensor_scalar(x3f[:, 0:512],
                                    asl[:, c, :], ccol[:, c, 7:8], None, ALU.mult)
                    v.tensor_scalar(x3f[:, 512:1024], bsl[:, c, :], ccol[:, c, 7:8],
                                    None, ALU.mult)
                    v.drain()
                    v.scalar_tensor_tensor(asl[:, c, :], asl[:, c, :], ccol[:, c, 6:7],
                                           x3f[:, 512:1024], ALU.mult, ALU.add)
                    v.drain()
                    cc = v.scalar_tensor_tensor(bsl[:, c, :], bsl[:, c, :], ccol[:, c, 6:7],
                                                x3f[:, 0:512], ALU.mult, ALU.subtract)
                    cV.inc(cc)
                v.wait_ge(cA.h, av0 + 13)
                v.drain()
                cc = v.tensor_single_scalar(bitsT[0:24, :], latb[0:24, :], 0.0, ALU.is_gt)
                cV.inc(cc)
                v.wait_ge(cP.h, pv0 + 26)
                cc = v.tensor_single_scalar(synb[0:12, :], PB[1][0:12, :], 0.5, ALU.is_gt)
                cV.inc(cc)
                v.wait_ge(cP.h, pv0 + 27)
                v.drain()
                v.tensor_single_scalar(latb[0:1, :], PB[2][0:1, :], 0.5, ALU.is_gt)
                v.tensor_single_scalar(x3f[0:1, 0:512], PB[2][0:1, :], 3.5, ALU.is_lt)
                v.drain()
                cc = v.tensor_tensor(fixb[0:1, :], latb[0:1, :], x3f[0:1, 0:512], ALU.mult)
                cV.inc(cc)
                v.wait_ge(cP.h, pv0 + 28)
                v.drain()
                v.tensor_tensor(x3f[0:12, 0:512], bitsT[0:12, :], synb[0:12, :], ALU.mult)
                v.drain()
                v.scalar_tensor_tensor(x3f[0:12, 0:512], x3f[0:12, 0:512], -2.0,
                                       synb[0:12, :], ALU.mult, ALU.add)
                v.drain()
                v.tensor_tensor(synb[0:12, :], x3f[0:12, 0:512], PB[3][0:12, :], ALU.mult)
                v.drain()
                v.tensor_copy(corT[0:24, :], bitsT[0:24, :])
                v.drain()
                cc = v.tensor_tensor(corT[0:12, :], corT[0:12, :], synb[0:12, :], ALU.add)
                cV.inc(cc)
                for c in range(8):
                    v.wait_ge(cP.h, pv0 + 28 + c + 1)
                    bank = PB[c][:, :] if c < 4 else PSQ[:, 512 * (c - 4):512 * (c - 3)]
                    cc = v.scalar_tensor_tensor(o2T[:, c, :], bank, blec[:, c, 0:1],
                                                x1T[:, c, :], ALU.add, ALU.add)
                    cV.inc(cc)
                for tc in range(4):
                    i2 = 4 * ts2 + tc
                    v.wait_ge(tO2.s[tc % 2].h, 128 * (2 * ts2 + tc // 2 + 1))
                    if tc == 0 and ts2 >= 1:
                        v.wait_ge(tH3.s[(ts2 - 1) % 2].h,
                                  512 * ((ts2 - 1) // 2 + 1))
                    if i2 >= 2:
                        qX3.wait(v, i2, 16 * ((i2 - 2) // 2 + 1))
                    v.drain()
                    v.tensor_tensor(xfpair[tc % 2], XBF[:, 4 * ts2 + tc, :],
                                    o2tm[tc % 2], ALU.add)
                    v.drain()
                    v.bn_stats(stat[:, 0:6], xfpair[tc % 2][:, 0:512])
                    v.bn_stats(stat[:, 6:12], xfpair[tc % 2][:, 512:1024])
                    v.drain()
                    v.bn_aggr(stat[:, 12:14], stat[:, 0:12])
                    v.drain().then_inc(cV.h, 1)
                    cV.bump()
                    v.drain().then_inc(cV.h, 1)
                    cV.bump()
                    v.wait_ge(cA.h, av0 + 14 + tc)
                    v.reciprocal(stat[:, 17:18], stat[:, 16:17])
                    v.drain()
                    cc = v.tensor_scalar(h3st[:, tc, :], xfpair[tc % 2], stat[:, 12:13],
                                         stat[:, 17:18], ALU.subtract, ALU.mult)
                    cV.inc(cc)
                v.wait_ge(cP.h, pv0 + 37)
                cc = v.tensor_copy(xrg[0:16, :], PB[0][0:16, :])
                cV.inc(cc)
                v.wait_ge(cP.h, pv0 + 38)
                cc = v.tensor_scalar(logf[0:8, :], PB[1][0:8, :], rnrm[0:8, 0:1],
                                     None, ALU.mult)
                cV.inc(cc)
                for tc in range(4):
                    bank = PB[2] if tc % 2 == 0 else PB[3]
                    v.wait_ge(cP.h, pv0 + 38 + tc + 1)
                    v.drain()
                    v.reduce_max(gsc2[:, 0:1], bank[:, 0:8], axis=AX.X)
                    v.drain()
                    cc = v.tensor_scalar(gsc2[:, 1:2], gsc2[:, 0:1], -1.0, None, ALU.mult)
                    cV.inc(cc)
                    v.wait_ge(cA.h, av0 + 17 + tc + 1)
                    v.drain()
                    v.reciprocal(gsc2[:, 3:4], gsc2[:, 2:3])
                    v.tensor_scalar(gsc[:, :], bank[:, 0:8], gsc2[:, 0:1], None, ALU.is_ge)
                    v.drain()
                    v.tensor_tensor_scan(gsc3[:, 0:8], gsc[:, :], zsc[:, :], 0.0,
                                         ALU.add, ALU.add)
                    v.drain()
                    v.tensor_single_scalar(gsc3[:, 8:16], gsc3[:, 0:8], 1.0, ALU.is_equal)
                    v.drain()
                    v.tensor_tensor(gsc[:, :], gsc[:, :], gsc3[:, 8:16], ALU.mult)
                    v.drain()
                    cc = v.tensor_scalar(gatG[:, 4 * ts2 + tc, :], gsc[:, :],
                                         gsc2[:, 3:4], None, ALU.mult)
                    cV.inc(cc)

            @block.scalar
            def _(sc):
                for tc in range(4):
                    sc.wait_ge(cV.h, vv0 + 2 * tc + 1)
                    a = sc.activation(stat[:, 16:17], stat[:, 13:14], AF.Sqrt,
                                      bias=epscol[:, :])
                    cA.inc(a)
                    sc.wait_ge(cV.h, vv0 + 2 * tc + 2)
                    if ts2 > 0 and tc == 0:
                        sc.wait_ge(cP.h, pv0 - PS_ + 16)
                    for k in range(8):
                        sc.dma_start_transpose(
                            h2T[:, k, 128 * tc:128 * (tc + 1)],
                            h2tm[tc % 2][:, 128 * k:128 * (k + 1)]).then_inc(tH2.s[tc % 2].h, 16)
                        tH2.s[tc % 2].bump()
                for c in range(8):
                    sc.wait_ge(cP.h, pv0 + 16 + c + 1)
                    bank = PB[c][:, :] if c < 4 else PSQ[:, 512 * (c - 4):512 * (c - 3)]
                    a = sc.activation(x1T[:, c, :], bank, AF.Copy, scale=float(2.0 ** -10))
                    cA.inc(a)
                sc.wait_ge(cP.h, pv0 + 25)
                a = sc.activation(latb[0:24, :], PB[0][0:24, :], AF.Identity,
                                  bias=belc[0:24, :])
                cA.inc(a)
                for tc in range(4):
                    sc.wait_ge(cV.h, vv0 + 44)
                    if tc >= 2:
                        sc.wait_ge(cV.h, vv0 + 44 + 3 * (tc - 2) + 1)
                    for k in range(8):
                        sc.dma_start_transpose(
                            o2tm[tc % 2][:, 128 * k:128 * (k + 1)],
                            o2T[:, k, 128 * tc:128 * (tc + 1)]).then_inc(tO2.s[tc % 2].h, 16)
                        tO2.s[tc % 2].bump()
                    sc.wait_ge(cV.h, vv0 + 44 + 3 * tc + 2)
                    a = sc.activation(stat[:, 16:17], stat[:, 13:14], AF.Sqrt,
                                      bias=epscol[:, :])
                    cA.inc(a)
                sc.wait_ge(cV.h, vv0 + 56)
                H3Ts = bfa.at(4096 * ts2, [128, 8, 512])
                for tc in range(4):
                    for k in range(8):
                        sc.dma_start_transpose(
                            H3Ts[:, k, 128 * tc:128 * (tc + 1)],
                            h3st[:, tc, 128 * k:128 * (k + 1)]).then_inc(tH3.s[ts2 % 2].h, 16)
                        tH3.s[ts2 % 2].bump()
                for tc in range(4):
                    bank = PB[2] if tc % 2 == 0 else PB[3]
                    sc.wait_ge(cV.h, vv0 + 58 + 2 * tc + 1)
                    a = sc.activation(gsc[:, :], bank[:, 0:8], AF.Exp,
                                      bias=gsc2[:, 1:2], accum_out=gsc2[:, 2:3])
                    cA.inc(a)

            @block.tensor
            def _(t):
                t.wait_ge(tH2.s[0].h, 128 * (2 * ts2 + 2))
                t.wait_ge(tH2.s[1].h, 128 * (2 * ts2 + 2))
                if ts2 == 0:
                    t.wait_ge(sG.h, c_sg1)
                    t.wait_ge(sQ.h, c_q1)
                if ts2 >= 1:
                    t.wait_ge(cV.h, vv0)
                for c in range(8):
                    bank = PB[c][:, :] if c < 4 else PSQ[:, 512 * (c - 4):512 * (c - 3)]
                    for cin in range(8):
                        mm = t.matmul(bank, Cb[:, cin, 128 * c:128 * (c + 1)],
                                      h2T[:, cin, :], start=(cin == 0), stop=(cin == 7))
                    cP.inc(mm)
                for c in range(8):
                    t.wait_ge(cV.h, vv0 + 8 + c + 1)
                    bank = PB[c][:, :] if c < 4 else PSQ[:, 512 * (c - 4):512 * (c - 3)]
                    for cin in range(8):
                        mm = t.matmul(bank, Sb[:, cin, 128 * c:128 * (c + 1)],
                                      h2T[:, cin, :], start=(cin == 0), stop=(cin == 7))
                    cP.inc(mm)
                t.wait_ge(cV.h, vv0 + 32)
                for c in range(8):
                    bank = PB[c][:, :] if c < 4 else PSQ[:, 512 * (c - 4):512 * (c - 3)]
                    for cin in range(8):
                        t.matmul(bank, Cb[:, cin, 128 * c:128 * (c + 1)],
                                 asl[:, cin, :], start=(cin == 0), stop=False)
                    for cin in range(8):
                        mm = t.matmul(bank, Sb[:, cin, 128 * c:128 * (c + 1)],
                                      bsl[:, cin, :], start=False, stop=(cin == 7))
                    cP.inc(mm)
                t.wait_ge(cA.h, av0 + 12)
                for cin in range(8):
                    mm = t.matmul(PB[0][0:24, :], welb[:, cin, :], x1T[:, cin, :],
                                  start=(cin == 0), stop=(cin == 7))
                cP.inc(mm)
                t.wait_ge(cV.h, vv0 + 33)
                mm = t.matmul(PB[1][0:12, :], hmtb[0:24, :], bitsT[0:24, :],
                              start=True, stop=True)
                cP.inc(mm)
                t.wait_ge(cV.h, vv0 + 34)
                mm = t.matmul(PB[2][0:1, :], onescb[0:12, :], synb[0:12, :],
                              start=True, stop=True)
                cP.inc(mm)
                t.wait_ge(cV.h, vv0 + 35)
                mm = t.matmul(PB[3][0:12, :], ones1b[0:1, 0:12], fixb[0:1, :],
                              start=True, stop=True)
                cP.inc(mm)
                t.wait_ge(cV.h, vv0 + 36)
                t.wait_ge(cA.h, av0 + 13)
                for c in range(8):
                    bank = PB[c][:, :] if c < 4 else PSQ[:, 512 * (c - 4):512 * (c - 3)]
                    mm = t.matmul(bank, wleb[0:24, 128 * c:128 * (c + 1)],
                                  corT[0:24, :], start=True, stop=True)
                    cP.inc(mm)
                t.wait_ge(tH3.s[ts2 % 2].h, 512 * (ts2 // 2 + 1))
                t.wait_ge(cV.h, vv0 + 44 + 1)
                H3Ts = bfa.at(4096 * ts2, [128, 8, 512])
                for cin in range(8):
                    mm = t.matmul(PB[0][0:16, :], wgrb[:, cin, :], H3Ts[:, cin, :],
                                  start=(cin == 0), stop=(cin == 7))
                cP.inc(mm)
                t.wait_ge(cV.h, vv0 + 57)
                mm = t.matmul(PB[1][0:8, :], wgtb[0:16, :], xrg[0:16, :],
                              start=True, stop=True)
                cP.inc(mm)
                t.wait_ge(cV.h, vv0 + 58)
                for tc in range(4):
                    bank = PB[2] if tc % 2 == 0 else PB[3]
                    if tc >= 2:
                        t.wait_ge(cV.h, vv0 + 60 + 2 * (tc - 2))
                    mm = t.transpose(bank[:, 0:8], logf[0:8, 128 * tc:128 * (tc + 1)],
                                     ident[0:8, 0:8])
                    cP.inc(mm)

            @block.sync
            def _(s):
                for tc in range(4):
                    i2 = 4 * ts2 + tc
                    s.wait_ge(cV.h, vv0 + 44 + 3 * tc + 1)
                    r0 = 512 * ts2 + 128 * tc
                    qX3.inc(s.dma_start(out=out_hbm[r0:r0 + 128, :],
                                        in_=xfpair[tc % 2]), i2)

        assert cP.n == c_p2 + PS_ * NSL, (cP.n, c_p2)
        assert cV.n == c_v6 + VS * NSL, (cV.n, c_v6)
        assert cA.n == c_a5 + AS_ * NSL, (cA.n, c_a5)
        bfa.pop()
        fpa.pop()
        return qX3

    if stop_after not in ("mha0", "mha1", "ffn0", "rwkv"):
        qX3g = emit_fft()


    def emit_moe():
        bfa.push()
        fpa.push()
        w1hb = bfa.alloc([128, 8, 2048])
        w2hb = bfa.alloc([128, 16, 1024])
        Hh = bfa.alloc([128, 16, 512])
        b2rb = bfa.alloc([1, 1024])
        wstg = [fpa.alloc([128, 2048]) for _ in range(2)]
        stgf = [fpa.alloc([128, 512]) for _ in range(4)]
        b1c = fpa.alloc([128, 32, 1])

        qD1 = PSem(es, nc, "qD1")
        qD = [Sem(es, nc, f"qD{i}", 16) for i in range(4)]
        cC0 = Sem(es, nc, "cC0")
        cC1 = Sem(es, nc, "cC1")

        dbar = phase_barrier(taps=[("x3c", out_hbm[:, :]),
                                   ("gatGd", gatG[:, :, :])])

        d_p0, d_v0, d_a0 = cP.n, cV.n, cA.n
        d_q0 = sQ.n
        d_sg0 = sG.n
        PR, VR, AR = 160, 64, 128
        nacc = [0]

        def w2bank(tc, ch):
            if tc < 2:
                return PB[2 * tc + ch][:, :]
            if tc == 3:
                return PB[ch][:, :]
            return PSQ[:, 1024 + 512 * ch:1536 + 512 * ch]

        for e in range(NE):
            @block.sync
            def _(s):
                if e == 0:
                    s.wait_ge(cB.h, dbar)
                else:
                    s.wait_ge(cA.h, d_a0 + AR * 2 * e)
                for k in range(32):
                    s.dma_start(out=b1c[:, k, :],
                                in_=b1m_in[e, 128 * k:128 * (k + 1)].unsqueeze(1)).then_inc(sQ.h, 16)
                sQ.bump(32)

            @block.gpsimd
            def _(g):
                if e == 0:
                    g.wait_ge(cB.h, dbar)
                else:
                    g.wait_ge(cP.h, d_p0 + PR * 2 * e)
                g.dma_start(out=b2rb[0:1, :], in_=b2m_in[e, :].unsqueeze(0)).then_inc(sG.h, 16)
                sG.bump()

            for hh in range(2):
                r = 2 * e + hh
                rp = d_p0 + PR * r
                rv = d_v0 + VR * r
                ra = d_a0 + AR * r

                @block.sync
                def _(s):
                    if e == 0 and hh == 0:
                        s.wait_ge(cB.h, dbar)
                    for i in range(24):
                        gi = 24 * r + i
                        if gi >= 2:
                            s.wait_ge(cC0.h if gi % 2 == 0 else cC1.h, gi // 2)
                        if i < 8:
                            dd = s.dma_start(
                                out=wstg[gi % 2],
                                in_=w1m_in[e, 128 * i:128 * (i + 1),
                                           2048 * hh:2048 * (hh + 1)])
                        else:
                            j = i - 8
                            dd = s.dma_start(
                                out=wstg[gi % 2][:, 0:1024],
                                in_=w2m_in[e, 2048 * hh + 128 * j:2048 * hh + 128 * (j + 1), :])
                        qD1.inc(dd, gi)

                @block.tensor
                def _(t):
                    t.wait_ge(cC0.h, 12 * r + 4)
                    t.wait_ge(cC1.h, 12 * r + 4)
                    if hh == 0:
                        t.wait_ge(sG.h, d_sg0 + 16 * (e + 1))
                    if r >= 1:
                        t.wait_ge(cV.h, d_v0 + VR * r)
                    for ts3 in range(NSL):
                        tsp = rp + 20 * ts3
                        H3Ts = bfa.at(4096 * ts3, [128, 8, 512])
                        for hc in range(16):
                            if ts3 > 0 or hc >= 2:
                                gidx = 16 * ts3 + hc - 2
                                t.wait_ge(cA.h, ra + gidx + 1)
                            bank = PSQ[:, 512 * (hc % 2):512 * (hc % 2 + 1)]
                            for cin in range(8):
                                mm = t.matmul(bank, w1hb[:, cin, 128 * hc:128 * (hc + 1)],
                                              H3Ts[:, cin, :], start=(cin == 0), stop=(cin == 7))
                            cP.inc(mm)
                        t.wait_ge(cA.h, ra + 16 * (ts3 + 1))
                        if ts3 == 0:
                            t.wait_ge(cC0.h, 12 * (r + 1))
                            t.wait_ge(cC1.h, 12 * (r + 1))
                        for tc in range(4):
                            if tc == 3:
                                t.wait_ge(cV.h, rv + 8 * ts3 + 2)
                            elif ts3 >= 1:
                                prev = rv + 8 * (ts3 - 1)
                                t.wait_ge(cV.h, prev + (8, 4, 6)[tc])
                            for hc in range(16):
                                for ch in range(2):
                                    mm = t.matmul(w2bank(tc, ch),
                                                  Hh[:, hc, 128 * tc:128 * (tc + 1)],
                                                  w2hb[:, hc, 512 * ch:512 * (ch + 1)],
                                                  start=(hc == 0),
                                                  stop=(hh == 1 and hc == 15))
                            if hh == 0:
                                for ch in range(2):
                                    mm = t.matmul(w2bank(tc, ch), ones1b[0:1, 0:128],
                                                  b2rb[0:1, 512 * ch:512 * (ch + 1)],
                                                  start=False, stop=True)
                            cP.inc(mm)

                @block.scalar
                def _(sc):
                    if r == 0:
                        sc.wait_ge(cB.h, dbar)
                    else:
                        sc.wait_ge(cP.h, d_p0 + PR * r)
                    for i in range(12):
                        gi = 24 * r + 2 * i
                        ii = 2 * i
                        qD1.wait(sc, gi, 16 * (gi // 2 + 1))
                        if ii < 8:
                            a = sc.activation(w1hb[:, ii, :], wstg[0],
                                              AF.Identity, bias=0.0, scale=1.0)
                        else:
                            a = sc.activation(w2hb[:, ii - 8, :], wstg[0][:, 0:1024],
                                              AF.Identity, bias=0.0, scale=1.0)
                        cC0.inc(a)
                    for ts3 in range(NSL):
                        tsp = rp + 20 * ts3
                        if ts3 == 0 and hh == 0:
                            sc.wait_ge(sQ.h, d_q0 + 32 * 16 * (e + 1))
                        for hc in range(16):
                            sc.wait_ge(cP.h, tsp + hc + 1)
                            bank = PSQ[:, 512 * (hc % 2):512 * (hc % 2 + 1)]
                            a = sc.activation(Hh[:, hc, :], bank, AF.Gelu,
                                              bias=b1c[:, 16 * hh + hc, :])
                            cA.inc(a)

                @block.vector
                def _(v):
                    v.drain()
                    if r == 0:
                        v.wait_ge(cB.h, dbar)
                    else:
                        v.wait_ge(cP.h, d_p0 + PR * r)
                    for i in range(12):
                        gi = 24 * r + 2 * i + 1
                        ii = 2 * i + 1
                        qD1.wait(v, gi, 16 * (gi // 2 + 1))
                        if ii < 8:
                            cc = v.tensor_copy(w1hb[:, ii, :], wstg[1])
                        else:
                            cc = v.tensor_copy(w2hb[:, ii - 8, :], wstg[1][:, 0:1024])
                        cC1.inc(cc)
                    v.drain()
                    for ts3 in range(NSL):
                        tsp = rp + 20 * ts3
                        for tc in range(4):
                            v.wait_ge(cP.h, tsp + 16 + tc + 1)
                            for ch in range(2):
                                g8 = 2 * tc + ch
                                i4 = nacc[0] + 8 * ts3 + g8
                                if i4 >= 4:
                                    v.wait_ge(qD[i4 % 4].h, 16 * ((i4 - 4) // 4 + 1))
                                cc = v.tensor_scalar(stgf[i4 % 4], w2bank(tc, ch),
                                                     gatG[:, 4 * ts3 + tc, e:e + 1],
                                                     None, ALU.mult)
                                cV.inc(cc)

                @block.gpsimd
                def _(g):
                    for ts3 in range(NSL):
                        for tc in range(4):
                            for ch in range(2):
                                g8 = 2 * tc + ch
                                i4 = nacc[0] + 8 * ts3 + g8
                                g.wait_ge(cV.h, rv + 8 * ts3 + g8 + 1)
                                if i4 >= 4:
                                    g.wait_ge(qD[i4 % 4].h, 16 * ((i4 - 4) // 4 + 1))
                                r0 = 512 * ts3 + 128 * tc
                                dd = g.dma_start(out=out_hbm[r0:r0 + 128,
                                                             512 * ch:512 * (ch + 1)],
                                                 in_=stgf[i4 % 4],
                                                 accum_op=ALU.add)
                                dd.then_inc(qD[i4 % 4].h, 16)
                                qD[i4 % 4].bump()
                nacc[0] += 64

        @block.gpsimd
        def _(g):
            for i in range(4):
                g.wait_ge(qD[i].h, qD[i].n)
            g.wait_ge(qX3g.s[0].h, qX3g.s[0].n)
            g.wait_ge(qX3g.s[1].h, qX3g.s[1].n)
        bfa.pop()
        fpa.pop()

    if stop_after not in ("mha0", "mha1", "ffn0", "rwkv", "fft"):
        emit_moe()

    @block.gpsimd
    def _(g):
        g.wait_ge(sQ.h, sQ.n)
        g.wait_ge(sG.h, sG.n)
        g.wait_ge(cV.h, cV.n)
        g.wait_ge(cP.h, cP.n)
        g.wait_ge(cA.h, cA.n)

    es.close()
    return nc, dbg


def _host_inputs(core_inputs):
    return core_inputs


_GOLAY_A = np.array(
    [[1, 1, 1, 0, 1, 1, 1, 0, 0, 0, 1, 1], [1, 1, 0, 1, 1, 1, 0, 0, 0, 1, 1, 0],
     [1, 0, 1, 1, 1, 0, 0, 0, 1, 1, 0, 1], [0, 1, 1, 1, 0, 0, 0, 1, 1, 0, 1, 1],
     [0, 0, 0, 0, 1, 1, 1, 1, 1, 1, 1, 1], [1, 1, 1, 1, 0, 0, 0, 0, 0, 0, 0, 0],
     [0, 0, 0, 0, 1, 1, 1, 1, 0, 0, 0, 0], [1, 1, 0, 0, 1, 1, 0, 0, 1, 1, 0, 0],
     [1, 0, 1, 0, 1, 0, 1, 0, 1, 0, 1, 0], [0, 1, 0, 1, 0, 1, 0, 1, 0, 1, 0, 1],
     [1, 0, 0, 1, 1, 0, 0, 1, 1, 0, 0, 1], [0, 1, 1, 0, 0, 1, 1, 0, 0, 1, 1, 0]],
    dtype=np.float32)
_HMT = np.concatenate([_GOLAY_A.T, np.eye(12, dtype=np.float32)], axis=1).T.copy()


def host_maps(inputs):
    f = np.float32
    def T(a):
        return np.ascontiguousarray(np.asarray(a, dtype=f).T)
    def C(a):
        return np.ascontiguousarray(np.asarray(a, dtype=f))
    common = {
        "wq1": T(inputs["L0_attn_wq1"]), "wk1": T(inputs["L0_attn_wk1"]),
        "wv1": T(inputs["L0_attn_wv1"]), "wo1": T(inputs["L0_attn_wo1"]),
        "wq2": T(inputs["L0_attn_wq2"]), "wk2": T(inputs["L0_attn_wk2"]),
        "wv2": T(inputs["L0_attn_wv2"]), "wo2": T(inputs["L0_attn_wo2"]),
        "bq1": C(inputs["L0_attn_bq1"]), "bk1": C(inputs["L0_attn_bk1"]),
        "bv1": C(inputs["L0_attn_bv1"]), "bo1": C(inputs["L0_attn_bo1"]),
        "bq2": C(inputs["L0_attn_bq2"]), "bk2": C(inputs["L0_attn_bk2"]),
        "bv2": C(inputs["L0_attn_bv2"]), "bo2": C(inputs["L0_attn_bo2"]),
        "g1": C(inputs["L0_attn_g1"]), "be1": C(inputs["L0_attn_be1"]),
        "g2": C(inputs["L0_attn_g2"]), "be2": C(inputs["L0_attn_be2"]),
        "wff1": T(inputs["L0_attn_w_ff1"]), "bff1": C(inputs["L0_attn_b_ff1"]),
        "wff2": T(inputs["L0_attn_w_ff2"]), "bff2": C(inputs["L0_attn_b_ff2"]),
        "gff": C(inputs["L0_attn_g_ff"]), "beff": C(inputs["L0_attn_be_ff"]),
        "wr": T(inputs["L0_ffn_W_r"]), "wkk": T(inputs["L0_ffn_W_k"]),
        "wvv": T(inputs["L0_ffn_W_v"]),
        "mur": C(inputs["L0_ffn_mu_r"]), "muk": C(inputs["L0_ffn_mu_k"]),
        "wel": C(inputs["L1_attn_W_el"]), "bel": C(inputs["L1_attn_b_el"]),
        "wle": C(inputs["L1_attn_W_le"]), "ble": C(inputs["L1_attn_b_le"]),
        "hmt": np.ascontiguousarray(_HMT),
        "wgred": T(inputs["L1_ffn_wg_red"]), "wgm": C(inputs["L1_ffn_wg"]),
        "wgt": T(inputs["L1_ffn_wg"]),
        "b1m": C(inputs["L1_ffn_b1"]), "b2m": C(inputs["L1_ffn_b2"]),
        "w1m": C(inputs["L1_ffn_w1"]), "w2m": C(inputs["L1_ffn_w2"]),
        "ident": np.eye(128, dtype=f),
    }
    x = np.asarray(inputs["L0_attn_x"], dtype=f)
    return [dict(common, x=np.ascontiguousarray(x[b])) for b in range(x.shape[0])]


def kernel(**inputs):
    maps = host_maps(inputs)
    nc, _ = build_program()
    res = run_bass_kernel_spmd(nc, maps, core_ids=list(range(len(maps))))
    return np.stack([np.asarray(r["out"], dtype=np.float32) for r in res.results])
